# version 06
# baseline (speedup 1.0000x reference)
import contextlib
import numpy as np
import ml_dtypes

import concourse.bass as bass
import concourse.mybir as mybir
from concourse.bass_utils import run_bass_kernel_spmd

F32 = mybir.dt.float32
BF16 = mybir.dt.bfloat16
AF = mybir.ActivationFunctionType
ALU = mybir.AluOpType
AX = mybir.AxisListType.X
BF = ml_dtypes.bfloat16

ENGINES = ("pe", "act", "dve", "pool", "sp")

D = 2048
NKC = 16
TT = 512
NCORE = 8


class Res:
    __slots__ = ("name", "writer", "readers")

    def __init__(self, name):
        self.name = name
        self.writer = None
        self.readers = []


class _Op:
    __slots__ = ("eng", "fn", "deps", "dma", "tag", "idx", "inc", "sem",
                 "semval", "ctr")

    def __init__(self, eng, fn, deps, dma, tag, idx, ctr):
        self.eng, self.fn, self.deps, self.dma, self.tag, self.idx = (
            eng, fn, deps, dma, tag, idx)
        self.inc = False
        self.sem = None
        self.semval = 0
        self.ctr = ctr


class Plan:
    def __init__(self, nc):
        self.nc = nc
        self.ops = {e: [] for e in ENGINES}
        self.ctr = 0

    def op(self, eng, fn, reads=(), writes=(), deps=(), dma=False, tag=""):
        dd = list(deps)
        for r in reads:
            if r.writer is not None:
                dd.append(r.writer)
        for w in writes:
            if w.writer is not None:
                dd.append(w.writer)
            dd.extend(w.readers)
        idx = len(self.ops[eng])
        tok = (eng, idx)
        dd = sorted({t for t in dd if not (t[0] == eng and eng == "pe")})
        self.ctr += 1
        self.ops[eng].append(_Op(eng, fn, dd, dma, tag, idx, self.ctr))
        for r in reads:
            r.readers.append(tok)
        for w in writes:
            w.writer = tok
            w.readers = []
        return tok

    def barrier(self, deps=()):
        tok = self.op("pool", None, deps=list(deps), tag="BARRIER")
        self.ops["pool"][tok[1]].dma = "BARRIER"
        return tok

    def local_join(self, deps=()):
        tok = self.op("pool", None, deps=list(deps), tag="JOIN")
        self.ops["pool"][tok[1]].dma = "JOIN"
        return tok

    def op_sw(self, eng, fn_p, reads=(), writes=(), deps=(), n_dma=1, tag=""):
        tok = self.op(eng, fn_p, reads=reads, writes=writes, deps=deps,
                      dma=True, tag=tag)
        o = self.ops[eng][tok[1]]
        o.dma = ("SW", n_dma)
        return tok

    def finalize(self, stack, final_deps, n_sp_sems=14, n_pool_sems=8, n_act_sems=2):
        nc = self.nc
        self.op("pool", lambda g: g.nop(), deps=list(final_deps), tag="final_wait")
        for e in ENGINES:
            for o in self.ops[e]:
                if o.dma and not isinstance(o.dma, str):
                    o.inc = True
                for (de, di) in o.deps:
                    self.ops[de][di].inc = True
        eng_sem = {e: stack.enter_context(nc.semaphore(f"sem_{e}"))
                   for e in ("pe", "act", "dve", "pool", "sp")}
        dma_sems = {
            "sp": [stack.enter_context(nc.semaphore(f"dsp{i}")) for i in range(n_sp_sems)],
            "pool": [stack.enter_context(nc.semaphore(f"dpl{i}")) for i in range(n_pool_sems)],
            "act": [stack.enter_context(nc.semaphore(f"dac{i}")) for i in range(n_act_sems)],
        }
        order = sorted(((o.ctr, e, o) for e in ENGINES for o in self.ops[e]),
                       key=lambda x: x[0])
        cum = {e: 0 for e in ENGINES}
        dma_rr = {q: 0 for q in dma_sems}
        dma_use = {q: [0] * len(dma_sems[q]) for q in dma_sems}
        dma_serial = {}
        bar_snap = {}
        for _, e, o in order:
            if o.dma in ("BARRIER", "JOIN"):
                snap = []
                for q in dma_sems:
                    for i, u in enumerate(dma_use[q]):
                        if u > 0:
                            snap.append((dma_sems[q][i], 16 * u))
                bar_snap[(e, o.idx)] = snap
                if o.inc:
                    cum["pool"] += 1
                    o.sem = eng_sem["pool"]
                    o.semval = cum["pool"]
                continue
            if not o.inc:
                continue
            if o.dma:
                q = e
                n_dma = o.dma[1] if isinstance(o.dma, tuple) else 1
                i = dma_rr[q] % len(dma_sems[q])
                dma_rr[q] += 1
                prev = dma_use[q][i]
                dma_use[q][i] += n_dma
                o.sem = dma_sems[q][i]
                o.semval = 16 * dma_use[q][i]
                if prev > 0:
                    dma_serial[(e, o.idx)] = (o.sem, 16 * prev)
            else:
                cum[e] += 1
                o.sem = eng_sem[e]
                o.semval = cum[e]
        engobjs = {"pe": nc.tensor, "act": nc.scalar, "dve": nc.vector,
                   "pool": nc.gpsimd, "sp": nc.sync}
        waited = {e: {} for e in ENGINES}

        def wait(e, sem, val):
            key = id(sem)
            if waited[e].get(key, 0) >= val:
                return
            engobjs[e].wait_ge(sem, val)
            waited[e][key] = val

        for _, e, o in order:
            if o.dma in ("BARRIER", "JOIN"):
                for (de, di) in o.deps:
                    dep = self.ops[de][di]
                    wait("pool", dep.sem, dep.semval)
                for (sem, val) in bar_snap[(e, o.idx)]:
                    wait("pool", sem, val)
                nc.all_core_barrier()
                floors = {}
                for (sem, val) in bar_snap[(e, o.idx)]:
                    floors[id(sem)] = val
                for e2 in ENGINES:
                    for o2 in self.ops[e2]:
                        if o2.inc and o2.ctr < o.ctr and o2.sem is not None                                 and not isinstance(o2.dma, str):
                            k2 = id(o2.sem)
                            floors[k2] = max(floors.get(k2, 0), o2.semval)
                for ee in ENGINES:
                    for k2, v2 in floors.items():
                        waited[ee][k2] = max(waited[ee].get(k2, 0), v2)
                if o.inc:
                    engobjs["pool"].nop().then_inc(o.sem, 1)
                continue
            for (de, di) in o.deps:
                dep = self.ops[de][di]
                wait(e, dep.sem, dep.semval)
            ser = dma_serial.get((e, o.idx))
            if ser is not None:
                wait(e, ser[0], ser[1])
            if isinstance(o.dma, tuple):
                sem_ = o.sem

                def attach(inst, sem_=sem_):
                    inst.then_inc(sem_, 16)
                o.fn(engobjs[e], attach)
            else:
                inst = o.fn(engobjs[e])
                if o.inc:
                    assert inst is not None, f"{o.tag} marked, fn returned None"
                    inst.then_inc(o.sem, 16 if o.dma else 1)

class K:
    def __init__(self, nc, plan, stack):
        self.nc = nc
        self.plan = plan
        self.stack = stack
        self.psum_f = [stack.enter_context(nc.psum_tensor(f"psf{i}", [128, 512], F32))
                       for i in range(6)]
        self.psum_fr = [Res(f"psf{i}") for i in range(6)]
        self.psum_b = [stack.enter_context(nc.psum_tensor(f"psb{i}", [128, 1024], BF16))
                       for i in range(2)]
        self.psum_br = [Res(f"psb{i}") for i in range(2)]
        self._sb = {}
        self._gen = 0
        self.cst = None
        self.store_toks = []

    def sbuf(self, name, shape, dtype):
        if name not in self._sb:
            uname = name if self._gen == 0 else f"{name}__g{self._gen}"
            t = self.stack.enter_context(self.nc.sbuf_tensor(uname, shape, dtype))
            self._sb[name] = (t.ap(), Res(uname))
        return self._sb[name]

    @contextlib.contextmanager
    def phase(self):
        saved = set(self._sb)
        self._gen += 1
        outer = self.stack
        with contextlib.ExitStack() as st:
            self.stack = st
            try:
                yield
            finally:
                self.stack = outer
                self.plan.local_join()
        for k in list(self._sb):
            if k not in saved:
                del self._sb[k]


CST_ID = slice(0, 128)
CST_ONES = slice(128, 256)
CST_NEG = slice(256, 384)


def host_cst():
    c = np.zeros((128, 512), np.float32)
    c[:, 0:128] = np.eye(128)
    c[:, 128:256] = 1.0
    c[:, 256:384] = -1.0
    return c.astype(BF)


def e_dma(K_, q, out_ap, in_ap, reads=(), writes=(), deps=(), tag="", **kw):
    return K_.plan.op(q, lambda e: e.dma_start(out=out_ap, in_=in_ap, **kw),
                      reads=reads, writes=writes, deps=deps, dma=True, tag=tag)


def e_xbar(K_, out_ap, in_ap, reads=(), writes=(), deps=(), tag=""):
    return K_.plan.op("sp", lambda e: e.dma_start(out=out_ap, in_=in_ap, transpose=True),
                      reads=reads, writes=writes, deps=deps, dma=True, tag=tag)


def cast_pass(K_, dst_bf_ap, src_f32_ap, rows, cols, res=None, tag="cast"):
    toks = []
    blk = max(128, min(rows, (8 << 20) // max(1, cols * 4)))
    r = 0
    while r < rows:
        n = min(blk, rows - r)
        toks.append(e_dma(K_, "pool", dst_bf_ap[r:r + n, :], src_f32_ap[r:r + n, :],
                          tag=f"{tag}:{r}"))
        r += n
    return toks



def ln_fm(K_, *, name, n_tok, x_at, y_sink, gains=None, in_bf16=False,
          gain_col=None, gain_res=None, gain_dep=None):
    nc, plan = K_.nc, K_.plan
    cst, cst_r = K_.cst
    nt = n_tok // TT
    sq, sq_r = K_.sbuf("ln_sq", [128, 2 * TT], BF16)
    xb, xb_r = K_.sbuf("ln_xb", [128, 2 * TT], BF16)
    rows, rows_r = K_.sbuf("ln_rows", [1, 5 * TT], F32)
    brow, brow_r = K_.sbuf("ln_brow", [1, 2 * TT], BF16)
    rB, rB_r = K_.sbuf("ln_rB", [128, TT], BF16)
    shB, shB_r = K_.sbuf("ln_shB", [128, TT], BF16)
    out_toks = []
    for t in range(nt):
        ps_s, ps_s_r = K_.psum_f[4], K_.psum_fr[4]
        ps_q, ps_q_r = K_.psum_f[5], K_.psum_fr[5]
        for k in range(NKC):
            x_ap, x_reads = x_at(k, t)
            par = k % 2
            sqa = sq[:, par * TT:(par + 1) * TT]
            tsq = plan.op("act", lambda e, o=sqa, i=x_ap: e.activation(o, i, AF.Square),
                          reads=list(x_reads), writes=[sq_r], tag=f"lnsq{name}")
            if in_bf16:
                xba = x_ap
                xba_r = list(x_reads)
                txb = None
            else:
                xba = xb[:, par * TT:(par + 1) * TT]
                xba_r = [xb_r]
                txb = plan.op("pool", lambda e, o=xba, i=x_ap: e.tensor_copy(o, i),
                              reads=list(x_reads), writes=[xb_r], tag=f"lnxb{name}")
            st, sp = (k == 0), (k == NKC - 1)
            plan.op("pe", lambda e, r=xba, s=st, p=sp: e.matmul(
                ps_s[0:1, :], cst[:, 128:129], r, start=s, stop=p),
                reads=xba_r + [cst_r], writes=[ps_s_r],
                deps=([txb] if txb else []), tag=f"lnS{name}")
            plan.op("pe", lambda e, r=sqa, s=st, p=sp: e.matmul(
                ps_q[0:1, :], cst[:, 128:129], r, start=s, stop=p),
                reads=[sq_r, cst_r], writes=[ps_q_r], deps=[tsq], tag=f"lnQ{name}")
        m = rows[0:1, 0 * TT:1 * TT]
        q2 = rows[0:1, 1 * TT:2 * TT]
        m2 = rows[0:1, 2 * TT:3 * TT]
        var = q2
        sd = m2
        rr = rows[0:1, 3 * TT:4 * TT]
        sh = rows[0:1, 4 * TT:5 * TT]
        plan.op("dve", lambda e: e.tensor_scalar_mul(m, ps_s[0:1, :], 1.0 / D),
                reads=[ps_s_r], writes=[rows_r], tag="ln_m")
        plan.op("dve", lambda e: e.tensor_scalar_mul(q2, ps_q[0:1, :], 1.0 / D),
                reads=[ps_q_r], writes=[rows_r], tag="ln_q")
        plan.op("dve", lambda e: e.tensor_mul(m2, m, m), reads=[rows_r],
                writes=[rows_r], tag="ln_m2")
        plan.op("dve", lambda e: e.tensor_sub(var, q2, m2), reads=[rows_r],
                writes=[rows_r], tag="ln_var")
        plan.op("dve", lambda e: e.tensor_scalar_add(var, var, 1e-5), reads=[rows_r],
                writes=[rows_r], tag="ln_vare")
        tsd = plan.op("act", lambda e: e.activation(sd, var, AF.Sqrt),
                      reads=[rows_r], writes=[rows_r], tag="ln_sd")
        trr = plan.op("dve", lambda e: e.reciprocal(rr, sd), reads=[rows_r],
                      writes=[rows_r], deps=[tsd], tag="ln_rr")
        tsh = plan.op("dve", lambda e: e.scalar_tensor_tensor(
            sh, m, -1.0, rr, ALU.mult, ALU.mult), reads=[rows_r], writes=[rows_r],
            deps=[trr], tag="ln_sh")
        tb1 = plan.op("dve", lambda e: e.tensor_copy(brow[0:1, 0:TT], rr),
                      reads=[rows_r], writes=[brow_r], deps=[trr], tag="ln_b1")
        tb2 = plan.op("dve", lambda e: e.tensor_copy(brow[0:1, TT:2 * TT], sh),
                      reads=[rows_r], writes=[brow_r], deps=[tsh], tag="ln_b2")
        tbr = plan.op("pe", lambda e: e.matmul(
            ps_s[:, :], cst[0:1, 128:256], brow[0:1, 0:TT], start=True, stop=True),
            reads=[brow_r, cst_r], writes=[ps_s_r], deps=[tb1], tag="ln_brB")
        tbs = plan.op("pe", lambda e: e.matmul(
            ps_q[:, :], cst[0:1, 128:256], brow[0:1, TT:2 * TT], start=True, stop=True),
            reads=[brow_r, cst_r], writes=[ps_q_r], deps=[tb2], tag="ln_bsB")
        te1 = plan.op("dve", lambda e: e.tensor_copy(rB[:, :], ps_s[:, :]),
                      reads=[ps_s_r], writes=[rB_r], deps=[tbr], tag="ln_erB")
        te2 = plan.op("dve", lambda e: e.tensor_copy(shB[:, :], ps_q[:, :]),
                      reads=[ps_q_r], writes=[shB_r], deps=[tbs], tag="ln_eshB")
        tmp, tmp_r = K_.sbuf("ln_tmp", [128, 2 * TT], F32)
        for k in range(NKC):
            x_ap, x_reads = x_at(k, t)
            par = k % 2
            tm = tmp[:, par * TT:(par + 1) * TT]
            tmul = plan.op("pool", lambda e, o=tm, i=x_ap: e.tensor_mul(o, i, rB[:, :]),
                           reads=list(x_reads) + [rB_r], writes=[tmp_r],
                           deps=[te1], tag=f"ln_mul{k}")
            if gain_col is None:
                def mk_emit(tm_=tm, tmul_=tmul):
                    def emit(out_ap, extra_writes):
                        return plan.op("dve", lambda e: e.tensor_add(out_ap, tm_, shB[:, :]),
                                       reads=[tmp_r, shB_r], writes=list(extra_writes),
                                       deps=[tmul_, te2], tag=f"ln_y{k}")
                    return emit
            else:
                tadd = plan.op("dve", lambda e, tm_=tm: e.tensor_add(tm_, tm_, shB[:, :]),
                               reads=[shB_r], writes=[tmp_r], deps=[tmul, te2],
                               tag=f"ln_ya{k}")
                def mk_emit(tm_=tm, tadd_=tadd, k_=k):
                    def emit(out_ap, extra_writes):
                        return plan.op("dve", lambda e: e.tensor_scalar(
                            out_ap, tm_, gain_col(k_), None, ALU.mult),
                            reads=[tmp_r, gain_res], writes=list(extra_writes),
                            deps=[tadd_] + ([gain_dep] if gain_dep else []),
                            tag=f"ln_yg{k_}")
                    return emit
            out_toks.append(y_sink(k, t, mk_emit()))
    return out_toks



def proj_fm(K_, *, name, wload, n_k, m_list, t_list, rhs_at, sink, mg=3,
            kc_limit=None):
    plan = K_.plan
    wbuf, wbuf_r = [], []
    for i in range(2):
        a, r = K_.sbuf(f"w_{name}_{i}", [128, mg * 128], BF16)
        wbuf.append(a)
        wbuf_r.append(r)
    nmg = (len(m_list) + mg - 1) // mg
    out = []
    for g in range(nmg):
        ms = m_list[g * mg:(g + 1) * mg]
        banks = list(range(len(ms) * len(t_list)))
        assert len(banks) <= 6
        last = {}
        for ki in range(n_k):
            wb = (g * n_k + ki) % 2
            wt = wload(ki, g, ms, wbuf[wb][:, 0:len(ms) * 128], wbuf_r[wb])
            for ti, t in enumerate(t_list):
                r_ap, r_reads = rhs_at(ki, t)
                rw = r_ap.shape[-1]
                for mi, m in enumerate(ms):
                    bank = mi * len(t_list) + ti
                    ps, ps_r = K_.psum_f[bank], K_.psum_fr[bank]
                    st, sp = (ki == 0), (ki == n_k - 1)
                    tok = plan.op("pe", lambda e, w=wbuf[wb][:, mi * 128:(mi + 1) * 128],
                                  r=r_ap, p=ps, s=st, q=sp, rw=rw: e.matmul(
                                      p[:, 0:rw], w, r, start=s, stop=q),
                                  reads=[wbuf_r[wb]] + list(r_reads),
                                  writes=[K_.psum_fr[bank]], deps=[wt],
                                  tag=f"mm_{name}_{g}_{ki}_{t}_{m}")
                    last[(mi, ti)] = tok
        for mi, m in enumerate(ms):
            for ti, t in enumerate(t_list):
                bank = mi * len(t_list) + ti
                out.append(sink(m, t, K_.psum_f[bank], K_.psum_fr[bank],
                                last[(mi, ti)]))
    return out


def wload_xbarT(K_, dram_bf_nat, deps=(), row_off=0):
    def wl(ki, g, ms, buf_ap, buf_res):
        m0 = ms[0]
        mn = len(ms)
        src = dram_bf_nat[row_off + m0 * 128: row_off + (m0 + mn) * 128,
                          ki * 128:(ki + 1) * 128]
        return e_xbar(K_, buf_ap, src, writes=[buf_res], deps=list(deps), tag="wxT")
    return wl


def wload_nat(K_, dram_f32_nat, deps=(), col_off=0):
    def wl(ki, g, ms, buf_ap, buf_res):
        m0 = ms[0]
        mn = len(ms)
        src = dram_f32_nat[ki * 128:(ki + 1) * 128,
                           col_off + m0 * 128: col_off + (m0 + mn) * 128]
        return e_dma(K_, "pool", buf_ap, src, writes=[buf_res], deps=list(deps),
                     tag="wnat")
    return wl



class Stager:
    def __init__(self, K_, name, width=512, nslots=4, dtype=BF16):
        self.K_ = K_
        self.slots = []
        self.width = width
        for i in range(nslots):
            a, r = K_.sbuf(f"stg_{name}_{i}", [128, width], dtype)
            self.slots.append((a, r))
        self.i = 0

    def store(self, src_ap, src_res, dst_ap, tok=None, w=None, evac=None,
              extra_reads=(), evac_eng="dve", rows=128):
        w = w or self.width
        a, r = self.slots[self.i % len(self.slots)]
        self.i += 1
        deps = [tok] if tok else []
        if evac is None:
            t1 = self.K_.plan.op(evac_eng, lambda e: e.tensor_copy(a[0:rows, 0:w], src_ap),
                                 reads=[src_res] + list(extra_reads), writes=[r],
                                 deps=deps, tag="stg_ev")
        else:
            t1 = self.K_.plan.op(evac_eng, lambda e: evac(e, a[0:rows, 0:w]),
                                 reads=[src_res] + list(extra_reads), writes=[r],
                                 deps=deps, tag="stg_evc")
        return e_dma(self.K_, "sp", dst_ap, a[0:rows, 0:w], reads=[r], tag="stg_st")



def softmax_attn(K_, *, name, n_h, kv_of, n_q, n_kv, scale, Tq,
                 qf, kf, vt, n_kvh,
                 kc_list, bias_tab=None, bias_col_of=None, n_bias_cols=0,
                 mask_tab=None, mask_idx_of=None, n_masks=0,
                 qrow_tab=None,
                 attn_dst=None, attn_dst_r=None, dst_dram=None, stg=None,
                 q_stripes=None, out_stripes=None,
                 dep_toks=()):
    nc, plan = K_.nc, K_.plan
    cst, cst_r = K_.cst
    nqt = n_q // Tq
    nkc = n_kv // 128
    kh, kh_r = K_.sbuf("at_kh", [128, n_kv], BF16)
    qh, qh_r = K_.sbuf("at_qh", [128, n_q], BF16)
    vh, vh_r = K_.sbuf("at_vh", [128, nkc * 128], BF16)
    sqs, sqs_r = K_.sbuf("at_sq", [128, max(n_kv, n_q)], BF16)
    sc, sc_r = [], []
    for i in range(2):
        a, r = K_.sbuf(f"at_sc_{i}", [128, 512], BF16)
        sc.append(a[:, 0:Tq])
        sc_r.append(r)
    rowb0, rowb_r = K_.sbuf("at_row", [1, 512], F32)
    rowb = rowb0[:, 0:Tq]
    rwb2, rwb2_r = K_.sbuf("at_row2", [1, 2], BF16)
    onesf, onesf_r = K_.sbuf("onesf", [1, 128], F32)
    if not getattr(K_, "_onesf_done", False):
        K_._onesf_done = True
        K_.plan.op("dve", lambda e: e.memset(onesf[:, :], 1.0), writes=[onesf_r],
                   tag="onesf")
    rec0, rec_r = K_.sbuf("at_rec", [128, 512], F32)
    rec = rec0[:, 0:Tq]
    bcol0, bcol_r = K_.sbuf("at_bc", [128, 160], F32)
    bcol = bcol0[:, 0:max(1, n_bias_cols)]
    nbc, nbc_r = K_.sbuf("at_nb", [128, 1], F32)
    if mask_tab is not None:
        msk0, msk_r = K_.sbuf("at_msk", [128, n_masks * Tq], BF16)
        msk = msk0[:, 0:n_masks * Tq]
        tml = e_dma(K_, "sp", msk, mask_tab, writes=[msk_r], tag="at_mskld")
    if qrow_tab is not None:
        qrow, qrow_r = K_.sbuf("at_qrow", [1, 2048], BF16)
    out_toks = []
    for h in range(n_h):
        kv = kv_of(h)
        tkh = e_dma(K_, "sp", kh[:, :], kf[kv * 128:(kv + 1) * 128, :],
                    writes=[kh_r], deps=list(dep_toks), tag=f"at_ldk{h}")
        if q_stripes is None:
            tqh = e_dma(K_, "sp", qh[:, 0:n_q], qf[h * 128:(h + 1) * 128, :],
                        writes=[qh_r], deps=list(dep_toks), tag=f"at_ldq{h}")
        else:
            def qpairs(p, h=h):
                return [(qh[:, j * 128:(j + 1) * 128],
                         qf[h * 128:(h + 1) * 128, g * 128:(g + 1) * 128])
                        for j, g in enumerate(q_stripes(p))]
            tqh = e_dma_sw(K_, "sp", qpairs, writes=[qh_r],
                           deps=list(dep_toks), tag=f"at_ldq{h}")
        tvh = None
        for kc_ in range(n_kv // 128):
            tvh = e_dma(K_, "sp", vh[:, kc_ * 128:(kc_ + 1) * 128],
                        vt[kc_ * 128:(kc_ + 1) * 128, kv * 128:(kv + 1) * 128],
                        writes=[vh_r], deps=list(dep_toks), tag=f"at_ldv{h}_{kc_}")
        tqr = None
        if qrow_tab is not None:
            tqr = e_dma(K_, "sp", qrow[0:1, 0:n_q], qrow_tab[h:h + 1, :],
                        writes=[qrow_r], tag=f"at_qr{h}")
        ps5, ps5_r = K_.psum_f[5], K_.psum_fr[5]
        rws, rws_r = K_.sbuf("at_rw", [1, 16], F32)
        def norm_max(src_ap, src_res, n, dst_col, dep, pfx, mcol=8):
            ns = n // 512
            tsq = plan.op("act", lambda e: e.activation(sqs[:, 0:n], src_ap, AF.Square),
                          reads=[src_res], writes=[sqs_r], deps=[dep], tag=f"{pfx}sq")
            toks = []
            for s in range(ns):
                tm = plan.op("pe", lambda e, s=s: e.matmul(
                    ps5[0:1, 0:512], cst[:, 128:129], sqs[:, s * 512:(s + 1) * 512],
                    start=True, stop=True),
                    reads=[sqs_r, cst_r], writes=[ps5_r], deps=[tsq], tag=f"{pfx}n{s}")
                toks.append(plan.op("dve", lambda e, s=s: e.reduce_max(
                    rws[0:1, dst_col + s:dst_col + s + 1], ps5[0:1, 0:512], axis=AX),
                    reads=[ps5_r], writes=[rws_r], deps=[tm], tag=f"{pfx}m{s}"))
            return plan.op("dve", lambda e, mcol=mcol, ns=ns: e.reduce_max(
                rws[0:1, mcol:mcol + 1], rws[0:1, dst_col:dst_col + ns], axis=AX),
                reads=[rws_r], writes=[rws_r], deps=toks, tag=f"{pfx}mm")
        tmk = norm_max(kh[:, :], kh_r, n_kv, 0, tkh, f"at_k{h}", mcol=8)
        tmq = norm_max(qh[:, :], qh_r, n_q, 4, tqh, f"at_q{h}", mcol=9)
        t7 = plan.op("act", lambda e: e.activation(rws[0:1, 8:10], rws[0:1, 8:10], AF.Sqrt),
                     reads=[rws_r], writes=[rws_r], deps=[tmk, tmq], tag="at_sqrt")
        t8 = plan.op("dve", lambda e: e.scalar_tensor_tensor(
            rws[0:1, 14:15], rws[0:1, 8:9], scale, rws[0:1, 9:10], ALU.mult, ALU.mult),
            reads=[rws_r], writes=[rws_r], deps=[t7], tag="at_B")
        t9 = plan.op("dve", lambda e: e.tensor_copy(rwb2[0:1, 0:1], rws[0:1, 14:15]),
                     reads=[rws_r], writes=[rwb2_r], deps=[t8], tag="at_Bb")
        t10 = plan.op("pe", lambda e: e.matmul(ps5[:, 0:1], cst[0:1, 256:384],
                                               rwb2[0:1, 0:1], start=True, stop=True),
                      reads=[rwb2_r, cst_r], writes=[ps5_r], deps=[t9], tag="at_Bbr")
        t11 = plan.op("dve", lambda e: e.tensor_copy(nbc[:, :], ps5[:, 0:1]),
                      reads=[ps5_r], writes=[nbc_r], deps=[t10], tag="at_negB")
        if bias_tab is not None:
            tbl = e_dma(K_, "sp", bcol[:, :], bias_tab[h * 128:(h + 1) * 128, :], writes=[bcol_r],
                        tag=f"at_bt{h}")
            tbc = plan.op("dve", lambda e: e.tensor_scalar(
                bcol[:, :], bcol[:, :], nbc[:, 0:1], None, ALU.add),
                reads=[nbc_r], writes=[bcol_r], deps=[tbl, t11], tag="at_bfold")
        else:
            tbc = plan.op("dve", lambda e: e.tensor_copy(bcol[:, 0:1], nbc[:, :]),
                          reads=[nbc_r], writes=[bcol_r], deps=[t11], tag="at_bcp")
        for qt in range(nqt):
            kcs = kc_list(qt)
            ps_sum, ps_sum_r = K_.psum_f[2], K_.psum_fr[2]
            ps_av, ps_av_r = K_.psum_f[3], K_.psum_fr[3]
            first = True
            for kc in kcs:
                bank = kc % 2
                psc, psc_r = K_.psum_f[bank], K_.psum_fr[bank]
                qstop = qrow_tab is None
                tmm = plan.op("pe", lambda e, kc=kc, psc=psc, qt=qt: e.matmul(
                    psc[:, 0:Tq], kh[:, kc * 128:(kc + 1) * 128],
                    qh[:, qt * Tq:(qt + 1) * Tq], start=True, stop=qstop),
                    reads=[kh_r, qh_r], writes=[psc_r], tag=f"at_s{h}_{qt}_{kc}")
                if qrow_tab is not None:
                    tmm = plan.op("pe", lambda e, psc=psc, qt=qt: e.matmul(
                        psc[:, 0:Tq], cst[0:1, 128:256],
                        qrow[0:1, qt * Tq:(qt + 1) * Tq], start=False, stop=True),
                        reads=[qrow_r, cst_r], writes=[psc_r], deps=[tqr],
                        tag=f"at_sq{h}_{qt}_{kc}")
                mi = mask_idx_of(qt, kc) if mask_idx_of else None
                tmsk = None
                if mi is not None:
                    tmsk = plan.op("dve", lambda e, psc=psc, m=mi: e.tensor_add(
                        psc[:, 0:Tq], psc[:, 0:Tq], msk[:, m * Tq:(m + 1) * Tq]),
                        reads=[msk_r], writes=[psc_r], deps=[tmm],
                        tag=f"at_m{h}_{qt}_{kc}")
                bi = bias_col_of(qt, kc) if bias_col_of else 0
                scs_ap, scs_res = sc[kc % 2], sc_r[kc % 2]
                tex = plan.op("act", lambda e, psc=psc, o=scs_ap, b=bi: e.activation(
                    o[:, :], psc[:, 0:Tq], AF.Exp, bias=bcol[:, b:b + 1], scale=scale),
                    reads=[psc_r, bcol_r], writes=[scs_res], deps=[tbc],
                    tag=f"at_e{h}_{qt}_{kc}")
                last = (kc == kcs[-1])
                plan.op("pe", lambda e, o=scs_ap, s=first, p=last: e.matmul(
                    ps_sum[0:1, 0:Tq], cst[:, 128:129], o[:, :], start=s, stop=p),
                    reads=[scs_res, cst_r], writes=[ps_sum_r], deps=[tex],
                    tag=f"at_sum{h}_{qt}_{kc}")
                plan.op("pe", lambda e, o=scs_ap, kc=kc, s=first, p=last: e.matmul(
                    ps_av[:, 0:Tq], vh[:, kc * 128:(kc + 1) * 128], o[:, :],
                    start=s, stop=p),
                    reads=[scs_res, vh_r], writes=[ps_av_r], deps=[tex, tvh],
                    tag=f"at_av{h}_{qt}_{kc}")
                first = False
            tsum = plan.op("dve", lambda e: e.reciprocal(rowb[0:1, :], ps_sum[0:1, 0:Tq]),
                           reads=[ps_sum_r], writes=[rowb_r], tag=f"at_rc{h}_{qt}")
            tbr = plan.op("pe", lambda e: e.matmul(ps_sum[:, 0:Tq], onesf[0:1, :],
                                                   rowb[0:1, :], start=True, stop=True),
                          reads=[rowb_r, onesf_r], writes=[ps_sum_r], deps=[tsum],
                          tag=f"at_rb{h}_{qt}")
            tre = plan.op("dve", lambda e: e.tensor_copy(rec[:, :], ps_sum[:, 0:Tq]),
                          reads=[ps_sum_r], writes=[rec_r], deps=[tbr], tag=f"at_re{h}_{qt}")
            if dst_dram is not None and out_stripes is not None:
                def mix_ev(e, out_ap):
                    return e.tensor_mul(out_ap, ps_av[:, 0:Tq], rec[:, :])
                a_, r_ = stg.slots[stg.i % len(stg.slots)]
                stg.i += 1
                tev = plan.op("dve", lambda e, a_=a_: mix_ev(e, a_[:, 0:Tq]),
                              reads=[ps_av_r, rec_r], writes=[r_], deps=[tre],
                              tag="at_oev")
                nstr = Tq // 128

                def opairs(p, h=h, qt=qt, a_=a_, nstr=nstr):
                    gs = q_stripes(p)[qt * nstr:(qt + 1) * nstr]
                    return [(dst_dram[h * 128:(h + 1) * 128,
                                      g * 128:(g + 1) * 128],
                             a_[:, j * 128:(j + 1) * 128])
                            for j, g in enumerate(gs)]
                tout = e_dma_sw(K_, "sp", opairs, reads=[r_], tag="at_osw")
            elif dst_dram is not None:
                def mix_ev(e, out_ap):
                    return e.tensor_mul(out_ap, ps_av[:, 0:Tq], rec[:, :])
                tout = stg.store(ps_av[:, 0:Tq], ps_av_r,
                                 dst_dram[h * 128:(h + 1) * 128,
                                          qt * Tq:(qt + 1) * Tq],
                                 tok=tre, w=Tq, evac=mix_ev, extra_reads=[rec_r])
            else:
                tout = plan.op("dve", lambda e, h=h, qt=qt: e.tensor_mul(
                    attn_dst[:, h * n_q + qt * Tq: h * n_q + (qt + 1) * Tq],
                    ps_av[:, 0:Tq], rec[:, :]),
                    reads=[ps_av_r, rec_r], writes=[attn_dst_r], deps=[tre],
                    tag=f"at_o{h}_{qt}")
            out_toks.append(tout)
    return out_toks



def ln_and_y(K_, A, cfg, uniq, x_dram, n_full):
    ntf = n_full // TT
    xt, xt_r = K_.sbuf("xt", [128, 8 * TT], F32)
    xt_rs = [Res(f"xt_{k}") for k in range(8)]
    y, y_r = K_.sbuf(f"y_{uniq}", [128, NKC * n_full], BF16)
    seen = {}

    def x_at(k, t):
        sl = k % 8
        key = (k, t)
        n = seen.get(key, 0)
        seen[key] = n + 1
        e_dma(K_, "sp", xt[:, sl * TT:(sl + 1) * TT],
              x_dram[k * 128:(k + 1) * 128, t * TT:(t + 1) * TT],
              writes=[xt_rs[sl]], tag=f"ldx{uniq}")
        return xt[:, sl * TT:(sl + 1) * TT], [xt_rs[sl]]

    def y_sink(k, t, emit):
        return emit(y[:, k * n_full + t * TT: k * n_full + (t + 1) * TT], [y_r])

    toks = ln_fm(K_, name=uniq, n_tok=n_full, x_at=x_at, y_sink=y_sink)
    return y, y_r, toks


def proj_to_dram(K_, *, name, wload, n_k, m_list, t_list, rhs_at, dst, stg,
                 row_of=None, mg=3, evac=None, evac_of=None, extra_reads=(),
                 evac_eng="dve"):
    def sink(m, t, ps, ps_r, tok):
        row = (row_of(m) if row_of else m * 128)
        ev = evac_of(m, t, ps) if evac_of else evac
        return stg.store(ps[:, :], ps_r, dst[row:row + 128, t * TT:(t + 1) * TT],
                         tok=tok, evac=ev, extra_reads=extra_reads,
                         evac_eng=evac_eng)
    return proj_fm(K_, name=name, wload=wload, n_k=n_k, m_list=m_list,
                   t_list=t_list, rhs_at=rhs_at, sink=sink, mg=mg)


def vproj_tm(K_, *, name, wT_dram_bf, wT_deps, y, y_r, n_full, n_vc, dst, stg,
             evac=None, nat_f32=False, evac_of=None, extra_reads=(),
             evac_eng="dve"):
    plan = K_.plan
    wt, wt_r = K_.sbuf("wvT", [128, NKC * TT], BF16)
    wt_rs = [Res(f"wvT_{k}") for k in range(NKC)]
    out = []
    ntb = n_full // 128
    nvt = (n_vc + TT - 1) // TT
    for vt_i in range(nvt):
        vw = min(TT, n_vc - vt_i * TT)
        wl = []
        for k in range(NKC):
            if nat_f32:
                wl.append(e_dma(K_, "pool", wt[:, k * TT:k * TT + vw],
                                wT_dram_bf[k * 128:(k + 1) * 128,
                                           vt_i * TT: vt_i * TT + vw],
                                writes=[wt_rs[k]], deps=list(wT_deps),
                                tag=f"wvT{name}"))
            else:
                wl.append(e_xbar(K_, wt[:, k * TT:k * TT + vw],
                                 wT_dram_bf[vt_i * TT: vt_i * TT + vw,
                                            k * 128:(k + 1) * 128],
                                 writes=[wt_rs[k]], deps=list(wT_deps),
                                 tag=f"wvT{name}"))
        for tb in range(ntb):
            bank = tb % 2
            ps, ps_r = K_.psum_f[bank], K_.psum_fr[bank]
            tok = None
            for k in range(NKC):
                st, sp = (k == 0), (k == NKC - 1)
                tok = plan.op("pe", lambda e, k=k, s=st, p=sp, vw=vw, tb=tb, ps=ps:
                              e.matmul(ps[:, 0:vw],
                                       y[:, k * n_full + tb * 128: k * n_full + (tb + 1) * 128],
                                       wt[:, k * TT: k * TT + vw],
                                       start=s, stop=p),
                              reads=[y_r, wt_rs[k]], writes=[ps_r], deps=[wl[k]],
                              tag=f"vmm{name}_{tb}_{vt_i}_{k}")
            ev = evac_of(vt_i, tb, ps) if evac_of else evac
            out.append(stg.store(ps[:, 0:vw], ps_r,
                                 dst[tb * 128:(tb + 1) * 128,
                                     vt_i * TT: vt_i * TT + vw],
                                 tok=tok, w=vw, evac=ev, extra_reads=extra_reads,
                                 evac_eng=evac_eng))
    return out


def emit_l0aa(K_, A, cfg):
    plan = K_.plan
    n_own, n_full = cfg["n_own"], cfg["n_full"]
    nto = n_own // TT
    cq = cast_pass(K_, A["l0aa_wq_bf"], A["l0aa_wq"], 2048, 2048, tag="cwq")
    ck = cast_pass(K_, A["l0aa_wk_bf"], A["l0aa_wk"], 1024, 2048, tag="cwk")
    cv = cast_pass(K_, A["l0aa_wv_bf"], A["l0aa_wv"], 1024, 2048, tag="cwv")
    co = cast_pass(K_, A["l0aa_wo_bf"], A["l0aa_wo"], 2048, 2048, tag="cwo")
    y, y_r, _ = ln_and_y(K_, A, cfg, "l0aa", A["x"], n_full)

    def y_at(k, t):
        return y[:, k * n_full + t * TT:(k * n_full) + (t + 1) * TT], [y_r]

    stg = Stager(K_, "s", 512, 3)
    tq = proj_to_dram(K_, name="l0aa_q", wload=wload_xbarT(K_, A["l0aa_wq_bf"], cq),
                      n_k=NKC, m_list=list(range(16)), t_list=list(range(nto)),
                      rhs_at=y_at, dst=A["l0aa_qf"], stg=stg)
    tk = []
    for tl in [list(range(0, min(2, n_full // TT))), list(range(2, n_full // TT))]:
        if tl:
            tk += proj_to_dram(K_, name="l0aa_k", wload=wload_xbarT(K_, A["l0aa_wk_bf"], ck),
                               n_k=NKC, m_list=list(range(8)), t_list=tl,
                               rhs_at=y_at, dst=A["l0aa_kf"], stg=stg)
    tv = vproj_tm(K_, name="l0aa", wT_dram_bf=A["l0aa_wv_bf"], wT_deps=cv,
                  y=y, y_r=y_r, n_full=n_full, n_vc=1024, dst=A["l0aa_vt"], stg=stg)
    attn, attn_r = K_.sbuf("l0aa_attn", [128, 16 * n_own], BF16)
    at = softmax_attn(K_, name="l0aa", n_h=16, kv_of=lambda h: h // 2,
                      n_q=n_own, n_kv=n_full, scale=1.0 / np.sqrt(128.0), Tq=512,
                      qf=A["l0aa_qf"], kf=A["l0aa_kf"], vt=A["l0aa_vt"], n_kvh=8,
                      kc_list=lambda qt: list(range(n_full // 128)),
                      attn_dst=attn, attn_dst_r=attn_r, dep_toks=tq + tk + tv)

    def attn_at(k, t):
        return attn[:, k * n_own + t * TT: k * n_own + (t + 1) * TT], [attn_r]

    xres, xres_r = [], []
    for i in range(3):
        a, r = K_.sbuf(f"xr{i}", [128, TT], F32)
        xres.append(a)
        xres_r.append(r)
    xi = [0]

    def wo_sink(m, t, ps, ps_r, tok):
        i = xi[0] % 3
        xi[0] += 1
        tl = e_dma(K_, "sp", xres[i][:, :],
                   A["x"][m * 128:(m + 1) * 128, t * TT:(t + 1) * TT],
                   writes=[xres_r[i]], tag="l0aa_xr")
        ta = plan.op("dve", lambda e: e.tensor_add(xres[i][:, :], xres[i][:, :], ps[:, :]),
                     reads=[ps_r], writes=[xres_r[i]], deps=[tl, tok], tag="l0aa_add")
        ts = e_dma(K_, "sp", A["xo"][m * 128:(m + 1) * 128, t * TT:(t + 1) * TT],
                   xres[i][:, :], reads=[xres_r[i]], tag="l0aa_st")
        K_.store_toks.append(ts)
        return ts

    proj_fm(K_, name="l0aa_wo", wload=wload_xbarT(K_, A["l0aa_wo_bf"], co),
            n_k=NKC, m_list=list(range(16)), t_list=list(range(nto)),
            rhs_at=attn_at, sink=wo_sink, mg=3)


def declare_l0aa(nc, A, cfg):
    n_own, n_full = cfg["n_own"], cfg["n_full"]
    for w, rows in (("wq", 2048), ("wk", 1024), ("wv", 1024), ("wo", 2048)):
        A[f"l0aa_{w}"] = nc.dram_tensor(f"l0aa_{w}", [rows, 2048], F32,
                                        kind="ExternalInput").ap()
        A[f"l0aa_{w}_bf"] = nc.dram_tensor(f"l0aa_{w}_bf", [rows, 2048], BF16,
                                           kind="Internal").ap()
    A["l0aa_qf"] = nc.dram_tensor("l0aa_qf", [2048, n_own], BF16, kind="Internal").ap()
    A["l0aa_kf"] = nc.dram_tensor("l0aa_kf", [1024, n_full], BF16, kind="Internal").ap()
    A["l0aa_vt"] = nc.dram_tensor("l0aa_vt", [n_full, 1024], BF16, kind="Internal").ap()


DECLARES = {"l0aa": declare_l0aa}
EMITTERS = {"l0aa": emit_l0aa}


def build_launch(subs, cfg):
    nc = bass.Bass("TRN2", target_bir_lowering=False)
    A = {}
    A["x"] = nc.dram_tensor("x", [2048, cfg["n_full"]], F32, kind="ExternalInput").ap()
    A["xo"] = nc.dram_tensor("xo", [2048, cfg["n_own"]], F32, kind="ExternalOutput").ap()
    A["cst_d"] = nc.dram_tensor("cst", [128, 512], BF16, kind="ExternalInput").ap()
    for s in subs:
        DECLARES[s](nc, A, cfg)
    with contextlib.ExitStack() as stack:
        plan = Plan(nc)
        K_ = K(nc, plan, stack)
        cstsb, cst_r = K_.sbuf("cstsb", [128, 512], BF16)
        e_dma(K_, "sp", cstsb[:, :], A["cst_d"][:, :], writes=[cst_r], tag="ldcst")
        K_.cst = (cstsb, cst_r)
        for s in subs:
            EMITTERS[s](K_, A, cfg)
        plan.finalize(stack, K_.store_toks)
    return nc


def perm_for(p, n=2048, stripe=128):
    s = np.arange(n).reshape(-1, stripe)
    own = s[p::2].ravel()
    oth = s[1 - p::2].ravel()
    return np.concatenate([own, oth])


def sim_one(nc, in_map, require_finite=False):
    from concourse.bass_interp import CoreSim
    sim = CoreSim(nc, require_finite=require_finite, require_nnan=False)
    for k, v in in_map.items():
        t = sim.tensor(k)
        t[:] = np.asarray(v).view(t.dtype) if v.dtype != t.dtype else v
    sim.simulate()
    return sim



def emit_l0ffn(K_, A, cfg):
    plan = K_.plan
    n_own = cfg["n_own"]
    nto = n_own // TT
    cst, cst_r = K_.cst
    wrow, wrow_r = [], []
    for i in range(2):
        a, r = K_.sbuf(f"f_wr{i}", [128, 2048], F32)
        wrow.append(a); wrow_r.append(r)
    wsg, wsg_r = [], []
    for i in range(2):
        a, r = K_.sbuf(f"f_ws{i}", [128, 2048], BF16)
        wsg.append(a); wsg_r.append(r)
    amx, amx_r = K_.sbuf("f_amx", [128, 2], F32)
    sgn_toks = []
    for rci in range(16):
        b = rci % 2
        tl = e_dma(K_, "sp", wrow[b][:, :], A["l0f_w"][rci * 128:(rci + 1) * 128, :],
                   writes=[wrow_r[b]], tag=f"fw{rci}")
        ts = plan.op("act", lambda e, b=b: e.activation(wsg[b][:, :], wrow[b][:, :], AF.Sign),
                     reads=[wrow_r[b]], writes=[wsg_r[b]], deps=[tl], tag=f"fsg{rci}")
        sgn_toks.append(e_dma(K_, "sp", A["l0f_ws_bf"][rci * 128:(rci + 1) * 128, :],
                              wsg[b][:, :], reads=[wsg_r[b]], deps=[ts], tag=f"fso{rci}"))
        if rci == 0:
            plan.op("dve", lambda e, b=b: e.reduce_max(
                amx[:, 0:1], wrow[b][:, :], axis=AX, apply_absolute_value=True),
                reads=[wrow_r[b]], writes=[amx_r], deps=[tl], tag="fmx0")
        else:
            plan.op("dve", lambda e, b=b: e.reduce_max(
                amx[:, 1:2], wrow[b][:, :], axis=AX, apply_absolute_value=True),
                reads=[wrow_r[b]], writes=[amx_r], deps=[tl], tag=f"fmx{rci}")
            plan.op("dve", lambda e: e.tensor_max(amx[:, 0:1], amx[:, 0:1], amx[:, 1:2]),
                    reads=[amx_r], writes=[amx_r], tag=f"fmm{rci}")
    tb1 = e_dma(K_, "sp", A["l0f_col"][:, :], amx[:, 0:1], reads=[amx_r], tag="fmxo")
    srow, srow_r = K_.sbuf("f_srow", [1, 130], F32)
    tb2 = e_dma(K_, "sp", srow[0:1, 0:128], A["l0f_col"].rearrange("p o -> o p"),
                deps=[tb1], writes=[srow_r], tag="fmxi")
    tmx = plan.op("dve", lambda e: e.reduce_max(srow[0:1, 128:129], srow[0:1, 0:128], axis=AX),
                  reads=[srow_r], writes=[srow_r], deps=[tb2], tag="fmxr")
    pcol, pcol_r = K_.sbuf("f_pcol", [128, 64], F32)
    pcolb, pcolb_r = K_.sbuf("f_pcolb", [128, 64], BF16)
    tpc = e_dma(K_, "sp", pcol[:, :], A["l0f_pcol"][:, :], writes=[pcol_r], tag="fpc")
    tpcb = plan.op("dve", lambda e: e.tensor_copy(pcolb[:, :], pcol[:, :]),
                   reads=[pcol_r], writes=[pcolb_r], deps=[tpc], tag="fpcb")
    srow_b, srow_b_r = K_.sbuf("f_srb", [1, 2], BF16)
    tsb = plan.op("dve", lambda e: e.tensor_copy(srow_b[0:1, 0:1], srow[0:1, 128:129]),
                  reads=[srow_r], writes=[srow_b_r], deps=[tmx], tag="fsb")
    ps5, ps5_r = K_.psum_f[5], K_.psum_fr[5]
    tbc = plan.op("pe", lambda e: e.matmul(ps5[:, 0:1], cst[0:1, 128:256],
                                           srow_b[0:1, 0:1], start=True, stop=True),
                  reads=[srow_b_r, cst_r], writes=[ps5_r], deps=[tsb], tag="fsbb")
    scol, scol_r = K_.sbuf("f_scol", [128, 18], F32)
    tsc = plan.op("dve", lambda e: e.tensor_copy(scol[:, 16:17], ps5[:, 0:1]),
                  reads=[ps5_r], writes=[scol_r], deps=[tbc], tag="fsc")
    tsg = plan.op("dve", lambda e: e.tensor_scalar(
        scol[:, 0:16], pcol[:, 0:16], scol[:, 16:17], None, ALU.mult),
        reads=[pcol_r, scol_r], writes=[scol_r], deps=[tsc, tpc], tag="fsg_col")
    y, y_r, _ = ln_and_y(K_, A, cfg, "l0f", A["x"], n_own)
    y2, y2_r = K_.sbuf("y2_l0f", [128, NKC * n_own], BF16)

    def x2_at(k, t):
        return y[:, k * n_own + t * TT: k * n_own + (t + 1) * TT], [y_r]

    def y2_sink(k, t, emit):
        return emit(y2[:, k * n_own + t * TT: k * n_own + (t + 1) * TT], [y2_r])

    ln_fm(K_, name="l0f2", n_tok=n_own, x_at=x2_at, y_sink=y2_sink, in_bf16=True,
          gain_col=(lambda k: pcol[:, 32 + k:33 + k]), gain_res=pcol_r,
          gain_dep=tpc)
    xres, xres_r = [], []
    for i in range(3):
        a, r = K_.sbuf(f"xr{i}", [128, TT], F32)
        xres.append(a); xres_r.append(r)
    ccol, ccol_r = K_.sbuf("f_ccol", [128, 16], F32)
    tmpo, tmpo_r = K_.sbuf("f_tmpo", [128, TT], F32)
    xi = [0]

    def rhs_at(k, t):
        if t == 99:
            return pcolb[:, 48 + k:49 + k], [pcolb_r]
        return y2[:, k * n_own + t * TT: k * n_own + (t + 1) * TT], [y2_r]

    def sink2(m, t, ps, ps_r, tok):
        if t == 99:
            return plan.op("dve", lambda e, m=m, ps=ps: e.tensor_copy(
                ccol[:, m:m + 1], ps[:, 0:1]),
                reads=[ps_r], writes=[ccol_r], deps=[tok], tag=f"fc{m}")
        i = xi[0] % 3
        xi[0] += 1
        tl = e_dma(K_, "sp", xres[i][:, :],
                   A["x"][m * 128:(m + 1) * 128, t * TT:(t + 1) * TT],
                   writes=[xres_r[i]], tag="f_xr")
        t1 = plan.op("dve", lambda e, m=m, ps=ps: e.tensor_scalar(
            tmpo[:, :], ps[:, :], ccol[:, m:m + 1], scol[:, m:m + 1],
            ALU.add, ALU.mult),
            reads=[ps_r, ccol_r, scol_r], writes=[tmpo_r], deps=[tok, tsg], tag="f_e1")
        t2 = plan.op("dve", lambda e, m=m, i=i: e.scalar_tensor_tensor(
            xres[i][:, :], tmpo[:, :], pcol[:, 16 + m:17 + m], xres[i][:, :],
            ALU.add, ALU.add),
            reads=[tmpo_r, pcol_r], writes=[xres_r[i]], deps=[t1, tl], tag="f_e2")
        ts = e_dma(K_, "sp", A["xo"][m * 128:(m + 1) * 128, t * TT:(t + 1) * TT],
                   xres[i][:, :], reads=[xres_r[i]], tag="f_st")
        K_.store_toks.append(ts)
        return ts

    proj_fm(K_, name="l0f_mm", wload=wload_xbarT(K_, A["l0f_ws_bf"], sgn_toks),
            n_k=NKC, m_list=list(range(16)), t_list=[99] + list(range(nto)),
            rhs_at=rhs_at, sink=sink2, mg=2)


def declare_l0ffn(nc, A, cfg):
    A["l0f_w"] = nc.dram_tensor("l0f_w", [2048, 2048], F32, kind="ExternalInput").ap()
    A["l0f_pcol"] = nc.dram_tensor("l0f_pcol", [128, 64], F32, kind="ExternalInput").ap()
    A["l0f_ws_bf"] = nc.dram_tensor("l0f_ws_bf", [2048, 2048], BF16, kind="Internal").ap()
    A["l0f_col"] = nc.dram_tensor("l0f_col", [128, 1], F32, kind="Internal").ap()


DECLARES["l0ffn"] = declare_l0ffn
EMITTERS["l0ffn"] = emit_l0ffn



def emit_l0ab(K_, A, cfg):
    plan = K_.plan
    n_own, n_full = cfg["n_own"], cfg["n_full"]
    nto = n_own // TT
    cst, cst_r = K_.cst
    y, y_r, _ = ln_and_y(K_, A, cfg, "l0ab", A["x"], n_full)

    def y_at(k, t):
        return y[:, k * n_full + t * TT:(k * n_full) + (t + 1) * TT], [y_r]

    bcols, bcols_r = K_.sbuf("ab_bcols", [128, 48], F32)
    tbc = e_dma(K_, "sp", bcols[:, :], A["ab_bcol"][:, :], writes=[bcols_r], tag="ab_bc")
    bvrow, bvrow_r = K_.sbuf("ab_bvr_s", [1, TT], F32)
    bvrow_b, bvrow_b_r = K_.sbuf("ab_bvrb_s", [1, TT], BF16)
    bvB, bvB_r = K_.sbuf("ab_bvB", [128, 2048], BF16)
    ps5, ps5_r = K_.psum_f[5], K_.psum_fr[5]
    for vt_i in range(4):
        tbv = e_dma(K_, "sp", bvrow[0:1, :], A["ab_bvrow"][0:1, vt_i * TT:(vt_i + 1) * TT],
                    writes=[bvrow_r], tag="ab_bv")
        tbvb = plan.op("dve", lambda e: e.tensor_copy(bvrow_b[0:1, :], bvrow[0:1, :]),
                       reads=[bvrow_r], writes=[bvrow_b_r], deps=[tbv], tag="ab_bvb")
        tm = plan.op("pe", lambda e: e.matmul(
            ps5[:, :], cst[0:1, 128:256], bvrow_b[0:1, :],
            start=True, stop=True),
            reads=[bvrow_b_r, cst_r], writes=[ps5_r], tag=f"ab_bvm{vt_i}")
        plan.op("dve", lambda e, v=vt_i: e.tensor_copy(bvB[:, v * TT:(v + 1) * TT], ps5[:, :]),
                reads=[ps5_r], writes=[bvB_r], deps=[tm], tag=f"ab_bve{vt_i}")
    stg = Stager(K_, "s", 512, 3)

    def bias_evac(col):
        def eo(m, t, ps):
            def ev(e, out_ap, m=m, ps=ps):
                return e.tensor_scalar(out_ap, ps[:, :],
                                       bcols[:, col + m:col + m + 1], None, ALU.add)
            return ev
        return eo

    tq = proj_to_dram(K_, name="l0ab_q", wload=wload_nat(K_, A["l0ab_wq"]),
                      n_k=NKC, m_list=list(range(16)), t_list=list(range(nto)),
                      rhs_at=y_at, dst=A["l0ab_qf"], stg=stg,
                      evac_of=bias_evac(0), extra_reads=[bcols_r])
    tk = []
    for tl in [list(range(0, min(2, n_full // TT))), list(range(2, n_full // TT))]:
        if tl:
            tk += proj_to_dram(K_, name="l0ab_k", wload=wload_nat(K_, A["l0ab_wk"]),
                               n_k=NKC, m_list=list(range(16)), t_list=tl,
                               rhs_at=y_at, dst=A["l0ab_kf"], stg=stg,
                               evac_of=bias_evac(16), extra_reads=[bcols_r])

    def v_evac(vt_i, tb, ps):
        def ev(e, out_ap):
            return e.tensor_add(out_ap, ps[:, :], bvB[:, vt_i * TT:(vt_i + 1) * TT])
        return ev

    tv = vproj_tm(K_, name="l0ab", wT_dram_bf=A["l0ab_wv"], wT_deps=[],
                  y=y, y_r=y_r, n_full=n_full, n_vc=2048, dst=A["l0ab_vt"],
                  stg=stg, nat_f32=True, evac_of=v_evac, extra_reads=[bvB_r])

    def kc_list(qt):
        own = list(range(0, 2 * qt + 2))
        oth = [8 + j for j in range(0, 2 * qt + 2)]
        return own + oth

    def mask_idx_of(qt, kc):
        cands = [2 * qt, 2 * qt + 1, 8 + 2 * qt, 8 + 2 * qt + 1]
        if kc in cands:
            return qt * 4 + cands.index(kc)
        return None

    attn, attn_r = K_.sbuf("l0aa_attn", [128, 16 * n_own], BF16)
    at = softmax_attn(K_, name="l0ab", n_h=16, kv_of=lambda h: h,
                      n_q=n_own, n_kv=n_full, scale=1.0 / np.sqrt(128.0), Tq=256,
                      qf=A["l0ab_qf"], kf=A["l0ab_kf"], vt=A["l0ab_vt"], n_kvh=16,
                      kc_list=kc_list,
                      bias_tab=A["ab_btab"], bias_col_of=lambda qt, kc: qt * 16 + kc,
                      n_bias_cols=64,
                      mask_tab=A["ab_mtab"][:, :], mask_idx_of=mask_idx_of, n_masks=16,
                      qrow_tab=A["ab_qrow"],
                      attn_dst=attn, attn_dst_r=attn_r, dep_toks=tq + tk + tv)

    def attn_at(k, t):
        return attn[:, k * n_own + t * TT: k * n_own + (t + 1) * TT], [attn_r]

    xres, xres_r = [], []
    for i in range(3):
        a, r = K_.sbuf(f"xr{i}", [128, TT], F32)
        xres.append(a); xres_r.append(r)
    xi = [0]

    def wo_sink(m, t, ps, ps_r, tok):
        i = xi[0] % 3
        xi[0] += 1
        tl = e_dma(K_, "sp", xres[i][:, :],
                   A["x"][m * 128:(m + 1) * 128, t * TT:(t + 1) * TT],
                   writes=[xres_r[i]], tag="ab_xr")
        ta = plan.op("dve", lambda e, m=m, i=i, ps=ps: e.scalar_tensor_tensor(
            xres[i][:, :], ps[:, :], bcols[:, 32 + m:33 + m], xres[i][:, :],
            ALU.add, ALU.add),
            reads=[ps_r, bcols_r], writes=[xres_r[i]], deps=[tl, tok], tag="ab_add")
        ts = e_dma(K_, "sp", A["xo"][m * 128:(m + 1) * 128, t * TT:(t + 1) * TT],
                   xres[i][:, :], reads=[xres_r[i]], tag="ab_st")
        K_.store_toks.append(ts)
        return ts

    proj_fm(K_, name="l0ab_wo", wload=wload_nat(K_, A["l0ab_wo"]),
            n_k=NKC, m_list=list(range(16)), t_list=list(range(nto)),
            rhs_at=attn_at, sink=wo_sink, mg=3)


def declare_l0ab(nc, A, cfg):
    n_own, n_full = cfg["n_own"], cfg["n_full"]
    for w in ("wq", "wk", "wv", "wo"):
        A[f"l0ab_{w}"] = nc.dram_tensor(f"l0ab_{w}", [2048, 2048], F32,
                                        kind="ExternalInput").ap()
    A["ab_bcol"] = nc.dram_tensor("ab_bcol", [128, 48], F32, kind="ExternalInput").ap()
    A["ab_bvrow"] = nc.dram_tensor("ab_bvrow", [1, 2048], F32, kind="ExternalInput").ap()
    A["ab_btab"] = nc.dram_tensor("ab_btab", [16 * 128, 64], F32, kind="ExternalInput").ap()
    A["ab_mtab"] = nc.dram_tensor("ab_mtab", [128, 16 * 256], BF16, kind="ExternalInput").ap()
    A["ab_qrow"] = nc.dram_tensor("ab_qrow", [16, n_own], BF16, kind="ExternalInput").ap()
    A["l0ab_qf"] = nc.dram_tensor("l0ab_qf", [2048, n_own], BF16, kind="Internal").ap()
    A["l0ab_kf"] = nc.dram_tensor("l0ab_kf", [2048, n_full], BF16, kind="Internal").ap()
    A["l0ab_vt"] = nc.dram_tensor("l0ab_vt", [n_full, 2048], BF16, kind="Internal").ap()


DECLARES["l0ab"] = declare_l0ab
EMITTERS["l0ab"] = emit_l0ab


def host_l0ab_tables(perm, n_own=1024, n_heads=16, alibi=True, scale=None):
    if scale is None:
        scale = 1.0 / np.sqrt(128.0)
    pos = perm.astype(np.float64)
    slopes = np.array([(2.0 ** (-8.0 / n_heads)) ** (i + 1) for i in range(n_heads)])
    if not alibi:
        slopes = slopes * 0.0
    nqt = n_own // 256
    nkc = len(perm) // 128
    btab = np.zeros((n_heads * 128, nqt * nkc), np.float32)
    mtab = np.zeros((128, nqt * 4 * 256), np.float32)
    for qt in range(nqt):
        qpos = pos[qt * 256:(qt + 1) * 256]
        for kc in range(nkc):
            kpos = pos[kc * 128:(kc + 1) * 128]
            for h in range(n_heads):
                if kpos.min() > qpos.max():
                    btab[h * 128:(h + 1) * 128, qt * nkc + kc] = -1e30
                else:
                    btab[h * 128:(h + 1) * 128, qt * nkc + kc] = (
                        slopes[h] * kpos).astype(np.float32)
        cands = [2 * qt, 2 * qt + 1, nkc // 2 + 2 * qt, nkc // 2 + 2 * qt + 1]
        for ci, kc in enumerate(cands):
            kpos = pos[kc * 128:(kc + 1) * 128]
            m = np.where(kpos[:, None] <= qpos[None, :], 0.0, -1e30).astype(np.float32)
            mtab[:, (qt * 4 + ci) * 256:(qt * 4 + ci + 1) * 256] = m
    qrow = np.zeros((n_heads, n_own), np.float32)
    for h in range(n_heads):
        qrow[h] = (-slopes[h] * pos[:n_own] / scale).astype(np.float32)
    return btab, mtab.astype(BF), qrow.astype(BF)



def emit_l1aa(K_, A, cfg):
    plan = K_.plan
    nf = cfg["n_full"]
    assert cfg["n_own"] == nf
    cst, cst_r = K_.cst
    H, Dh, BS, NB = 32, 64, 256, nf // 256
    qkv = A["l1aa_qkv"]
    qk_src = qkv.rearrange("(h r) c -> h r c", h=H)[:, 0:128, :]
    v_src = qkv.rearrange("(h r) c -> h r c", h=H)[:, 128:192, :]
    cqk = [e_dma(K_, "pool", A["l1aa_qk_bf"].rearrange("(h r) c -> h r c", h=H),
                 qk_src, tag="c_qk")]
    cv = [e_dma(K_, "pool", A["l1aa_v_bf"].rearrange("(h r) c -> h r c", h=H),
                v_src, tag="c_v")]
    cg = cast_pass(K_, A["l1aa_gw_bf"], A["l1aa_gw"], 2048, 2048, tag="c_gw")
    cow = cast_pass(K_, A["l1aa_ow_bf"], A["l1aa_ow"], 2048, 2048, tag="c_ow")
    y, y_r, _ = ln_and_y(K_, A, cfg, "l1aa", A["x"], nf)

    def y_at(k, t):
        return y[:, k * nf + t * TT:(k * nf) + (t + 1) * TT], [y_r]

    stg = Stager(K_, "s", 512, 3)

    def silu_ev(m, t, ps):
        def ev(e, out_ap, ps=ps):
            return e.activation(out_ap, ps[:, 0:out_ap.shape[-1]], AF.Silu)
        return ev

    tqk = []
    ntf_ = nf // TT
    for tl in [list(range(i, min(i + 2, ntf_))) for i in range(0, ntf_, 2)]:
        tqk += proj_to_dram(K_, name="l1aa_qk", wload=wload_xbarT(K_, A["l1aa_qk_bf"], cqk),
                            n_k=NKC, m_list=list(range(H)), t_list=tl,
                            rhs_at=y_at, dst=A["l1aa_qkf"], stg=stg,
                            evac_of=silu_ev, evac_eng="act")

    def vsilu_ev(vt_i, tb, ps):
        def ev(e, out_ap, ps=ps):
            return e.activation(out_ap, ps[:, 0:out_ap.shape[-1]], AF.Silu)
        return ev

    tv = vproj_tm(K_, name="l1aa", wT_dram_bf=A["l1aa_v_bf"], wT_deps=cv,
                  y=y, y_r=y_r, n_full=nf, n_vc=2048, dst=A["l1aa_vtm"], stg=stg,
                  evac_of=vsilu_ev, evac_eng="act")
    qkb, qkb_r = K_.sbuf("aa_qkb", [64, 2 * 512], BF16)
    qkb_rs = [Res("aa_qkb0"), Res("aa_qkb1")]
    vb, vb_r = K_.sbuf("aa_vb", [128, 4 * Dh], BF16)
    vb_rs = [Res("aa_vb0"), Res("aa_vb1")]
    ddt, ddt_r = K_.sbuf("aa_ddt", [128, 2 * BS], BF16)
    qdB, qdB_r = K_.sbuf("aa_qdB", [128, BS], F32)
    qdrow, qdrow_r = K_.sbuf("aa_qdrow", [1, BS], BF16)
    decay, decay_r = K_.sbuf("aa_decay", [128, 128], F32)
    tdec = e_dma(K_, "sp", decay[:, :], A["l1aa_decay"][:, :], writes=[decay_r], tag="aa_dec")
    awd, awd_r = K_.sbuf("aa_awd", [128, 2 * BS], BF16)
    vkd, vkd_r = K_.sbuf("aa_vkd", [128, 2 * Dh], BF16)
    ktm, ktm_r = K_.sbuf("aa_ktm", [128, 2 * Dh], BF16)
    St, St_r = K_.sbuf("aa_S", [64, Dh], F32)
    Sb, Sb_r = K_.sbuf("aa_Sb", [64, Dh], BF16)
    ob, ob_r = K_.sbuf("aa_ob", [64, BS], F32)
    for h in range(H):
        tdd = e_dma(K_, "sp", ddt.rearrange("p (c s) -> p c s", c=2),
                    A["l1aa_ddT"][h * 256:(h + 1) * 256, :]
                    .rearrange("(c p) s -> p c s", p=128), writes=[ddt_r], tag=f"aa_dd{h}")
        tqd = e_dma(K_, "sp", qdrow[0:1, :], A["l1aa_qd"][h:h + 1, :],
                    writes=[qdrow_r], tag=f"aa_qd{h}")
        ps5, ps5_r = K_.psum_f[5], K_.psum_fr[5]
        tqb = plan.op("pe", lambda e: e.matmul(ps5[:, 0:BS], cst[0:1, 128:256],
                                               qdrow[0:1, :], start=True, stop=True),
                      reads=[qdrow_r, cst_r], writes=[ps5_r], deps=[tqd], tag=f"aa_qb{h}")
        tqe = plan.op("dve", lambda e: e.tensor_copy(qdB[:, :], ps5[:, 0:BS]),
                      reads=[ps5_r], writes=[qdB_r], deps=[tqb], tag=f"aa_qe{h}")
        tS0 = plan.op("dve", lambda e: e.memset(St[:, :], 0.0), writes=[St_r],
                      tag=f"aa_S0{h}")
        tSb = plan.op("dve", lambda e: e.tensor_copy(Sb[:, :], St[:, :]),
                      reads=[St_r], writes=[Sb_r], deps=[tS0], tag=f"aa_Sb{h}")
        for blk in range(NB):
            sl = blk % 2
            qk = qkb[:, sl * 512:(sl + 1) * 512]
            tq = e_dma(K_, "sp", qk.rearrange("p (b s) -> p b s", b=2),
                       A["l1aa_qkf"][h * 128:(h + 1) * 128, blk * BS:(blk + 1) * BS]
                       .rearrange("(b p) s -> p b s", b=2),
                       writes=[qkb_rs[sl]], deps=tqk, tag=f"aa_lq{h}_{blk}")
            vsl = vb[:, sl * 2 * Dh:(sl + 1) * 2 * Dh]
            tvl = e_dma(K_, "sp", vsl.rearrange("p (c d) -> p c d", c=2),
                        A["l1aa_vtm"][blk * BS:(blk + 1) * BS, h * Dh:(h + 1) * Dh]
                        .rearrange("(c p) d -> p c d", p=128),
                        writes=[vb_rs[sl]], deps=tv, tag=f"aa_lv{h}_{blk}")
            taw = []
            for c in range(2):
                psn, psn_r = K_.psum_f[c], K_.psum_fr[c]
                t1 = plan.op("pe", lambda e, c=c, qk=qk, psn=psn: e.matmul(
                    psn[:, 0:BS], qk[:, 256 + c * 128:256 + (c + 1) * 128],
                    qk[:, 0:BS], start=True, stop=True),
                    reads=[qkb_rs[sl]], writes=[psn_r], deps=[tq],
                    tag=f"aa_aw{h}_{blk}_{c}")
                taw.append(plan.op("dve", lambda e, c=c, psn=psn: e.tensor_mul(
                    awd[:, c * BS:(c + 1) * BS], psn[:, 0:BS],
                    ddt[:, c * BS:(c + 1) * BS]),
                    reads=[psn_r, ddt_r], writes=[awd_r], deps=[t1, tdd],
                    tag=f"aa_awd{h}_{blk}_{c}"))
            ps2, ps2_r = K_.psum_f[2], K_.psum_fr[2]
            ps3, ps3_r = K_.psum_f[3], K_.psum_fr[3]
            tintra = None
            for c in range(2):
                tintra = plan.op("pe", lambda e, c=c, vsl=vsl: e.matmul(
                    ps2[0:64, 0:BS], vsl[:, c * Dh:(c + 1) * Dh],
                    awd[:, c * BS:(c + 1) * BS], start=(c == 0), stop=(c == 1)),
                    reads=[vb_rs[sl], awd_r], writes=[ps2_r],
                    deps=[tvl] + taw, tag=f"aa_in{h}_{blk}_{c}")
            tinter = plan.op("pe", lambda e, qk=qk: e.matmul(
                ps3[0:64, 0:BS], Sb[:, :], qk[:, 0:BS], start=True, stop=True),
                reads=[Sb_r, qkb_rs[sl]], writes=[ps3_r], deps=[tq],
                tag=f"aa_it{h}_{blk}")
            tob = plan.op("dve", lambda e: e.tensor_mul(ob[:, :], ps3[0:64, 0:BS],
                                                        qdB[0:64, :]),
                          reads=[ps3_r, qdB_r], writes=[ob_r], deps=[tinter, tqe],
                          tag=f"aa_ob{h}_{blk}")

            def attn_ev(e, out_ap):
                return e.tensor_add(out_ap[0:64, :], ob[:, :], ps2[0:64, 0:BS])
            tst = stg.store(ps2[0:64, 0:BS], ps2_r,
                            A["l1aa_attn_d"][h * Dh:(h + 1) * Dh,
                                             blk * BS:(blk + 1) * BS],
                            tok=tintra, w=BS, evac=attn_ev, extra_reads=[ob_r],
                            rows=64)
            tS = []
            for c in range(2):
                pb, pb_r = K_.psum_b[c], K_.psum_br[c]
                tt = plan.op("pe", lambda e, c=c, qk=qk, pb=pb: e.transpose(
                    pb[:, 0:64], qk[:, 256 + c * 128:256 + (c + 1) * 128],
                    cst[0:64, 0:64]),
                    reads=[qkb_rs[sl], cst_r], writes=[pb_r], deps=[tq],
                    tag=f"aa_tr{h}_{blk}_{c}")
                te = plan.op("dve", lambda e, c=c, pb=pb: e.tensor_copy(
                    ktm[:, c * Dh:(c + 1) * Dh], pb[:, 0:64]),
                    reads=[pb_r], writes=[ktm_r], deps=[tt], tag=f"aa_ke{h}_{blk}_{c}")
                tk2 = plan.op("dve", lambda e, c=c, vsl=vsl, h=h: e.tensor_scalar(
                    vkd[:, c * Dh:(c + 1) * Dh], vsl[:, c * Dh:(c + 1) * Dh],
                    decay[:, 2 * h + c:2 * h + c + 1], None, ALU.mult),
                    reads=[vb_rs[sl], decay_r], writes=[vkd_r], deps=[tvl, tdec],
                    tag=f"aa_vk{h}_{blk}_{c}")
                tS.append(plan.op("pe", lambda e, c=c: e.matmul(
                    ps5[0:64, 0:Dh], ktm[:, c * Dh:(c + 1) * Dh],
                    vkd[:, c * Dh:(c + 1) * Dh], start=(c == 0), stop=(c == 1)),
                    reads=[ktm_r, vkd_r], writes=[ps5_r], deps=[te, tk2],
                    tag=f"aa_su{h}_{blk}_{c}"))
            tupd = plan.op("dve", lambda e, h=h: e.scalar_tensor_tensor(
                St[:, :], St[:, :], decay[0:64, 64 + h:65 + h], ps5[0:64, 0:Dh],
                ALU.mult, ALU.add),
                reads=[ps5_r, decay_r], writes=[St_r], deps=tS, tag=f"aa_up{h}_{blk}")
            tSb = plan.op("dve", lambda e: e.tensor_copy(Sb[:, :], St[:, :]),
                          reads=[St_r], writes=[Sb_r], deps=[tupd], tag=f"aa_sb{h}_{blk}")
            if h == 0 and "aa_dbgS" in A:
                K_.store_toks.append(e_dma(K_, "sp",
                    A["aa_dbgS"][blk * 64:(blk + 1) * 64, :], St[:, :],
                    reads=[St_r], tag=f"aa_dbg{blk}"))
    nwcols, nwcols_r = K_.sbuf("aa_nw", [128, 16], F32)
    tnw = e_dma(K_, "sp", nwcols[:, :], A["l1aa_nw"][:, :], writes=[nwcols_r], tag="aa_nw")
    an, an_r = K_.sbuf("aa_an", [128, 2 * TT], BF16)
    gsb, gsb_r = K_.sbuf("aa_gsb", [128, TT], BF16)
    gated, gated_r = K_.sbuf("aa_gated", [128, NKC * TT], BF16)
    gated_rs = [Res(f"aa_gated{k}") for k in range(NKC)]
    ach, ach_r = K_.sbuf("aa_ach", [128, 2 * TT], BF16)
    ach_rs = [Res("aa_ach0"), Res("aa_ach1")]
    rstB, rstB_r = K_.sbuf("aa_rstB", [128, TT], BF16)
    rrow, rrow_r = K_.sbuf("aa_rrow", [1, 2 * TT], F32)
    rrbf, rrbf_r = K_.sbuf("aa_rrbf", [1, TT], BF16)
    sqs2, sqs2_r = K_.sbuf("aa_sq", [128, 2 * TT], BF16)
    xres, xres_r = [], []
    for i in range(3):
        a, r = K_.sbuf(f"xr{i}", [128, TT], F32)
        xres.append(a); xres_r.append(r)
    xi = [0]
    for t in range(nf // TT):
        ps4, ps4_r = K_.psum_f[4], K_.psum_fr[4]
        for k in range(NKC):
            sl = k % 2
            tach = e_dma(K_, "sp", ach[:, sl * TT:(sl + 1) * TT],
                         A["l1aa_attn_d"][k * 128:(k + 1) * 128, t * TT:(t + 1) * TT],
                         writes=[ach_rs[sl]], tag=f"aa_la{t}_{k}")
            tsq = plan.op("act", lambda e, sl=sl: e.activation(
                sqs2[:, sl * TT:(sl + 1) * TT], ach[:, sl * TT:(sl + 1) * TT], AF.Square),
                reads=[ach_rs[sl]], writes=[sqs2_r], deps=[tach], tag=f"aa_sq{t}_{k}")
            plan.op("pe", lambda e, sl=sl, s=(k == 0), p=(k == NKC - 1): e.matmul(
                ps4[0:1, 0:TT], cst[:, 128:129], sqs2[:, sl * TT:(sl + 1) * TT],
                start=s, stop=p),
                reads=[sqs2_r, cst_r], writes=[ps4_r], deps=[tsq], tag=f"aa_ss{t}_{k}")
        t1 = plan.op("dve", lambda e: e.tensor_scalar(
            rrow[0:1, 0:TT], ps4[0:1, 0:TT], 1.0 / D, 1e-5, ALU.mult, ALU.add),
            reads=[ps4_r], writes=[rrow_r], tag=f"aa_ms{t}")
        t2 = plan.op("act", lambda e: e.activation(rrow[0:1, TT:2 * TT], rrow[0:1, 0:TT],
                                                   AF.Sqrt),
                     reads=[rrow_r], writes=[rrow_r], deps=[t1], tag=f"aa_sd{t}")
        t3 = plan.op("dve", lambda e: e.reciprocal(rrow[0:1, 0:TT], rrow[0:1, TT:2 * TT]),
                     reads=[rrow_r], writes=[rrow_r], deps=[t2], tag=f"aa_rc{t}")
        t4 = plan.op("dve", lambda e: e.tensor_copy(rrbf[0:1, :], rrow[0:1, 0:TT]),
                     reads=[rrow_r], writes=[rrbf_r], deps=[t3], tag=f"aa_rb{t}")
        ps5b, ps5b_r = K_.psum_f[5], K_.psum_fr[5]
        t5 = plan.op("pe", lambda e: e.matmul(ps5b[:, 0:TT], cst[0:1, 128:256],
                                              rrbf[0:1, :], start=True, stop=True),
                     reads=[rrbf_r, cst_r], writes=[ps5b_r], deps=[t4], tag=f"aa_rB{t}")
        t6 = plan.op("dve", lambda e: e.tensor_copy(rstB[:, :], ps5b[:, 0:TT]),
                     reads=[ps5b_r], writes=[rstB_r], deps=[t5], tag=f"aa_rE{t}")
        gw_wl = wload_xbarT(K_, A["l1aa_gw_bf"], cg)
        wbufg, wbufg_r = K_.sbuf("aa_wg", [128, 128], BF16)
        for m in range(NKC):
            tokg = None
            for ki in range(NKC):
                twg = gw_wl(ki, 0, [m], wbufg[:, 0:128], wbufg_r)
                tokg = plan.op("pe", lambda e, ki=ki, s=(ki == 0), p=(ki == NKC - 1),
                               t=t: e.matmul(K_.psum_f[3][:, 0:TT], wbufg[:, 0:128],
                                        y[:, ki * nf + t * TT: ki * nf + (t + 1) * TT],
                                        start=s, stop=p),
                               reads=[wbufg_r, y_r], writes=[K_.psum_fr[3]],
                               deps=[twg], tag=f"aa_gm{t}_{m}_{ki}")
            tsig = plan.op("act", lambda e: e.activation(gsb[:, :], K_.psum_f[3][:, 0:TT],
                                                         AF.Sigmoid),
                           reads=[K_.psum_fr[3]], writes=[gsb_r], deps=[tokg],
                           tag=f"aa_sg{t}_{m}")
            sl = m % 2
            tach = e_dma(K_, "sp", ach[:, sl * TT:(sl + 1) * TT],
                         A["l1aa_attn_d"][m * 128:(m + 1) * 128, t * TT:(t + 1) * TT],
                         writes=[ach_rs[sl]], tag=f"aa_la2{t}_{m}")
            tn1 = plan.op("pool", lambda e, sl=sl: e.tensor_mul(
                an[:, sl * TT:(sl + 1) * TT], ach[:, sl * TT:(sl + 1) * TT], rstB[:, :]),
                reads=[ach_rs[sl], rstB_r], writes=[an_r], deps=[tach, t6],
                tag=f"aa_n1{t}_{m}")
            tn2 = plan.op("dve", lambda e, sl=sl, m=m: e.tensor_scalar(
                an[:, sl * TT:(sl + 1) * TT], an[:, sl * TT:(sl + 1) * TT],
                nwcols[:, m:m + 1], None, ALU.mult),
                reads=[nwcols_r], writes=[an_r], deps=[tn1, tnw], tag=f"aa_n2{t}_{m}")
            tg = plan.op("pool", lambda e, sl=sl, m=m: e.tensor_mul(
                gated[:, m * TT:(m + 1) * TT], an[:, sl * TT:(sl + 1) * TT], gsb[:, :]),
                reads=[an_r, gsb_r], writes=[gated_rs[m]], deps=[tn2, tsig],
                tag=f"aa_g{t}_{m}")
        def g_at(kk, tt):
            return gated[:, kk * TT:(kk + 1) * TT], [gated_rs[kk]]

        def o_sink(m, tt, ps, ps_r, tok):
            i = xi[0] % 3
            xi[0] += 1
            tl = e_dma(K_, "sp", xres[i][:, :],
                       A["x"][m * 128:(m + 1) * 128, t * TT:(t + 1) * TT],
                       writes=[xres_r[i]], tag="aa_xr")
            ta = plan.op("dve", lambda e, i=i, ps=ps: e.tensor_add(
                xres[i][:, :], xres[i][:, :], ps[:, :]),
                reads=[ps_r], writes=[xres_r[i]], deps=[tl, tok], tag="aa_add")
            ts = e_dma(K_, "sp", A["xo"][m * 128:(m + 1) * 128, t * TT:(t + 1) * TT],
                       xres[i][:, :], reads=[xres_r[i]], tag="aa_st")
            K_.store_toks.append(ts)
            return ts

        proj_fm(K_, name="l1aa_ow", wload=wload_xbarT(K_, A["l1aa_ow_bf"], cow),
                n_k=NKC, m_list=list(range(16)), t_list=[t], rhs_at=g_at,
                sink=o_sink, mg=3)


def declare_l1aa(nc, A, cfg):
    nf = cfg["n_full"]
    A["l1aa_qkv"] = nc.dram_tensor("l1aa_qkv", [6144, 2048], F32, kind="ExternalInput").ap()
    A["l1aa_gw"] = nc.dram_tensor("l1aa_gw", [2048, 2048], F32, kind="ExternalInput").ap()
    A["l1aa_ow"] = nc.dram_tensor("l1aa_ow", [2048, 2048], F32, kind="ExternalInput").ap()
    A["l1aa_nw"] = nc.dram_tensor("l1aa_nw", [128, 16], F32, kind="ExternalInput").ap()
    A["l1aa_ddT"] = nc.dram_tensor("l1aa_ddT", [32 * 256, 256], BF16, kind="ExternalInput").ap()
    A["l1aa_qd"] = nc.dram_tensor("l1aa_qd", [32, 256], BF16, kind="ExternalInput").ap()
    A["l1aa_decay"] = nc.dram_tensor("l1aa_decay", [128, 128], F32, kind="ExternalInput").ap()
    A["l1aa_qk_bf"] = nc.dram_tensor("l1aa_qk_bf", [4096, 2048], BF16, kind="Internal").ap()
    A["l1aa_v_bf"] = nc.dram_tensor("l1aa_v_bf", [2048, 2048], BF16, kind="Internal").ap()
    A["l1aa_gw_bf"] = nc.dram_tensor("l1aa_gw_bf", [2048, 2048], BF16, kind="Internal").ap()
    A["l1aa_ow_bf"] = nc.dram_tensor("l1aa_ow_bf", [2048, 2048], BF16, kind="Internal").ap()
    A["l1aa_qkf"] = nc.dram_tensor("l1aa_qkf", [4096, nf], BF16, kind="Internal").ap()
    A["l1aa_vtm"] = nc.dram_tensor("l1aa_vtm", [nf, 2048], BF16, kind="Internal").ap()
    A["l1aa_attn_d"] = nc.dram_tensor("l1aa_attn_d", [2048, nf], BF16, kind="Internal").ap()


DECLARES["l1aa"] = declare_l1aa
EMITTERS["l1aa"] = emit_l1aa


def host_l1aa_tables():
    H, BS = 32, 256
    base = 1.0 / 2.0 ** (8.0 / H)
    rate = base ** (np.arange(H, dtype=np.float64) + 1.0)
    factor = 1.0 - 0.0 / (32 - 1 + 1e-05) + 1e-05
    slope = rate * factor
    r = np.arange(BS, dtype=np.float64) + 1.0
    ddT = np.zeros((H * 256, 256), np.float32)
    qd = np.zeros((H, 256), np.float32)
    decay = np.zeros((128, 128), np.float32)
    for h in range(H):
        dd = slope[h] * (r[:, None] - r[None, :])
        dd = np.where(dd >= 0, np.exp(-dd), 0.0)
        ddT[h * 256:(h + 1) * 256, :] = dd.T.astype(np.float32)
        qd[h] = np.exp(-slope[h] * r).astype(np.float32)
        kd = np.exp(-slope[h] * (BS - r)).astype(np.float32)
        decay[:, 2 * h] = kd[0:128]
        decay[:, 2 * h + 1] = kd[128:256]
        decay[:, 64 + h] = np.exp(-slope[h] * BS)
    return ddT.astype(BF), qd.astype(BF), decay



def emit_l1moe(K_, A, cfg):
    plan = K_.plan
    n_own = cfg["n_own"]
    nto4 = n_own // 256
    cst, cst_r = K_.cst
    E, I = 8, 1024
    c1 = cast_pass(K_, A["moe_w1_bf"], A["moe_w1"], 8 * I, 2048, tag="cW1")
    c3 = cast_pass(K_, A["moe_w3_bf"], A["moe_w3"], 8 * I, 2048, tag="cW3")
    c2 = cast_pass(K_, A["moe_w2_bf"], A["moe_w2"], 8 * 2048, I, tag="cW2")
    cs1 = cast_pass(K_, A["moe_ws1_bf"], A["moe_ws1"], I, 2048, tag="cS1")
    cs3 = cast_pass(K_, A["moe_ws3_bf"], A["moe_ws3"], I, 2048, tag="cS3")
    cs2 = cast_pass(K_, A["moe_ws2_bf"], A["moe_ws2"], 2048, I, tag="cS2")
    gws, gws_r = K_.sbuf("mg_gw", [8, 2048], BF16)
    tgw = e_dma(K_, "pool", gws[:, :], A["moe_gw"][:, :], writes=[gws_r], tag="mg_gw")
    gwT, gwT_r = K_.sbuf("mg_gwT", [128, 16 * 8], BF16)
    for k in range(NKC):
        pb, pb_r = K_.psum_b[0], K_.psum_br[0]
        tt = plan.op("pe", lambda e, k=k: e.transpose(
            pb[:, 0:8], gws[:, k * 128:(k + 1) * 128], cst[0:8, 0:8]),
            reads=[gws_r, cst_r], writes=[pb_r], deps=[tgw], tag=f"mg_t{k}")
        plan.op("dve", lambda e, k=k: e.tensor_copy(gwT[:, k * 8:(k + 1) * 8], pb[:, 0:8]),
                reads=[pb_r], writes=[gwT_r], deps=[tt], tag=f"mg_e{k}")
    gbrow, gbrow_r = K_.sbuf("mg_gbr", [1, 8], F32)
    gbrowb, gbrowb_r = K_.sbuf("mg_gbrb", [1, 8], BF16)
    tgb = e_dma(K_, "sp", gbrow[0:1, :], A["moe_gb"][:, :], writes=[gbrow_r], tag="mg_gb")
    tgbb = plan.op("dve", lambda e: e.tensor_copy(gbrowb[0:1, :], gbrow[0:1, :]),
                   reads=[gbrow_r], writes=[gbrowb_r], deps=[tgb], tag="mg_gbb")
    gbB, gbB_r = K_.sbuf("mg_gbB", [128, 8], F32)
    ps5, ps5_r = K_.psum_f[5], K_.psum_fr[5]
    tb1 = plan.op("pe", lambda e: e.matmul(ps5[:, 0:8], cst[0:1, 128:256], gbrowb[0:1, :],
                                           start=True, stop=True),
                  reads=[gbrowb_r, cst_r], writes=[ps5_r], deps=[tgbb], tag="mg_gbm")
    tb2 = plan.op("dve", lambda e: e.tensor_copy(gbB[:, :], ps5[:, 0:8]),
                  reads=[ps5_r], writes=[gbB_r], deps=[tb1], tag="mg_gbe")
    y, y_r, _ = ln_and_y(K_, A, cfg, "moe", A["x"], n_own)
    rt, rt_r = K_.sbuf("mg_rt", [128, 64], F32)
    dwT, dwT_r = K_.sbuf("mg_dwT", [8, 256], BF16)
    dwB, dwB_r = K_.sbuf("mg_dwB", [128, 8 * 256], F32)
    hall, hall_r = K_.sbuf("mg_h", [128, 9 * 8 * 256], BF16)
    hall_rs = [Res(f"mg_h{e}_{m}") for e in range(9) for m in range(8)]
    sa, sa_r = K_.sbuf("mg_sa", [128, 256], BF16)
    hf, hf_r = K_.sbuf("mg_hf", [128, 256], F32)
    xres, xres_r = [], []
    for i in range(3):
        a, r = K_.sbuf(f"xr{i}", [128, TT], F32)
        xres.append(a); xres_r.append(r)
    xi = [0]

    def y_at256(k, tq):
        return y[:, k * n_own + tq * 256: k * n_own + (tq + 1) * 256], [y_r]

    for tq in range(nto4):
        for tb in range(2):
            psl, psl_r = K_.psum_f[4], K_.psum_fr[4]
            tokl = None
            for k in range(NKC):
                tokl = plan.op("pe", lambda e, k=k, tq=tq, tb=tb, s=(k == 0), p=(k == NKC - 1):
                               e.matmul(psl[:, 0:8],
                                        y[:, k * n_own + tq * 256 + tb * 128:
                                           k * n_own + tq * 256 + (tb + 1) * 128],
                                        gwT[:, k * 8:(k + 1) * 8], start=s, stop=p),
                               reads=[y_r, gwT_r], writes=[psl_r], tag=f"mg_l{tq}_{tb}_{k}")
            sc_ = rt[:, 0:8]; mx_ = rt[:, 8:9]; nmx = rt[:, 9:10]
            sm_ = rt[:, 10:11]; rc_ = rt[:, 11:12]
            bi_ = rt[:, 16:24]; e1_ = rt[:, 24:32]; b2_ = rt[:, 32:40]
            m2_ = rt[:, 40:41]; e2_ = rt[:, 41:49]; msk_ = rt[:, 49:57]
            t1 = plan.op("dve", lambda e: e.reduce_max(mx_, psl[:, 0:8], axis=AX),
                         reads=[psl_r], writes=[rt_r], tag="mg_mx")
            t2 = plan.op("dve", lambda e: e.tensor_scalar_mul(nmx, mx_, -1.0),
                         reads=[rt_r], writes=[rt_r], deps=[t1], tag="mg_nm")
            t3 = plan.op("act", lambda e: e.activation(sc_, psl[:, 0:8], AF.Exp,
                                                       bias=nmx, scale=1.0),
                         reads=[psl_r, rt_r], writes=[rt_r], deps=[t2], tag="mg_ex")
            t4 = plan.op("dve", lambda e: e.reduce_sum(sm_, sc_, axis=AX),
                         reads=[rt_r], writes=[rt_r], deps=[t3], tag="mg_sm")
            t5 = plan.op("dve", lambda e: e.reciprocal(rc_, sm_),
                         reads=[rt_r], writes=[rt_r], deps=[t4], tag="mg_rc")
            t6 = plan.op("dve", lambda e: e.tensor_scalar(sc_, sc_, rc_, None, ALU.mult),
                         reads=[rt_r], writes=[rt_r], deps=[t5], tag="mg_sc")
            t7 = plan.op("dve", lambda e: e.tensor_add(bi_, sc_, gbB[:, :]),
                         reads=[rt_r, gbB_r], writes=[rt_r], deps=[t6, tb2], tag="mg_bi")
            t8 = plan.op("dve", lambda e: e.reduce_max(mx_, bi_, axis=AX),
                         reads=[rt_r], writes=[rt_r], deps=[t7], tag="mg_m1")
            t9 = plan.op("dve", lambda e: e.tensor_scalar(e1_, bi_, mx_, None, ALU.is_equal),
                         reads=[rt_r], writes=[rt_r], deps=[t8], tag="mg_e1")
            t10 = plan.op("dve", lambda e: e.scalar_tensor_tensor(
                b2_, e1_, -1e30, bi_, ALU.mult, ALU.add),
                reads=[rt_r], writes=[rt_r], deps=[t9], tag="mg_b2")
            t11 = plan.op("dve", lambda e: e.reduce_max(m2_, b2_, axis=AX),
                          reads=[rt_r], writes=[rt_r], deps=[t10], tag="mg_m2")
            t12 = plan.op("dve", lambda e: e.tensor_scalar(e2_, b2_, m2_, None, ALU.is_equal),
                          reads=[rt_r], writes=[rt_r], deps=[t11], tag="mg_e2")
            t13 = plan.op("dve", lambda e: e.tensor_add(msk_, e1_, e2_),
                          reads=[rt_r], writes=[rt_r], deps=[t12], tag="mg_mk")
            t14 = plan.op("dve", lambda e: e.tensor_mul(sc_, sc_, msk_),
                          reads=[rt_r], writes=[rt_r], deps=[t13], tag="mg_dw")
            dwb_ = rt[:, 57:58]
            dwbf, dwbf_r = K_.sbuf("mg_dwbf", [128, 8], BF16)
            t14b = plan.op("dve", lambda e: e.tensor_copy(dwbf[:, :], sc_),
                           reads=[rt_r], writes=[dwbf_r], deps=[t14], tag="mg_dwb")
            pst, pst_r = K_.psum_b[1], K_.psum_br[1]
            t15 = plan.op("pe", lambda e: e.transpose(pst[0:8, 0:128], dwbf[:, :],
                                                      cst[:, 0:128]),
                          reads=[dwbf_r, cst_r], writes=[pst_r], deps=[t14b], tag="mg_tr")
            t16 = plan.op("dve", lambda e, tb=tb: e.tensor_copy(dwT[:, tb * 128:(tb + 1) * 128],
                                                                pst[0:8, 0:128]),
                          reads=[pst_r], writes=[dwT_r], deps=[t15], tag="mg_te")
        tdo = e_dma(K_, "sp", A["moe_dwd"][:, :], dwT[:, :], reads=[dwT_r], tag="mg_dwo")
        dwrow, dwrow_r = K_.sbuf("mg_dwrow", [1, 8 * 256], BF16)
        tdi = e_dma(K_, "sp", dwrow[0:1, :],
                    A["moe_dwd"].rearrange("e s -> (e s)").rearrange("(o f) -> o f", o=1),
                    writes=[dwrow_r], deps=[tdo], tag="mg_dwi")
        for ee in range(E):
            psb, psb_r = K_.psum_f[4], K_.psum_fr[4]
            tbr = plan.op("pe", lambda e, ee=ee: e.matmul(
                psb[:, 0:256], cst[0:1, 128:256], dwrow[0:1, ee * 256:(ee + 1) * 256],
                start=True, stop=True),
                reads=[dwrow_r, cst_r], writes=[psb_r], deps=[tdi], tag=f"mg_db{tq}_{ee}")
            plan.op("dve", lambda e, ee=ee: e.tensor_copy(dwB[:, ee * 256:(ee + 1) * 256],
                                                          psb[:, 0:256]),
                    reads=[psb_r], writes=[dwB_r], deps=[tbr], tag=f"mg_de{tq}_{ee}")
        for ee in range(9):
            if ee < E:
                w1 = wload_xbarT(K_, A["moe_w1_bf"], c1, row_off=ee * I)
                w3 = wload_xbarT(K_, A["moe_w3_bf"], c3, row_off=ee * I)
            else:
                w1 = wload_xbarT(K_, A["moe_ws1_bf"], cs1)
                w3 = wload_xbarT(K_, A["moe_ws3_bf"], cs3)

            def ha_sink(m, t, ps, ps_r, tok, ee=ee):
                return plan.op("act", lambda e, ps=ps: e.activation(
                    sa[:, :], ps[:, 0:256], AF.Silu),
                    reads=[ps_r], writes=[sa_r], deps=[tok], tag=f"mg_si{tq}_{ee}_{m}")

            def hb_sink(m, t, ps, ps_r, tok, ee=ee):
                slot = ee * 8 + m
                t1_ = plan.op("dve", lambda e, ps=ps: e.tensor_mul(
                    hf[:, :], sa[:, :], ps[:, 0:256]),
                    reads=[ps_r, sa_r], writes=[hf_r], deps=[tok], tag=f"mg_hm{tq}_{ee}_{m}")
                if ee < E:
                    return plan.op("pool", lambda e, slot=slot, ee=ee: e.tensor_mul(
                        hall[:, slot * 256:(slot + 1) * 256], hf[:, :],
                        dwB[:, ee * 256:(ee + 1) * 256]),
                        reads=[hf_r, dwB_r], writes=[hall_rs[slot]], deps=[t1_],
                        tag=f"mg_hd{tq}_{ee}_{m}")
                return plan.op("pool", lambda e, slot=slot: e.tensor_copy(
                    hall[:, slot * 256:(slot + 1) * 256], hf[:, :]),
                    reads=[hf_r], writes=[hall_rs[slot]], deps=[t1_],
                    tag=f"mg_hc{tq}_{ee}_{m}")

            for m in range(8):
                pa, pa_r = K_.psum_f[4], K_.psum_fr[4]
                pbk, pbk_r = K_.psum_f[5], K_.psum_fr[5]
                wb1, wb1_r = K_.sbuf("mg_wb1", [128, 128], BF16)
                wb3, wb3_r = K_.sbuf("mg_wb3", [128, 128], BF16)
                tok1 = tok3 = None
                for k in range(NKC):
                    tw1 = w1(k, 0, [m], wb1[:, :], wb1_r)
                    tw3 = w3(k, 0, [m], wb3[:, :], wb3_r)
                    r_ap, r_reads = y_at256(k, tq)
                    tok1 = plan.op("pe", lambda e, k=k, s=(k == 0), p=(k == NKC - 1),
                                   r=r_ap: e.matmul(pa[:, 0:256], wb1[:, :], r,
                                                    start=s, stop=p),
                                   reads=[wb1_r] + r_reads, writes=[pa_r], deps=[tw1],
                                   tag=f"mgA{tq}_{ee}_{m}_{k}")
                    tok3 = plan.op("pe", lambda e, k=k, s=(k == 0), p=(k == NKC - 1),
                                   r=r_ap: e.matmul(pbk[:, 0:256], wb3[:, :], r,
                                                    start=s, stop=p),
                                   reads=[wb3_r] + r_reads, writes=[pbk_r], deps=[tw3],
                                   tag=f"mgB{tq}_{ee}_{m}_{k}")
                ts_ = ha_sink(m, tq, pa, pa_r, tok1)
                hb_sink(m, tq, pbk, pbk_r, tok3)
        for dg in range(4):
            banks = [0, 1, 2, 3]
            lastmm = {}
            for ee in range(9):
                w2src = A["moe_w2_bf"] if ee < E else A["moe_ws2_bf"]
                w2dep = c2 if ee < E else cs2
                roff = ee * 2048 if ee < E else 0
                for ki in range(8):
                    wb2, wb2_r = K_.sbuf("mg_wb2", [128, 512], BF16)
                    tw2 = e_xbar(K_, wb2[:, :],
                                 w2src[roff + dg * 512: roff + (dg + 1) * 512,
                                       ki * 128:(ki + 1) * 128],
                                 writes=[wb2_r], deps=w2dep, tag="mg_w2")
                    slot = ee * 8 + ki
                    for mi in range(4):
                        bank = banks[mi]
                        st = (ee == 0 and ki == 0)
                        sp = (ee == 8 and ki == 7)
                        lastmm[mi] = plan.op("pe", lambda e, mi=mi, slot=slot, st=st,
                                             sp=sp, bank=bank: e.matmul(
                            K_.psum_f[bank][:, 0:256], wb2[:, mi * 128:(mi + 1) * 128],
                            hall[:, slot * 256:(slot + 1) * 256], start=st, stop=sp),
                            reads=[wb2_r, hall_rs[slot]], writes=[K_.psum_fr[bank]],
                            deps=[tw2], tag=f"mgC{tq}_{dg}_{ee}_{ki}_{mi}")
            for mi in range(4):
                m = dg * 4 + mi
                i = xi[0] % 3
                xi[0] += 1
                tl = e_dma(K_, "sp", xres[i][:, 0:256],
                           A["x"][m * 128:(m + 1) * 128, tq * 256:(tq + 1) * 256],
                           writes=[xres_r[i]], tag="mg_xr")
                ta = plan.op("dve", lambda e, i=i, mi=mi: e.tensor_add(
                    xres[i][:, 0:256], xres[i][:, 0:256], K_.psum_f[banks[mi]][:, 0:256]),
                    reads=[K_.psum_fr[banks[mi]]], writes=[xres_r[i]],
                    deps=[tl, lastmm[mi]], tag="mg_add")
                ts = e_dma(K_, "sp", A["xo"][m * 128:(m + 1) * 128, tq * 256:(tq + 1) * 256],
                           xres[i][:, 0:256], reads=[xres_r[i]], tag="mg_st")
                K_.store_toks.append(ts)


def declare_l1moe(nc, A, cfg):
    E, I = 8, 1024
    A["moe_gw"] = nc.dram_tensor("moe_gw", [8, 2048], F32, kind="ExternalInput").ap()
    A["moe_gb"] = nc.dram_tensor("moe_gb", [1, 8], F32, kind="ExternalInput").ap()
    A["moe_w1"] = nc.dram_tensor("moe_w1", [E * I, 2048], F32, kind="ExternalInput").ap()
    A["moe_w3"] = nc.dram_tensor("moe_w3", [E * I, 2048], F32, kind="ExternalInput").ap()
    A["moe_w2"] = nc.dram_tensor("moe_w2", [E * 2048, I], F32, kind="ExternalInput").ap()
    A["moe_ws1"] = nc.dram_tensor("moe_ws1", [I, 2048], F32, kind="ExternalInput").ap()
    A["moe_ws3"] = nc.dram_tensor("moe_ws3", [I, 2048], F32, kind="ExternalInput").ap()
    A["moe_ws2"] = nc.dram_tensor("moe_ws2", [2048, I], F32, kind="ExternalInput").ap()
    A["moe_dwd"] = nc.dram_tensor("moe_dwd", [8, 256], BF16, kind="Internal").ap()
    for nm, rows, cols in (("w1", E * I, 2048), ("w3", E * I, 2048), ("w2", E * 2048, I),
                           ("ws1", I, 2048), ("ws3", I, 2048), ("ws2", 2048, I)):
        A[f"moe_{nm}_bf"] = nc.dram_tensor(f"moe_{nm}_bf", [rows, cols], BF16,
                                           kind="Internal").ap()


DECLARES["l1moe"] = declare_l1moe
EMITTERS["l1moe"] = emit_l1moe



def fm_softmax_clip(K_, A, name, src_dram, dst_sb, dst_r, h, n_tok, dep=(),
                    sfx=""):
    plan = K_.plan
    cst, cst_r = K_.cst
    raw, raw_r0 = K_.sbuf(f"db_raw{sfx}", [128, 2048], BF16)
    raw_r = raw_r0
    expb, expb_r = K_.sbuf(f"db_exp{sfx}", [128, TT], BF16)
    srow, srow_r = K_.sbuf(f"db_srow{sfx}", [1, TT + 144], F32)
    brow, brow_r = K_.sbuf(f"db_brow{sfx}", [1, TT], BF16)
    colm, colm_r = K_.sbuf(f"db_colm{sfx}", [128, 4], F32)
    nb, nb_r = K_.sbuf(f"db_nb{sfx}", [128, 1], F32)
    rB, rB_r = K_.sbuf(f"db_rB{sfx}", [128, TT], F32)
    tld = e_dma(K_, "sp", raw[:, 0:n_tok], src_dram[h * 128:(h + 1) * 128, :],
                writes=[raw_r], deps=list(dep), tag=f"db_ld{name}{h}")
    cold = A["db_cold" + sfx]
    t1 = plan.op("dve", lambda e: e.reduce_max(colm[:, 0:1], raw[:, 0:n_tok], axis=AX),
                 reads=[raw_r], writes=[colm_r], deps=[tld], tag=f"db_m1{name}")
    t2 = e_dma(K_, "sp", cold[:, :], colm[:, 0:1], reads=[colm_r], tag=f"db_mo{name}")
    t3 = e_dma(K_, "sp", srow[0:1, 0:128], cold.rearrange("p o -> o p"),
               deps=[t2], writes=[srow_r], tag=f"db_mi{name}")
    t4 = plan.op("dve", lambda e: e.reduce_max(srow[0:1, 128:129], srow[0:1, 0:128],
                                               axis=AX),
                 reads=[srow_r], writes=[srow_r], deps=[t3], tag=f"db_mm{name}")
    t5 = plan.op("dve", lambda e: e.tensor_scalar_mul(srow[0:1, 129:130],
                                                      srow[0:1, 128:129], -1.0),
                 reads=[srow_r], writes=[srow_r], deps=[t4], tag=f"db_ng{name}")
    nbrow, nbrow_r = K_.sbuf(f"db_nbrow{sfx}", [1, 2], BF16)
    t5b = plan.op("dve", lambda e: e.tensor_copy(nbrow[0:1, 0:1], srow[0:1, 129:130]),
                  reads=[srow_r], writes=[nbrow_r], deps=[t5], tag=f"db_nb1{name}")
    ps5, ps5_r = K_.psum_f[5], K_.psum_fr[5]
    t6 = plan.op("pe", lambda e: e.matmul(ps5[:, 0:1], cst[0:1, 128:256], nbrow[0:1, 0:1],
                                          start=True, stop=True),
                 reads=[nbrow_r, cst_r], writes=[ps5_r], deps=[t5b], tag=f"db_nB{name}")
    t7 = plan.op("dve", lambda e: e.tensor_copy(nb[:, :], ps5[:, 0:1]),
                 reads=[ps5_r], writes=[nb_r], deps=[t6], tag=f"db_ne{name}")
    toks = []
    for s in range(n_tok // TT):
        te = plan.op("act", lambda e, s=s: e.activation(
            expb[:, 0:TT], raw[:, s * TT:(s + 1) * TT], AF.Exp,
            bias=nb[:, 0:1], scale=1.0),
            reads=[raw_r, nb_r], writes=[expb_r], deps=[t7], tag=f"db_ex{name}{s}")
        tm = plan.op("pe", lambda e, s=s: e.matmul(ps5[0:1, 0:TT], cst[:, 128:129],
                                                   expb[:, 0:TT],
                                                   start=True, stop=True),
                     reads=[expb_r, cst_r], writes=[ps5_r], deps=[te], tag=f"db_sm{name}{s}")
        tr = plan.op("dve", lambda e, s=s: e.reciprocal(srow[0:1, 144:144 + TT],
                                                        ps5[0:1, 0:TT]),
                     reads=[ps5_r], writes=[srow_r], deps=[tm], tag=f"db_rc{name}{s}")
        trb = plan.op("dve", lambda e, s=s: e.tensor_copy(brow[0:1, 0:TT],
                                                          srow[0:1, 144:144 + TT]),
                      reads=[srow_r], writes=[brow_r], deps=[tr], tag=f"db_rb{name}{s}")
        tB = plan.op("pe", lambda e, s=s: e.matmul(ps5[:, 0:TT], cst[0:1, 128:256],
                                                   brow[0:1, 0:TT],
                                                   start=True, stop=True),
                     reads=[brow_r, cst_r], writes=[ps5_r], deps=[trb], tag=f"db_rB{name}{s}")
        tE = plan.op("dve", lambda e: e.tensor_copy(rB[:, :], ps5[:, 0:TT]),
                     reads=[ps5_r], writes=[rB_r], deps=[tB], tag=f"db_rE{name}{s}")
        tmu = plan.op("pool", lambda e, s=s: e.tensor_mul(
            dst_sb[:, s * TT:(s + 1) * TT], expb[:, 0:TT], rB[:, :]),
            reads=[expb_r, rB_r], writes=[dst_r], deps=[tE, te], tag=f"db_mu{name}{s}")
        tcl = plan.op("dve", lambda e, s=s: e.tensor_scalar(
            dst_sb[:, s * TT:(s + 1) * TT], dst_sb[:, s * TT:(s + 1) * TT],
            1e-6, 1.0 - 1e-6, ALU.max, ALU.min),
            reads=[], writes=[dst_r], deps=[tmu], tag=f"db_cl{name}{s}")
        toks.append(tcl)
    return raw, raw_r, tld, toks


def emit_l1ab(K_, A, cfg):
    plan = K_.plan
    nf = cfg["n_full"]
    assert cfg["n_own"] == nf
    cst, cst_r = K_.cst
    Hh, DH, C = 16, 128, 64
    NCh = nf // C
    y, y_r, _ = ln_and_y(K_, A, cfg, "l1ab", A["x"], nf)

    def y_at(k, t):
        return y[:, k * nf + t * TT:(k * nf) + (t + 1) * TT], [y_r]

    stg = Stager(K_, "s", 512, 3)
    ntf_ = nf // TT
    tq = tk = []
    for tl in [list(range(i, min(i + 2, ntf_))) for i in range(0, ntf_, 2)]:
        tq = tq + proj_to_dram(K_, name="ab_q", wload=wload_nat(K_, A["l1ab_wq"]),
                               n_k=NKC, m_list=list(range(16)), t_list=tl,
                               rhs_at=y_at, dst=A["l1ab_qraw"], stg=stg)
        tk = tk + proj_to_dram(K_, name="ab_k", wload=wload_nat(K_, A["l1ab_wk"]),
                               n_k=NKC, m_list=list(range(16)), t_list=tl,
                               rhs_at=y_at, dst=A["l1ab_kraw"], stg=stg)
    tv = vproj_tm(K_, name="l1ab", wT_dram_bf=A["l1ab_wv"], wT_deps=[],
                  y=y, y_r=y_r, n_full=nf, n_vc=2048, dst=A["l1ab_vtm"], stg=stg,
                  nat_f32=True)
    nqt = nf // 256

    def kc_list(qt):
        return list(range(0, 2 * qt + 2))

    def mask_idx_of(qt, kc):
        if kc in (2 * qt, 2 * qt + 1):
            return qt * 2 + (kc - 2 * qt)
        return None

    tbase = softmax_attn(K_, name="l1ab_b", n_h=16, kv_of=lambda h: h,
                         n_q=nf, n_kv=nf, scale=1.0 / np.sqrt(128.0), Tq=256,
                         qf=A["l1ab_qraw"], kf=A["l1ab_kraw"], vt=A["l1ab_vtm"],
                         n_kvh=16, kc_list=kc_list,
                         bias_tab=A["abg_btab"],
                         bias_col_of=lambda qt, kc: qt * (nf // 128) + kc,
                         n_bias_cols=nqt * (nf // 128),
                         mask_tab=A["abg_mtab"][:, :], mask_idx_of=mask_idx_of,
                         n_masks=nqt * 2,
                         dst_dram=A["l1ab_obase"], stg=stg, dep_toks=tq + tk + tv)
    qs, qs_r = K_.sbuf("db_qs", [128, 2048], BF16)
    ks, ks_r = K_.sbuf("db_ks", [128, 2048], BF16)
    beta, beta_r = K_.sbuf("db_beta", [128, 2048], BF16)
    kb, kb_r = K_.sbuf("db_kb", [128, 2048], BF16)
    lgt, lgt_r = K_.sbuf("db_lg", [128, 2 * TT], BF16)
    dcst, dcst_r = K_.sbuf("db_csts", [64, 128], F32)
    tdc = e_dma(K_, "sp", dcst[:, :], A["db_cst"][:, :], writes=[dcst_r], tag="db_cst")
    St, St_r = K_.sbuf("db_S", [128, 128], F32)
    Sbp, Sbp_r = K_.sbuf("db_Sbp", [128, 128], BF16)
    Sbs, Sbs_r = K_.sbuf("db_Sbs", [128, 128], BF16)
    Arm, Arm_r = K_.sbuf("db_Arm", [64, 64 * 12], BF16)
    Rb, Rb_r = K_.sbuf("db_R", [64, 128], BF16)
    Gt, Gt_r = K_.sbuf("db_G", [64, 128], BF16)
    atn, atn_r = K_.sbuf("db_atn", [64, 64], BF16)
    vbc, vbc_r = K_.sbuf("db_vbc", [64, 128], BF16)
    ktc, ktc_r = K_.sbuf("db_ktc", [64, 128], BF16)
    scale_q = float(128.0 ** -0.5)
    olin_toks = {}
    for h in range(Hh):
        _, _, _, tqs = fm_softmax_clip(K_, A, "q", A["l1ab_qraw"], qs, qs_r, h, nf)
        kraw, kraw_r, tldk, tks = fm_softmax_clip(K_, A, "k", A["l1ab_kraw"], ks, ks_r, h, nf)
        tbet = []
        for s in range(nf // TT):
            sl_ = slice(s * TT, (s + 1) * TT)
            a1 = plan.op("act", lambda e, sl_=sl_: e.activation(
                lgt[:, 0:TT], kraw[:, sl_], AF.Abs),
                reads=[kraw_r], writes=[lgt_r], deps=[tldk], tag=f"db_ab{h}{s}")
            a2 = plan.op("act", lambda e: e.activation(lgt[:, TT:2 * TT], lgt[:, 0:TT],
                                                       AF.Exp, scale=-1.0),
                         reads=[lgt_r], writes=[lgt_r], deps=[a1], tag=f"db_e{h}{s}")
            a3 = plan.op("dve", lambda e: e.tensor_scalar_add(lgt[:, TT:2 * TT],
                                                              lgt[:, TT:2 * TT], 1.0),
                         reads=[lgt_r], writes=[lgt_r], deps=[a2], tag=f"db_p1{h}{s}")
            a4 = plan.op("act", lambda e: e.activation(lgt[:, TT:2 * TT], lgt[:, TT:2 * TT],
                                                       AF.Ln),
                         reads=[lgt_r], writes=[lgt_r], deps=[a3], tag=f"db_ln{h}{s}")
            a5 = plan.op("dve", lambda e, sl_=sl_: e.tensor_scalar_min(
                lgt[:, 0:TT], kraw[:, sl_], 0.0),
                reads=[kraw_r], writes=[lgt_r], deps=[a4], tag=f"db_mn{h}{s}")
            a6 = plan.op("dve", lambda e: e.tensor_sub(lgt[:, 0:TT], lgt[:, 0:TT],
                                                       lgt[:, TT:2 * TT]),
                         reads=[lgt_r], writes=[lgt_r], deps=[a5], tag=f"db_sb{h}{s}")
            a7 = plan.op("dve", lambda e, sl_=sl_: e.tensor_scalar(
                beta[:, sl_], lgt[:, 0:TT], 1.0 / 16.0, -16.0, ALU.mult, ALU.max),
                reads=[lgt_r], writes=[beta_r], deps=[a6], tag=f"db_bt{h}{s}")
            tbet.append(a7)
        tkb = []
        for s in range(nf // TT):
            sl_ = slice(s * TT, (s + 1) * TT)
            tkb.append(plan.op("dve", lambda e, sl_=sl_: e.tensor_mul(
                kb[:, sl_], ks[:, sl_], beta[:, sl_]),
                reads=[ks_r, beta_r], writes=[kb_r], deps=tks + tbet, tag=f"db_kb{h}{s}"))
        tS0 = plan.op("dve", lambda e: e.memset(St[:, :], 0.0), writes=[St_r], tag=f"db_S0{h}")
        tSp = plan.op("dve", lambda e: e.tensor_copy(Sbp[:, :], St[:, :]),
                      reads=[St_r], writes=[Sbp_r], deps=[tS0], tag=f"db_Sp{h}")
        tSs = plan.op("dve", lambda e: e.tensor_scalar_mul(Sbs[:, :], St[:, :], scale_q),
                      reads=[St_r], writes=[Sbs_r], deps=[tS0], tag=f"db_Ss{h}")
        for ci in range(NCh):
            cs = slice(ci * C, (ci + 1) * C)
            pb0, pb0_r = K_.psum_b[0], K_.psum_br[0]
            tt1 = plan.op("pe", lambda e, cs=cs: e.transpose(
                pb0[0:64, 0:128], beta[:, cs], cst[:, 0:128]),
                reads=[beta_r, cst_r], writes=[pb0_r], deps=tbet, tag=f"db_tb{h}_{ci}")
            tvb = e_dma(K_, "sp", vbc.rearrange("p d -> p d"),
                        A["l1ab_vtm"][ci * C:(ci + 1) * C, h * 128:(h + 1) * 128],
                        writes=[vbc_r], deps=tv, tag=f"db_lv{h}_{ci}")
            tvb2 = plan.op("dve", lambda e: e.tensor_mul(vbc[:, :], vbc[:, :], pb0[0:64, :].rearrange("p d -> p d") if False else pb0[0:64, 0:128]),
                           reads=[pb0_r], writes=[vbc_r], deps=[tvb, tt1], tag=f"db_vb{h}_{ci}")
            pb1, pb1_r = K_.psum_b[1], K_.psum_br[1]
            tt2 = plan.op("pe", lambda e, cs=cs: e.transpose(
                pb1[0:64, 0:128], ks[:, cs], cst[:, 0:128]),
                reads=[ks_r, cst_r], writes=[pb1_r], deps=tks, tag=f"db_tk{h}_{ci}")
            tkt = plan.op("dve", lambda e: e.tensor_copy(ktc[:, :], pb1[0:64, 0:128]),
                          reads=[pb1_r], writes=[ktc_r], deps=[tt2], tag=f"db_ke{h}_{ci}")
            ps0, ps0_r = K_.psum_f[0], K_.psum_fr[0]
            tL = plan.op("pe", lambda e, cs=cs: e.matmul(
                ps0[0:64, 0:64], kb[:, cs], ks[:, cs], start=True, stop=True),
                reads=[kb_r, ks_r], writes=[ps0_r], deps=tkb, tag=f"db_L{h}_{ci}")
            tA = plan.op("dve", lambda e: e.tensor_mul(Arm[:, 0:64], ps0[0:64, 0:64],
                                                       dcst[:, 0:64]),
                         reads=[ps0_r, dcst_r], writes=[Arm_r], deps=[tL, tdc],
                         tag=f"db_A{h}_{ci}")
            tAc = plan.op("pe", lambda e: e.transpose(pb0[0:64, 0:64], Arm[:, 0:64],
                                                      cst[0:64, 0:64]),
                          reads=[Arm_r, cst_r], writes=[pb0_r], deps=[tA],
                          tag=f"db_Ac{h}_{ci}")
            tAce = plan.op("dve", lambda e: e.tensor_copy(Arm[:, 64:128], pb0[0:64, 0:64]),
                           reads=[pb0_r], writes=[Arm_r], deps=[tAc], tag=f"db_Ae{h}_{ci}")
            prev_rm, prev_cm = 0, 1
            tpow = [tA, tAce]
            for lv in range(1, 6):
                rm_s, cm_s = 2 * lv, 2 * lv + 1
                t1_ = plan.op("pe", lambda e, a=prev_cm, b=prev_rm: e.matmul(
                    ps0[0:64, 0:64], Arm[:, a * 64:(a + 1) * 64],
                    Arm[:, b * 64:(b + 1) * 64], start=True, stop=True),
                    reads=[Arm_r], writes=[ps0_r], deps=tpow, tag=f"db_p{h}_{ci}_{lv}a")
                t2_ = plan.op("dve", lambda e, s=rm_s: e.tensor_copy(
                    Arm[:, s * 64:(s + 1) * 64], ps0[0:64, 0:64]),
                    reads=[ps0_r], writes=[Arm_r], deps=[t1_], tag=f"db_p{h}_{ci}_{lv}b")
                t3_ = plan.op("pe", lambda e, a=prev_rm, b=prev_cm: e.matmul(
                    ps0[0:64, 0:64], Arm[:, a * 64:(a + 1) * 64],
                    Arm[:, b * 64:(b + 1) * 64], start=True, stop=True),
                    reads=[Arm_r], writes=[ps0_r], deps=[t2_], tag=f"db_p{h}_{ci}_{lv}c")
                t4_ = plan.op("dve", lambda e, s=cm_s: e.tensor_copy(
                    Arm[:, s * 64:(s + 1) * 64], ps0[0:64, 0:64]),
                    reads=[ps0_r], writes=[Arm_r], deps=[t3_], tag=f"db_p{h}_{ci}_{lv}d")
                tpow = [t2_, t4_]
                prev_rm, prev_cm = rm_s, cm_s
            ps1, ps1_r = K_.psum_f[1], K_.psum_fr[1]
            tG1 = plan.op("pe", lambda e, cs=cs: e.matmul(
                ps1[0:64, 0:128], kb[:, cs], Sbp[:, :], start=True, stop=True),
                reads=[kb_r, Sbp_r], writes=[ps1_r], deps=tkb + [tSp],
                tag=f"db_G{h}_{ci}")
            tG2 = plan.op("dve", lambda e: e.tensor_sub(Gt[:, :], vbc[:, :],
                                                        ps1[0:64, 0:128]),
                          reads=[ps1_r, vbc_r], writes=[Gt_r], deps=[tG1, tvb2],
                          tag=f"db_Ge{h}_{ci}")
            tR = tG2
            Rsrc, Rsrc_r = Gt, Gt_r
            for lv in range(6):
                a_s = 2 * lv if lv > 0 else 0
                aslot = 1 if lv == 0 else 2 * lv + 1
                t1_ = plan.op("pe", lambda e, aslot=aslot, Rsrc=Rsrc:
                              e.matmul(ps1[0:64, 0:128],
                                       Arm[:, aslot * 64:(aslot + 1) * 64],
                                       Rsrc[:, :], start=True, stop=False),
                              reads=[Arm_r, Rsrc_r], writes=[ps1_r], deps=[tR],
                              tag=f"db_ap{h}_{ci}_{lv}a")
                t2_ = plan.op("pe", lambda e, Rsrc=Rsrc: e.matmul(
                    ps1[0:64, 0:128], cst[0:64, 0:64],
                    Rsrc[:, :], start=False, stop=True),
                    reads=[cst_r, Rsrc_r], writes=[ps1_r], deps=[t1_],
                    tag=f"db_ap{h}_{ci}_{lv}b")
                tR = plan.op("dve", lambda e: e.tensor_copy(Rb[:, :], ps1[0:64, 0:128]),
                             reads=[ps1_r], writes=[Rb_r], deps=[t2_],
                             tag=f"db_ap{h}_{ci}_{lv}c")
                Rsrc, Rsrc_r = Rb, Rb_r
            ps2, ps2_r = K_.psum_f[2], K_.psum_fr[2]
            ta1 = plan.op("pe", lambda e, cs=cs: e.matmul(
                ps2[0:64, 0:64], ks[:, cs], qs[:, cs], start=True, stop=True),
                reads=[ks_r, qs_r], writes=[ps2_r], deps=tks + tqs, tag=f"db_at{h}_{ci}")
            ta2 = plan.op("dve", lambda e: e.tensor_mul(atn[:, :], ps2[0:64, 0:64],
                                                        dcst[:, 64:128]),
                          reads=[ps2_r, dcst_r], writes=[atn_r], deps=[ta1, tdc],
                          tag=f"db_am{h}_{ci}")
            ps3, ps3_r = K_.psum_f[3], K_.psum_fr[3]
            to1 = plan.op("pe", lambda e, cs=cs: e.matmul(
                ps3[:, 0:64], Sbs[:, :], qs[:, cs], start=True, stop=False),
                reads=[Sbs_r, qs_r], writes=[ps3_r], deps=[tSs] + tqs,
                tag=f"db_o1{h}_{ci}")
            to2 = plan.op("pe", lambda e: e.matmul(ps3[:, 0:64], Rb[:, :], atn[:, :],
                                                   start=False, stop=True),
                          reads=[Rb_r, atn_r], writes=[ps3_r], deps=[tR, ta2],
                          tag=f"db_o2{h}_{ci}")
            tst = stg.store(ps3[:, 0:64], ps3_r,
                            A["l1ab_olin"][h * 128:(h + 1) * 128, ci * C:(ci + 1) * C],
                            tok=to2, w=64)
            olin_toks.setdefault(h, []).append(tst)
            ps4, ps4_r = K_.psum_f[4], K_.psum_fr[4]
            tS1 = plan.op("pe", lambda e: e.matmul(ps4[:, 0:128], ktc[:, :], Rb[:, :],
                                                   start=True, stop=True),
                          reads=[ktc_r, Rb_r], writes=[ps4_r], deps=[tkt, tR],
                          tag=f"db_S1{h}_{ci}")
            tS2 = plan.op("dve", lambda e: e.tensor_add(St[:, :], St[:, :], ps4[:, 0:128]),
                          reads=[ps4_r], writes=[St_r], deps=[tS1], tag=f"db_S2{h}_{ci}")
            tSp = plan.op("dve", lambda e: e.tensor_copy(Sbp[:, :], St[:, :]),
                          reads=[St_r], writes=[Sbp_r], deps=[tS2], tag=f"db_S3{h}_{ci}")
            tSs = plan.op("dve", lambda e: e.tensor_scalar_mul(Sbs[:, :], St[:, :], scale_q),
                          reads=[St_r], writes=[Sbs_r], deps=[tS2], tag=f"db_S4{h}_{ci}")
    mixb, mixb_r = K_.sbuf("db_mix", [128, 8 * TT], BF16)
    mixb_rs = [Res(f"db_mix{k}") for k in range(8)]
    mo, mo_r = K_.sbuf("db_mo", [128, 2 * TT], BF16)
    mo_rs = [Res("db_mo0"), Res("db_mo1")]
    xres, xres_r = [], []
    for i in range(3):
        a, r = K_.sbuf(f"xr{i}", [128, TT], F32)
        xres.append(a); xres_r.append(r)
    xi = [0]
    for t in range(ntf_):
        def m_at(kk, tt, t=t):
            sl8 = kk % 8
            sl_ = kk % 2
            tl1 = e_dma(K_, "sp", mo[:, sl_ * TT:(sl_ + 1) * TT],
                        A["l1ab_obase"][kk * 128:(kk + 1) * 128, t * TT:(t + 1) * TT],
                        writes=[mo_rs[sl_]], deps=tbase, tag=f"db_lb{t}_{kk}")
            tl2 = e_dma(K_, "pool", mixb[:, sl8 * TT:(sl8 + 1) * TT],
                        A["l1ab_olin"][kk * 128:(kk + 1) * 128, t * TT:(t + 1) * TT],
                        writes=[mixb_rs[sl8]], deps=olin_toks[kk], tag=f"db_ll{t}_{kk}")
            plan.op("dve", lambda e, sl8=sl8, sl_=sl_: e.tensor_add(
                mixb[:, sl8 * TT:(sl8 + 1) * TT], mixb[:, sl8 * TT:(sl8 + 1) * TT],
                mo[:, sl_ * TT:(sl_ + 1) * TT]),
                reads=[mo_rs[sl_]], writes=[mixb_rs[sl8]], deps=[tl1, tl2],
                tag=f"db_mx{t}_{kk}")
            return mixb[:, sl8 * TT:(sl8 + 1) * TT], [mixb_rs[sl8]]

        def o_sink(m, tt, ps, ps_r, tok, t=t):
            i = xi[0] % 3
            xi[0] += 1
            tl = e_dma(K_, "sp", xres[i][:, :],
                       A["x"][m * 128:(m + 1) * 128, t * TT:(t + 1) * TT],
                       writes=[xres_r[i]], tag="db_xr")
            ta = plan.op("dve", lambda e, i=i, ps=ps: e.scalar_tensor_tensor(
                xres[i][:, :], ps[:, :], 0.5, xres[i][:, :], ALU.mult, ALU.add),
                reads=[ps_r], writes=[xres_r[i]], deps=[tl, tok], tag="db_add")
            ts = e_dma(K_, "sp", A["xo"][m * 128:(m + 1) * 128, t * TT:(t + 1) * TT],
                       xres[i][:, :], reads=[xres_r[i]], tag="db_st")
            K_.store_toks.append(ts)
            return ts

        proj_fm(K_, name="db_wo", wload=wload_nat(K_, A["l1ab_wo"]),
                n_k=NKC, m_list=list(range(16)), t_list=[t], rhs_at=m_at,
                sink=o_sink, mg=3)


def declare_l1ab(nc, A, cfg):
    nf = cfg["n_full"]
    for w in ("wq", "wk", "wv", "wo"):
        A[f"l1ab_{w}"] = nc.dram_tensor(f"l1ab_{w}", [2048, 2048], F32,
                                        kind="ExternalInput").ap()
    nqt = nf // 256
    A["abg_btab"] = nc.dram_tensor("abg_btab", [16 * 128, nqt * (nf // 128)], F32,
                                   kind="ExternalInput").ap()
    A["abg_mtab"] = nc.dram_tensor("abg_mtab", [128, nqt * 2 * 256], BF16,
                                   kind="ExternalInput").ap()
    A["db_cst"] = nc.dram_tensor("db_cst", [64, 128], F32, kind="ExternalInput").ap()
    A["db_cold"] = nc.dram_tensor("db_cold", [128, 1], F32, kind="Internal").ap()
    A["l1ab_qraw"] = nc.dram_tensor("l1ab_qraw", [2048, nf], BF16, kind="Internal").ap()
    A["l1ab_kraw"] = nc.dram_tensor("l1ab_kraw", [2048, nf], BF16, kind="Internal").ap()
    A["l1ab_vtm"] = nc.dram_tensor("l1ab_vtm", [nf, 2048], BF16, kind="Internal").ap()
    A["l1ab_obase"] = nc.dram_tensor("l1ab_obase", [2048, nf], BF16, kind="Internal").ap()
    A["l1ab_olin"] = nc.dram_tensor("l1ab_olin", [2048, nf], BF16, kind="Internal").ap()


DECLARES["l1ab"] = declare_l1ab
EMITTERS["l1ab"] = emit_l1ab


def host_l1ab_tables(n_own=2048):
    pos = np.arange(n_own).astype(np.float64)
    nqt = n_own // 256
    nkc = n_own // 128
    btab = np.zeros((16 * 128, nqt * nkc), np.float32)
    mtab = np.zeros((128, nqt * 2 * 256), np.float32)
    for qt in range(nqt):
        qpos = pos[qt * 256:(qt + 1) * 256]
        for kc in range(nkc):
            kpos = pos[kc * 128:(kc + 1) * 128]
            if kpos.min() > qpos.max():
                btab[:, qt * nkc + kc] = -1e30
        for ci, kc in enumerate((2 * qt, 2 * qt + 1)):
            kpos = pos[kc * 128:(kc + 1) * 128]
            m = np.where(kpos[:, None] <= qpos[None, :], 0.0, -1e30).astype(np.float32)
            mtab[:, (qt * 2 + ci) * 256:(qt * 2 + ci + 1) * 256] = m
    dcst = np.zeros((64, 128), np.float32)
    ii = np.arange(64)
    dcst[:, 0:64] = np.where(ii[:, None] > ii[None, :], -1.0, 0.0)
    dcst[:, 64:128] = np.where(ii[:, None] <= ii[None, :], 128.0 ** -0.5, 0.0)
    return btab, mtab.astype(BF), dcst


def e_dma_sw(K_, q, pairs_of, reads=(), writes=(), deps=(), tag=""):
    n0 = len(pairs_of(0))
    assert len(pairs_of(1)) == n0

    def fn(eng, attach):
        for arm in K_.nc.switch_core_id(engines=[eng], n=128):
            p = int(arm) % 2
            for (o_ap, i_ap) in pairs_of(p):
                attach(eng.dma_start(out=o_ap, in_=i_ap))
    return K_.plan.op_sw(q, fn, reads=reads, writes=writes, deps=deps,
                         n_dma=n0, tag=tag)



NOWN = 1024


def host_striped_tables(p, n_heads=16, alibi=True, Tq=256, scale=None):
    if scale is None:
        scale = 1.0 / np.sqrt(128.0)
    own = np.arange(p, 16, 2)
    pos = np.arange(2048).astype(np.float64)
    spt = Tq // 128
    nqt = NOWN // Tq
    nkc = 16
    slopes = np.array([(2.0 ** (-8.0 / n_heads)) ** (i + 1) for i in range(n_heads)])
    if not alibi:
        slopes = slopes * 0.0
    btab = np.zeros((n_heads * 128, nqt * nkc), np.float32)
    mtab = np.zeros((128, nqt * 4 * Tq), np.float32)
    qrow = np.zeros((n_heads, NOWN), np.float32)
    for qt in range(nqt):
        gs = own[qt * spt:(qt + 1) * spt]
        qpos = np.concatenate([pos[g * 128:(g + 1) * 128] for g in gs])
        for kc in range(nkc):
            kpos = pos[kc * 128:(kc + 1) * 128]
            for h in range(n_heads):
                if kpos.min() > qpos.max():
                    btab[h * 128:(h + 1) * 128, qt * nkc + kc] = -1e30
                else:
                    btab[h * 128:(h + 1) * 128, qt * nkc + kc] = (
                        slopes[h] * kpos).astype(np.float32)
        cands = striped_cands(qt, Tq)
        for ci, kc in enumerate(cands):
            kpos = pos[kc * 128:(kc + 1) * 128]
            m = np.where(kpos[:, None] <= qpos[None, :], 0.0, -1e30)
            mtab[:, (qt * 4 + ci) * Tq:(qt * 4 + ci + 1) * Tq] = m.astype(np.float32)
    for h in range(n_heads):
        lpos = np.concatenate([pos[g * 128:(g + 1) * 128] for g in own])
        qrow[h] = (-slopes[h] * lpos / scale).astype(np.float32)
    return btab, mtab.astype(BF), qrow.astype(BF)


def striped_cands(qt, Tq=256):
    spt = Tq // 128
    base = 2 * spt * qt
    return [base + j for j in range(4) if base + j < 16]


def striped_kc_list(qt, Tq=256):
    spt = Tq // 128
    return list(range(0, min(16, 2 * spt * qt + spt + 2)))


def xres_at(K_, k, t, TT_=TT):
    x, xrs = K_.x_sb, K_.x_rs
    return x[:, k * NOWN + t * TT_:(k * NOWN) + t * TT_ + TT_], [xrs[k]]


def ln_resident(K_, uniq, ysh=None):
    y, y_r = K_.sbuf(f"y_{uniq}", [128, NKC * NOWN], BF16)

    def x_at(k, t):
        return xres_at(K_, k, t)

    def y_sink(k, t, emit):
        return emit(y[:, k * NOWN + t * TT: k * NOWN + (t + 1) * TT], [y_r])

    ln_fm(K_, name=uniq, n_tok=NOWN, x_at=x_at, y_sink=y_sink)
    tst = None
    if ysh is not None:
        def pairs(p):
            return [(ysh[k * 128:(k + 1) * 128, p * NOWN:(p + 1) * NOWN],
                     y[:, k * NOWN:(k + 1) * NOWN]) for k in range(NKC)]
        tst = e_dma_sw(K_, "sp", pairs, reads=[y_r], tag=f"ysh_{uniq}")
    return y, y_r, tst


def proj_to_shared(K_, *, name, wload, m_list, rhs_at, sh, stg, n_k=NKC, mg=3,
                   evac_of=None, extra_reads=(), evac_eng="dve", row_of=None):
    toks = []

    def sink(m, t, ps, ps_r, tok):
        a_, r_ = stg.slots[stg.i % len(stg.slots)]
        stg.i += 1
        ev = evac_of(m, t, ps) if evac_of else None
        if ev is None:
            t1 = K_.plan.op(evac_eng, lambda e, ps=ps: e.tensor_copy(a_[:, 0:TT], ps[:, 0:TT]),
                            reads=[ps_r] + list(extra_reads), writes=[r_], deps=[tok],
                            tag="pshev")
        else:
            t1 = K_.plan.op(evac_eng, lambda e, ev=ev: ev(e, a_[:, 0:TT]),
                            reads=[ps_r] + list(extra_reads), writes=[r_], deps=[tok],
                            tag="pshevc")
        row = row_of(m) if row_of else m * 128

        def pairs(p, row=row, t=t, a_=a_):
            return [(sh[row:row + 128, p * NOWN + t * TT: p * NOWN + (t + 1) * TT],
                     a_[:, 0:TT])]
        toks.append(e_dma_sw(K_, "sp", pairs, reads=[r_], tag="pshst"))
        return toks[-1]

    proj_fm(K_, name=name, wload=wload, n_k=n_k, m_list=m_list,
            t_list=[0, 1], rhs_at=rhs_at, sink=sink, mg=mg)
    return toks


def dram_rhs_loader(K_, name, src, n_cols_per_chunk, nslots=4, sw_own=False,
                    deps=()):
    buf, _ = K_.sbuf(f"rl_{name}", [128, nslots * TT], BF16)
    rs = [Res(f"rl_{name}{i}") for i in range(nslots)]
    ctr = [0]

    def rhs_at(k, t):
        i = ctr[0] % nslots
        ctr[0] += 1
        sl = buf[:, i * TT:(i + 1) * TT]
        if sw_own:
            def pairs(p, k=k, t=t, sl=sl):
                return [(sl, src[k * 128:(k + 1) * 128,
                                 p * NOWN + t * TT: p * NOWN + (t + 1) * TT])]
            e_dma_sw(K_, "sp", pairs, writes=[rs[i]], deps=list(deps), tag=f"rl{name}")
        else:
            e_dma(K_, "sp", sl, src[k * 128:(k + 1) * 128, t * TT:(t + 1) * TT],
                  writes=[rs[i]], deps=list(deps), tag=f"rl{name}")
        return sl, [rs[i]]
    return rhs_at


def resid_sink(K_, scale=None, bias_col=None, bias_res=None):
    def sink(m, t, ps, ps_r, tok):
        x, xrs = K_.x_sb, K_.x_rs
        sl = x[:, m * NOWN + t * TT: m * NOWN + (t + 1) * TT]
        extra = []
        if bias_res is not None:
            extra.append(bias_res)
        if scale is not None:
            t1 = K_.plan.op("dve", lambda e, ps=ps, sl=sl: e.scalar_tensor_tensor(
                sl, ps[:, 0:TT], scale, sl, ALU.mult, ALU.add),
                reads=[ps_r] + extra, writes=[xrs[m]], deps=[tok], tag="rs_add")
        elif bias_col is not None:
            t1 = K_.plan.op("dve", lambda e, ps=ps, sl=sl, m=m: e.scalar_tensor_tensor(
                sl, ps[:, 0:TT], bias_col(m), sl, ALU.add, ALU.add),
                reads=[ps_r] + extra, writes=[xrs[m]], deps=[tok], tag="rs_addb")
        else:
            t1 = K_.plan.op("dve", lambda e, ps=ps, sl=sl: e.tensor_add(
                sl, sl, ps[:, 0:TT]),
                reads=[ps_r] + extra, writes=[xrs[m]], deps=[tok], tag="rs_add0")
        return t1
    return sink


def emit_l0aa_F(K_, A):
    plan = K_.plan
    cq = cast_pass(K_, A["l0aa_wq_bf"], A["l0aa_wq"], 2048, 2048, tag="cwq")
    ck = cast_pass(K_, A["l0aa_wk_bf"], A["l0aa_wk"], 1024, 2048, tag="cwk")
    cv = cast_pass(K_, A["l0aa_wv_bf"], A["l0aa_wv"], 1024, 2048, tag="cwv")
    co = cast_pass(K_, A["l0aa_wo_bf"], A["l0aa_wo"], 2048, 2048, tag="cwo")
    with K_.phase():
        stg = Stager(K_, "s", 512, 3)
        y, y_r, tys = ln_resident(K_, "l0aa", ysh=A["sh1_y"])

        def y_at(k, t):
            return y[:, k * NOWN + t * TT: k * NOWN + (t + 1) * TT], [y_r]

        tq = proj_to_shared(K_, name="f0q", wload=wload_xbarT(K_, A["l0aa_wq_bf"], cq),
                            m_list=list(range(16)), rhs_at=y_at, sh=A["sh1_q"], stg=stg)
        tk = proj_to_shared(K_, name="f0k", wload=wload_xbarT(K_, A["l0aa_wk_bf"], ck),
                            m_list=list(range(8)), rhs_at=y_at, sh=A["sh1_k"], stg=stg)
        tv = []
        wt, wt_r = K_.sbuf("fwvT", [128, NKC * TT], BF16)
        wt_rs = [Res(f"fwvT{k}") for k in range(NKC)]
        for vt_i in range(2):
            wl = []
            for k in range(NKC):
                wl.append(e_xbar(K_, wt[:, k * TT:(k + 1) * TT],
                                 A["l0aa_wv_bf"][vt_i * TT:(vt_i + 1) * TT,
                                                 k * 128:(k + 1) * 128],
                                 writes=[wt_rs[k]], deps=cv, tag="fwvT"))
            for tb in range(8):
                ps, ps_r = K_.psum_f[tb % 2], K_.psum_fr[tb % 2]
                tok = None
                for k in range(NKC):
                    tok = plan.op("pe", lambda e, k=k, s=(k == 0), p=(k == NKC - 1),
                                  tb=tb, ps=ps: e.matmul(
                        ps[:, 0:TT], y[:, k * NOWN + tb * 128:k * NOWN + (tb + 1) * 128],
                        wt[:, k * TT:(k + 1) * TT], start=s, stop=p),
                        reads=[y_r, wt_rs[k]], writes=[ps_r], deps=[wl[k]],
                        tag=f"fvm{vt_i}_{tb}_{k}")
                a_, r_ = stg.slots[stg.i % len(stg.slots)]
                stg.i += 1
                t1 = plan.op("dve", lambda e, ps=ps, a_=a_: e.tensor_copy(
                    a_[:, 0:TT], ps[:, 0:TT]),
                    reads=[ps_r], writes=[r_], deps=[tok], tag="fvev")

                def pairs(p, tb=tb, vt_i=vt_i, a_=a_):
                    return [(A["sh1_vt"][p * NOWN + tb * 128: p * NOWN + (tb + 1) * 128,
                                         vt_i * TT:(vt_i + 1) * TT], a_[:, 0:TT])]
                tv.append(e_dma_sw(K_, "sp", pairs, reads=[r_], tag="fvst"))
        plan.barrier(deps=tq + tk + tv + [tys])
    with K_.phase():
        stg = Stager(K_, "s", 512, 3)
        softmax_attn(K_, name="f0a", n_h=16, kv_of=lambda h: h // 2,
                     n_q=NOWN, n_kv=2048, scale=1.0 / np.sqrt(128.0), Tq=512,
                     qf=A["sh1_q"], kf=A["sh1_k"], vt=A["sh1_vt"], n_kvh=8,
                     kc_list=lambda qt: list(range(16)),
                     dst_dram=A["sh1_a"], stg=stg,
                     q_stripes=lambda p: list(range(p, 16, 2)),
                     out_stripes=True)
        plan.barrier()
    with K_.phase():
        rhs = dram_rhs_loader(K_, "f0wo", A["sh1_a"], NOWN, sw_own=True)
        proj_fm(K_, name="f0wo", wload=wload_xbarT(K_, A["l0aa_wo_bf"], co),
                n_k=NKC, m_list=list(range(16)), t_list=[0, 1], rhs_at=rhs,
                sink=resid_sink(K_), mg=3)


def emit_l0ffn_F(K_, A, first):
    plan = K_.plan
    with K_.phase():
        sgn_toks = []
        tb1 = None
        if first:
            wrow, wrow_r = [], []
            for i in range(2):
                a, r = K_.sbuf(f"f_wr{i}", [128, 2048], F32)
                wrow.append(a); wrow_r.append(r)
            wsg, wsg_r = [], []
            for i in range(2):
                a, r = K_.sbuf(f"f_ws{i}", [128, 2048], BF16)
                wsg.append(a); wsg_r.append(r)
            amx, amx_r = K_.sbuf("f_amx", [128, 2], F32)
            for rci in range(16):
                b = rci % 2
                tl = e_dma(K_, "sp", wrow[b][:, :], A["l0f_w"][rci * 128:(rci + 1) * 128, :],
                           writes=[wrow_r[b]], tag=f"fw{rci}")
                ts = plan.op("act", lambda e, b=b: e.activation(wsg[b][:, :], wrow[b][:, :],
                                                                AF.Sign),
                             reads=[wrow_r[b]], writes=[wsg_r[b]], deps=[tl], tag=f"fsg{rci}")
                sgn_toks.append(e_dma(K_, "sp", A["l0f_ws_bf"][rci * 128:(rci + 1) * 128, :],
                                      wsg[b][:, :], reads=[wsg_r[b]], deps=[ts], tag=f"fso{rci}"))
                if rci == 0:
                    plan.op("dve", lambda e, b=b: e.reduce_max(
                        amx[:, 0:1], wrow[b][:, :], axis=AX, apply_absolute_value=True),
                        reads=[wrow_r[b]], writes=[amx_r], deps=[tl], tag="fmx0")
                else:
                    plan.op("dve", lambda e, b=b: e.reduce_max(
                        amx[:, 1:2], wrow[b][:, :], axis=AX, apply_absolute_value=True),
                        reads=[wrow_r[b]], writes=[amx_r], deps=[tl], tag=f"fmx{rci}")
                    plan.op("dve", lambda e: e.tensor_max(amx[:, 0:1], amx[:, 0:1], amx[:, 1:2]),
                            reads=[amx_r], writes=[amx_r], tag=f"fmm{rci}")
            tb1 = e_dma(K_, "sp", A["l0f_col"][:, :], amx[:, 0:1], reads=[amx_r], tag="fmxo")
        srow, srow_r = K_.sbuf("f_srow", [1, 130], F32)
        tb2 = e_dma(K_, "sp", srow[0:1, 0:128], A["l0f_col"].rearrange("p o -> o p"),
                    deps=([tb1] if tb1 else []), writes=[srow_r], tag="fmxi")
        tmx = plan.op("dve", lambda e: e.reduce_max(srow[0:1, 128:129], srow[0:1, 0:128],
                                                    axis=AX),
                      reads=[srow_r], writes=[srow_r], deps=[tb2], tag="fmxr")
        pcol, pcol_r = K_.sbuf("f_pcol", [128, 64], F32)
        pcolb, pcolb_r = K_.sbuf("f_pcolb", [128, 64], BF16)
        tpc = e_dma(K_, "sp", pcol[:, :], A["l0f_pcol"][:, :], writes=[pcol_r], tag="fpc")
        tpcb = plan.op("dve", lambda e: e.tensor_copy(pcolb[:, :], pcol[:, :]),
                       reads=[pcol_r], writes=[pcolb_r], deps=[tpc], tag="fpcb")
        srow_b, srow_b_r = K_.sbuf("f_srb", [1, 2], BF16)
        tsb = plan.op("dve", lambda e: e.tensor_copy(srow_b[0:1, 0:1], srow[0:1, 128:129]),
                      reads=[srow_r], writes=[srow_b_r], deps=[tmx], tag="fsb")
        cst, cst_r = K_.cst
        ps5, ps5_r = K_.psum_f[5], K_.psum_fr[5]
        tbc = plan.op("pe", lambda e: e.matmul(ps5[:, 0:1], cst[0:1, 128:256],
                                               srow_b[0:1, 0:1], start=True, stop=True),
                      reads=[srow_b_r, cst_r], writes=[ps5_r], deps=[tsb], tag="fsbb")
        scol, scol_r = K_.sbuf("f_scol", [128, 18], F32)
        tsc = plan.op("dve", lambda e: e.tensor_copy(scol[:, 16:17], ps5[:, 0:1]),
                      reads=[ps5_r], writes=[scol_r], deps=[tbc], tag="fsc")
        tsg = plan.op("dve", lambda e: e.tensor_scalar(
            scol[:, 0:16], pcol[:, 0:16], scol[:, 16:17], None, ALU.mult),
            reads=[pcol_r, scol_r], writes=[scol_r], deps=[tsc, tpc], tag="fsg_col")
        y, y_r, _ = ln_resident(K_, "l0f")
        y2, y2_r = K_.sbuf("y2_l0f", [128, NKC * NOWN], BF16)

        def x2_at(k, t):
            return y[:, k * NOWN + t * TT: k * NOWN + (t + 1) * TT], [y_r]

        def y2_sink(k, t, emit):
            return emit(y2[:, k * NOWN + t * TT: k * NOWN + (t + 1) * TT], [y2_r])

        ln_fm(K_, name="l0f2", n_tok=NOWN, x_at=x2_at, y_sink=y2_sink, in_bf16=True,
              gain_col=(lambda k: pcol[:, 32 + k:33 + k]), gain_res=pcol_r,
              gain_dep=tpc)
        ccol, ccol_r = K_.sbuf("f_ccol", [128, 16], F32)
        tmpo, tmpo_r = K_.sbuf("f_tmpo", [128, TT], F32)

        def rhs_at(k, t):
            if t == 99:
                return pcolb[:, 48 + k:49 + k], [pcolb_r]
            return y2[:, k * NOWN + t * TT: k * NOWN + (t + 1) * TT], [y2_r]

        def sink2(m, t, ps, ps_r, tok):
            if t == 99:
                return plan.op("dve", lambda e, m=m, ps=ps: e.tensor_copy(
                    ccol[:, m:m + 1], ps[:, 0:1]),
                    reads=[ps_r], writes=[ccol_r], deps=[tok], tag=f"fc{m}")
            x, xrs = K_.x_sb, K_.x_rs
            sl = x[:, m * NOWN + t * TT: m * NOWN + (t + 1) * TT]
            t1 = plan.op("dve", lambda e, m=m, ps=ps: e.tensor_scalar(
                tmpo[:, :], ps[:, 0:TT], ccol[:, m:m + 1], scol[:, m:m + 1],
                ALU.add, ALU.mult),
                reads=[ps_r, ccol_r, scol_r], writes=[tmpo_r], deps=[tok, tsg],
                tag="f_e1")
            t2 = plan.op("dve", lambda e, m=m, sl=sl: e.scalar_tensor_tensor(
                sl, tmpo[:, :], pcol[:, 16 + m:17 + m], sl, ALU.add, ALU.add),
                reads=[tmpo_r, pcol_r], writes=[K_.x_rs[m]], deps=[t1], tag="f_e2")
            return t2

        proj_fm(K_, name="f_mm", wload=wload_xbarT(K_, A["l0f_ws_bf"], sgn_toks),
                n_k=NKC, m_list=list(range(16)), t_list=[99, 0, 1],
                rhs_at=rhs_at, sink=sink2, mg=2)


def emit_l0ab_F(K_, A):
    plan = K_.plan
    cst, cst_r = K_.cst
    with K_.phase():
        stg = Stager(K_, "s", 512, 3)
        y, y_r, tys = ln_resident(K_, "l0ab", ysh=A["sh2_y"])

        def y_at(k, t):
            return y[:, k * NOWN + t * TT: k * NOWN + (t + 1) * TT], [y_r]

        bcols, bcols_r = K_.sbuf("ab_bcols", [128, 48], F32)
        tbc = e_dma(K_, "sp", bcols[:, :], A["ab_bcol"][:, :], writes=[bcols_r], tag="ab_bc")
        bvrow, bvrow_r = K_.sbuf("ab_bvr_s", [1, TT], F32)
        bvrow_b, bvrow_b_r = K_.sbuf("ab_bvrb_s", [1, TT], BF16)
        bvB, bvB_r = K_.sbuf("ab_bvB", [128, 2048], BF16)
        ps5, ps5_r = K_.psum_f[5], K_.psum_fr[5]
        for vt_i in range(4):
            tbv = e_dma(K_, "sp", bvrow[0:1, :], A["ab_bvrow"][0:1, vt_i * TT:(vt_i + 1) * TT],
                        writes=[bvrow_r], tag="ab_bv")
            tbvb = plan.op("dve", lambda e: e.tensor_copy(bvrow_b[0:1, :], bvrow[0:1, :]),
                           reads=[bvrow_r], writes=[bvrow_b_r], deps=[tbv], tag="ab_bvb")
            tm = plan.op("pe", lambda e: e.matmul(ps5[:, :], cst[0:1, 128:256],
                                                  bvrow_b[0:1, :], start=True, stop=True),
                         reads=[bvrow_b_r, cst_r], writes=[ps5_r], tag=f"ab_bvm{vt_i}")
            plan.op("dve", lambda e, v=vt_i: e.tensor_copy(bvB[:, v * TT:(v + 1) * TT],
                                                           ps5[:, :]),
                    reads=[ps5_r], writes=[bvB_r], deps=[tm], tag=f"ab_bve{vt_i}")

        def bias_evac(col):
            def eo(m, t, ps):
                def ev(e, out_ap, m=m, ps=ps):
                    return e.tensor_scalar(out_ap, ps[:, 0:TT],
                                           bcols[:, col + m:col + m + 1], None, ALU.add)
                return ev
            return eo

        tq = proj_to_shared(K_, name="f2q", wload=wload_nat(K_, A["l0ab_wq"]),
                            m_list=list(range(16)), rhs_at=y_at, sh=A["sh2_q"], stg=stg,
                            evac_of=bias_evac(0), extra_reads=[bcols_r])
        tk = proj_to_shared(K_, name="f2k", wload=wload_nat(K_, A["l0ab_wk"]),
                            m_list=list(range(16)), rhs_at=y_at, sh=A["sh2_k"], stg=stg,
                            evac_of=bias_evac(16), extra_reads=[bcols_r])
        tv = []
        wt, wt_r = K_.sbuf("fwvT", [128, NKC * TT], BF16)
        wt_rs = [Res(f"fwvT{k}") for k in range(NKC)]
        for vt_i in range(4):
            wl = []
            for k in range(NKC):
                wl.append(e_dma(K_, "pool", wt[:, k * TT:(k + 1) * TT],
                                A["l0ab_wv"][k * 128:(k + 1) * 128,
                                             vt_i * TT:(vt_i + 1) * TT],
                                writes=[wt_rs[k]], tag="fwvT2"))
            for tb in range(8):
                ps, ps_r = K_.psum_f[tb % 2], K_.psum_fr[tb % 2]
                tok = None
                for k in range(NKC):
                    tok = plan.op("pe", lambda e, k=k, s=(k == 0), p=(k == NKC - 1),
                                  tb=tb, ps=ps: e.matmul(
                        ps[:, 0:TT], y[:, k * NOWN + tb * 128:k * NOWN + (tb + 1) * 128],
                        wt[:, k * TT:(k + 1) * TT], start=s, stop=p),
                        reads=[y_r, wt_rs[k]], writes=[ps_r], deps=[wl[k]],
                        tag=f"f2vm{vt_i}_{tb}_{k}")
                a_, r_ = stg.slots[stg.i % len(stg.slots)]
                stg.i += 1
                t1 = plan.op("dve", lambda e, ps=ps, a_=a_, v=vt_i: e.tensor_add(
                    a_[:, 0:TT], ps[:, 0:TT], bvB[:, v * TT:(v + 1) * TT]),
                    reads=[ps_r, bvB_r], writes=[r_], deps=[tok], tag="f2vev")

                def pairs(p, tb=tb, vt_i=vt_i, a_=a_):
                    return [(A["sh2_vt"][p * NOWN + tb * 128: p * NOWN + (tb + 1) * 128,
                                         vt_i * TT:(vt_i + 1) * TT], a_[:, 0:TT])]
                tv.append(e_dma_sw(K_, "sp", pairs, reads=[r_], tag="f2vst"))
        plan.barrier(deps=tq + tk + tv + [tys])
        stg2 = Stager(K_, "s2", 512, 3)
        softmax_attn(K_, name="f2a", n_h=16, kv_of=lambda h: h,
                     n_q=NOWN, n_kv=2048, scale=1.0 / np.sqrt(128.0), Tq=256,
                     qf=A["sh2_q"], kf=A["sh2_k"], vt=A["sh2_vt"], n_kvh=16,
                     kc_list=lambda qt: striped_kc_list(qt, 256),
                     bias_tab=A["ab_btab"],
                     bias_col_of=lambda qt, kc: qt * 16 + kc, n_bias_cols=64,
                     mask_tab=A["ab_mtab"][:, :],
                     mask_idx_of=lambda qt, kc: (qt * 4 + striped_cands(qt, 256).index(kc))
                     if kc in striped_cands(qt, 256) else None,
                     n_masks=16, qrow_tab=A["ab_qrow"],
                     dst_dram=A["sh2_a"], stg=stg2,
                     q_stripes=lambda p: list(range(p, 16, 2)),
                     out_stripes=True)
        plan.barrier()
        rhs = dram_rhs_loader(K_, "f2wo", A["sh2_a"], NOWN, sw_own=True)
        proj_fm(K_, name="f2wo", wload=wload_nat(K_, A["l0ab_wo"]),
                n_k=NKC, m_list=list(range(16)), t_list=[0, 1], rhs_at=rhs,
                sink=resid_sink(K_, bias_col=lambda m: bcols[:, 32 + m:33 + m],
                                bias_res=bcols_r), mg=3)


def emit_l1aa_F(K_, A):
    plan = K_.plan
    cst, cst_r = K_.cst
    H, Dh, BS, NB = 16, 64, 256, 8
    qkv = A["l1aa_qkv"]
    qk_src = qkv.rearrange("(h r) c -> h r c", h=H)[:, 0:128, :]
    v_src = qkv.rearrange("(h r) c -> h r c", h=H)[:, 128:192, :]
    cqk = [e_dma(K_, "pool", A["l1aa_qk_bf"].rearrange("(h r) c -> h r c", h=H),
                 qk_src, tag="c_qk")]
    cv = [e_dma(K_, "pool", A["l1aa_v_bf"].rearrange("(h r) c -> h r c", h=H),
                v_src, tag="c_v")]
    cg = cast_pass(K_, A["l1aa_gw_bf"], A["l1aa_gw"], 2048, 2048, tag="c_gw")
    cow = cast_pass(K_, A["l1aa_ow_bf"], A["l1aa_ow"], 2048, 2048, tag="c_ow")
    with K_.phase():
        stg = Stager(K_, "s", 512, 3)
        y, y_r, tys = ln_resident(K_, "l1aa", ysh=A["sh3_y"])
        plan.barrier(deps=[tys])
        yld = dram_rhs_loader(K_, "f3y", A["sh3_y"], 2048)
        tqk = []
        for tl in ([0, 1], [2, 3]):
            tqk += proj_to_dram(K_, name="f3qk", wload=wload_xbarT(K_, A["l1aa_qk_bf"], cqk),
                                n_k=NKC, m_list=list(range(H)), t_list=tl,
                                rhs_at=yld, dst=A["l1aa_qkf"], stg=stg,
                                evac_of=lambda m, t, ps: (lambda e, out_ap, ps=ps:
                                    e.activation(out_ap, ps[:, 0:out_ap.shape[-1]], AF.Silu)),
                                evac_eng="act")
        tv = []
        wt, wt_r = K_.sbuf("f3wvT", [128, NKC * TT], BF16)
        wt_rs = [Res(f"f3wvT{k}") for k in range(NKC)]
        for vt_i in range(2):
            wl = []
            for k in range(NKC):
                wl.append(e_xbar(K_, wt[:, k * TT:(k + 1) * TT],
                                 A["l1aa_v_bf"][vt_i * TT:(vt_i + 1) * TT,
                                                k * 128:(k + 1) * 128],
                                 writes=[wt_rs[k]], deps=cv, tag="f3wvT"))
            for tb in range(16):
                ps, ps_r = K_.psum_f[tb % 2], K_.psum_fr[tb % 2]
                tok = None
                for k in range(NKC):
                    r_ap, r_reads = yld(k, tb // 4)
                    sl = r_ap[:, (tb % 4) * 128:(tb % 4 + 1) * 128]
                    tok = plan.op("pe", lambda e, k=k, s=(k == 0), p=(k == NKC - 1),
                                  sl=sl, ps=ps: e.matmul(
                        ps[:, 0:TT], sl, wt[:, k * TT:(k + 1) * TT], start=s, stop=p),
                        reads=r_reads + [wt_rs[k]], writes=[ps_r], deps=[wl[k]],
                        tag=f"f3vm{vt_i}_{tb}_{k}")
                tv.append(stg.store(ps[:, 0:TT], ps_r,
                                    A["l1aa_vtm"][tb * 128:(tb + 1) * 128,
                                                  vt_i * TT:(vt_i + 1) * TT],
                                    tok=tok, evac=lambda e, out_ap, ps=ps: e.activation(
                                        out_ap, ps[:, 0:TT], AF.Silu),
                                    evac_eng="act"))
        qkb, qkb_r = K_.sbuf("aa_qkb", [64, 2 * 512], BF16)
        qkb_rs = [Res("aa_qkb0"), Res("aa_qkb1")]
        vb, vb_r = K_.sbuf("aa_vb", [128, 4 * Dh], BF16)
        vb_rs = [Res("aa_vb0"), Res("aa_vb1")]
        ddt, ddt_r = K_.sbuf("aa_ddt", [128, 2 * BS], BF16)
        qdB, qdB_r = K_.sbuf("aa_qdB", [128, BS], F32)
        qdrow, qdrow_r = K_.sbuf("aa_qdrow", [1, BS], BF16)
        decay, decay_r = K_.sbuf("aa_decay", [128, 128], F32)
        tdec = e_dma(K_, "sp", decay[:, :], A["l1aa_decay"][:, :], writes=[decay_r],
                     tag="aa_dec")
        awd, awd_r = K_.sbuf("aa_awd", [128, 2 * BS], BF16)
        vkd, vkd_r = K_.sbuf("aa_vkd", [128, 2 * Dh], BF16)
        ktm, ktm_r = K_.sbuf("aa_ktm", [128, 2 * Dh], BF16)
        St, St_r = K_.sbuf("aa_S", [64, Dh], F32)
        Sb, Sb_r = K_.sbuf("aa_Sb", [64, Dh], BF16)
        ob, ob_r = K_.sbuf("aa_ob", [64, BS], F32)
        ash_toks = []
        for h in range(H):
            tdd = e_dma(K_, "sp", ddt.rearrange("p (c s) -> p c s", c=2),
                        A["l1aa_ddT"][h * 256:(h + 1) * 256, :]
                        .rearrange("(c p) s -> p c s", p=128), writes=[ddt_r],
                        tag=f"aa_dd{h}")
            tqd = e_dma(K_, "sp", qdrow[0:1, :], A["l1aa_qd"][h:h + 1, :],
                        writes=[qdrow_r], tag=f"aa_qd{h}")
            ps5, ps5_r = K_.psum_f[5], K_.psum_fr[5]
            tqb = plan.op("pe", lambda e: e.matmul(ps5[:, 0:BS], cst[0:1, 128:256],
                                                   qdrow[0:1, :], start=True, stop=True),
                          reads=[qdrow_r, cst_r], writes=[ps5_r], deps=[tqd],
                          tag=f"aa_qb{h}")
            tqe = plan.op("dve", lambda e: e.tensor_copy(qdB[:, :], ps5[:, 0:BS]),
                          reads=[ps5_r], writes=[qdB_r], deps=[tqb], tag=f"aa_qe{h}")
            tS0 = plan.op("dve", lambda e: e.memset(St[:, :], 0.0), writes=[St_r],
                          tag=f"aa_S0{h}")
            tSb = plan.op("dve", lambda e: e.tensor_copy(Sb[:, :], St[:, :]),
                          reads=[St_r], writes=[Sb_r], deps=[tS0], tag=f"aa_Sb{h}")
            for blk in range(NB):
                sl = blk % 2
                qk = qkb[:, sl * 512:(sl + 1) * 512]
                tq = e_dma(K_, "sp", qk.rearrange("p (b s) -> p b s", b=2),
                           A["l1aa_qkf"][h * 128:(h + 1) * 128, blk * BS:(blk + 1) * BS]
                           .rearrange("(b p) s -> p b s", b=2),
                           writes=[qkb_rs[sl]], deps=tqk, tag=f"aa_lq{h}_{blk}")
                vsl = vb[:, sl * 2 * Dh:(sl + 1) * 2 * Dh]
                tvl = e_dma(K_, "sp", vsl.rearrange("p (c d) -> p c d", c=2),
                            A["l1aa_vtm"][blk * BS:(blk + 1) * BS, h * Dh:(h + 1) * Dh]
                            .rearrange("(c p) d -> p c d", p=128),
                            writes=[vb_rs[sl]], deps=tv, tag=f"aa_lv{h}_{blk}")
                taw = []
                for c in range(2):
                    psn, psn_r = K_.psum_f[c], K_.psum_fr[c]
                    t1 = plan.op("pe", lambda e, c=c, qk=qk, psn=psn: e.matmul(
                        psn[:, 0:BS], qk[:, 256 + c * 128:256 + (c + 1) * 128],
                        qk[:, 0:BS], start=True, stop=True),
                        reads=[qkb_rs[sl]], writes=[psn_r], deps=[tq],
                        tag=f"aa_aw{h}_{blk}_{c}")
                    taw.append(plan.op("dve", lambda e, c=c, psn=psn: e.tensor_mul(
                        awd[:, c * BS:(c + 1) * BS], psn[:, 0:BS],
                        ddt[:, c * BS:(c + 1) * BS]),
                        reads=[psn_r, ddt_r], writes=[awd_r], deps=[t1, tdd],
                        tag=f"aa_awd{h}_{blk}_{c}"))
                ps2, ps2_r = K_.psum_f[2], K_.psum_fr[2]
                ps3, ps3_r = K_.psum_f[3], K_.psum_fr[3]
                tintra = None
                for c in range(2):
                    tintra = plan.op("pe", lambda e, c=c, vsl=vsl: e.matmul(
                        ps2[0:64, 0:BS], vsl[:, c * Dh:(c + 1) * Dh],
                        awd[:, c * BS:(c + 1) * BS], start=(c == 0), stop=(c == 1)),
                        reads=[vb_rs[sl], awd_r], writes=[ps2_r],
                        deps=[tvl] + taw, tag=f"aa_in{h}_{blk}_{c}")
                tinter = plan.op("pe", lambda e, qk=qk: e.matmul(
                    ps3[0:64, 0:BS], Sb[:, :], qk[:, 0:BS], start=True, stop=True),
                    reads=[Sb_r, qkb_rs[sl]], writes=[ps3_r], deps=[tq],
                    tag=f"aa_it{h}_{blk}")
                tob = plan.op("dve", lambda e: e.tensor_mul(ob[:, :], ps3[0:64, 0:BS],
                                                            qdB[0:64, :]),
                              reads=[ps3_r, qdB_r], writes=[ob_r], deps=[tinter, tqe],
                              tag=f"aa_ob{h}_{blk}")
                a_, r_ = stg.slots[stg.i % len(stg.slots)]
                stg.i += 1
                tev = plan.op("dve", lambda e, a_=a_: e.tensor_add(
                    a_[0:64, 0:BS], ob[:, :], ps2[0:64, 0:BS]),
                    reads=[ps2_r, ob_r], writes=[r_], deps=[tintra, tob], tag="aa_oev")

                def pairs(p, h=h, blk=blk, a_=a_):
                    return [(A["sh3_a"][p * 1024 + h * Dh: p * 1024 + (h + 1) * Dh,
                                        blk * BS:(blk + 1) * BS], a_[0:64, 0:BS])]
                ash_toks.append(e_dma_sw(K_, "sp", pairs, reads=[r_], tag="aa_ost"))
                tS = []
                for c in range(2):
                    pb, pb_r = K_.psum_b[c], K_.psum_br[c]
                    tt = plan.op("pe", lambda e, c=c, qk=qk, pb=pb: e.transpose(
                        pb[:, 0:64], qk[:, 256 + c * 128:256 + (c + 1) * 128],
                        cst[0:64, 0:64]),
                        reads=[qkb_rs[sl], cst_r], writes=[pb_r], deps=[tq],
                        tag=f"aa_tr{h}_{blk}_{c}")
                    te = plan.op("dve", lambda e, c=c, pb=pb: e.tensor_copy(
                        ktm[:, c * Dh:(c + 1) * Dh], pb[:, 0:64]),
                        reads=[pb_r], writes=[ktm_r], deps=[tt], tag=f"aa_ke{h}_{blk}_{c}")
                    tk2 = plan.op("dve", lambda e, c=c, vsl=vsl, h=h: e.tensor_scalar(
                        vkd[:, c * Dh:(c + 1) * Dh], vsl[:, c * Dh:(c + 1) * Dh],
                        decay[:, 2 * h + c:2 * h + c + 1], None, ALU.mult),
                        reads=[vb_rs[sl], decay_r], writes=[vkd_r], deps=[tvl, tdec],
                        tag=f"aa_vk{h}_{blk}_{c}")
                    tS.append(plan.op("pe", lambda e, c=c: e.matmul(
                        ps5[0:64, 0:Dh], ktm[:, c * Dh:(c + 1) * Dh],
                        vkd[:, c * Dh:(c + 1) * Dh], start=(c == 0), stop=(c == 1)),
                        reads=[ktm_r, vkd_r], writes=[ps5_r], deps=[te, tk2],
                        tag=f"aa_su{h}_{blk}_{c}"))
                tupd = plan.op("dve", lambda e, h=h: e.scalar_tensor_tensor(
                    St[:, :], St[:, :], decay[0:64, 64 + h:65 + h], ps5[0:64, 0:Dh],
                    ALU.mult, ALU.add),
                    reads=[ps5_r, decay_r], writes=[St_r], deps=tS, tag=f"aa_up{h}_{blk}")
                tSb = plan.op("dve", lambda e: e.tensor_copy(Sb[:, :], St[:, :]),
                              reads=[St_r], writes=[Sb_r], deps=[tupd],
                              tag=f"aa_sb{h}_{blk}")
        plan.barrier(deps=ash_toks)
        ach, ach_r0 = K_.sbuf("aa_ach", [128, 2 * TT], BF16)
        ach_rs = [Res("aa_ach0"), Res("aa_ach1")]

        def ach_load(k, t, sl):
            def pairs(p, k=k, t=t, sl=sl):
                return [(ach[:, sl * TT:(sl + 1) * TT],
                         A["sh3_a"][k * 128:(k + 1) * 128,
                                    p * NOWN + t * TT: p * NOWN + (t + 1) * TT])]
            return e_dma_sw(K_, "sp", pairs, writes=[ach_rs[sl]],
                            tag=f"aa_la{t}_{k}")
        nwcols, nwcols_r = K_.sbuf("aa_nw", [128, 16], F32)
        tnw = e_dma(K_, "sp", nwcols[:, :], A["l1aa_nw"][:, :], writes=[nwcols_r],
                    tag="aa_nw")
        an, an_r = K_.sbuf("aa_an", [128, 2 * TT], BF16)
        gsb, gsb_r = K_.sbuf("aa_gsb", [128, TT], BF16)
        gated, gated_r = K_.sbuf("aa_gated", [128, NKC * TT], BF16)
        gated_rs = [Res(f"aa_gated{k}") for k in range(NKC)]
        rstB, rstB_r = K_.sbuf("aa_rstB", [128, TT], BF16)
        rrow, rrow_r = K_.sbuf("aa_rrow", [1, 2 * TT], F32)
        rrbf, rrbf_r = K_.sbuf("aa_rrbf", [1, TT], BF16)
        sqs2, sqs2_r = K_.sbuf("aa_sq", [128, 2 * TT], BF16)
        wbufg, wbufg_r = K_.sbuf("aa_wg", [128, 128], BF16)
        gw_wl = wload_xbarT(K_, A["l1aa_gw_bf"], cg)
        for t in range(2):
            ps4, ps4_r = K_.psum_f[4], K_.psum_fr[4]
            for k in range(NKC):
                sl = k % 2
                tach = ach_load(k, t, sl)
                tsq = plan.op("act", lambda e, sl=sl: e.activation(
                    sqs2[:, sl * TT:(sl + 1) * TT],
                    ach[:, sl * TT:(sl + 1) * TT], AF.Square),
                    reads=[ach_rs[sl]], writes=[sqs2_r], deps=[tach],
                    tag=f"aa_sq{t}_{k}")
                plan.op("pe", lambda e, sl=sl, s=(k == 0), p=(k == NKC - 1): e.matmul(
                    ps4[0:1, 0:TT], cst[:, 128:129], sqs2[:, sl * TT:(sl + 1) * TT],
                    start=s, stop=p),
                    reads=[sqs2_r, cst_r], writes=[ps4_r], deps=[tsq],
                    tag=f"aa_ss{t}_{k}")
            t1 = plan.op("dve", lambda e: e.tensor_scalar(
                rrow[0:1, 0:TT], ps4[0:1, 0:TT], 1.0 / D, 1e-5, ALU.mult, ALU.add),
                reads=[ps4_r], writes=[rrow_r], tag=f"aa_ms{t}")
            t2 = plan.op("act", lambda e: e.activation(rrow[0:1, TT:2 * TT],
                                                       rrow[0:1, 0:TT], AF.Sqrt),
                         reads=[rrow_r], writes=[rrow_r], deps=[t1], tag=f"aa_sd{t}")
            t3 = plan.op("dve", lambda e: e.reciprocal(rrow[0:1, 0:TT],
                                                       rrow[0:1, TT:2 * TT]),
                         reads=[rrow_r], writes=[rrow_r], deps=[t2], tag=f"aa_rc{t}")
            t4 = plan.op("dve", lambda e: e.tensor_copy(rrbf[0:1, :], rrow[0:1, 0:TT]),
                         reads=[rrow_r], writes=[rrbf_r], deps=[t3], tag=f"aa_rb{t}")
            ps5b, ps5b_r = K_.psum_f[5], K_.psum_fr[5]
            t5 = plan.op("pe", lambda e: e.matmul(ps5b[:, 0:TT], cst[0:1, 128:256],
                                                  rrbf[0:1, :], start=True, stop=True),
                         reads=[rrbf_r, cst_r], writes=[ps5b_r], deps=[t4],
                         tag=f"aa_rB{t}")
            t6 = plan.op("dve", lambda e: e.tensor_copy(rstB[:, :], ps5b[:, 0:TT]),
                         reads=[ps5b_r], writes=[rstB_r], deps=[t5], tag=f"aa_rE{t}")
            y_, y_r_ = K_.sbuf("y_l1aa", [128, NKC * NOWN], BF16)
            for m in range(NKC):
                tokg = None
                for ki in range(NKC):
                    twg = gw_wl(ki, 0, [m], wbufg[:, 0:128], wbufg_r)
                    tokg = plan.op("pe", lambda e, ki=ki, s=(ki == 0),
                                   p=(ki == NKC - 1), t=t: e.matmul(
                        K_.psum_f[3][:, 0:TT], wbufg[:, 0:128],
                        y_[:, ki * NOWN + t * TT: ki * NOWN + (t + 1) * TT],
                        start=s, stop=p),
                        reads=[wbufg_r, y_r_], writes=[K_.psum_fr[3]], deps=[twg],
                        tag=f"aa_gm{t}_{m}_{ki}")
                tsig = plan.op("act", lambda e: e.activation(
                    gsb[:, :], K_.psum_f[3][:, 0:TT], AF.Sigmoid),
                    reads=[K_.psum_fr[3]], writes=[gsb_r], deps=[tokg],
                    tag=f"aa_sg{t}_{m}")
                sl = m % 2
                tach = ach_load(m, t, sl)
                tn1 = plan.op("pool", lambda e, sl=sl: e.tensor_mul(
                    an[:, sl * TT:(sl + 1) * TT],
                    ach[:, sl * TT:(sl + 1) * TT], rstB[:, :]),
                    reads=[ach_rs[sl], rstB_r], writes=[an_r], deps=[tach, t6],
                    tag=f"aa_n1{t}_{m}")
                tn2 = plan.op("dve", lambda e, sl=sl, m=m: e.tensor_scalar(
                    an[:, sl * TT:(sl + 1) * TT], an[:, sl * TT:(sl + 1) * TT],
                    nwcols[:, m:m + 1], None, ALU.mult),
                    reads=[nwcols_r], writes=[an_r], deps=[tn1, tnw], tag=f"aa_n2{t}_{m}")
                plan.op("pool", lambda e, sl=sl, m=m: e.tensor_mul(
                    gated[:, m * TT:(m + 1) * TT], an[:, sl * TT:(sl + 1) * TT],
                    gsb[:, :]),
                    reads=[an_r, gsb_r], writes=[gated_rs[m]], deps=[tn2, tsig],
                    tag=f"aa_g{t}_{m}")

            def g_at(kk, tt):
                return gated[:, kk * TT:(kk + 1) * TT], [gated_rs[kk]]

            proj_fm(K_, name="f3ow", wload=wload_xbarT(K_, A["l1aa_ow_bf"], cow),
                    n_k=NKC, m_list=list(range(16)), t_list=[t], rhs_at=g_at,
                    sink=resid_sink(K_), mg=3)


def emit_moe_casts(K_, A):
    def fn(eng, attach):
        for arm in K_.nc.switch_core_id(engines=[eng], n=128):
            q = int(arm) % 2
            attach(eng.dma_start(out=A["moe_w1t_bf"][q * 1024:(q + 1) * 1024, :],
                                 in_=A["moe_w1t"][q * 1024:(q + 1) * 1024, :]))
            attach(eng.dma_start(out=A["moe_w3t_bf"][q * 1024:(q + 1) * 1024, :],
                                 in_=A["moe_w3t"][q * 1024:(q + 1) * 1024, :]))
            attach(eng.dma_start(out=A["moe_w2t_bf"][q * 512:(q + 1) * 512, :],
                                 in_=A["moe_w2t"][q * 512:(q + 1) * 512, :]))
    return K_.plan.op_sw("pool", fn, n_dma=3, tag="moecast")


def emit_moe_F(K_, A, first):
    plan = K_.plan
    cst, cst_r = K_.cst
    E, I = 8, 1024
    NT = NOWN // 512
    if first:
        K_.moe_bar = plan.barrier(deps=[K_.moe_cast_tok])
    bar = K_.moe_bar
    with K_.phase():
        gws, gws_r = K_.sbuf("mg_gw", [8, 2048], BF16)
        tgw = e_dma(K_, "pool", gws[:, :], A["moe_gw"][:, :], writes=[gws_r], tag="mg_gw")
        gwT, gwT_r = K_.sbuf("mg_gwT", [128, 16 * 8], BF16)
        for k in range(NKC):
            pb, pb_r = K_.psum_b[0], K_.psum_br[0]
            tt = plan.op("pe", lambda e, k=k: e.transpose(
                pb[:, 0:8], gws[:, k * 128:(k + 1) * 128], cst[0:8, 0:8]),
                reads=[gws_r, cst_r], writes=[pb_r], deps=[tgw], tag=f"mg_t{k}")
            plan.op("dve", lambda e, k=k: e.tensor_copy(gwT[:, k * 8:(k + 1) * 8],
                                                        pb[:, 0:8]),
                    reads=[pb_r], writes=[gwT_r], deps=[tt], tag=f"mg_e{k}")
        gbrow, gbrow_r = K_.sbuf("mg_gbr", [1, 8], F32)
        gbrowb, gbrowb_r = K_.sbuf("mg_gbrb", [1, 8], BF16)
        tgb = e_dma(K_, "sp", gbrow[0:1, :], A["moe_gb"][:, :], writes=[gbrow_r], tag="mg_gb")
        tgbb = plan.op("dve", lambda e: e.tensor_copy(gbrowb[0:1, :], gbrow[0:1, :]),
                       reads=[gbrow_r], writes=[gbrowb_r], deps=[tgb], tag="mg_gbb")
        gbB, gbB_r = K_.sbuf("mg_gbB", [128, 8], F32)
        ps5, ps5_r = K_.psum_f[5], K_.psum_fr[5]
        tb1 = plan.op("pe", lambda e: e.matmul(ps5[:, 0:8], cst[0:1, 128:256],
                                               gbrowb[0:1, :], start=True, stop=True),
                      reads=[gbrowb_r, cst_r], writes=[ps5_r], deps=[tgbb], tag="mg_gbm")
        tb2 = plan.op("dve", lambda e: e.tensor_copy(gbB[:, :], ps5[:, 0:8]),
                      reads=[ps5_r], writes=[gbB_r], deps=[tb1], tag="mg_gbe")
        y, y_r, _ = ln_resident(K_, "moe")
        rt, rt_r = K_.sbuf("mg_rt", [128, 64], F32)
        dwT, dwT_r = K_.sbuf("mg_dwT", [8, NOWN], BF16)
        dwbf, dwbf_r = K_.sbuf("mg_dwbf", [128, 8], BF16)
        for ts in range(NOWN // 128):
            psl, psl_r = K_.psum_f[4], K_.psum_fr[4]
            tokl = None
            for k in range(NKC):
                tokl = plan.op("pe", lambda e, k=k, ts=ts, s=(k == 0),
                               p=(k == NKC - 1): e.matmul(
                    psl[:, 0:8],
                    y[:, k * NOWN + ts * 128: k * NOWN + (ts + 1) * 128],
                    gwT[:, k * 8:(k + 1) * 8], start=s, stop=p),
                    reads=[y_r, gwT_r], writes=[psl_r], tag=f"mg_l{ts}_{k}")
            sc_ = rt[:, 0:8]; mx_ = rt[:, 8:9]; nmx = rt[:, 9:10]
            sm_ = rt[:, 10:11]; rc_ = rt[:, 11:12]
            bi_ = rt[:, 16:24]; e1_ = rt[:, 24:32]; b2_ = rt[:, 32:40]
            m2_ = rt[:, 40:41]; e2_ = rt[:, 41:49]; msk_ = rt[:, 49:57]
            t1 = plan.op("dve", lambda e: e.reduce_max(mx_, psl[:, 0:8], axis=AX),
                         reads=[psl_r], writes=[rt_r], deps=[tokl], tag="mg_mx")
            t2 = plan.op("dve", lambda e: e.tensor_scalar_mul(nmx, mx_, -1.0),
                         reads=[rt_r], writes=[rt_r], deps=[t1], tag="mg_nm")
            t3 = plan.op("act", lambda e: e.activation(sc_, psl[:, 0:8], AF.Exp,
                                                       bias=nmx, scale=1.0),
                         reads=[psl_r, rt_r], writes=[rt_r], deps=[t2], tag="mg_ex")
            t4 = plan.op("dve", lambda e: e.reduce_sum(sm_, sc_, axis=AX),
                         reads=[rt_r], writes=[rt_r], deps=[t3], tag="mg_sm")
            t5 = plan.op("dve", lambda e: e.reciprocal(rc_, sm_),
                         reads=[rt_r], writes=[rt_r], deps=[t4], tag="mg_rc")
            t6 = plan.op("dve", lambda e: e.tensor_scalar(sc_, sc_, rc_, None,
                                                          ALU.mult),
                         reads=[rt_r], writes=[rt_r], deps=[t5], tag="mg_sc")
            t7 = plan.op("dve", lambda e: e.tensor_add(bi_, sc_, gbB[:, :]),
                         reads=[rt_r, gbB_r], writes=[rt_r], deps=[t6, tb2],
                         tag="mg_bi")
            t8 = plan.op("dve", lambda e: e.reduce_max(mx_, bi_, axis=AX),
                         reads=[rt_r], writes=[rt_r], deps=[t7], tag="mg_m1")
            t9 = plan.op("dve", lambda e: e.tensor_scalar(e1_, bi_, mx_, None,
                                                          ALU.is_equal),
                         reads=[rt_r], writes=[rt_r], deps=[t8], tag="mg_e1")
            t10 = plan.op("dve", lambda e: e.scalar_tensor_tensor(
                b2_, e1_, -1e30, bi_, ALU.mult, ALU.add),
                reads=[rt_r], writes=[rt_r], deps=[t9], tag="mg_b2")
            t11 = plan.op("dve", lambda e: e.reduce_max(m2_, b2_, axis=AX),
                          reads=[rt_r], writes=[rt_r], deps=[t10], tag="mg_m2")
            t12 = plan.op("dve", lambda e: e.tensor_scalar(e2_, b2_, m2_, None,
                                                           ALU.is_equal),
                          reads=[rt_r], writes=[rt_r], deps=[t11], tag="mg_e2")
            t13 = plan.op("dve", lambda e: e.tensor_add(msk_, e1_, e2_),
                          reads=[rt_r], writes=[rt_r], deps=[t12], tag="mg_mk")
            t14 = plan.op("dve", lambda e: e.tensor_mul(sc_, sc_, msk_),
                          reads=[rt_r], writes=[rt_r], deps=[t13], tag="mg_dw")
            t14b = plan.op("dve", lambda e: e.tensor_copy(dwbf[:, :], sc_),
                           reads=[rt_r], writes=[dwbf_r], deps=[t14], tag="mg_dwb")
            pst, pst_r = K_.psum_b[1], K_.psum_br[1]
            t15 = plan.op("pe", lambda e: e.transpose(pst[0:8, 0:128], dwbf[:, :],
                                                      cst[:, 0:128]),
                          reads=[dwbf_r, cst_r], writes=[pst_r], deps=[t14b],
                          tag="mg_tr")
            plan.op("dve", lambda e, ts=ts: e.tensor_copy(
                dwT[:, ts * 128:(ts + 1) * 128], pst[0:8, 0:128]),
                reads=[pst_r], writes=[dwT_r], deps=[t15], tag="mg_te")
        tdo = e_dma(K_, "sp", A["moe_dwd"][:, :], dwT[:, :], reads=[dwT_r], tag="mg_dwo")
        dwd_flat = A["moe_dwd"].rearrange("e s -> (e s)").rearrange("(o f) -> o f", o=1)
        hall, _ = K_.sbuf("mg_h", [128, 8 * NOWN], BF16)
        hall_rs = [Res(f"mg_h{s}") for s in range(8 * NT)]
        w1s, _ = K_.sbuf("mg_w1s", [128, 2 * NKC * 128], BF16)
        w3s, _ = K_.sbuf("mg_w3s", [128, 2 * NKC * 128], BF16)
        w1rs = [Res(f"mg_w1s{i}") for i in range(2)]
        w3rs = [Res(f"mg_w3s{i}") for i in range(2)]
        w2s, _ = K_.sbuf("mg_w2s", [128, 2 * 8 * 256], BF16)
        w2rs = [Res(f"mg_w2s{i}") for i in range(2)]
        sa, _ = K_.sbuf("mg_sa", [128, 2 * 512], BF16)
        sa_rs = [Res(f"mg_sa{i}") for i in range(2)]
        hf, _ = K_.sbuf("mg_hf", [128, 2 * 512], F32)
        hf_rs = [Res(f"mg_hf{i}") for i in range(2)]
        dwrow, _ = K_.sbuf("mg_dwrow", [1, 2 * NOWN], BF16)
        dwr_rs = [Res(f"mg_dwrow{i}") for i in range(2)]
        dwB, _ = K_.sbuf("mg_dwB", [128, 2 * NT * 512], BF16)
        dwB_rs = [Res(f"mg_dwB{i}") for i in range(2)]
        mgc = [0]; w2c = [0]; abc = [0]; cc_ = [0]
        for ee in range(9):
            dsl = ee % 2
            if ee < E:
                tdi = e_dma(K_, "sp", dwrow[0:1, dsl * NOWN:(dsl + 1) * NOWN],
                            dwd_flat[0:1, ee * NOWN:(ee + 1) * NOWN],
                            writes=[dwr_rs[dsl]], deps=[tdo], tag=f"mg_dwi{ee}")
                for tq in range(NT):
                    pi = 4 + (cc_[0] % 2); cc_[0] += 1
                    pbb, pbb_r = K_.psum_f[pi], K_.psum_fr[pi]
                    tbr = plan.op("pe", lambda e, dsl=dsl, tq=tq, pbb=pbb: e.matmul(
                        pbb[:, 0:512], cst[0:1, 128:256],
                        dwrow[0:1, dsl * NOWN + tq * 512: dsl * NOWN + (tq + 1) * 512],
                        start=True, stop=True),
                        reads=[dwr_rs[dsl], cst_r], writes=[pbb_r], deps=[tdi],
                        tag=f"mg_db{ee}_{tq}")
                    plan.op("dve", lambda e, dsl=dsl, tq=tq, pbb=pbb: e.tensor_copy(
                        dwB[:, (dsl * NT + tq) * 512:(dsl * NT + tq + 1) * 512],
                        pbb[:, 0:512]),
                        reads=[pbb_r], writes=[dwB_rs[dsl]], deps=[tbr],
                        tag=f"mg_de{ee}_{tq}")
            for m in range(8):
                sl = mgc[0] % 2; mgc[0] += 1
                base = sl * NKC * 128
                d1 = w1s[:, base:base + NKC * 128].rearrange("p (k c) -> p k c", k=NKC)
                d3 = w3s[:, base:base + NKC * 128].rearrange("p (k c) -> p k c", k=NKC)
                s1 = A["moe_w1t_bf"][:, ee * I + m * 128: ee * I + (m + 1) * 128]\
                    .rearrange("(k p) c -> p k c", p=128)
                s3 = A["moe_w3t_bf"][:, ee * I + m * 128: ee * I + (m + 1) * 128]\
                    .rearrange("(k p) c -> p k c", p=128)
                tw1 = e_dma(K_, "sp", d1, s1, writes=[w1rs[sl]], deps=[bar],
                            tag=f"mgw1_{ee}_{m}")
                tw3 = e_dma(K_, "sp", d3, s3, writes=[w3rs[sl]], deps=[bar],
                            tag=f"mgw3_{ee}_{m}")
                for tq in range(NT):
                    pi = abc[0] % 2; abc[0] += 1
                    pa, pa_r = K_.psum_f[pi], K_.psum_fr[pi]
                    pb2, pb2_r = K_.psum_f[2 + pi], K_.psum_fr[2 + pi]
                    tokA = tokB = None
                    for k in range(NKC):
                        l1 = w1s[:, base + k * 128: base + (k + 1) * 128]
                        l3 = w3s[:, base + k * 128: base + (k + 1) * 128]
                        r_ = y[:, k * NOWN + tq * 512: k * NOWN + (tq + 1) * 512]
                        tokA = plan.op("pe", lambda e, l1=l1, r_=r_, pa=pa,
                                       s=(k == 0), p=(k == NKC - 1): e.matmul(
                            pa[:, 0:512], l1, r_, start=s, stop=p),
                            reads=[w1rs[sl], y_r], writes=[pa_r], deps=[tw1],
                            tag=f"mgA{ee}_{m}_{tq}_{k}")
                        tokB = plan.op("pe", lambda e, l3=l3, r_=r_, pb2=pb2,
                                       s=(k == 0), p=(k == NKC - 1): e.matmul(
                            pb2[:, 0:512], l3, r_, start=s, stop=p),
                            reads=[w3rs[sl], y_r], writes=[pb2_r], deps=[tw3],
                            tag=f"mgB{ee}_{m}_{tq}_{k}")
                    si = (m * NT + tq) % 2
                    sa_ = sa[:, si * 512:(si + 1) * 512]
                    hf_ = hf[:, si * 512:(si + 1) * 512]
                    ts_ = plan.op("act", lambda e, sa_=sa_, pa=pa: e.activation(
                        sa_, pa[:, 0:512], AF.Silu),
                        reads=[pa_r], writes=[sa_rs[si]], deps=[tokA],
                        tag=f"mg_si{ee}_{m}_{tq}")
                    tm_ = plan.op("dve", lambda e, hf_=hf_, sa_=sa_, pb2=pb2:
                                  e.tensor_mul(hf_, sa_, pb2[:, 0:512]),
                                  reads=[pb2_r, sa_rs[si]], writes=[hf_rs[si]],
                                  deps=[tokB, ts_], tag=f"mg_hm{ee}_{m}_{tq}")
                    slot = m * NT + tq
                    hsl = hall[:, slot * 512:(slot + 1) * 512]
                    if ee < E:
                        plan.op("pool", lambda e, hsl=hsl, hf_=hf_, dsl=dsl, tq=tq:
                                e.tensor_mul(hsl, hf_,
                                             dwB[:, (dsl * NT + tq) * 512:
                                                 (dsl * NT + tq + 1) * 512]),
                                reads=[hf_rs[si], dwB_rs[dsl]], writes=[hall_rs[slot]],
                                deps=[tm_], tag=f"mg_hd{ee}_{m}_{tq}")
                    else:
                        plan.op("pool", lambda e, hsl=hsl, hf_=hf_:
                                e.tensor_copy(hsl, hf_),
                                reads=[hf_rs[si]], writes=[hall_rs[slot]],
                                deps=[tm_], tag=f"mg_hc{ee}_{m}_{tq}")
            for og in range(8):
                sl2 = w2c[0] % 2; w2c[0] += 1
                b2s = sl2 * 8 * 256
                d2 = w2s[:, b2s:b2s + 8 * 256].rearrange("p (k c) -> p k c", k=8)
                s2 = A["moe_w2t_bf"][:, ee * 2048 + og * 256: ee * 2048 + (og + 1) * 256]\
                    .rearrange("(k p) c -> p k c", p=128)
                tw2 = e_dma(K_, "sp", d2, s2, writes=[w2rs[sl2]], deps=[bar],
                            tag=f"mgw2_{ee}_{og}")
                for m_in in range(2):
                    m_out = og * 2 + m_in
                    for tq in range(NT):
                        pi = 4 + (cc_[0] % 2); cc_[0] += 1
                        pc, pc_r = K_.psum_f[pi], K_.psum_fr[pi]
                        tokC = None
                        for ki in range(8):
                            l2 = w2s[:, b2s + ki * 256 + m_in * 128:
                                     b2s + ki * 256 + (m_in + 1) * 128]
                            r_ = hall[:, (ki * NT + tq) * 512:(ki * NT + tq + 1) * 512]
                            tokC = plan.op("pe", lambda e, l2=l2, r_=r_, pc=pc,
                                           s=(ki == 0), p=(ki == 7): e.matmul(
                                pc[:, 0:512], l2, r_, start=s, stop=p),
                                reads=[w2rs[sl2], hall_rs[ki * NT + tq]],
                                writes=[pc_r], deps=[tw2],
                                tag=f"mgC{ee}_{m_out}_{tq}_{ki}")
                        xsl = K_.x_sb[:, m_out * NOWN + tq * 512:
                                      m_out * NOWN + (tq + 1) * 512]
                        plan.op("dve", lambda e, xsl=xsl, pc=pc: e.tensor_add(
                            xsl, xsl, pc[:, 0:512]),
                            reads=[pc_r], writes=[K_.x_rs[m_out]], deps=[tokC],
                            tag=f"mg_add{ee}_{m_out}_{tq}")


def emit_l1ab_F(K_, A):
    plan = K_.plan
    cst, cst_r = K_.cst
    Hh, DH, C = 8, 128, 64
    NCh = 2048 // C
    with K_.phase():
        stg = Stager(K_, "s", 512, 3)
        y, y_r, tys = ln_resident(K_, "l1ab", ysh=A["sh4_y"])
        plan.barrier(deps=[tys])
        yld = dram_rhs_loader(K_, "f4y", A["sh4_y"], 2048)
        tq = tk = tv = []
        for tl in ([0, 1], [2, 3]):
            tq = tq + proj_to_dram(K_, name="f4q", wload=wload_nat(K_, A["l1ab_wq"]),
                                   n_k=NKC, m_list=list(range(8)), t_list=tl,
                                   rhs_at=yld, dst=A["l1ab_qraw"], stg=stg)
            tk = tk + proj_to_dram(K_, name="f4k", wload=wload_nat(K_, A["l1ab_wk"]),
                                   n_k=NKC, m_list=list(range(8)), t_list=tl,
                                   rhs_at=yld, dst=A["l1ab_kraw"], stg=stg)
        wt, wt_r = K_.sbuf("f4wvT", [128, NKC * TT], BF16)
        wt_rs = [Res(f"f4wvT{k}") for k in range(NKC)]
        tv = []
        for vt_i in range(2):
            wl = []
            for k in range(NKC):
                wl.append(e_dma(K_, "pool", wt[:, k * TT:(k + 1) * TT],
                                A["l1ab_wv"][k * 128:(k + 1) * 128,
                                             vt_i * TT:(vt_i + 1) * TT],
                                writes=[wt_rs[k]], tag="f4wvT"))
            for tb in range(16):
                ps, ps_r = K_.psum_f[tb % 2], K_.psum_fr[tb % 2]
                tok = None
                for k in range(NKC):
                    r_ap, r_reads = yld(k, tb // 4)
                    slc = r_ap[:, (tb % 4) * 128:(tb % 4 + 1) * 128]
                    tok = plan.op("pe", lambda e, k=k, s=(k == 0), p=(k == NKC - 1),
                                  slc=slc, ps=ps: e.matmul(
                        ps[:, 0:TT], slc, wt[:, k * TT:(k + 1) * TT], start=s, stop=p),
                        reads=r_reads + [wt_rs[k]], writes=[ps_r], deps=[wl[k]],
                        tag=f"f4vm{vt_i}_{tb}_{k}")
                tv.append(stg.store(ps[:, 0:TT], ps_r,
                                    A["l1ab_vtm"][tb * 128:(tb + 1) * 128,
                                                  vt_i * TT:(vt_i + 1) * TT], tok=tok))
    with K_.phase():
        stg = Stager(K_, "s", 512, 3)
        stg2 = Stager(K_, "s2", 512, 3)
        nqt = 2048 // 256

        def kc_list(qt):
            return list(range(0, 2 * qt + 2))

        def mask_idx_of(qt, kc):
            if kc in (2 * qt, 2 * qt + 1):
                return qt * 2 + (kc - 2 * qt)
            return None

        tbase = softmax_attn(K_, name="f4a", n_h=8, kv_of=lambda h: h,
                             n_q=2048, n_kv=2048, scale=1.0 / np.sqrt(128.0), Tq=256,
                             qf=A["l1ab_qraw"], kf=A["l1ab_kraw"], vt=A["l1ab_vtm"],
                             n_kvh=8, kc_list=kc_list,
                             bias_tab=A["abg_btab"],
                             bias_col_of=lambda qt, kc: qt * 16 + kc,
                             n_bias_cols=nqt * 16,
                             mask_tab=A["abg_mtab"][:, :], mask_idx_of=mask_idx_of,
                             n_masks=nqt * 2,
                             dst_dram=A["l1ab_obase"], stg=stg2, dep_toks=tq + tk + tv)
        dcst, dcst_r = K_.sbuf("db_csts", [64, 128], F32)
        tdc = e_dma(K_, "sp", dcst[:, :], A["db_cst"][:, :], writes=[dcst_r], tag="db_cst")
        scale_q = float(128.0 ** -0.5)
        olin_toks = {}
        bufs = []
        for sub in range(2):
            sfx = "" if sub == 0 else "_1"
            B = {"sfx": sfx}
            B["qs"], B["qs_r"] = K_.sbuf(f"db_qs{sfx}", [128, 2048], BF16)
            B["ks"], B["ks_r"] = K_.sbuf(f"db_ks{sfx}", [128, 2048], BF16)
            B["beta"], B["beta_r"] = K_.sbuf(f"db_beta{sfx}", [128, 2048], BF16)
            B["kb"], B["kb_r"] = K_.sbuf(f"db_kb{sfx}", [128, 2048], BF16)
            B["lgt"], B["lgt_r"] = K_.sbuf(f"db_lg{sfx}", [128, 2 * TT], BF16)
            B["St"], B["St_r"] = K_.sbuf(f"db_S{sfx}", [128, 128], F32)
            B["Sbp"], B["Sbp_r"] = K_.sbuf(f"db_Sbp{sfx}", [128, 128], BF16)
            B["Sbs"], B["Sbs_r"] = K_.sbuf(f"db_Sbs{sfx}", [128, 128], BF16)
            B["Arm"], B["Arm_r"] = K_.sbuf(f"db_Arm{sfx}", [64, 64 * 12], BF16)
            B["Rb"], B["Rb_r"] = K_.sbuf(f"db_R{sfx}", [64, 128], BF16)
            B["Gt"], B["Gt_r"] = K_.sbuf(f"db_G{sfx}", [64, 128], BF16)
            B["atn"], B["atn_r"] = K_.sbuf(f"db_atn{sfx}", [64, 64], BF16)
            B["vbc"], B["vbc_r"] = K_.sbuf(f"db_vbc{sfx}", [64, 128], BF16)
            B["ktc"], B["ktc_r"] = K_.sbuf(f"db_ktc{sfx}", [64, 128], BF16)
            B["pL"], B["pL_r"] = K_.psum_f[3 * sub + 0], K_.psum_fr[3 * sub + 0]
            B["pG"], B["pG_r"] = K_.psum_f[3 * sub + 1], K_.psum_fr[3 * sub + 1]
            B["pO"], B["pO_r"] = K_.psum_f[3 * sub + 2], K_.psum_fr[3 * sub + 2]
            B["ptr"], B["ptr_r"] = K_.psum_b[sub], K_.psum_br[sub]
            bufs.append(B)

        def db_prep(h, B):
            sfx = B["sfx"]
            qs, qs_r = B["qs"], B["qs_r"]
            ks, ks_r = B["ks"], B["ks_r"]
            beta, beta_r = B["beta"], B["beta_r"]
            kb, kb_r = B["kb"], B["kb_r"]
            lgt, lgt_r = B["lgt"], B["lgt_r"]
            _, _, _, tqs = fm_softmax_clip(K_, A, "q", A["l1ab_qraw"], qs, qs_r, h,
                                           2048, sfx=sfx)
            kraw, kraw_r, tldk, tks = fm_softmax_clip(K_, A, "k", A["l1ab_kraw"], ks,
                                                      ks_r, h, 2048, sfx=sfx)
            tbet = []
            for s in range(4):
                sl_ = slice(s * TT, (s + 1) * TT)
                a1 = plan.op("act", lambda e, sl_=sl_: e.activation(
                    lgt[:, 0:TT], kraw[:, sl_], AF.Abs),
                    reads=[kraw_r], writes=[lgt_r], deps=[tldk], tag=f"db_ab{h}{s}")
                a2 = plan.op("act", lambda e: e.activation(lgt[:, TT:2 * TT],
                                                           lgt[:, 0:TT], AF.Exp,
                                                           scale=-1.0),
                             reads=[lgt_r], writes=[lgt_r], deps=[a1], tag=f"db_e{h}{s}")
                a3 = plan.op("dve", lambda e: e.tensor_scalar_add(lgt[:, TT:2 * TT],
                                                                  lgt[:, TT:2 * TT], 1.0),
                             reads=[lgt_r], writes=[lgt_r], deps=[a2], tag=f"db_p1{h}{s}")
                a4 = plan.op("act", lambda e: e.activation(lgt[:, TT:2 * TT],
                                                           lgt[:, TT:2 * TT], AF.Ln),
                             reads=[lgt_r], writes=[lgt_r], deps=[a3], tag=f"db_ln{h}{s}")
                a5 = plan.op("dve", lambda e, sl_=sl_: e.tensor_scalar_min(
                    lgt[:, 0:TT], kraw[:, sl_], 0.0),
                    reads=[kraw_r], writes=[lgt_r], deps=[a4], tag=f"db_mn{h}{s}")
                a6 = plan.op("dve", lambda e: e.tensor_sub(lgt[:, 0:TT], lgt[:, 0:TT],
                                                           lgt[:, TT:2 * TT]),
                             reads=[lgt_r], writes=[lgt_r], deps=[a5], tag=f"db_sb{h}{s}")
                a7 = plan.op("dve", lambda e, sl_=sl_: e.tensor_scalar(
                    beta[:, sl_], lgt[:, 0:TT], 1.0 / 16.0, -16.0, ALU.mult, ALU.max),
                    reads=[lgt_r], writes=[beta_r], deps=[a6], tag=f"db_bt{h}{s}")
                tbet.append(a7)
            tkb = []
            for s in range(4):
                sl_ = slice(s * TT, (s + 1) * TT)
                tkb.append(plan.op("dve", lambda e, sl_=sl_: e.tensor_mul(
                    kb[:, sl_], ks[:, sl_], beta[:, sl_]),
                    reads=[ks_r, beta_r], writes=[kb_r], deps=tks + tbet,
                    tag=f"db_kb{h}{s}"))
            tS0 = plan.op("dve", lambda e: e.memset(B["St"][:, :], 0.0),
                          writes=[B["St_r"]], tag=f"db_S0{h}")
            B["tSp"] = plan.op("dve", lambda e: e.tensor_copy(B["Sbp"][:, :], B["St"][:, :]),
                               reads=[B["St_r"]], writes=[B["Sbp_r"]], deps=[tS0],
                               tag=f"db_Sp{h}")
            B["tSs"] = plan.op("dve", lambda e: e.tensor_scalar_mul(
                B["Sbs"][:, :], B["St"][:, :], scale_q),
                reads=[B["St_r"]], writes=[B["Sbs_r"]], deps=[tS0], tag=f"db_Ss{h}")
            B["tqs"] = tqs; B["tks"] = tks; B["tbet"] = tbet; B["tkb"] = tkb

        def db_step(h, ci, B):
            qs, qs_r = B["qs"], B["qs_r"]
            ks, ks_r = B["ks"], B["ks_r"]
            beta, beta_r = B["beta"], B["beta_r"]
            kb, kb_r = B["kb"], B["kb_r"]
            St, St_r = B["St"], B["St_r"]
            Sbp, Sbp_r = B["Sbp"], B["Sbp_r"]
            Sbs, Sbs_r = B["Sbs"], B["Sbs_r"]
            Arm, Arm_r = B["Arm"], B["Arm_r"]
            Rb, Rb_r = B["Rb"], B["Rb_r"]
            Gt, Gt_r = B["Gt"], B["Gt_r"]
            atn, atn_r = B["atn"], B["atn_r"]
            vbc, vbc_r = B["vbc"], B["vbc_r"]
            ktc, ktc_r = B["ktc"], B["ktc_r"]
            pb0, pb0_r = B["ptr"], B["ptr_r"]
            ps0, ps0_r = B["pL"], B["pL_r"]
            ps1, ps1_r = B["pG"], B["pG_r"]
            psO, psO_r = B["pO"], B["pO_r"]
            tbet, tkb, tks, tqs = B["tbet"], B["tkb"], B["tks"], B["tqs"]
            tSp, tSs = B["tSp"], B["tSs"]
            cs = slice(ci * C, (ci + 1) * C)
            tt1 = plan.op("pe", lambda e, cs=cs: e.transpose(
                pb0[0:64, 0:128], beta[:, cs], cst[:, 0:128]),
                reads=[beta_r, cst_r], writes=[pb0_r], deps=tbet,
                tag=f"db_tb{h}_{ci}")
            tvb = e_dma(K_, "sp", vbc[:, :],
                        A["l1ab_vtm"][ci * C:(ci + 1) * C, h * 128:(h + 1) * 128],
                        writes=[vbc_r], deps=tv, tag=f"db_lv{h}_{ci}")
            tvb2 = plan.op("dve", lambda e: e.tensor_mul(vbc[:, :], vbc[:, :],
                                                         pb0[0:64, 0:128]),
                           reads=[pb0_r], writes=[vbc_r], deps=[tvb, tt1],
                           tag=f"db_vb{h}_{ci}")
            tt2 = plan.op("pe", lambda e, cs=cs: e.transpose(
                pb0[0:64, 0:128], ks[:, cs], cst[:, 0:128]),
                reads=[ks_r, cst_r], writes=[pb0_r], deps=tks,
                tag=f"db_tk{h}_{ci}")
            tkt = plan.op("dve", lambda e: e.tensor_copy(ktc[:, :], pb0[0:64, 0:128]),
                          reads=[pb0_r], writes=[ktc_r], deps=[tt2],
                          tag=f"db_ke{h}_{ci}")
            tL = plan.op("pe", lambda e, cs=cs: e.matmul(
                ps0[0:64, 0:64], kb[:, cs], ks[:, cs], start=True, stop=True),
                reads=[kb_r, ks_r], writes=[ps0_r], deps=tkb, tag=f"db_L{h}_{ci}")
            tA = plan.op("dve", lambda e: e.tensor_mul(Arm[:, 0:64], ps0[0:64, 0:64],
                                                       dcst[:, 0:64]),
                         reads=[ps0_r, dcst_r], writes=[Arm_r], deps=[tL, tdc],
                         tag=f"db_A{h}_{ci}")
            tAc = plan.op("pe", lambda e: e.transpose(pb0[0:64, 0:64], Arm[:, 0:64],
                                                      cst[0:64, 0:64]),
                          reads=[Arm_r, cst_r], writes=[pb0_r], deps=[tA],
                          tag=f"db_Ac{h}_{ci}")
            tAce = plan.op("dve", lambda e: e.tensor_copy(Arm[:, 64:128],
                                                          pb0[0:64, 0:64]),
                           reads=[pb0_r], writes=[Arm_r], deps=[tAc],
                           tag=f"db_Ae{h}_{ci}")
            prev_rm, prev_cm = 0, 1
            tpow = [tA, tAce]
            for lv in range(1, 6):
                rm_s, cm_s = 2 * lv, 2 * lv + 1
                t1_ = plan.op("pe", lambda e, a=prev_cm, b=prev_rm: e.matmul(
                    ps0[0:64, 0:64], Arm[:, a * 64:(a + 1) * 64],
                    Arm[:, b * 64:(b + 1) * 64], start=True, stop=True),
                    reads=[Arm_r], writes=[ps0_r], deps=tpow,
                    tag=f"db_p{h}_{ci}_{lv}a")
                t2_ = plan.op("dve", lambda e, s=rm_s: e.tensor_copy(
                    Arm[:, s * 64:(s + 1) * 64], ps0[0:64, 0:64]),
                    reads=[ps0_r], writes=[Arm_r], deps=[t1_],
                    tag=f"db_p{h}_{ci}_{lv}b")
                t3_ = plan.op("pe", lambda e, a=prev_rm, b=prev_cm: e.matmul(
                    ps0[0:64, 0:64], Arm[:, a * 64:(a + 1) * 64],
                    Arm[:, b * 64:(b + 1) * 64], start=True, stop=True),
                    reads=[Arm_r], writes=[ps0_r], deps=[t2_],
                    tag=f"db_p{h}_{ci}_{lv}c")
                t4_ = plan.op("dve", lambda e, s=cm_s: e.tensor_copy(
                    Arm[:, s * 64:(s + 1) * 64], ps0[0:64, 0:64]),
                    reads=[ps0_r], writes=[Arm_r], deps=[t3_],
                    tag=f"db_p{h}_{ci}_{lv}d")
                tpow = [t2_, t4_]
                prev_rm, prev_cm = rm_s, cm_s
            tG1 = plan.op("pe", lambda e, cs=cs: e.matmul(
                ps1[0:64, 0:128], kb[:, cs], Sbp[:, :], start=True, stop=True),
                reads=[kb_r, Sbp_r], writes=[ps1_r], deps=tkb + [tSp],
                tag=f"db_G{h}_{ci}")
            tG2 = plan.op("dve", lambda e: e.tensor_sub(Gt[:, :], vbc[:, :],
                                                        ps1[0:64, 0:128]),
                          reads=[ps1_r, vbc_r], writes=[Gt_r], deps=[tG1, tvb2],
                          tag=f"db_Ge{h}_{ci}")
            tR = tG2
            Rsrc, Rsrc_r = Gt, Gt_r
            for lv in range(6):
                aslot = 1 if lv == 0 else 2 * lv + 1
                t1_ = plan.op("pe", lambda e, aslot=aslot, Rsrc=Rsrc: e.matmul(
                    ps1[0:64, 0:128], Arm[:, aslot * 64:(aslot + 1) * 64],
                    Rsrc[:, :], start=True, stop=False),
                    reads=[Arm_r, Rsrc_r], writes=[ps1_r], deps=[tR],
                    tag=f"db_ap{h}_{ci}_{lv}a")
                t2_ = plan.op("pe", lambda e, Rsrc=Rsrc: e.matmul(
                    ps1[0:64, 0:128], cst[0:64, 0:64], Rsrc[:, :],
                    start=False, stop=True),
                    reads=[cst_r, Rsrc_r], writes=[ps1_r], deps=[t1_],
                    tag=f"db_ap{h}_{ci}_{lv}b")
                tR = plan.op("dve", lambda e: e.tensor_copy(Rb[:, :],
                                                            ps1[0:64, 0:128]),
                             reads=[ps1_r], writes=[Rb_r], deps=[t2_],
                             tag=f"db_ap{h}_{ci}_{lv}c")
                Rsrc, Rsrc_r = Rb, Rb_r
            ta1 = plan.op("pe", lambda e, cs=cs: e.matmul(
                ps0[0:64, 0:64], ks[:, cs], qs[:, cs], start=True, stop=True),
                reads=[ks_r, qs_r], writes=[ps0_r], deps=tks + tqs,
                tag=f"db_at{h}_{ci}")
            ta2 = plan.op("dve", lambda e: e.tensor_mul(atn[:, :], ps0[0:64, 0:64],
                                                        dcst[:, 64:128]),
                          reads=[ps0_r, dcst_r], writes=[atn_r], deps=[ta1, tdc],
                          tag=f"db_am{h}_{ci}")
            to1 = plan.op("pe", lambda e, cs=cs: e.matmul(
                psO[:, 0:64], Sbs[:, :], qs[:, cs], start=True, stop=False),
                reads=[Sbs_r, qs_r], writes=[psO_r], deps=[tSs] + tqs,
                tag=f"db_o1{h}_{ci}")
            to2 = plan.op("pe", lambda e: e.matmul(psO[:, 0:64], Rb[:, :], atn[:, :],
                                                   start=False, stop=True),
                          reads=[Rb_r, atn_r], writes=[psO_r], deps=[tR, ta2],
                          tag=f"db_o2{h}_{ci}")
            tst = stg.store(psO[:, 0:64], psO_r,
                            A["l1ab_olin"][h * 128:(h + 1) * 128, ci * C:(ci + 1) * C],
                            tok=to2, w=64)
            olin_toks.setdefault(h, []).append(tst)
            tS1 = plan.op("pe", lambda e: e.matmul(ps1[:, 0:128], ktc[:, :], Rb[:, :],
                                                   start=True, stop=True),
                          reads=[ktc_r, Rb_r], writes=[ps1_r], deps=[tkt, tR],
                          tag=f"db_S1{h}_{ci}")
            tS2 = plan.op("dve", lambda e: e.tensor_add(St[:, :], St[:, :],
                                                        ps1[:, 0:128]),
                          reads=[ps1_r], writes=[St_r], deps=[tS1],
                          tag=f"db_S2{h}_{ci}")
            B["tSp"] = plan.op("dve", lambda e: e.tensor_copy(Sbp[:, :], St[:, :]),
                               reads=[St_r], writes=[Sbp_r], deps=[tS2],
                               tag=f"db_S3{h}_{ci}")
            B["tSs"] = plan.op("dve", lambda e: e.tensor_scalar_mul(
                Sbs[:, :], St[:, :], scale_q),
                reads=[St_r], writes=[Sbs_r], deps=[tS2],
                tag=f"db_S4{h}_{ci}")

        for hp in range(Hh // 2):
            for sub in range(2):
                db_prep(2 * hp + sub, bufs[sub])
            for ci in range(NCh):
                for sub in range(2):
                    db_step(2 * hp + sub, ci, bufs[sub])
        mo, mo_r = K_.sbuf("db_mo", [128, 2 * TT], BF16)
        mo_rs = [Res("db_mo0"), Res("db_mo1")]
        mx, mx_r0 = K_.sbuf("db_mixx", [128, TT], BF16)
        mix_toks = []
        for h in range(Hh):
            for t in range(4):
                sl_ = h % 2
                tl1 = e_dma(K_, "sp", mo[:, sl_ * TT:(sl_ + 1) * TT],
                            A["l1ab_obase"][h * 128:(h + 1) * 128, t * TT:(t + 1) * TT],
                            writes=[mo_rs[sl_]], deps=tbase, tag=f"db_lb{h}_{t}")
                tl2 = e_dma(K_, "pool", mx[:, :],
                            A["l1ab_olin"][h * 128:(h + 1) * 128, t * TT:(t + 1) * TT],
                            writes=[mx_r0], deps=olin_toks[h],
                            tag=f"db_ll{h}_{t}")
                tmx = plan.op("dve", lambda e, sl_=sl_: e.tensor_add(
                    mx[:, :], mx[:, :], mo[:, sl_ * TT:(sl_ + 1) * TT]),
                    reads=[mo_rs[sl_]], writes=[mx_r0], deps=[tl1, tl2],
                    tag=f"db_mx{h}_{t}")

                def pairs(p, h=h, t=t):
                    return [(A["sh4_o"][p * 1024 + h * 128: p * 1024 + (h + 1) * 128,
                                        t * TT:(t + 1) * TT], mx[:, :])]
                mix_toks.append(e_dma_sw(K_, "sp", pairs, reads=[mx_r0],
                                         tag=f"db_om{h}_{t}"))
        plan.barrier(deps=mix_toks)
        rhs = dram_rhs_loader(K_, "f4wo", A["sh4_o"], NOWN, sw_own=True)
        proj_fm(K_, name="f4wo", wload=wload_nat(K_, A["l1ab_wo"]),
                n_k=NKC, m_list=list(range(16)), t_list=[0, 1], rhs_at=rhs,
                sink=resid_sink(K_, scale=0.5), mg=3)


def host_l1aa_tables_F(p):
    ddT, qd, decay = host_l1aa_tables()
    hs = np.arange(p * 16, (p + 1) * 16)
    ddT_p = np.ascontiguousarray(ddT.reshape(32, 256, 256)[hs].reshape(16 * 256, 256))
    qd_p = np.ascontiguousarray(qd[hs])
    dec = np.zeros((128, 128), np.float32)
    for hl, g in enumerate(hs):
        dec[:, 2 * hl] = decay[:, 2 * g]
        dec[:, 2 * hl + 1] = decay[:, 2 * g + 1]
        dec[:, 64 + hl] = decay[:, 64 + g]
    return ddT_p, qd_p, dec


def declare_fused(nc):
    A = {}
    A["x0"] = nc.dram_tensor("x0", [2048, NOWN], F32, kind="ExternalInput").ap()
    A["xo"] = nc.dram_tensor("xo", [2048, NOWN], F32, kind="ExternalOutput").ap()
    A["cst_d"] = nc.dram_tensor("cst", [128, 512], BF16, kind="ExternalInput").ap()

    def inp(name, shape, dt=F32):
        A[name] = nc.dram_tensor(name, shape, dt, kind="ExternalInput").ap()

    def loc(name, shape, dt=BF16):
        A[name] = nc.dram_tensor(name, shape, dt, kind="Internal").ap()

    def sh(name, shape):
        A[name] = nc.dram_tensor(name, shape, BF16, kind="Internal",
                                 addr_space="Shared").ap()

    for w, rows in (("wq", 2048), ("wk", 1024), ("wv", 1024), ("wo", 2048)):
        inp(f"l0aa_{w}", [rows, 2048])
        loc(f"l0aa_{w}_bf", [rows, 2048])
    sh("sh1_y", [2048, 2048]); sh("sh1_q", [2048, 2048]); sh("sh1_k", [1024, 2048])
    sh("sh1_vt", [2048, 1024]); sh("sh1_a", [2048, 2048])
    inp("l0f_w", [2048, 2048]); inp("l0f_pcol", [128, 64])
    loc("l0f_ws_bf", [2048, 2048]); loc("l0f_col", [128, 1], F32)
    for w in ("wq", "wk", "wv", "wo"):
        inp(f"l0ab_{w}", [2048, 2048])
    inp("ab_bcol", [128, 48]); inp("ab_bvrow", [1, 2048])
    inp("ab_btab", [16 * 128, 64]); inp("ab_mtab", [128, 16 * 256], BF16)
    inp("ab_qrow", [16, NOWN], BF16)
    sh("sh2_y", [2048, 2048]); sh("sh2_q", [2048, 2048]); sh("sh2_k", [2048, 2048])
    sh("sh2_vt", [2048, 2048]); sh("sh2_a", [2048, 2048])
    inp("l1aa_qkv", [3072, 2048]); inp("l1aa_gw", [2048, 2048])
    inp("l1aa_ow", [2048, 2048]); inp("l1aa_nw", [128, 16])
    inp("l1aa_ddT", [16 * 256, 256], BF16); inp("l1aa_qd", [16, 256], BF16)
    inp("l1aa_decay", [128, 128])
    loc("l1aa_qk_bf", [2048, 2048]); loc("l1aa_v_bf", [1024, 2048])
    loc("l1aa_gw_bf", [2048, 2048]); loc("l1aa_ow_bf", [2048, 2048])
    loc("l1aa_qkf", [2048, 2048]); loc("l1aa_vtm", [2048, 1024])
    sh("sh3_y", [2048, 2048]); sh("sh3_a", [2048, 2048])
    inp("moe_gw", [8, 2048]); inp("moe_gb", [1, 8])
    inp("moe_w1t", [2048, 9 * 1024]); inp("moe_w3t", [2048, 9 * 1024])
    inp("moe_w2t", [1024, 9 * 2048])
    loc("moe_dwd", [8, NOWN])
    sh("moe_w1t_bf", [2048, 9 * 1024]); sh("moe_w3t_bf", [2048, 9 * 1024])
    sh("moe_w2t_bf", [1024, 9 * 2048])
    for w in ("wq", "wk", "wv"):
        inp(f"l1ab_{w}", [2048, 1024])
    inp("l1ab_wo", [2048, 2048])
    inp("abg_btab", [8 * 128, 128]); inp("abg_mtab", [128, 16 * 256], BF16)
    inp("db_cst", [64, 128])
    loc("db_cold", [128, 1], F32); loc("db_cold_1", [128, 1], F32)
    loc("l1ab_qraw", [1024, 2048]); loc("l1ab_kraw", [1024, 2048])
    loc("l1ab_vtm", [2048, 1024])
    loc("l1ab_obase", [1024, 2048]); loc("l1ab_olin", [1024, 2048])
    sh("sh4_y", [2048, 2048]); sh("sh4_o", [2048, 2048])
    return A


def build_fused():
    nc = bass.Bass("TRN2", target_bir_lowering=False, num_devices=8)
    A = declare_fused(nc)
    with contextlib.ExitStack() as stack:
        plan = Plan(nc)
        K_ = K(nc, plan, stack)
        cstsb, cst_r = K_.sbuf("cstsb", [128, 512], BF16)
        e_dma(K_, "sp", cstsb[:, :], A["cst_d"][:, :], writes=[cst_r], tag="ldcst")
        K_.cst = (cstsb, cst_r)
        x_sb, _ = K_.sbuf("x_res", [128, NKC * NOWN], F32)
        K_.x_sb = x_sb
        K_.x_rs = [Res(f"x_res{k}") for k in range(NKC)]
        for k in range(NKC):
            e_dma(K_, "sp", x_sb[:, k * NOWN:(k + 1) * NOWN],
                  A["x0"][k * 128:(k + 1) * 128, :], writes=[K_.x_rs[k]],
                  tag=f"ldx{k}")
        K_.moe_cast_tok = emit_moe_casts(K_, A)
        emit_l0aa_F(K_, A)
        emit_l0ffn_F(K_, A, True)
        emit_l0ab_F(K_, A)
        emit_l0ffn_F(K_, A, False)
        emit_l1aa_F(K_, A)
        emit_moe_F(K_, A, True)
        emit_l1ab_F(K_, A)
        emit_moe_F(K_, A, False)
        for k in range(NKC):
            K_.store_toks.append(e_dma(K_, "sp", A["xo"][k * 128:(k + 1) * 128, :],
                                       x_sb[:, k * NOWN:(k + 1) * NOWN],
                                       reads=[K_.x_rs[k]], tag=f"stx{k}"))
        plan.finalize(stack, K_.store_toks)
    return nc



_FUSED_NC = [None]


def _pcol(gamma, beta, lnw, lnb):
    p = np.zeros((128, 64), np.float32)
    for m in range(16):
        sl = slice(m * 128, (m + 1) * 128)
        p[:, m] = gamma[sl]; p[:, 16 + m] = beta[sl]
        p[:, 32 + m] = lnw[sl]; p[:, 48 + m] = lnb[sl]
    return p


def _cols16(v):
    p = np.zeros((128, 16), np.float32)
    for m in range(16):
        p[:, m] = v[m * 128:(m + 1) * 128]
    return p


def kernel(**inputs):
    I = {k: np.ascontiguousarray(np.asarray(v)) for k, v in inputs.items()}
    x = I["L0_attn_a_hidden_states"].astype(np.float32)
    CST = host_cst()
    pc = _pcol(I["L0_ffn_gamma"], I["L0_ffn_beta"], I["L0_ffn_ln_weight"],
               I["L0_ffn_ln_bias"])
    bcol = np.zeros((128, 48), np.float32)
    for m in range(16):
        sl = slice(m * 128, (m + 1) * 128)
        bcol[:, m] = I["L0_attn_b_bq"][sl]
        bcol[:, 16 + m] = I["L0_attn_b_bk"][sl]
        bcol[:, 32 + m] = I["L0_attn_b_bo"][sl]
    abt = [host_striped_tables(p, 16, True, 256) for p in range(2)]
    aat = [host_l1aa_tables_F(p) for p in range(2)]
    W1T = np.ascontiguousarray(np.concatenate(
        [I["L1_ffn_W1"].transpose(2, 0, 1).reshape(2048, 8192),
         I["L1_ffn_Ws1"].T], axis=1))
    W3T = np.ascontiguousarray(np.concatenate(
        [I["L1_ffn_W3"].transpose(2, 0, 1).reshape(2048, 8192),
         I["L1_ffn_Ws3"].T], axis=1))
    W2T = np.ascontiguousarray(np.concatenate(
        [I["L1_ffn_W2"].transpose(2, 0, 1).reshape(1024, 16384),
         I["L1_ffn_Ws2"].T], axis=1))
    nw = _cols16(I["L1_attn_a_norm_w"])
    btab, mtab, dcst = host_l1ab_tables(2048)
    ims = []
    for c in range(8):
        b, p = c // 2, c % 2
        im = {
            "x0": np.ascontiguousarray(x[b, p * 1024:(p + 1) * 1024].T),
            "cst": CST,
            "l0aa_wq": I["L0_attn_a_wq"], "l0aa_wk": I["L0_attn_a_wk"],
            "l0aa_wv": I["L0_attn_a_wv"], "l0aa_wo": I["L0_attn_a_wo"],
            "l0f_w": I["L0_ffn_weight"], "l0f_pcol": pc,
            "l0ab_wq": I["L0_attn_b_wq"], "l0ab_wk": I["L0_attn_b_wk"],
            "l0ab_wv": I["L0_attn_b_wv"], "l0ab_wo": I["L0_attn_b_wo"],
            "ab_bcol": bcol, "ab_bvrow": I["L0_attn_b_bv"].reshape(1, 2048),
            "ab_btab": abt[p][0], "ab_mtab": abt[p][1], "ab_qrow": abt[p][2],
            "l1aa_qkv": np.ascontiguousarray(
                I["L1_attn_a_qkv_w"][p * 3072:(p + 1) * 3072]),
            "l1aa_gw": I["L1_attn_a_gate_w"], "l1aa_ow": I["L1_attn_a_out_w"],
            "l1aa_nw": nw, "l1aa_ddT": aat[p][0], "l1aa_qd": aat[p][1],
            "l1aa_decay": aat[p][2],
            "moe_gw": I["L1_ffn_gate_w"], "moe_gb": I["L1_ffn_gate_b"].reshape(1, 8),
            "moe_w1t": W1T, "moe_w3t": W3T, "moe_w2t": W2T,
            "l1ab_wq": np.ascontiguousarray(
                I["L1_attn_b_Wq"][:, p * 1024:(p + 1) * 1024]),
            "l1ab_wk": np.ascontiguousarray(
                I["L1_attn_b_Wk"][:, p * 1024:(p + 1) * 1024]),
            "l1ab_wv": np.ascontiguousarray(
                I["L1_attn_b_Wv"][:, p * 1024:(p + 1) * 1024]),
            "l1ab_wo": I["L1_attn_b_Wo"],
            "abg_btab": np.ascontiguousarray(btab[0:1024, :]),
            "abg_mtab": mtab, "db_cst": dcst,
        }
        ims.append(im)
    if _FUSED_NC[0] is None:
        _FUSED_NC[0] = build_fused()
    res = run_bass_kernel_spmd(_FUSED_NC[0], ims, core_ids=list(range(8)))
    out = np.empty((4, 2048, 2048), np.float32)
    for c in range(8):
        b, p = c // 2, c % 2
        out[b, p * 1024:(p + 1) * 1024] = res.results[c]["xo"].T
    return out


# version 08
# speedup vs baseline: 1.1851x; 1.1851x over previous
import contextlib
import numpy as np
import ml_dtypes

import concourse.bass as bass
import concourse.mybir as mybir
from concourse.bass_utils import run_bass_kernel_spmd

F32 = mybir.dt.float32
BF16 = mybir.dt.bfloat16
AF = mybir.ActivationFunctionType
ALU = mybir.AluOpType
AX = mybir.AxisListType.X
BF = ml_dtypes.bfloat16

ENGINES = ("pe", "act", "dve", "pool", "sp")

D = 2048
NKC = 16
TT = 512
NCORE = 8


class Res:
    __slots__ = ("name", "writer", "readers")

    def __init__(self, name):
        self.name = name
        self.writer = None
        self.readers = []


class _Op:
    __slots__ = ("eng", "fn", "deps", "dma", "tag", "idx", "inc", "sem",
                 "semval", "ctr")

    def __init__(self, eng, fn, deps, dma, tag, idx, ctr):
        self.eng, self.fn, self.deps, self.dma, self.tag, self.idx = (
            eng, fn, deps, dma, tag, idx)
        self.inc = False
        self.sem = None
        self.semval = 0
        self.ctr = ctr


class Plan:
    def __init__(self, nc):
        self.nc = nc
        self.ops = {e: [] for e in ENGINES}
        self.ctr = 0

    def op(self, eng, fn, reads=(), writes=(), deps=(), dma=False, tag=""):
        dd = list(deps)
        for r in reads:
            if r.writer is not None:
                dd.append(r.writer)
        for w in writes:
            if w.writer is not None:
                dd.append(w.writer)
            dd.extend(w.readers)
        idx = len(self.ops[eng])
        tok = (eng, idx)
        dd = sorted({t for t in dd if not (t[0] == eng and eng == "pe")})
        self.ctr += 1
        self.ops[eng].append(_Op(eng, fn, dd, dma, tag, idx, self.ctr))
        for r in reads:
            r.readers.append(tok)
        for w in writes:
            w.writer = tok
            w.readers = []
        return tok

    def barrier(self, deps=()):
        tok = self.op("pool", None, deps=list(deps), tag="BARRIER")
        self.ops["pool"][tok[1]].dma = "BARRIER"
        return tok

    def local_join(self, deps=()):
        tok = self.op("pool", None, deps=list(deps), tag="JOIN")
        self.ops["pool"][tok[1]].dma = "JOIN"
        return tok

    def op_sw(self, eng, fn_p, reads=(), writes=(), deps=(), n_dma=1, tag=""):
        tok = self.op(eng, fn_p, reads=reads, writes=writes, deps=deps,
                      dma=True, tag=tag)
        o = self.ops[eng][tok[1]]
        o.dma = ("SW", n_dma)
        return tok

    def finalize(self, stack, final_deps, n_sp_sems=14, n_pool_sems=8, n_act_sems=2):
        nc = self.nc
        self.op("pool", lambda g: g.nop(), deps=list(final_deps), tag="final_wait")
        for e in ENGINES:
            for o in self.ops[e]:
                if o.dma and not isinstance(o.dma, str):
                    o.inc = True
                for (de, di) in o.deps:
                    self.ops[de][di].inc = True
        eng_sem = {e: stack.enter_context(nc.semaphore(f"sem_{e}"))
                   for e in ("pe", "act", "dve", "pool", "sp")}
        dma_sems = {
            "sp": [stack.enter_context(nc.semaphore(f"dsp{i}")) for i in range(n_sp_sems)],
            "pool": [stack.enter_context(nc.semaphore(f"dpl{i}")) for i in range(n_pool_sems)],
            "act": [stack.enter_context(nc.semaphore(f"dac{i}")) for i in range(n_act_sems)],
        }
        order = sorted(((o.ctr, e, o) for e in ENGINES for o in self.ops[e]),
                       key=lambda x: x[0])
        cum = {e: 0 for e in ENGINES}
        dma_rr = {q: 0 for q in dma_sems}
        dma_use = {q: [0] * len(dma_sems[q]) for q in dma_sems}
        dma_serial = {}
        bar_snap = {}
        for _, e, o in order:
            if o.dma in ("BARRIER", "JOIN"):
                snap = []
                for q in dma_sems:
                    for i, u in enumerate(dma_use[q]):
                        if u > 0:
                            snap.append((dma_sems[q][i], 16 * u))
                bar_snap[(e, o.idx)] = snap
                if o.inc:
                    cum["pool"] += 1
                    o.sem = eng_sem["pool"]
                    o.semval = cum["pool"]
                continue
            if not o.inc:
                continue
            if o.dma:
                q = e
                n_dma = o.dma[1] if isinstance(o.dma, tuple) else 1
                i = dma_rr[q] % len(dma_sems[q])
                dma_rr[q] += 1
                prev = dma_use[q][i]
                dma_use[q][i] += n_dma
                o.sem = dma_sems[q][i]
                o.semval = 16 * dma_use[q][i]
                if prev > 0:
                    dma_serial[(e, o.idx)] = (o.sem, 16 * prev)
            else:
                cum[e] += 1
                o.sem = eng_sem[e]
                o.semval = cum[e]
        engobjs = {"pe": nc.tensor, "act": nc.scalar, "dve": nc.vector,
                   "pool": nc.gpsimd, "sp": nc.sync}
        waited = {e: {} for e in ENGINES}

        def wait(e, sem, val):
            key = id(sem)
            if waited[e].get(key, 0) >= val:
                return
            engobjs[e].wait_ge(sem, val)
            waited[e][key] = val

        for _, e, o in order:
            if o.dma in ("BARRIER", "JOIN"):
                for (de, di) in o.deps:
                    dep = self.ops[de][di]
                    wait("pool", dep.sem, dep.semval)
                for (sem, val) in bar_snap[(e, o.idx)]:
                    wait("pool", sem, val)
                nc.all_core_barrier()
                floors = {}
                for (sem, val) in bar_snap[(e, o.idx)]:
                    floors[id(sem)] = val
                for e2 in ENGINES:
                    for o2 in self.ops[e2]:
                        if o2.inc and o2.ctr < o.ctr and o2.sem is not None                                 and not isinstance(o2.dma, str):
                            k2 = id(o2.sem)
                            floors[k2] = max(floors.get(k2, 0), o2.semval)
                for ee in ENGINES:
                    for k2, v2 in floors.items():
                        waited[ee][k2] = max(waited[ee].get(k2, 0), v2)
                if o.inc:
                    engobjs["pool"].nop().then_inc(o.sem, 1)
                continue
            for (de, di) in o.deps:
                dep = self.ops[de][di]
                wait(e, dep.sem, dep.semval)
            ser = dma_serial.get((e, o.idx))
            if ser is not None:
                wait(e, ser[0], ser[1])
            if isinstance(o.dma, tuple):
                sem_ = o.sem

                def attach(inst, sem_=sem_):
                    inst.then_inc(sem_, 16)
                o.fn(engobjs[e], attach)
            else:
                inst = o.fn(engobjs[e])
                if o.inc:
                    assert inst is not None, f"{o.tag} marked, fn returned None"
                    inst.then_inc(o.sem, 16 if o.dma else 1)

class K:
    def __init__(self, nc, plan, stack):
        self.nc = nc
        self.plan = plan
        self.stack = stack
        self.psum_f = [stack.enter_context(nc.psum_tensor(f"psf{i}", [128, 512], F32))
                       for i in range(6)]
        self.psum_fr = [Res(f"psf{i}") for i in range(6)]
        self.psum_b = [stack.enter_context(nc.psum_tensor(f"psb{i}", [128, 1024], BF16))
                       for i in range(2)]
        self.psum_br = [Res(f"psb{i}") for i in range(2)]
        self._sb = {}
        self._gen = 0
        self.cst = None
        self.store_toks = []

    def sbuf(self, name, shape, dtype):
        if name not in self._sb:
            uname = name if self._gen == 0 else f"{name}__g{self._gen}"
            t = self.stack.enter_context(self.nc.sbuf_tensor(uname, shape, dtype))
            self._sb[name] = (t.ap(), Res(uname))
        return self._sb[name]

    @contextlib.contextmanager
    def phase(self):
        saved = set(self._sb)
        self._gen += 1
        outer = self.stack
        with contextlib.ExitStack() as st:
            self.stack = st
            try:
                yield
            finally:
                self.stack = outer
                self.plan.local_join()
        for k in list(self._sb):
            if k not in saved:
                del self._sb[k]


CST_ID = slice(0, 128)
CST_ONES = slice(128, 256)
CST_NEG = slice(256, 384)


def host_cst():
    c = np.zeros((128, 512), np.float32)
    c[:, 0:128] = np.eye(128)
    c[:, 128:256] = 1.0
    c[:, 256:384] = -1.0
    return c.astype(BF)


def e_dma(K_, q, out_ap, in_ap, reads=(), writes=(), deps=(), tag="", **kw):
    return K_.plan.op(q, lambda e: e.dma_start(out=out_ap, in_=in_ap, **kw),
                      reads=reads, writes=writes, deps=deps, dma=True, tag=tag)


def e_xbar(K_, out_ap, in_ap, reads=(), writes=(), deps=(), tag=""):
    return K_.plan.op("sp", lambda e: e.dma_start(out=out_ap, in_=in_ap, transpose=True),
                      reads=reads, writes=writes, deps=deps, dma=True, tag=tag)


def cast_pass(K_, dst_bf_ap, src_f32_ap, rows, cols, res=None, tag="cast"):
    toks = []
    blk = max(128, min(rows, (8 << 20) // max(1, cols * 4)))
    r = 0
    while r < rows:
        n = min(blk, rows - r)
        toks.append(e_dma(K_, "pool", dst_bf_ap[r:r + n, :], src_f32_ap[r:r + n, :],
                          tag=f"{tag}:{r}"))
        r += n
    return toks



def ln_fm(K_, *, name, n_tok, x_at, y_sink, gains=None, in_bf16=False,
          gain_col=None, gain_res=None, gain_dep=None):
    nc, plan = K_.nc, K_.plan
    cst, cst_r = K_.cst
    nt = n_tok // TT
    sq, sq_r = K_.sbuf("ln_sq", [128, 2 * TT], BF16)
    xb, xb_r = K_.sbuf("ln_xb", [128, 2 * TT], BF16)
    rows, rows_r = K_.sbuf("ln_rows", [1, 5 * TT], F32)
    brow, brow_r = K_.sbuf("ln_brow", [1, 2 * TT], BF16)
    rB, rB_r = K_.sbuf("ln_rB", [128, TT], BF16)
    shB, shB_r = K_.sbuf("ln_shB", [128, TT], BF16)
    out_toks = []
    for t in range(nt):
        ps_s, ps_s_r = K_.psum_f[4], K_.psum_fr[4]
        ps_q, ps_q_r = K_.psum_f[5], K_.psum_fr[5]
        for k in range(NKC):
            x_ap, x_reads = x_at(k, t)
            par = k % 2
            sqa = sq[:, par * TT:(par + 1) * TT]
            tsq = plan.op("act", lambda e, o=sqa, i=x_ap: e.activation(o, i, AF.Square),
                          reads=list(x_reads), writes=[sq_r], tag=f"lnsq{name}")
            if in_bf16:
                xba = x_ap
                xba_r = list(x_reads)
                txb = None
            else:
                xba = xb[:, par * TT:(par + 1) * TT]
                xba_r = [xb_r]
                txb = plan.op("pool", lambda e, o=xba, i=x_ap: e.tensor_copy(o, i),
                              reads=list(x_reads), writes=[xb_r], tag=f"lnxb{name}")
            st, sp = (k == 0), (k == NKC - 1)
            plan.op("pe", lambda e, r=xba, s=st, p=sp: e.matmul(
                ps_s[0:1, :], cst[:, 128:129], r, start=s, stop=p),
                reads=xba_r + [cst_r], writes=[ps_s_r],
                deps=([txb] if txb else []), tag=f"lnS{name}")
            plan.op("pe", lambda e, r=sqa, s=st, p=sp: e.matmul(
                ps_q[0:1, :], cst[:, 128:129], r, start=s, stop=p),
                reads=[sq_r, cst_r], writes=[ps_q_r], deps=[tsq], tag=f"lnQ{name}")
        m = rows[0:1, 0 * TT:1 * TT]
        q2 = rows[0:1, 1 * TT:2 * TT]
        m2 = rows[0:1, 2 * TT:3 * TT]
        var = q2
        sd = m2
        rr = rows[0:1, 3 * TT:4 * TT]
        sh = rows[0:1, 4 * TT:5 * TT]
        plan.op("dve", lambda e: e.tensor_scalar_mul(m, ps_s[0:1, :], 1.0 / D),
                reads=[ps_s_r], writes=[rows_r], tag="ln_m")
        plan.op("dve", lambda e: e.tensor_scalar_mul(q2, ps_q[0:1, :], 1.0 / D),
                reads=[ps_q_r], writes=[rows_r], tag="ln_q")
        plan.op("dve", lambda e: e.tensor_mul(m2, m, m), reads=[rows_r],
                writes=[rows_r], tag="ln_m2")
        plan.op("dve", lambda e: e.tensor_sub(var, q2, m2), reads=[rows_r],
                writes=[rows_r], tag="ln_var")
        plan.op("dve", lambda e: e.tensor_scalar_add(var, var, 1e-5), reads=[rows_r],
                writes=[rows_r], tag="ln_vare")
        tsd = plan.op("act", lambda e: e.activation(sd, var, AF.Sqrt),
                      reads=[rows_r], writes=[rows_r], tag="ln_sd")
        trr = plan.op("dve", lambda e: e.reciprocal(rr, sd), reads=[rows_r],
                      writes=[rows_r], deps=[tsd], tag="ln_rr")
        tsh = plan.op("dve", lambda e: e.scalar_tensor_tensor(
            sh, m, -1.0, rr, ALU.mult, ALU.mult), reads=[rows_r], writes=[rows_r],
            deps=[trr], tag="ln_sh")
        tb1 = plan.op("dve", lambda e: e.tensor_copy(brow[0:1, 0:TT], rr),
                      reads=[rows_r], writes=[brow_r], deps=[trr], tag="ln_b1")
        tb2 = plan.op("dve", lambda e: e.tensor_copy(brow[0:1, TT:2 * TT], sh),
                      reads=[rows_r], writes=[brow_r], deps=[tsh], tag="ln_b2")
        tbr = plan.op("pe", lambda e: e.matmul(
            ps_s[:, :], cst[0:1, 128:256], brow[0:1, 0:TT], start=True, stop=True),
            reads=[brow_r, cst_r], writes=[ps_s_r], deps=[tb1], tag="ln_brB")
        tbs = plan.op("pe", lambda e: e.matmul(
            ps_q[:, :], cst[0:1, 128:256], brow[0:1, TT:2 * TT], start=True, stop=True),
            reads=[brow_r, cst_r], writes=[ps_q_r], deps=[tb2], tag="ln_bsB")
        te1 = plan.op("dve", lambda e: e.tensor_copy(rB[:, :], ps_s[:, :]),
                      reads=[ps_s_r], writes=[rB_r], deps=[tbr], tag="ln_erB")
        te2 = plan.op("dve", lambda e: e.tensor_copy(shB[:, :], ps_q[:, :]),
                      reads=[ps_q_r], writes=[shB_r], deps=[tbs], tag="ln_eshB")
        tmp, tmp_r = K_.sbuf("ln_tmp", [128, 2 * TT], F32)
        for k in range(NKC):
            x_ap, x_reads = x_at(k, t)
            par = k % 2
            tm = tmp[:, par * TT:(par + 1) * TT]
            tmul = plan.op("pool", lambda e, o=tm, i=x_ap: e.tensor_mul(o, i, rB[:, :]),
                           reads=list(x_reads) + [rB_r], writes=[tmp_r],
                           deps=[te1], tag=f"ln_mul{k}")
            if gain_col is None:
                def mk_emit(tm_=tm, tmul_=tmul):
                    def emit(out_ap, extra_writes):
                        return plan.op("dve", lambda e: e.tensor_add(out_ap, tm_, shB[:, :]),
                                       reads=[tmp_r, shB_r], writes=list(extra_writes),
                                       deps=[tmul_, te2], tag=f"ln_y{k}")
                    return emit
            else:
                tadd = plan.op("dve", lambda e, tm_=tm: e.tensor_add(tm_, tm_, shB[:, :]),
                               reads=[shB_r], writes=[tmp_r], deps=[tmul, te2],
                               tag=f"ln_ya{k}")
                def mk_emit(tm_=tm, tadd_=tadd, k_=k):
                    def emit(out_ap, extra_writes):
                        return plan.op("dve", lambda e: e.tensor_scalar(
                            out_ap, tm_, gain_col(k_), None, ALU.mult),
                            reads=[tmp_r, gain_res], writes=list(extra_writes),
                            deps=[tadd_] + ([gain_dep] if gain_dep else []),
                            tag=f"ln_yg{k_}")
                    return emit
            out_toks.append(y_sink(k, t, mk_emit()))
    return out_toks



def proj_fm(K_, *, name, wload, n_k, m_list, t_list, rhs_at, sink, mg=3,
            kc_limit=None):
    plan = K_.plan
    wbuf, wbuf_r = [], []
    for i in range(2):
        a, r = K_.sbuf(f"w_{name}_{i}", [128, mg * 128], BF16)
        wbuf.append(a)
        wbuf_r.append(r)
    nmg = (len(m_list) + mg - 1) // mg
    out = []
    for g in range(nmg):
        ms = m_list[g * mg:(g + 1) * mg]
        banks = list(range(len(ms) * len(t_list)))
        assert len(banks) <= 6
        last = {}
        for ki in range(n_k):
            wb = (g * n_k + ki) % 2
            wt = wload(ki, g, ms, wbuf[wb][:, 0:len(ms) * 128], wbuf_r[wb])
            for ti, t in enumerate(t_list):
                r_ap, r_reads = rhs_at(ki, t)
                rw = r_ap.shape[-1]
                for mi, m in enumerate(ms):
                    bank = mi * len(t_list) + ti
                    ps, ps_r = K_.psum_f[bank], K_.psum_fr[bank]
                    st, sp = (ki == 0), (ki == n_k - 1)
                    tok = plan.op("pe", lambda e, w=wbuf[wb][:, mi * 128:(mi + 1) * 128],
                                  r=r_ap, p=ps, s=st, q=sp, rw=rw: e.matmul(
                                      p[:, 0:rw], w, r, start=s, stop=q),
                                  reads=[wbuf_r[wb]] + list(r_reads),
                                  writes=[K_.psum_fr[bank]], deps=[wt],
                                  tag=f"mm_{name}_{g}_{ki}_{t}_{m}")
                    last[(mi, ti)] = tok
        for mi, m in enumerate(ms):
            for ti, t in enumerate(t_list):
                bank = mi * len(t_list) + ti
                out.append(sink(m, t, K_.psum_f[bank], K_.psum_fr[bank],
                                last[(mi, ti)]))
    return out


def wload_xbarT(K_, dram_bf_nat, deps=(), row_off=0):
    def wl(ki, g, ms, buf_ap, buf_res):
        m0 = ms[0]
        mn = len(ms)
        src = dram_bf_nat[row_off + m0 * 128: row_off + (m0 + mn) * 128,
                          ki * 128:(ki + 1) * 128]
        return e_xbar(K_, buf_ap, src, writes=[buf_res], deps=list(deps), tag="wxT")
    return wl


def wload_nat(K_, dram_f32_nat, deps=(), col_off=0):
    def wl(ki, g, ms, buf_ap, buf_res):
        m0 = ms[0]
        mn = len(ms)
        src = dram_f32_nat[ki * 128:(ki + 1) * 128,
                           col_off + m0 * 128: col_off + (m0 + mn) * 128]
        return e_dma(K_, "pool", buf_ap, src, writes=[buf_res], deps=list(deps),
                     tag="wnat")
    return wl



class Stager:
    def __init__(self, K_, name, width=512, nslots=4, dtype=BF16):
        self.K_ = K_
        self.slots = []
        self.width = width
        for i in range(nslots):
            a, r = K_.sbuf(f"stg_{name}_{i}", [128, width], dtype)
            self.slots.append((a, r))
        self.i = 0

    def store(self, src_ap, src_res, dst_ap, tok=None, w=None, evac=None,
              extra_reads=(), evac_eng="dve", rows=128):
        w = w or self.width
        a, r = self.slots[self.i % len(self.slots)]
        self.i += 1
        deps = [tok] if tok else []
        if evac is None:
            t1 = self.K_.plan.op(evac_eng, lambda e: e.tensor_copy(a[0:rows, 0:w], src_ap),
                                 reads=[src_res] + list(extra_reads), writes=[r],
                                 deps=deps, tag="stg_ev")
        else:
            t1 = self.K_.plan.op(evac_eng, lambda e: evac(e, a[0:rows, 0:w]),
                                 reads=[src_res] + list(extra_reads), writes=[r],
                                 deps=deps, tag="stg_evc")
        return e_dma(self.K_, "sp", dst_ap, a[0:rows, 0:w], reads=[r], tag="stg_st")



def softmax_attn(K_, *, name, n_h, kv_of, n_q, n_kv, scale, Tq,
                 qf, kf, vt, n_kvh,
                 kc_list, bias_tab=None, bias_col_of=None, n_bias_cols=0,
                 mask_tab=None, mask_idx_of=None, n_masks=0,
                 qrow_tab=None,
                 attn_dst=None, attn_dst_r=None, dst_dram=None, stg=None,
                 q_stripes=None, out_stripes=None,
                 dep_toks=()):
    nc, plan = K_.nc, K_.plan
    cst, cst_r = K_.cst
    nqt = n_q // Tq
    nkc = n_kv // 128
    onesf, onesf_r = K_.sbuf("onesf", [1, 128], F32)
    if not getattr(K_, "_onesf_done", False):
        K_._onesf_done = True
        K_.plan.op("dve", lambda e: e.memset(onesf[:, :], 1.0), writes=[onesf_r],
                   tag="onesf")
    if mask_tab is not None:
        msk0, msk_r = K_.sbuf("at_msk", [128, n_masks * Tq], BF16)
        msk = msk0[:, 0:n_masks * Tq]
        tml = e_dma(K_, "sp", msk, mask_tab, writes=[msk_r], tag="at_mskld")
    slots = []
    for s in range(2):
        sx = f"_{s}"
        S = {}
        S["kh"], S["kh_r"] = K_.sbuf(f"at_kh{sx}", [128, n_kv], BF16)
        S["qh"], S["qh_r"] = K_.sbuf(f"at_qh{sx}", [128, n_q], BF16)
        S["vh"], S["vh_r"] = K_.sbuf(f"at_vh{sx}", [128, nkc * 128], BF16)
        S["sqs"], S["sqs_r"] = K_.sbuf(f"at_sq{sx}", [128, 512], BF16)
        S["sc"] = []; S["sc_r"] = []
        for i in range(2):
            a_, r_ = K_.sbuf(f"at_sc{sx}_{i}", [128, 512], BF16)
            S["sc"].append(a_[:, 0:Tq]); S["sc_r"].append(r_)
        rowb0, S["rowb_r"] = K_.sbuf(f"at_row{sx}", [1, 512], F32)
        S["rowb"] = rowb0[:, 0:Tq]
        S["rwb2"], S["rwb2_r"] = K_.sbuf(f"at_row2{sx}", [1, 2], BF16)
        rec0, S["rec_r"] = K_.sbuf(f"at_rec{sx}", [128, 512], F32)
        S["rec"] = rec0[:, 0:Tq]
        bcol0, S["bcol_r"] = K_.sbuf(f"at_bc{sx}", [128, max(1, n_bias_cols)], F32)
        S["bcol"] = bcol0[:, 0:max(1, n_bias_cols)]
        S["nbc"], S["nbc_r"] = K_.sbuf(f"at_nb{sx}", [128, 1], F32)
        S["rws"], S["rws_r"] = K_.sbuf(f"at_rw{sx}", [1, 16], F32)
        if qrow_tab is not None:
            S["qrow"], S["qrow_r"] = K_.sbuf(f"at_qrow{sx}", [1, n_q], BF16)
        S["b_sc"], S["b_sc_r"] = K_.psum_f[3 * s + 0], K_.psum_fr[3 * s + 0]
        S["b_sum"], S["b_sum_r"] = K_.psum_f[3 * s + 1], K_.psum_fr[3 * s + 1]
        S["b_av"], S["b_av_r"] = K_.psum_f[3 * s + 2], K_.psum_fr[3 * s + 2]
        slots.append(S)
    out_toks = []

    def norm_max_gen(S, src_full, src_res, n, dst_col, dep, pfx, mcol, out_key):
        sqs, sqs_r = S["sqs"], S["sqs_r"]
        rws, rws_r = S["rws"], S["rws_r"]
        ps5, ps5_r = S["b_sc"], S["b_sc_r"]
        ns = n // 512
        toks = []
        for s in range(ns):
            tsq = plan.op("act", lambda e, s=s: e.activation(
                sqs[:, 0:512], src_full[:, s * 512:(s + 1) * 512], AF.Square),
                reads=[src_res], writes=[sqs_r], deps=[dep], tag=f"{pfx}sq{s}")
            yield
            tm = plan.op("pe", lambda e: e.matmul(
                ps5[0:1, 0:512], cst[:, 128:129], sqs[:, 0:512],
                start=True, stop=True),
                reads=[sqs_r, cst_r], writes=[ps5_r], deps=[tsq], tag=f"{pfx}n{s}")
            yield
            toks.append(plan.op("dve", lambda e, s=s: e.reduce_max(
                rws[0:1, dst_col + s:dst_col + s + 1], ps5[0:1, 0:512], axis=AX),
                reads=[ps5_r], writes=[rws_r], deps=[tm], tag=f"{pfx}m{s}"))
            yield
        S[out_key] = plan.op("dve", lambda e, mcol=mcol, ns=ns: e.reduce_max(
            rws[0:1, mcol:mcol + 1], rws[0:1, dst_col:dst_col + ns], axis=AX),
            reads=[rws_r], writes=[rws_r], deps=toks, tag=f"{pfx}mm")
        yield

    def head_gen(h, S):
        kh, kh_r = S["kh"], S["kh_r"]
        qh, qh_r = S["qh"], S["qh_r"]
        vh, vh_r = S["vh"], S["vh_r"]
        rowb, rowb_r = S["rowb"], S["rowb_r"]
        rwb2, rwb2_r = S["rwb2"], S["rwb2_r"]
        rec, rec_r = S["rec"], S["rec_r"]
        bcol, bcol_r = S["bcol"], S["bcol_r"]
        nbc, nbc_r = S["nbc"], S["nbc_r"]
        rws, rws_r = S["rws"], S["rws_r"]
        ps5, ps5_r = S["b_sc"], S["b_sc_r"]
        kv = kv_of(h)
        tkh = e_dma(K_, "sp", kh[:, :], kf[kv * 128:(kv + 1) * 128, :],
                    writes=[kh_r], deps=list(dep_toks), tag=f"at_ldk{h}")
        yield
        if q_stripes is None:
            tqh = e_dma(K_, "sp", qh[:, 0:n_q], qf[h * 128:(h + 1) * 128, :],
                        writes=[qh_r], deps=list(dep_toks), tag=f"at_ldq{h}")
        else:
            def qpairs(p, h=h):
                return [(qh[:, j * 128:(j + 1) * 128],
                         qf[h * 128:(h + 1) * 128, g * 128:(g + 1) * 128])
                        for j, g in enumerate(q_stripes(p))]
            tqh = e_dma_sw(K_, "sp", qpairs, writes=[qh_r],
                           deps=list(dep_toks), tag=f"at_ldq{h}")
        yield
        tvh = None
        for kc_ in range(n_kv // 128):
            tvh = e_dma(K_, "sp", vh[:, kc_ * 128:(kc_ + 1) * 128],
                        vt[kc_ * 128:(kc_ + 1) * 128, kv * 128:(kv + 1) * 128],
                        writes=[vh_r], deps=list(dep_toks), tag=f"at_ldv{h}_{kc_}")
            yield
        tqr = None
        if qrow_tab is not None:
            qrow, qrow_r = S["qrow"], S["qrow_r"]
            tqr = e_dma(K_, "sp", qrow[0:1, 0:n_q], qrow_tab[h:h + 1, :],
                        writes=[qrow_r], tag=f"at_qr{h}")
            yield
        yield from norm_max_gen(S, kh[:, :], kh_r, n_kv, 0, tkh, f"at_k{h}", 8, "tmk")
        yield from norm_max_gen(S, qh[:, 0:n_q], qh_r, n_q, 4, tqh, f"at_q{h}", 9, "tmq")
        tmk, tmq = S["tmk"], S["tmq"]
        t7 = plan.op("act", lambda e: e.activation(rws[0:1, 8:10], rws[0:1, 8:10], AF.Sqrt),
                     reads=[rws_r], writes=[rws_r], deps=[tmk, tmq], tag="at_sqrt")
        yield
        t8 = plan.op("dve", lambda e: e.scalar_tensor_tensor(
            rws[0:1, 14:15], rws[0:1, 8:9], scale, rws[0:1, 9:10], ALU.mult, ALU.mult),
            reads=[rws_r], writes=[rws_r], deps=[t7], tag="at_B")
        yield
        t9 = plan.op("dve", lambda e: e.tensor_copy(rwb2[0:1, 0:1], rws[0:1, 14:15]),
                     reads=[rws_r], writes=[rwb2_r], deps=[t8], tag="at_Bb")
        yield
        t10 = plan.op("pe", lambda e: e.matmul(ps5[:, 0:1], cst[0:1, 256:384],
                                               rwb2[0:1, 0:1], start=True, stop=True),
                      reads=[rwb2_r, cst_r], writes=[ps5_r], deps=[t9], tag="at_Bbr")
        yield
        t11 = plan.op("dve", lambda e: e.tensor_copy(nbc[:, :], ps5[:, 0:1]),
                      reads=[ps5_r], writes=[nbc_r], deps=[t10], tag="at_negB")
        yield
        if bias_tab is not None:
            tbl = e_dma(K_, "sp", bcol[:, :], bias_tab[h * 128:(h + 1) * 128, :], writes=[bcol_r],
                        tag=f"at_bt{h}")
            yield
            tbc = plan.op("dve", lambda e: e.tensor_scalar(
                bcol[:, :], bcol[:, :], nbc[:, 0:1], None, ALU.add),
                reads=[nbc_r], writes=[bcol_r], deps=[tbl, t11], tag="at_bfold")
        else:
            tbc = plan.op("dve", lambda e: e.tensor_copy(bcol[:, 0:1], nbc[:, :]),
                          reads=[nbc_r], writes=[bcol_r], deps=[t11], tag="at_bcp")
        yield
        for qt in range(nqt):
            kcs = kc_list(qt)
            ps_sum, ps_sum_r = S["b_sum"], S["b_sum_r"]
            ps_av, ps_av_r = S["b_av"], S["b_av_r"]
            psc, psc_r = S["b_sc"], S["b_sc_r"]
            first = True
            for kc in kcs:
                qstop = qrow_tab is None
                tmm = plan.op("pe", lambda e, kc=kc, psc=psc, qt=qt: e.matmul(
                    psc[:, 0:Tq], kh[:, kc * 128:(kc + 1) * 128],
                    qh[:, qt * Tq:(qt + 1) * Tq], start=True, stop=qstop),
                    reads=[kh_r, qh_r], writes=[psc_r], tag=f"at_s{h}_{qt}_{kc}")
                yield
                if qrow_tab is not None:
                    tmm = plan.op("pe", lambda e, psc=psc, qt=qt: e.matmul(
                        psc[:, 0:Tq], cst[0:1, 128:256],
                        qrow[0:1, qt * Tq:(qt + 1) * Tq], start=False, stop=True),
                        reads=[qrow_r, cst_r], writes=[psc_r], deps=[tqr],
                        tag=f"at_sq{h}_{qt}_{kc}")
                    yield
                mi = mask_idx_of(qt, kc) if mask_idx_of else None
                tmsk = None
                if mi is not None:
                    tmsk = plan.op("dve", lambda e, psc=psc, m=mi: e.tensor_add(
                        psc[:, 0:Tq], psc[:, 0:Tq], msk[:, m * Tq:(m + 1) * Tq]),
                        reads=[msk_r], writes=[psc_r], deps=[tmm],
                        tag=f"at_m{h}_{qt}_{kc}")
                    yield
                bi = bias_col_of(qt, kc) if bias_col_of else 0
                scs_ap, scs_res = S["sc"][kc % 2], S["sc_r"][kc % 2]
                tex = plan.op("act", lambda e, psc=psc, o=scs_ap, b=bi: e.activation(
                    o[:, :], psc[:, 0:Tq], AF.Exp, bias=bcol[:, b:b + 1], scale=scale),
                    reads=[psc_r, bcol_r], writes=[scs_res], deps=[tbc],
                    tag=f"at_e{h}_{qt}_{kc}")
                yield
                last = (kc == kcs[-1])
                plan.op("pe", lambda e, o=scs_ap, s=first, p=last: e.matmul(
                    ps_sum[0:1, 0:Tq], cst[:, 128:129], o[:, :], start=s, stop=p),
                    reads=[scs_res, cst_r], writes=[ps_sum_r], deps=[tex],
                    tag=f"at_sum{h}_{qt}_{kc}")
                yield
                plan.op("pe", lambda e, o=scs_ap, kc=kc, s=first, p=last: e.matmul(
                    ps_av[:, 0:Tq], vh[:, kc * 128:(kc + 1) * 128], o[:, :],
                    start=s, stop=p),
                    reads=[scs_res, vh_r], writes=[ps_av_r], deps=[tex, tvh],
                    tag=f"at_av{h}_{qt}_{kc}")
                yield
                first = False
            tsum = plan.op("dve", lambda e: e.reciprocal(rowb[0:1, :], ps_sum[0:1, 0:Tq]),
                           reads=[ps_sum_r], writes=[rowb_r], tag=f"at_rc{h}_{qt}")
            yield
            tbr = plan.op("pe", lambda e: e.matmul(ps_sum[:, 0:Tq], onesf[0:1, :],
                                                   rowb[0:1, :], start=True, stop=True),
                          reads=[rowb_r, onesf_r], writes=[ps_sum_r], deps=[tsum],
                          tag=f"at_rb{h}_{qt}")
            yield
            tre = plan.op("dve", lambda e: e.tensor_copy(rec[:, :], ps_sum[:, 0:Tq]),
                          reads=[ps_sum_r], writes=[rec_r], deps=[tbr], tag=f"at_re{h}_{qt}")
            yield
            if dst_dram is not None and out_stripes is not None:
                def mix_ev(e, out_ap):
                    return e.tensor_mul(out_ap, ps_av[:, 0:Tq], rec[:, :])
                a_, r_ = stg.slots[stg.i % len(stg.slots)]
                stg.i += 1
                tev = plan.op("dve", lambda e, a_=a_: mix_ev(e, a_[:, 0:Tq]),
                              reads=[ps_av_r, rec_r], writes=[r_], deps=[tre],
                              tag="at_oev")
                yield
                nstr = Tq // 128

                def opairs(p, h=h, qt=qt, a_=a_, nstr=nstr):
                    gs = q_stripes(p)[qt * nstr:(qt + 1) * nstr]
                    return [(dst_dram[h * 128:(h + 1) * 128,
                                      g * 128:(g + 1) * 128],
                             a_[:, j * 128:(j + 1) * 128])
                            for j, g in enumerate(gs)]
                tout = e_dma_sw(K_, "sp", opairs, reads=[r_], tag="at_osw")
                yield
            elif dst_dram is not None:
                def mix_ev(e, out_ap):
                    return e.tensor_mul(out_ap, ps_av[:, 0:Tq], rec[:, :])
                tout = stg.store(ps_av[:, 0:Tq], ps_av_r,
                                 dst_dram[h * 128:(h + 1) * 128,
                                          qt * Tq:(qt + 1) * Tq],
                                 tok=tre, w=Tq, evac=mix_ev, extra_reads=[rec_r])
                yield
            else:
                tout = plan.op("dve", lambda e, h=h, qt=qt: e.tensor_mul(
                    attn_dst[:, h * n_q + qt * Tq: h * n_q + (qt + 1) * Tq],
                    ps_av[:, 0:Tq], rec[:, :]),
                    reads=[ps_av_r, rec_r], writes=[attn_dst_r], deps=[tre],
                    tag=f"at_o{h}_{qt}")
                yield
            out_toks.append(tout)

    for hp in range((n_h + 1) // 2):
        gens = [head_gen(2 * hp, slots[0])]
        if 2 * hp + 1 < n_h:
            gens.append(head_gen(2 * hp + 1, slots[1]))
        while gens:
            keep = []
            for g in gens:
                try:
                    next(g)
                    keep.append(g)
                except StopIteration:
                    pass
            gens = keep
    return out_toks



def ln_and_y(K_, A, cfg, uniq, x_dram, n_full):
    ntf = n_full // TT
    xt, xt_r = K_.sbuf("xt", [128, 8 * TT], F32)
    xt_rs = [Res(f"xt_{k}") for k in range(8)]
    y, y_r = K_.sbuf(f"y_{uniq}", [128, NKC * n_full], BF16)
    seen = {}

    def x_at(k, t):
        sl = k % 8
        key = (k, t)
        n = seen.get(key, 0)
        seen[key] = n + 1
        e_dma(K_, "sp", xt[:, sl * TT:(sl + 1) * TT],
              x_dram[k * 128:(k + 1) * 128, t * TT:(t + 1) * TT],
              writes=[xt_rs[sl]], tag=f"ldx{uniq}")
        return xt[:, sl * TT:(sl + 1) * TT], [xt_rs[sl]]

    def y_sink(k, t, emit):
        return emit(y[:, k * n_full + t * TT: k * n_full + (t + 1) * TT], [y_r])

    toks = ln_fm(K_, name=uniq, n_tok=n_full, x_at=x_at, y_sink=y_sink)
    return y, y_r, toks


def proj_to_dram(K_, *, name, wload, n_k, m_list, t_list, rhs_at, dst, stg,
                 row_of=None, mg=3, evac=None, evac_of=None, extra_reads=(),
                 evac_eng="dve"):
    def sink(m, t, ps, ps_r, tok):
        row = (row_of(m) if row_of else m * 128)
        ev = evac_of(m, t, ps) if evac_of else evac
        return stg.store(ps[:, :], ps_r, dst[row:row + 128, t * TT:(t + 1) * TT],
                         tok=tok, evac=ev, extra_reads=extra_reads,
                         evac_eng=evac_eng)
    return proj_fm(K_, name=name, wload=wload, n_k=n_k, m_list=m_list,
                   t_list=t_list, rhs_at=rhs_at, sink=sink, mg=mg)


def vproj_tm(K_, *, name, wT_dram_bf, wT_deps, y, y_r, n_full, n_vc, dst, stg,
             evac=None, nat_f32=False, evac_of=None, extra_reads=(),
             evac_eng="dve"):
    plan = K_.plan
    wt, wt_r = K_.sbuf("wvT", [128, NKC * TT], BF16)
    wt_rs = [Res(f"wvT_{k}") for k in range(NKC)]
    out = []
    ntb = n_full // 128
    nvt = (n_vc + TT - 1) // TT
    for vt_i in range(nvt):
        vw = min(TT, n_vc - vt_i * TT)
        wl = []
        for k in range(NKC):
            if nat_f32:
                wl.append(e_dma(K_, "pool", wt[:, k * TT:k * TT + vw],
                                wT_dram_bf[k * 128:(k + 1) * 128,
                                           vt_i * TT: vt_i * TT + vw],
                                writes=[wt_rs[k]], deps=list(wT_deps),
                                tag=f"wvT{name}"))
            else:
                wl.append(e_xbar(K_, wt[:, k * TT:k * TT + vw],
                                 wT_dram_bf[vt_i * TT: vt_i * TT + vw,
                                            k * 128:(k + 1) * 128],
                                 writes=[wt_rs[k]], deps=list(wT_deps),
                                 tag=f"wvT{name}"))
        for tb in range(ntb):
            bank = tb % 2
            ps, ps_r = K_.psum_f[bank], K_.psum_fr[bank]
            tok = None
            for k in range(NKC):
                st, sp = (k == 0), (k == NKC - 1)
                tok = plan.op("pe", lambda e, k=k, s=st, p=sp, vw=vw, tb=tb, ps=ps:
                              e.matmul(ps[:, 0:vw],
                                       y[:, k * n_full + tb * 128: k * n_full + (tb + 1) * 128],
                                       wt[:, k * TT: k * TT + vw],
                                       start=s, stop=p),
                              reads=[y_r, wt_rs[k]], writes=[ps_r], deps=[wl[k]],
                              tag=f"vmm{name}_{tb}_{vt_i}_{k}")
            ev = evac_of(vt_i, tb, ps) if evac_of else evac
            out.append(stg.store(ps[:, 0:vw], ps_r,
                                 dst[tb * 128:(tb + 1) * 128,
                                     vt_i * TT: vt_i * TT + vw],
                                 tok=tok, w=vw, evac=ev, extra_reads=extra_reads,
                                 evac_eng=evac_eng))
    return out


def emit_l0aa(K_, A, cfg):
    plan = K_.plan
    n_own, n_full = cfg["n_own"], cfg["n_full"]
    nto = n_own // TT
    cq = cast_pass(K_, A["l0aa_wq_bf"], A["l0aa_wq"], 2048, 2048, tag="cwq")
    ck = cast_pass(K_, A["l0aa_wk_bf"], A["l0aa_wk"], 1024, 2048, tag="cwk")
    cv = cast_pass(K_, A["l0aa_wv_bf"], A["l0aa_wv"], 1024, 2048, tag="cwv")
    co = cast_pass(K_, A["l0aa_wo_bf"], A["l0aa_wo"], 2048, 2048, tag="cwo")
    y, y_r, _ = ln_and_y(K_, A, cfg, "l0aa", A["x"], n_full)

    def y_at(k, t):
        return y[:, k * n_full + t * TT:(k * n_full) + (t + 1) * TT], [y_r]

    stg = Stager(K_, "s", 512, 3)
    tq = proj_to_dram(K_, name="l0aa_q", wload=wload_xbarT(K_, A["l0aa_wq_bf"], cq),
                      n_k=NKC, m_list=list(range(16)), t_list=list(range(nto)),
                      rhs_at=y_at, dst=A["l0aa_qf"], stg=stg)
    tk = []
    for tl in [list(range(0, min(2, n_full // TT))), list(range(2, n_full // TT))]:
        if tl:
            tk += proj_to_dram(K_, name="l0aa_k", wload=wload_xbarT(K_, A["l0aa_wk_bf"], ck),
                               n_k=NKC, m_list=list(range(8)), t_list=tl,
                               rhs_at=y_at, dst=A["l0aa_kf"], stg=stg)
    tv = vproj_tm(K_, name="l0aa", wT_dram_bf=A["l0aa_wv_bf"], wT_deps=cv,
                  y=y, y_r=y_r, n_full=n_full, n_vc=1024, dst=A["l0aa_vt"], stg=stg)
    attn, attn_r = K_.sbuf("l0aa_attn", [128, 16 * n_own], BF16)
    at = softmax_attn(K_, name="l0aa", n_h=16, kv_of=lambda h: h // 2,
                      n_q=n_own, n_kv=n_full, scale=1.0 / np.sqrt(128.0), Tq=512,
                      qf=A["l0aa_qf"], kf=A["l0aa_kf"], vt=A["l0aa_vt"], n_kvh=8,
                      kc_list=lambda qt: list(range(n_full // 128)),
                      attn_dst=attn, attn_dst_r=attn_r, dep_toks=tq + tk + tv)

    def attn_at(k, t):
        return attn[:, k * n_own + t * TT: k * n_own + (t + 1) * TT], [attn_r]

    xres, xres_r = [], []
    for i in range(3):
        a, r = K_.sbuf(f"xr{i}", [128, TT], F32)
        xres.append(a)
        xres_r.append(r)
    xi = [0]

    def wo_sink(m, t, ps, ps_r, tok):
        i = xi[0] % 3
        xi[0] += 1
        tl = e_dma(K_, "sp", xres[i][:, :],
                   A["x"][m * 128:(m + 1) * 128, t * TT:(t + 1) * TT],
                   writes=[xres_r[i]], tag="l0aa_xr")
        ta = plan.op("dve", lambda e: e.tensor_add(xres[i][:, :], xres[i][:, :], ps[:, :]),
                     reads=[ps_r], writes=[xres_r[i]], deps=[tl, tok], tag="l0aa_add")
        ts = e_dma(K_, "sp", A["xo"][m * 128:(m + 1) * 128, t * TT:(t + 1) * TT],
                   xres[i][:, :], reads=[xres_r[i]], tag="l0aa_st")
        K_.store_toks.append(ts)
        return ts

    proj_fm(K_, name="l0aa_wo", wload=wload_xbarT(K_, A["l0aa_wo_bf"], co),
            n_k=NKC, m_list=list(range(16)), t_list=list(range(nto)),
            rhs_at=attn_at, sink=wo_sink, mg=3)


def declare_l0aa(nc, A, cfg):
    n_own, n_full = cfg["n_own"], cfg["n_full"]
    for w, rows in (("wq", 2048), ("wk", 1024), ("wv", 1024), ("wo", 2048)):
        A[f"l0aa_{w}"] = nc.dram_tensor(f"l0aa_{w}", [rows, 2048], F32,
                                        kind="ExternalInput").ap()
        A[f"l0aa_{w}_bf"] = nc.dram_tensor(f"l0aa_{w}_bf", [rows, 2048], BF16,
                                           kind="Internal").ap()
    A["l0aa_qf"] = nc.dram_tensor("l0aa_qf", [2048, n_own], BF16, kind="Internal").ap()
    A["l0aa_kf"] = nc.dram_tensor("l0aa_kf", [1024, n_full], BF16, kind="Internal").ap()
    A["l0aa_vt"] = nc.dram_tensor("l0aa_vt", [n_full, 1024], BF16, kind="Internal").ap()


DECLARES = {"l0aa": declare_l0aa}
EMITTERS = {"l0aa": emit_l0aa}


def build_launch(subs, cfg):
    nc = bass.Bass("TRN2", target_bir_lowering=False)
    A = {}
    A["x"] = nc.dram_tensor("x", [2048, cfg["n_full"]], F32, kind="ExternalInput").ap()
    A["xo"] = nc.dram_tensor("xo", [2048, cfg["n_own"]], F32, kind="ExternalOutput").ap()
    A["cst_d"] = nc.dram_tensor("cst", [128, 512], BF16, kind="ExternalInput").ap()
    for s in subs:
        DECLARES[s](nc, A, cfg)
    with contextlib.ExitStack() as stack:
        plan = Plan(nc)
        K_ = K(nc, plan, stack)
        cstsb, cst_r = K_.sbuf("cstsb", [128, 512], BF16)
        e_dma(K_, "sp", cstsb[:, :], A["cst_d"][:, :], writes=[cst_r], tag="ldcst")
        K_.cst = (cstsb, cst_r)
        for s in subs:
            EMITTERS[s](K_, A, cfg)
        plan.finalize(stack, K_.store_toks)
    return nc


def perm_for(p, n=2048, stripe=128):
    s = np.arange(n).reshape(-1, stripe)
    own = s[p::2].ravel()
    oth = s[1 - p::2].ravel()
    return np.concatenate([own, oth])


def sim_one(nc, in_map, require_finite=False):
    from concourse.bass_interp import CoreSim
    sim = CoreSim(nc, require_finite=require_finite, require_nnan=False)
    for k, v in in_map.items():
        t = sim.tensor(k)
        t[:] = np.asarray(v).view(t.dtype) if v.dtype != t.dtype else v
    sim.simulate()
    return sim



def emit_l0ffn(K_, A, cfg):
    plan = K_.plan
    n_own = cfg["n_own"]
    nto = n_own // TT
    cst, cst_r = K_.cst
    wrow, wrow_r = [], []
    for i in range(2):
        a, r = K_.sbuf(f"f_wr{i}", [128, 2048], F32)
        wrow.append(a); wrow_r.append(r)
    wsg, wsg_r = [], []
    for i in range(2):
        a, r = K_.sbuf(f"f_ws{i}", [128, 2048], BF16)
        wsg.append(a); wsg_r.append(r)
    amx, amx_r = K_.sbuf("f_amx", [128, 2], F32)
    sgn_toks = []
    for rci in range(16):
        b = rci % 2
        tl = e_dma(K_, "sp", wrow[b][:, :], A["l0f_w"][rci * 128:(rci + 1) * 128, :],
                   writes=[wrow_r[b]], tag=f"fw{rci}")
        ts = plan.op("act", lambda e, b=b: e.activation(wsg[b][:, :], wrow[b][:, :], AF.Sign),
                     reads=[wrow_r[b]], writes=[wsg_r[b]], deps=[tl], tag=f"fsg{rci}")
        sgn_toks.append(e_dma(K_, "sp", A["l0f_ws_bf"][rci * 128:(rci + 1) * 128, :],
                              wsg[b][:, :], reads=[wsg_r[b]], deps=[ts], tag=f"fso{rci}"))
        if rci == 0:
            plan.op("dve", lambda e, b=b: e.reduce_max(
                amx[:, 0:1], wrow[b][:, :], axis=AX, apply_absolute_value=True),
                reads=[wrow_r[b]], writes=[amx_r], deps=[tl], tag="fmx0")
        else:
            plan.op("dve", lambda e, b=b: e.reduce_max(
                amx[:, 1:2], wrow[b][:, :], axis=AX, apply_absolute_value=True),
                reads=[wrow_r[b]], writes=[amx_r], deps=[tl], tag=f"fmx{rci}")
            plan.op("dve", lambda e: e.tensor_max(amx[:, 0:1], amx[:, 0:1], amx[:, 1:2]),
                    reads=[amx_r], writes=[amx_r], tag=f"fmm{rci}")
    tb1 = e_dma(K_, "sp", A["l0f_col"][:, :], amx[:, 0:1], reads=[amx_r], tag="fmxo")
    srow, srow_r = K_.sbuf("f_srow", [1, 130], F32)
    tb2 = e_dma(K_, "sp", srow[0:1, 0:128], A["l0f_col"].rearrange("p o -> o p"),
                deps=[tb1], writes=[srow_r], tag="fmxi")
    tmx = plan.op("dve", lambda e: e.reduce_max(srow[0:1, 128:129], srow[0:1, 0:128], axis=AX),
                  reads=[srow_r], writes=[srow_r], deps=[tb2], tag="fmxr")
    pcol, pcol_r = K_.sbuf("f_pcol", [128, 64], F32)
    pcolb, pcolb_r = K_.sbuf("f_pcolb", [128, 64], BF16)
    tpc = e_dma(K_, "sp", pcol[:, :], A["l0f_pcol"][:, :], writes=[pcol_r], tag="fpc")
    tpcb = plan.op("dve", lambda e: e.tensor_copy(pcolb[:, :], pcol[:, :]),
                   reads=[pcol_r], writes=[pcolb_r], deps=[tpc], tag="fpcb")
    srow_b, srow_b_r = K_.sbuf("f_srb", [1, 2], BF16)
    tsb = plan.op("dve", lambda e: e.tensor_copy(srow_b[0:1, 0:1], srow[0:1, 128:129]),
                  reads=[srow_r], writes=[srow_b_r], deps=[tmx], tag="fsb")
    ps5, ps5_r = K_.psum_f[5], K_.psum_fr[5]
    tbc = plan.op("pe", lambda e: e.matmul(ps5[:, 0:1], cst[0:1, 128:256],
                                           srow_b[0:1, 0:1], start=True, stop=True),
                  reads=[srow_b_r, cst_r], writes=[ps5_r], deps=[tsb], tag="fsbb")
    scol, scol_r = K_.sbuf("f_scol", [128, 18], F32)
    tsc = plan.op("dve", lambda e: e.tensor_copy(scol[:, 16:17], ps5[:, 0:1]),
                  reads=[ps5_r], writes=[scol_r], deps=[tbc], tag="fsc")
    tsg = plan.op("dve", lambda e: e.tensor_scalar(
        scol[:, 0:16], pcol[:, 0:16], scol[:, 16:17], None, ALU.mult),
        reads=[pcol_r, scol_r], writes=[scol_r], deps=[tsc, tpc], tag="fsg_col")
    y, y_r, _ = ln_and_y(K_, A, cfg, "l0f", A["x"], n_own)
    y2, y2_r = K_.sbuf("y2_l0f", [128, NKC * n_own], BF16)

    def x2_at(k, t):
        return y[:, k * n_own + t * TT: k * n_own + (t + 1) * TT], [y_r]

    def y2_sink(k, t, emit):
        return emit(y2[:, k * n_own + t * TT: k * n_own + (t + 1) * TT], [y2_r])

    ln_fm(K_, name="l0f2", n_tok=n_own, x_at=x2_at, y_sink=y2_sink, in_bf16=True,
          gain_col=(lambda k: pcol[:, 32 + k:33 + k]), gain_res=pcol_r,
          gain_dep=tpc)
    xres, xres_r = [], []
    for i in range(3):
        a, r = K_.sbuf(f"xr{i}", [128, TT], F32)
        xres.append(a); xres_r.append(r)
    ccol, ccol_r = K_.sbuf("f_ccol", [128, 16], F32)
    tmpo, tmpo_r = K_.sbuf("f_tmpo", [128, TT], F32)
    xi = [0]

    def rhs_at(k, t):
        if t == 99:
            return pcolb[:, 48 + k:49 + k], [pcolb_r]
        return y2[:, k * n_own + t * TT: k * n_own + (t + 1) * TT], [y2_r]

    def sink2(m, t, ps, ps_r, tok):
        if t == 99:
            return plan.op("dve", lambda e, m=m, ps=ps: e.tensor_copy(
                ccol[:, m:m + 1], ps[:, 0:1]),
                reads=[ps_r], writes=[ccol_r], deps=[tok], tag=f"fc{m}")
        i = xi[0] % 3
        xi[0] += 1
        tl = e_dma(K_, "sp", xres[i][:, :],
                   A["x"][m * 128:(m + 1) * 128, t * TT:(t + 1) * TT],
                   writes=[xres_r[i]], tag="f_xr")
        t1 = plan.op("dve", lambda e, m=m, ps=ps: e.tensor_scalar(
            tmpo[:, :], ps[:, :], ccol[:, m:m + 1], scol[:, m:m + 1],
            ALU.add, ALU.mult),
            reads=[ps_r, ccol_r, scol_r], writes=[tmpo_r], deps=[tok, tsg], tag="f_e1")
        t2 = plan.op("dve", lambda e, m=m, i=i: e.scalar_tensor_tensor(
            xres[i][:, :], tmpo[:, :], pcol[:, 16 + m:17 + m], xres[i][:, :],
            ALU.add, ALU.add),
            reads=[tmpo_r, pcol_r], writes=[xres_r[i]], deps=[t1, tl], tag="f_e2")
        ts = e_dma(K_, "sp", A["xo"][m * 128:(m + 1) * 128, t * TT:(t + 1) * TT],
                   xres[i][:, :], reads=[xres_r[i]], tag="f_st")
        K_.store_toks.append(ts)
        return ts

    proj_fm(K_, name="l0f_mm", wload=wload_xbarT(K_, A["l0f_ws_bf"], sgn_toks),
            n_k=NKC, m_list=list(range(16)), t_list=[99] + list(range(nto)),
            rhs_at=rhs_at, sink=sink2, mg=2)


def declare_l0ffn(nc, A, cfg):
    A["l0f_w"] = nc.dram_tensor("l0f_w", [2048, 2048], F32, kind="ExternalInput").ap()
    A["l0f_pcol"] = nc.dram_tensor("l0f_pcol", [128, 64], F32, kind="ExternalInput").ap()
    A["l0f_ws_bf"] = nc.dram_tensor("l0f_ws_bf", [2048, 2048], BF16, kind="Internal").ap()
    A["l0f_col"] = nc.dram_tensor("l0f_col", [128, 1], F32, kind="Internal").ap()


DECLARES["l0ffn"] = declare_l0ffn
EMITTERS["l0ffn"] = emit_l0ffn



def emit_l0ab(K_, A, cfg):
    plan = K_.plan
    n_own, n_full = cfg["n_own"], cfg["n_full"]
    nto = n_own // TT
    cst, cst_r = K_.cst
    y, y_r, _ = ln_and_y(K_, A, cfg, "l0ab", A["x"], n_full)

    def y_at(k, t):
        return y[:, k * n_full + t * TT:(k * n_full) + (t + 1) * TT], [y_r]

    bcols, bcols_r = K_.sbuf("ab_bcols", [128, 48], F32)
    tbc = e_dma(K_, "sp", bcols[:, :], A["ab_bcol"][:, :], writes=[bcols_r], tag="ab_bc")
    bvrow, bvrow_r = K_.sbuf("ab_bvr_s", [1, TT], F32)
    bvrow_b, bvrow_b_r = K_.sbuf("ab_bvrb_s", [1, TT], BF16)
    bvB, bvB_r = K_.sbuf("ab_bvB", [128, 2048], BF16)
    ps5, ps5_r = K_.psum_f[5], K_.psum_fr[5]
    for vt_i in range(4):
        tbv = e_dma(K_, "sp", bvrow[0:1, :], A["ab_bvrow"][0:1, vt_i * TT:(vt_i + 1) * TT],
                    writes=[bvrow_r], tag="ab_bv")
        tbvb = plan.op("dve", lambda e: e.tensor_copy(bvrow_b[0:1, :], bvrow[0:1, :]),
                       reads=[bvrow_r], writes=[bvrow_b_r], deps=[tbv], tag="ab_bvb")
        tm = plan.op("pe", lambda e: e.matmul(
            ps5[:, :], cst[0:1, 128:256], bvrow_b[0:1, :],
            start=True, stop=True),
            reads=[bvrow_b_r, cst_r], writes=[ps5_r], tag=f"ab_bvm{vt_i}")
        plan.op("dve", lambda e, v=vt_i: e.tensor_copy(bvB[:, v * TT:(v + 1) * TT], ps5[:, :]),
                reads=[ps5_r], writes=[bvB_r], deps=[tm], tag=f"ab_bve{vt_i}")
    stg = Stager(K_, "s", 512, 3)

    def bias_evac(col):
        def eo(m, t, ps):
            def ev(e, out_ap, m=m, ps=ps):
                return e.tensor_scalar(out_ap, ps[:, :],
                                       bcols[:, col + m:col + m + 1], None, ALU.add)
            return ev
        return eo

    tq = proj_to_dram(K_, name="l0ab_q", wload=wload_nat(K_, A["l0ab_wq"]),
                      n_k=NKC, m_list=list(range(16)), t_list=list(range(nto)),
                      rhs_at=y_at, dst=A["l0ab_qf"], stg=stg,
                      evac_of=bias_evac(0), extra_reads=[bcols_r])
    tk = []
    for tl in [list(range(0, min(2, n_full // TT))), list(range(2, n_full // TT))]:
        if tl:
            tk += proj_to_dram(K_, name="l0ab_k", wload=wload_nat(K_, A["l0ab_wk"]),
                               n_k=NKC, m_list=list(range(16)), t_list=tl,
                               rhs_at=y_at, dst=A["l0ab_kf"], stg=stg,
                               evac_of=bias_evac(16), extra_reads=[bcols_r])

    def v_evac(vt_i, tb, ps):
        def ev(e, out_ap):
            return e.tensor_add(out_ap, ps[:, :], bvB[:, vt_i * TT:(vt_i + 1) * TT])
        return ev

    tv = vproj_tm(K_, name="l0ab", wT_dram_bf=A["l0ab_wv"], wT_deps=[],
                  y=y, y_r=y_r, n_full=n_full, n_vc=2048, dst=A["l0ab_vt"],
                  stg=stg, nat_f32=True, evac_of=v_evac, extra_reads=[bvB_r])

    def kc_list(qt):
        own = list(range(0, 2 * qt + 2))
        oth = [8 + j for j in range(0, 2 * qt + 2)]
        return own + oth

    def mask_idx_of(qt, kc):
        cands = [2 * qt, 2 * qt + 1, 8 + 2 * qt, 8 + 2 * qt + 1]
        if kc in cands:
            return qt * 4 + cands.index(kc)
        return None

    attn, attn_r = K_.sbuf("l0aa_attn", [128, 16 * n_own], BF16)
    at = softmax_attn(K_, name="l0ab", n_h=16, kv_of=lambda h: h,
                      n_q=n_own, n_kv=n_full, scale=1.0 / np.sqrt(128.0), Tq=256,
                      qf=A["l0ab_qf"], kf=A["l0ab_kf"], vt=A["l0ab_vt"], n_kvh=16,
                      kc_list=kc_list,
                      bias_tab=A["ab_btab"], bias_col_of=lambda qt, kc: qt * 16 + kc,
                      n_bias_cols=64,
                      mask_tab=A["ab_mtab"][:, :], mask_idx_of=mask_idx_of, n_masks=16,
                      qrow_tab=A["ab_qrow"],
                      attn_dst=attn, attn_dst_r=attn_r, dep_toks=tq + tk + tv)

    def attn_at(k, t):
        return attn[:, k * n_own + t * TT: k * n_own + (t + 1) * TT], [attn_r]

    xres, xres_r = [], []
    for i in range(3):
        a, r = K_.sbuf(f"xr{i}", [128, TT], F32)
        xres.append(a); xres_r.append(r)
    xi = [0]

    def wo_sink(m, t, ps, ps_r, tok):
        i = xi[0] % 3
        xi[0] += 1
        tl = e_dma(K_, "sp", xres[i][:, :],
                   A["x"][m * 128:(m + 1) * 128, t * TT:(t + 1) * TT],
                   writes=[xres_r[i]], tag="ab_xr")
        ta = plan.op("dve", lambda e, m=m, i=i, ps=ps: e.scalar_tensor_tensor(
            xres[i][:, :], ps[:, :], bcols[:, 32 + m:33 + m], xres[i][:, :],
            ALU.add, ALU.add),
            reads=[ps_r, bcols_r], writes=[xres_r[i]], deps=[tl, tok], tag="ab_add")
        ts = e_dma(K_, "sp", A["xo"][m * 128:(m + 1) * 128, t * TT:(t + 1) * TT],
                   xres[i][:, :], reads=[xres_r[i]], tag="ab_st")
        K_.store_toks.append(ts)
        return ts

    proj_fm(K_, name="l0ab_wo", wload=wload_nat(K_, A["l0ab_wo"]),
            n_k=NKC, m_list=list(range(16)), t_list=list(range(nto)),
            rhs_at=attn_at, sink=wo_sink, mg=3)


def declare_l0ab(nc, A, cfg):
    n_own, n_full = cfg["n_own"], cfg["n_full"]
    for w in ("wq", "wk", "wv", "wo"):
        A[f"l0ab_{w}"] = nc.dram_tensor(f"l0ab_{w}", [2048, 2048], F32,
                                        kind="ExternalInput").ap()
    A["ab_bcol"] = nc.dram_tensor("ab_bcol", [128, 48], F32, kind="ExternalInput").ap()
    A["ab_bvrow"] = nc.dram_tensor("ab_bvrow", [1, 2048], F32, kind="ExternalInput").ap()
    A["ab_btab"] = nc.dram_tensor("ab_btab", [16 * 128, 64], F32, kind="ExternalInput").ap()
    A["ab_mtab"] = nc.dram_tensor("ab_mtab", [128, 16 * 256], BF16, kind="ExternalInput").ap()
    A["ab_qrow"] = nc.dram_tensor("ab_qrow", [16, n_own], BF16, kind="ExternalInput").ap()
    A["l0ab_qf"] = nc.dram_tensor("l0ab_qf", [2048, n_own], BF16, kind="Internal").ap()
    A["l0ab_kf"] = nc.dram_tensor("l0ab_kf", [2048, n_full], BF16, kind="Internal").ap()
    A["l0ab_vt"] = nc.dram_tensor("l0ab_vt", [n_full, 2048], BF16, kind="Internal").ap()


DECLARES["l0ab"] = declare_l0ab
EMITTERS["l0ab"] = emit_l0ab


def host_l0ab_tables(perm, n_own=1024, n_heads=16, alibi=True, scale=None):
    if scale is None:
        scale = 1.0 / np.sqrt(128.0)
    pos = perm.astype(np.float64)
    slopes = np.array([(2.0 ** (-8.0 / n_heads)) ** (i + 1) for i in range(n_heads)])
    if not alibi:
        slopes = slopes * 0.0
    nqt = n_own // 256
    nkc = len(perm) // 128
    btab = np.zeros((n_heads * 128, nqt * nkc), np.float32)
    mtab = np.zeros((128, nqt * 4 * 256), np.float32)
    for qt in range(nqt):
        qpos = pos[qt * 256:(qt + 1) * 256]
        for kc in range(nkc):
            kpos = pos[kc * 128:(kc + 1) * 128]
            for h in range(n_heads):
                if kpos.min() > qpos.max():
                    btab[h * 128:(h + 1) * 128, qt * nkc + kc] = -1e30
                else:
                    btab[h * 128:(h + 1) * 128, qt * nkc + kc] = (
                        slopes[h] * kpos).astype(np.float32)
        cands = [2 * qt, 2 * qt + 1, nkc // 2 + 2 * qt, nkc // 2 + 2 * qt + 1]
        for ci, kc in enumerate(cands):
            kpos = pos[kc * 128:(kc + 1) * 128]
            m = np.where(kpos[:, None] <= qpos[None, :], 0.0, -1e30).astype(np.float32)
            mtab[:, (qt * 4 + ci) * 256:(qt * 4 + ci + 1) * 256] = m
    qrow = np.zeros((n_heads, n_own), np.float32)
    for h in range(n_heads):
        qrow[h] = (-slopes[h] * pos[:n_own] / scale).astype(np.float32)
    return btab, mtab.astype(BF), qrow.astype(BF)



def emit_l1aa(K_, A, cfg):
    plan = K_.plan
    nf = cfg["n_full"]
    assert cfg["n_own"] == nf
    cst, cst_r = K_.cst
    H, Dh, BS, NB = 32, 64, 256, nf // 256
    qkv = A["l1aa_qkv"]
    qk_src = qkv.rearrange("(h r) c -> h r c", h=H)[:, 0:128, :]
    v_src = qkv.rearrange("(h r) c -> h r c", h=H)[:, 128:192, :]
    cqk = [e_dma(K_, "pool", A["l1aa_qk_bf"].rearrange("(h r) c -> h r c", h=H),
                 qk_src, tag="c_qk")]
    cv = [e_dma(K_, "pool", A["l1aa_v_bf"].rearrange("(h r) c -> h r c", h=H),
                v_src, tag="c_v")]
    cg = cast_pass(K_, A["l1aa_gw_bf"], A["l1aa_gw"], 2048, 2048, tag="c_gw")
    cow = cast_pass(K_, A["l1aa_ow_bf"], A["l1aa_ow"], 2048, 2048, tag="c_ow")
    y, y_r, _ = ln_and_y(K_, A, cfg, "l1aa", A["x"], nf)

    def y_at(k, t):
        return y[:, k * nf + t * TT:(k * nf) + (t + 1) * TT], [y_r]

    stg = Stager(K_, "s", 512, 3)

    def silu_ev(m, t, ps):
        def ev(e, out_ap, ps=ps):
            return e.activation(out_ap, ps[:, 0:out_ap.shape[-1]], AF.Silu)
        return ev

    tqk = []
    ntf_ = nf // TT
    for tl in [list(range(i, min(i + 2, ntf_))) for i in range(0, ntf_, 2)]:
        tqk += proj_to_dram(K_, name="l1aa_qk", wload=wload_xbarT(K_, A["l1aa_qk_bf"], cqk),
                            n_k=NKC, m_list=list(range(H)), t_list=tl,
                            rhs_at=y_at, dst=A["l1aa_qkf"], stg=stg,
                            evac_of=silu_ev, evac_eng="act")

    def vsilu_ev(vt_i, tb, ps):
        def ev(e, out_ap, ps=ps):
            return e.activation(out_ap, ps[:, 0:out_ap.shape[-1]], AF.Silu)
        return ev

    tv = vproj_tm(K_, name="l1aa", wT_dram_bf=A["l1aa_v_bf"], wT_deps=cv,
                  y=y, y_r=y_r, n_full=nf, n_vc=2048, dst=A["l1aa_vtm"], stg=stg,
                  evac_of=vsilu_ev, evac_eng="act")
    qkb, qkb_r = K_.sbuf("aa_qkb", [64, 2 * 512], BF16)
    qkb_rs = [Res("aa_qkb0"), Res("aa_qkb1")]
    vb, vb_r = K_.sbuf("aa_vb", [128, 4 * Dh], BF16)
    vb_rs = [Res("aa_vb0"), Res("aa_vb1")]
    ddt, ddt_r = K_.sbuf("aa_ddt", [128, 2 * BS], BF16)
    qdB, qdB_r = K_.sbuf("aa_qdB", [128, BS], F32)
    qdrow, qdrow_r = K_.sbuf("aa_qdrow", [1, BS], BF16)
    decay, decay_r = K_.sbuf("aa_decay", [128, 128], F32)
    tdec = e_dma(K_, "sp", decay[:, :], A["l1aa_decay"][:, :], writes=[decay_r], tag="aa_dec")
    awd, awd_r = K_.sbuf("aa_awd", [128, 2 * BS], BF16)
    vkd, vkd_r = K_.sbuf("aa_vkd", [128, 2 * Dh], BF16)
    ktm, ktm_r = K_.sbuf("aa_ktm", [128, 2 * Dh], BF16)
    St, St_r = K_.sbuf("aa_S", [64, Dh], F32)
    Sb, Sb_r = K_.sbuf("aa_Sb", [64, Dh], BF16)
    ob, ob_r = K_.sbuf("aa_ob", [64, BS], F32)
    for h in range(H):
        tdd = e_dma(K_, "sp", ddt.rearrange("p (c s) -> p c s", c=2),
                    A["l1aa_ddT"][h * 256:(h + 1) * 256, :]
                    .rearrange("(c p) s -> p c s", p=128), writes=[ddt_r], tag=f"aa_dd{h}")
        tqd = e_dma(K_, "sp", qdrow[0:1, :], A["l1aa_qd"][h:h + 1, :],
                    writes=[qdrow_r], tag=f"aa_qd{h}")
        ps5, ps5_r = K_.psum_f[5], K_.psum_fr[5]
        tqb = plan.op("pe", lambda e: e.matmul(ps5[:, 0:BS], cst[0:1, 128:256],
                                               qdrow[0:1, :], start=True, stop=True),
                      reads=[qdrow_r, cst_r], writes=[ps5_r], deps=[tqd], tag=f"aa_qb{h}")
        tqe = plan.op("dve", lambda e: e.tensor_copy(qdB[:, :], ps5[:, 0:BS]),
                      reads=[ps5_r], writes=[qdB_r], deps=[tqb], tag=f"aa_qe{h}")
        tS0 = plan.op("dve", lambda e: e.memset(St[:, :], 0.0), writes=[St_r],
                      tag=f"aa_S0{h}")
        tSb = plan.op("dve", lambda e: e.tensor_copy(Sb[:, :], St[:, :]),
                      reads=[St_r], writes=[Sb_r], deps=[tS0], tag=f"aa_Sb{h}")
        for blk in range(NB):
            sl = blk % 2
            qk = qkb[:, sl * 512:(sl + 1) * 512]
            tq = e_dma(K_, "sp", qk.rearrange("p (b s) -> p b s", b=2),
                       A["l1aa_qkf"][h * 128:(h + 1) * 128, blk * BS:(blk + 1) * BS]
                       .rearrange("(b p) s -> p b s", b=2),
                       writes=[qkb_rs[sl]], deps=tqk, tag=f"aa_lq{h}_{blk}")
            vsl = vb[:, sl * 2 * Dh:(sl + 1) * 2 * Dh]
            tvl = e_dma(K_, "sp", vsl.rearrange("p (c d) -> p c d", c=2),
                        A["l1aa_vtm"][blk * BS:(blk + 1) * BS, h * Dh:(h + 1) * Dh]
                        .rearrange("(c p) d -> p c d", p=128),
                        writes=[vb_rs[sl]], deps=tv, tag=f"aa_lv{h}_{blk}")
            taw = []
            for c in range(2):
                psn, psn_r = K_.psum_f[c], K_.psum_fr[c]
                t1 = plan.op("pe", lambda e, c=c, qk=qk, psn=psn: e.matmul(
                    psn[:, 0:BS], qk[:, 256 + c * 128:256 + (c + 1) * 128],
                    qk[:, 0:BS], start=True, stop=True),
                    reads=[qkb_rs[sl]], writes=[psn_r], deps=[tq],
                    tag=f"aa_aw{h}_{blk}_{c}")
                taw.append(plan.op("dve", lambda e, c=c, psn=psn: e.tensor_mul(
                    awd[:, c * BS:(c + 1) * BS], psn[:, 0:BS],
                    ddt[:, c * BS:(c + 1) * BS]),
                    reads=[psn_r, ddt_r], writes=[awd_r], deps=[t1, tdd],
                    tag=f"aa_awd{h}_{blk}_{c}"))
            ps2, ps2_r = K_.psum_f[2], K_.psum_fr[2]
            ps3, ps3_r = K_.psum_f[3], K_.psum_fr[3]
            tintra = None
            for c in range(2):
                tintra = plan.op("pe", lambda e, c=c, vsl=vsl: e.matmul(
                    ps2[0:64, 0:BS], vsl[:, c * Dh:(c + 1) * Dh],
                    awd[:, c * BS:(c + 1) * BS], start=(c == 0), stop=(c == 1)),
                    reads=[vb_rs[sl], awd_r], writes=[ps2_r],
                    deps=[tvl] + taw, tag=f"aa_in{h}_{blk}_{c}")
            tinter = plan.op("pe", lambda e, qk=qk: e.matmul(
                ps3[0:64, 0:BS], Sb[:, :], qk[:, 0:BS], start=True, stop=True),
                reads=[Sb_r, qkb_rs[sl]], writes=[ps3_r], deps=[tq],
                tag=f"aa_it{h}_{blk}")
            tob = plan.op("dve", lambda e: e.tensor_mul(ob[:, :], ps3[0:64, 0:BS],
                                                        qdB[0:64, :]),
                          reads=[ps3_r, qdB_r], writes=[ob_r], deps=[tinter, tqe],
                          tag=f"aa_ob{h}_{blk}")

            def attn_ev(e, out_ap):
                return e.tensor_add(out_ap[0:64, :], ob[:, :], ps2[0:64, 0:BS])
            tst = stg.store(ps2[0:64, 0:BS], ps2_r,
                            A["l1aa_attn_d"][h * Dh:(h + 1) * Dh,
                                             blk * BS:(blk + 1) * BS],
                            tok=tintra, w=BS, evac=attn_ev, extra_reads=[ob_r],
                            rows=64)
            tS = []
            for c in range(2):
                pb, pb_r = K_.psum_b[c], K_.psum_br[c]
                tt = plan.op("pe", lambda e, c=c, qk=qk, pb=pb: e.transpose(
                    pb[:, 0:64], qk[:, 256 + c * 128:256 + (c + 1) * 128],
                    cst[0:64, 0:64]),
                    reads=[qkb_rs[sl], cst_r], writes=[pb_r], deps=[tq],
                    tag=f"aa_tr{h}_{blk}_{c}")
                te = plan.op("dve", lambda e, c=c, pb=pb: e.tensor_copy(
                    ktm[:, c * Dh:(c + 1) * Dh], pb[:, 0:64]),
                    reads=[pb_r], writes=[ktm_r], deps=[tt], tag=f"aa_ke{h}_{blk}_{c}")
                tk2 = plan.op("dve", lambda e, c=c, vsl=vsl, h=h: e.tensor_scalar(
                    vkd[:, c * Dh:(c + 1) * Dh], vsl[:, c * Dh:(c + 1) * Dh],
                    decay[:, 2 * h + c:2 * h + c + 1], None, ALU.mult),
                    reads=[vb_rs[sl], decay_r], writes=[vkd_r], deps=[tvl, tdec],
                    tag=f"aa_vk{h}_{blk}_{c}")
                tS.append(plan.op("pe", lambda e, c=c: e.matmul(
                    ps5[0:64, 0:Dh], ktm[:, c * Dh:(c + 1) * Dh],
                    vkd[:, c * Dh:(c + 1) * Dh], start=(c == 0), stop=(c == 1)),
                    reads=[ktm_r, vkd_r], writes=[ps5_r], deps=[te, tk2],
                    tag=f"aa_su{h}_{blk}_{c}"))
            tupd = plan.op("dve", lambda e, h=h: e.scalar_tensor_tensor(
                St[:, :], St[:, :], decay[0:64, 64 + h:65 + h], ps5[0:64, 0:Dh],
                ALU.mult, ALU.add),
                reads=[ps5_r, decay_r], writes=[St_r], deps=tS, tag=f"aa_up{h}_{blk}")
            tSb = plan.op("dve", lambda e: e.tensor_copy(Sb[:, :], St[:, :]),
                          reads=[St_r], writes=[Sb_r], deps=[tupd], tag=f"aa_sb{h}_{blk}")
            if h == 0 and "aa_dbgS" in A:
                K_.store_toks.append(e_dma(K_, "sp",
                    A["aa_dbgS"][blk * 64:(blk + 1) * 64, :], St[:, :],
                    reads=[St_r], tag=f"aa_dbg{blk}"))
    nwcols, nwcols_r = K_.sbuf("aa_nw", [128, 16], F32)
    tnw = e_dma(K_, "sp", nwcols[:, :], A["l1aa_nw"][:, :], writes=[nwcols_r], tag="aa_nw")
    an, an_r = K_.sbuf("aa_an", [128, 2 * TT], BF16)
    gsb, gsb_r = K_.sbuf("aa_gsb", [128, TT], BF16)
    gated, gated_r = K_.sbuf("aa_gated", [128, NKC * TT], BF16)
    gated_rs = [Res(f"aa_gated{k}") for k in range(NKC)]
    ach, ach_r = K_.sbuf("aa_ach", [128, 2 * TT], BF16)
    ach_rs = [Res("aa_ach0"), Res("aa_ach1")]
    rstB, rstB_r = K_.sbuf("aa_rstB", [128, TT], BF16)
    rrow, rrow_r = K_.sbuf("aa_rrow", [1, 2 * TT], F32)
    rrbf, rrbf_r = K_.sbuf("aa_rrbf", [1, TT], BF16)
    sqs2, sqs2_r = K_.sbuf("aa_sq", [128, 2 * TT], BF16)
    xres, xres_r = [], []
    for i in range(3):
        a, r = K_.sbuf(f"xr{i}", [128, TT], F32)
        xres.append(a); xres_r.append(r)
    xi = [0]
    for t in range(nf // TT):
        ps4, ps4_r = K_.psum_f[4], K_.psum_fr[4]
        for k in range(NKC):
            sl = k % 2
            tach = e_dma(K_, "sp", ach[:, sl * TT:(sl + 1) * TT],
                         A["l1aa_attn_d"][k * 128:(k + 1) * 128, t * TT:(t + 1) * TT],
                         writes=[ach_rs[sl]], tag=f"aa_la{t}_{k}")
            tsq = plan.op("act", lambda e, sl=sl: e.activation(
                sqs2[:, sl * TT:(sl + 1) * TT], ach[:, sl * TT:(sl + 1) * TT], AF.Square),
                reads=[ach_rs[sl]], writes=[sqs2_r], deps=[tach], tag=f"aa_sq{t}_{k}")
            plan.op("pe", lambda e, sl=sl, s=(k == 0), p=(k == NKC - 1): e.matmul(
                ps4[0:1, 0:TT], cst[:, 128:129], sqs2[:, sl * TT:(sl + 1) * TT],
                start=s, stop=p),
                reads=[sqs2_r, cst_r], writes=[ps4_r], deps=[tsq], tag=f"aa_ss{t}_{k}")
        t1 = plan.op("dve", lambda e: e.tensor_scalar(
            rrow[0:1, 0:TT], ps4[0:1, 0:TT], 1.0 / D, 1e-5, ALU.mult, ALU.add),
            reads=[ps4_r], writes=[rrow_r], tag=f"aa_ms{t}")
        t2 = plan.op("act", lambda e: e.activation(rrow[0:1, TT:2 * TT], rrow[0:1, 0:TT],
                                                   AF.Sqrt),
                     reads=[rrow_r], writes=[rrow_r], deps=[t1], tag=f"aa_sd{t}")
        t3 = plan.op("dve", lambda e: e.reciprocal(rrow[0:1, 0:TT], rrow[0:1, TT:2 * TT]),
                     reads=[rrow_r], writes=[rrow_r], deps=[t2], tag=f"aa_rc{t}")
        t4 = plan.op("dve", lambda e: e.tensor_copy(rrbf[0:1, :], rrow[0:1, 0:TT]),
                     reads=[rrow_r], writes=[rrbf_r], deps=[t3], tag=f"aa_rb{t}")
        ps5b, ps5b_r = K_.psum_f[5], K_.psum_fr[5]
        t5 = plan.op("pe", lambda e: e.matmul(ps5b[:, 0:TT], cst[0:1, 128:256],
                                              rrbf[0:1, :], start=True, stop=True),
                     reads=[rrbf_r, cst_r], writes=[ps5b_r], deps=[t4], tag=f"aa_rB{t}")
        t6 = plan.op("dve", lambda e: e.tensor_copy(rstB[:, :], ps5b[:, 0:TT]),
                     reads=[ps5b_r], writes=[rstB_r], deps=[t5], tag=f"aa_rE{t}")
        gw_wl = wload_xbarT(K_, A["l1aa_gw_bf"], cg)
        wbufg, wbufg_r = K_.sbuf("aa_wg", [128, 128], BF16)
        for m in range(NKC):
            tokg = None
            for ki in range(NKC):
                twg = gw_wl(ki, 0, [m], wbufg[:, 0:128], wbufg_r)
                tokg = plan.op("pe", lambda e, ki=ki, s=(ki == 0), p=(ki == NKC - 1),
                               t=t: e.matmul(K_.psum_f[3][:, 0:TT], wbufg[:, 0:128],
                                        y[:, ki * nf + t * TT: ki * nf + (t + 1) * TT],
                                        start=s, stop=p),
                               reads=[wbufg_r, y_r], writes=[K_.psum_fr[3]],
                               deps=[twg], tag=f"aa_gm{t}_{m}_{ki}")
            tsig = plan.op("act", lambda e: e.activation(gsb[:, :], K_.psum_f[3][:, 0:TT],
                                                         AF.Sigmoid),
                           reads=[K_.psum_fr[3]], writes=[gsb_r], deps=[tokg],
                           tag=f"aa_sg{t}_{m}")
            sl = m % 2
            tach = e_dma(K_, "sp", ach[:, sl * TT:(sl + 1) * TT],
                         A["l1aa_attn_d"][m * 128:(m + 1) * 128, t * TT:(t + 1) * TT],
                         writes=[ach_rs[sl]], tag=f"aa_la2{t}_{m}")
            tn1 = plan.op("pool", lambda e, sl=sl: e.tensor_mul(
                an[:, sl * TT:(sl + 1) * TT], ach[:, sl * TT:(sl + 1) * TT], rstB[:, :]),
                reads=[ach_rs[sl], rstB_r], writes=[an_r], deps=[tach, t6],
                tag=f"aa_n1{t}_{m}")
            tn2 = plan.op("dve", lambda e, sl=sl, m=m: e.tensor_scalar(
                an[:, sl * TT:(sl + 1) * TT], an[:, sl * TT:(sl + 1) * TT],
                nwcols[:, m:m + 1], None, ALU.mult),
                reads=[nwcols_r], writes=[an_r], deps=[tn1, tnw], tag=f"aa_n2{t}_{m}")
            tg = plan.op("pool", lambda e, sl=sl, m=m: e.tensor_mul(
                gated[:, m * TT:(m + 1) * TT], an[:, sl * TT:(sl + 1) * TT], gsb[:, :]),
                reads=[an_r, gsb_r], writes=[gated_rs[m]], deps=[tn2, tsig],
                tag=f"aa_g{t}_{m}")
        def g_at(kk, tt):
            return gated[:, kk * TT:(kk + 1) * TT], [gated_rs[kk]]

        def o_sink(m, tt, ps, ps_r, tok):
            i = xi[0] % 3
            xi[0] += 1
            tl = e_dma(K_, "sp", xres[i][:, :],
                       A["x"][m * 128:(m + 1) * 128, t * TT:(t + 1) * TT],
                       writes=[xres_r[i]], tag="aa_xr")
            ta = plan.op("dve", lambda e, i=i, ps=ps: e.tensor_add(
                xres[i][:, :], xres[i][:, :], ps[:, :]),
                reads=[ps_r], writes=[xres_r[i]], deps=[tl, tok], tag="aa_add")
            ts = e_dma(K_, "sp", A["xo"][m * 128:(m + 1) * 128, t * TT:(t + 1) * TT],
                       xres[i][:, :], reads=[xres_r[i]], tag="aa_st")
            K_.store_toks.append(ts)
            return ts

        proj_fm(K_, name="l1aa_ow", wload=wload_xbarT(K_, A["l1aa_ow_bf"], cow),
                n_k=NKC, m_list=list(range(16)), t_list=[t], rhs_at=g_at,
                sink=o_sink, mg=3)


def declare_l1aa(nc, A, cfg):
    nf = cfg["n_full"]
    A["l1aa_qkv"] = nc.dram_tensor("l1aa_qkv", [6144, 2048], F32, kind="ExternalInput").ap()
    A["l1aa_gw"] = nc.dram_tensor("l1aa_gw", [2048, 2048], F32, kind="ExternalInput").ap()
    A["l1aa_ow"] = nc.dram_tensor("l1aa_ow", [2048, 2048], F32, kind="ExternalInput").ap()
    A["l1aa_nw"] = nc.dram_tensor("l1aa_nw", [128, 16], F32, kind="ExternalInput").ap()
    A["l1aa_ddT"] = nc.dram_tensor("l1aa_ddT", [32 * 256, 256], BF16, kind="ExternalInput").ap()
    A["l1aa_qd"] = nc.dram_tensor("l1aa_qd", [32, 256], BF16, kind="ExternalInput").ap()
    A["l1aa_decay"] = nc.dram_tensor("l1aa_decay", [128, 128], F32, kind="ExternalInput").ap()
    A["l1aa_qk_bf"] = nc.dram_tensor("l1aa_qk_bf", [4096, 2048], BF16, kind="Internal").ap()
    A["l1aa_v_bf"] = nc.dram_tensor("l1aa_v_bf", [2048, 2048], BF16, kind="Internal").ap()
    A["l1aa_gw_bf"] = nc.dram_tensor("l1aa_gw_bf", [2048, 2048], BF16, kind="Internal").ap()
    A["l1aa_ow_bf"] = nc.dram_tensor("l1aa_ow_bf", [2048, 2048], BF16, kind="Internal").ap()
    A["l1aa_qkf"] = nc.dram_tensor("l1aa_qkf", [4096, nf], BF16, kind="Internal").ap()
    A["l1aa_vtm"] = nc.dram_tensor("l1aa_vtm", [nf, 2048], BF16, kind="Internal").ap()
    A["l1aa_attn_d"] = nc.dram_tensor("l1aa_attn_d", [2048, nf], BF16, kind="Internal").ap()


DECLARES["l1aa"] = declare_l1aa
EMITTERS["l1aa"] = emit_l1aa


def host_l1aa_tables():
    H, BS = 32, 256
    base = 1.0 / 2.0 ** (8.0 / H)
    rate = base ** (np.arange(H, dtype=np.float64) + 1.0)
    factor = 1.0 - 0.0 / (32 - 1 + 1e-05) + 1e-05
    slope = rate * factor
    r = np.arange(BS, dtype=np.float64) + 1.0
    ddT = np.zeros((H * 256, 256), np.float32)
    qd = np.zeros((H, 256), np.float32)
    decay = np.zeros((128, 128), np.float32)
    for h in range(H):
        dd = slope[h] * (r[:, None] - r[None, :])
        dd = np.where(dd >= 0, np.exp(-dd), 0.0)
        ddT[h * 256:(h + 1) * 256, :] = dd.T.astype(np.float32)
        qd[h] = np.exp(-slope[h] * r).astype(np.float32)
        kd = np.exp(-slope[h] * (BS - r)).astype(np.float32)
        decay[:, 2 * h] = kd[0:128]
        decay[:, 2 * h + 1] = kd[128:256]
        decay[:, 64 + h] = np.exp(-slope[h] * BS)
    return ddT.astype(BF), qd.astype(BF), decay



def emit_l1moe(K_, A, cfg):
    plan = K_.plan
    n_own = cfg["n_own"]
    nto4 = n_own // 256
    cst, cst_r = K_.cst
    E, I = 8, 1024
    c1 = cast_pass(K_, A["moe_w1_bf"], A["moe_w1"], 8 * I, 2048, tag="cW1")
    c3 = cast_pass(K_, A["moe_w3_bf"], A["moe_w3"], 8 * I, 2048, tag="cW3")
    c2 = cast_pass(K_, A["moe_w2_bf"], A["moe_w2"], 8 * 2048, I, tag="cW2")
    cs1 = cast_pass(K_, A["moe_ws1_bf"], A["moe_ws1"], I, 2048, tag="cS1")
    cs3 = cast_pass(K_, A["moe_ws3_bf"], A["moe_ws3"], I, 2048, tag="cS3")
    cs2 = cast_pass(K_, A["moe_ws2_bf"], A["moe_ws2"], 2048, I, tag="cS2")
    gws, gws_r = K_.sbuf("mg_gw", [8, 2048], BF16)
    tgw = e_dma(K_, "pool", gws[:, :], A["moe_gw"][:, :], writes=[gws_r], tag="mg_gw")
    gwT, gwT_r = K_.sbuf("mg_gwT", [128, 16 * 8], BF16)
    for k in range(NKC):
        pb, pb_r = K_.psum_b[0], K_.psum_br[0]
        tt = plan.op("pe", lambda e, k=k: e.transpose(
            pb[:, 0:8], gws[:, k * 128:(k + 1) * 128], cst[0:8, 0:8]),
            reads=[gws_r, cst_r], writes=[pb_r], deps=[tgw], tag=f"mg_t{k}")
        plan.op("dve", lambda e, k=k: e.tensor_copy(gwT[:, k * 8:(k + 1) * 8], pb[:, 0:8]),
                reads=[pb_r], writes=[gwT_r], deps=[tt], tag=f"mg_e{k}")
    gbrow, gbrow_r = K_.sbuf("mg_gbr", [1, 8], F32)
    gbrowb, gbrowb_r = K_.sbuf("mg_gbrb", [1, 8], BF16)
    tgb = e_dma(K_, "sp", gbrow[0:1, :], A["moe_gb"][:, :], writes=[gbrow_r], tag="mg_gb")
    tgbb = plan.op("dve", lambda e: e.tensor_copy(gbrowb[0:1, :], gbrow[0:1, :]),
                   reads=[gbrow_r], writes=[gbrowb_r], deps=[tgb], tag="mg_gbb")
    gbB, gbB_r = K_.sbuf("mg_gbB", [128, 8], F32)
    ps5, ps5_r = K_.psum_f[5], K_.psum_fr[5]
    tb1 = plan.op("pe", lambda e: e.matmul(ps5[:, 0:8], cst[0:1, 128:256], gbrowb[0:1, :],
                                           start=True, stop=True),
                  reads=[gbrowb_r, cst_r], writes=[ps5_r], deps=[tgbb], tag="mg_gbm")
    tb2 = plan.op("dve", lambda e: e.tensor_copy(gbB[:, :], ps5[:, 0:8]),
                  reads=[ps5_r], writes=[gbB_r], deps=[tb1], tag="mg_gbe")
    y, y_r, _ = ln_and_y(K_, A, cfg, "moe", A["x"], n_own)
    rt, rt_r = K_.sbuf("mg_rt", [128, 64], F32)
    dwT, dwT_r = K_.sbuf("mg_dwT", [8, 256], BF16)
    dwB, dwB_r = K_.sbuf("mg_dwB", [128, 8 * 256], F32)
    hall, hall_r = K_.sbuf("mg_h", [128, 9 * 8 * 256], BF16)
    hall_rs = [Res(f"mg_h{e}_{m}") for e in range(9) for m in range(8)]
    sa, sa_r = K_.sbuf("mg_sa", [128, 256], BF16)
    hf, hf_r = K_.sbuf("mg_hf", [128, 256], F32)
    xres, xres_r = [], []
    for i in range(3):
        a, r = K_.sbuf(f"xr{i}", [128, TT], F32)
        xres.append(a); xres_r.append(r)
    xi = [0]

    def y_at256(k, tq):
        return y[:, k * n_own + tq * 256: k * n_own + (tq + 1) * 256], [y_r]

    for tq in range(nto4):
        for tb in range(2):
            psl, psl_r = K_.psum_f[4], K_.psum_fr[4]
            tokl = None
            for k in range(NKC):
                tokl = plan.op("pe", lambda e, k=k, tq=tq, tb=tb, s=(k == 0), p=(k == NKC - 1):
                               e.matmul(psl[:, 0:8],
                                        y[:, k * n_own + tq * 256 + tb * 128:
                                           k * n_own + tq * 256 + (tb + 1) * 128],
                                        gwT[:, k * 8:(k + 1) * 8], start=s, stop=p),
                               reads=[y_r, gwT_r], writes=[psl_r], tag=f"mg_l{tq}_{tb}_{k}")
            sc_ = rt[:, 0:8]; mx_ = rt[:, 8:9]; nmx = rt[:, 9:10]
            sm_ = rt[:, 10:11]; rc_ = rt[:, 11:12]
            bi_ = rt[:, 16:24]; e1_ = rt[:, 24:32]; b2_ = rt[:, 32:40]
            m2_ = rt[:, 40:41]; e2_ = rt[:, 41:49]; msk_ = rt[:, 49:57]
            t1 = plan.op("dve", lambda e: e.reduce_max(mx_, psl[:, 0:8], axis=AX),
                         reads=[psl_r], writes=[rt_r], tag="mg_mx")
            t2 = plan.op("dve", lambda e: e.tensor_scalar_mul(nmx, mx_, -1.0),
                         reads=[rt_r], writes=[rt_r], deps=[t1], tag="mg_nm")
            t3 = plan.op("act", lambda e: e.activation(sc_, psl[:, 0:8], AF.Exp,
                                                       bias=nmx, scale=1.0),
                         reads=[psl_r, rt_r], writes=[rt_r], deps=[t2], tag="mg_ex")
            t4 = plan.op("dve", lambda e: e.reduce_sum(sm_, sc_, axis=AX),
                         reads=[rt_r], writes=[rt_r], deps=[t3], tag="mg_sm")
            t5 = plan.op("dve", lambda e: e.reciprocal(rc_, sm_),
                         reads=[rt_r], writes=[rt_r], deps=[t4], tag="mg_rc")
            t6 = plan.op("dve", lambda e: e.tensor_scalar(sc_, sc_, rc_, None, ALU.mult),
                         reads=[rt_r], writes=[rt_r], deps=[t5], tag="mg_sc")
            t7 = plan.op("dve", lambda e: e.tensor_add(bi_, sc_, gbB[:, :]),
                         reads=[rt_r, gbB_r], writes=[rt_r], deps=[t6, tb2], tag="mg_bi")
            t8 = plan.op("dve", lambda e: e.reduce_max(mx_, bi_, axis=AX),
                         reads=[rt_r], writes=[rt_r], deps=[t7], tag="mg_m1")
            t9 = plan.op("dve", lambda e: e.tensor_scalar(e1_, bi_, mx_, None, ALU.is_equal),
                         reads=[rt_r], writes=[rt_r], deps=[t8], tag="mg_e1")
            t10 = plan.op("dve", lambda e: e.scalar_tensor_tensor(
                b2_, e1_, -1e30, bi_, ALU.mult, ALU.add),
                reads=[rt_r], writes=[rt_r], deps=[t9], tag="mg_b2")
            t11 = plan.op("dve", lambda e: e.reduce_max(m2_, b2_, axis=AX),
                          reads=[rt_r], writes=[rt_r], deps=[t10], tag="mg_m2")
            t12 = plan.op("dve", lambda e: e.tensor_scalar(e2_, b2_, m2_, None, ALU.is_equal),
                          reads=[rt_r], writes=[rt_r], deps=[t11], tag="mg_e2")
            t13 = plan.op("dve", lambda e: e.tensor_add(msk_, e1_, e2_),
                          reads=[rt_r], writes=[rt_r], deps=[t12], tag="mg_mk")
            t14 = plan.op("dve", lambda e: e.tensor_mul(sc_, sc_, msk_),
                          reads=[rt_r], writes=[rt_r], deps=[t13], tag="mg_dw")
            dwb_ = rt[:, 57:58]
            dwbf, dwbf_r = K_.sbuf("mg_dwbf", [128, 8], BF16)
            t14b = plan.op("dve", lambda e: e.tensor_copy(dwbf[:, :], sc_),
                           reads=[rt_r], writes=[dwbf_r], deps=[t14], tag="mg_dwb")
            pst, pst_r = K_.psum_b[1], K_.psum_br[1]
            t15 = plan.op("pe", lambda e: e.transpose(pst[0:8, 0:128], dwbf[:, :],
                                                      cst[:, 0:128]),
                          reads=[dwbf_r, cst_r], writes=[pst_r], deps=[t14b], tag="mg_tr")
            t16 = plan.op("dve", lambda e, tb=tb: e.tensor_copy(dwT[:, tb * 128:(tb + 1) * 128],
                                                                pst[0:8, 0:128]),
                          reads=[pst_r], writes=[dwT_r], deps=[t15], tag="mg_te")
        tdo = e_dma(K_, "sp", A["moe_dwd"][:, :], dwT[:, :], reads=[dwT_r], tag="mg_dwo")
        dwrow, dwrow_r = K_.sbuf("mg_dwrow", [1, 8 * 256], BF16)
        tdi = e_dma(K_, "sp", dwrow[0:1, :],
                    A["moe_dwd"].rearrange("e s -> (e s)").rearrange("(o f) -> o f", o=1),
                    writes=[dwrow_r], deps=[tdo], tag="mg_dwi")
        for ee in range(E):
            psb, psb_r = K_.psum_f[4], K_.psum_fr[4]
            tbr = plan.op("pe", lambda e, ee=ee: e.matmul(
                psb[:, 0:256], cst[0:1, 128:256], dwrow[0:1, ee * 256:(ee + 1) * 256],
                start=True, stop=True),
                reads=[dwrow_r, cst_r], writes=[psb_r], deps=[tdi], tag=f"mg_db{tq}_{ee}")
            plan.op("dve", lambda e, ee=ee: e.tensor_copy(dwB[:, ee * 256:(ee + 1) * 256],
                                                          psb[:, 0:256]),
                    reads=[psb_r], writes=[dwB_r], deps=[tbr], tag=f"mg_de{tq}_{ee}")
        for ee in range(9):
            if ee < E:
                w1 = wload_xbarT(K_, A["moe_w1_bf"], c1, row_off=ee * I)
                w3 = wload_xbarT(K_, A["moe_w3_bf"], c3, row_off=ee * I)
            else:
                w1 = wload_xbarT(K_, A["moe_ws1_bf"], cs1)
                w3 = wload_xbarT(K_, A["moe_ws3_bf"], cs3)

            def ha_sink(m, t, ps, ps_r, tok, ee=ee):
                return plan.op("act", lambda e, ps=ps: e.activation(
                    sa[:, :], ps[:, 0:256], AF.Silu),
                    reads=[ps_r], writes=[sa_r], deps=[tok], tag=f"mg_si{tq}_{ee}_{m}")

            def hb_sink(m, t, ps, ps_r, tok, ee=ee):
                slot = ee * 8 + m
                t1_ = plan.op("dve", lambda e, ps=ps: e.tensor_mul(
                    hf[:, :], sa[:, :], ps[:, 0:256]),
                    reads=[ps_r, sa_r], writes=[hf_r], deps=[tok], tag=f"mg_hm{tq}_{ee}_{m}")
                if ee < E:
                    return plan.op("pool", lambda e, slot=slot, ee=ee: e.tensor_mul(
                        hall[:, slot * 256:(slot + 1) * 256], hf[:, :],
                        dwB[:, ee * 256:(ee + 1) * 256]),
                        reads=[hf_r, dwB_r], writes=[hall_rs[slot]], deps=[t1_],
                        tag=f"mg_hd{tq}_{ee}_{m}")
                return plan.op("pool", lambda e, slot=slot: e.tensor_copy(
                    hall[:, slot * 256:(slot + 1) * 256], hf[:, :]),
                    reads=[hf_r], writes=[hall_rs[slot]], deps=[t1_],
                    tag=f"mg_hc{tq}_{ee}_{m}")

            for m in range(8):
                pa, pa_r = K_.psum_f[4], K_.psum_fr[4]
                pbk, pbk_r = K_.psum_f[5], K_.psum_fr[5]
                wb1, wb1_r = K_.sbuf("mg_wb1", [128, 128], BF16)
                wb3, wb3_r = K_.sbuf("mg_wb3", [128, 128], BF16)
                tok1 = tok3 = None
                for k in range(NKC):
                    tw1 = w1(k, 0, [m], wb1[:, :], wb1_r)
                    tw3 = w3(k, 0, [m], wb3[:, :], wb3_r)
                    r_ap, r_reads = y_at256(k, tq)
                    tok1 = plan.op("pe", lambda e, k=k, s=(k == 0), p=(k == NKC - 1),
                                   r=r_ap: e.matmul(pa[:, 0:256], wb1[:, :], r,
                                                    start=s, stop=p),
                                   reads=[wb1_r] + r_reads, writes=[pa_r], deps=[tw1],
                                   tag=f"mgA{tq}_{ee}_{m}_{k}")
                    tok3 = plan.op("pe", lambda e, k=k, s=(k == 0), p=(k == NKC - 1),
                                   r=r_ap: e.matmul(pbk[:, 0:256], wb3[:, :], r,
                                                    start=s, stop=p),
                                   reads=[wb3_r] + r_reads, writes=[pbk_r], deps=[tw3],
                                   tag=f"mgB{tq}_{ee}_{m}_{k}")
                ts_ = ha_sink(m, tq, pa, pa_r, tok1)
                hb_sink(m, tq, pbk, pbk_r, tok3)
        for dg in range(4):
            banks = [0, 1, 2, 3]
            lastmm = {}
            for ee in range(9):
                w2src = A["moe_w2_bf"] if ee < E else A["moe_ws2_bf"]
                w2dep = c2 if ee < E else cs2
                roff = ee * 2048 if ee < E else 0
                for ki in range(8):
                    wb2, wb2_r = K_.sbuf("mg_wb2", [128, 512], BF16)
                    tw2 = e_xbar(K_, wb2[:, :],
                                 w2src[roff + dg * 512: roff + (dg + 1) * 512,
                                       ki * 128:(ki + 1) * 128],
                                 writes=[wb2_r], deps=w2dep, tag="mg_w2")
                    slot = ee * 8 + ki
                    for mi in range(4):
                        bank = banks[mi]
                        st = (ee == 0 and ki == 0)
                        sp = (ee == 8 and ki == 7)
                        lastmm[mi] = plan.op("pe", lambda e, mi=mi, slot=slot, st=st,
                                             sp=sp, bank=bank: e.matmul(
                            K_.psum_f[bank][:, 0:256], wb2[:, mi * 128:(mi + 1) * 128],
                            hall[:, slot * 256:(slot + 1) * 256], start=st, stop=sp),
                            reads=[wb2_r, hall_rs[slot]], writes=[K_.psum_fr[bank]],
                            deps=[tw2], tag=f"mgC{tq}_{dg}_{ee}_{ki}_{mi}")
            for mi in range(4):
                m = dg * 4 + mi
                i = xi[0] % 3
                xi[0] += 1
                tl = e_dma(K_, "sp", xres[i][:, 0:256],
                           A["x"][m * 128:(m + 1) * 128, tq * 256:(tq + 1) * 256],
                           writes=[xres_r[i]], tag="mg_xr")
                ta = plan.op("dve", lambda e, i=i, mi=mi: e.tensor_add(
                    xres[i][:, 0:256], xres[i][:, 0:256], K_.psum_f[banks[mi]][:, 0:256]),
                    reads=[K_.psum_fr[banks[mi]]], writes=[xres_r[i]],
                    deps=[tl, lastmm[mi]], tag="mg_add")
                ts = e_dma(K_, "sp", A["xo"][m * 128:(m + 1) * 128, tq * 256:(tq + 1) * 256],
                           xres[i][:, 0:256], reads=[xres_r[i]], tag="mg_st")
                K_.store_toks.append(ts)


def declare_l1moe(nc, A, cfg):
    E, I = 8, 1024
    A["moe_gw"] = nc.dram_tensor("moe_gw", [8, 2048], F32, kind="ExternalInput").ap()
    A["moe_gb"] = nc.dram_tensor("moe_gb", [1, 8], F32, kind="ExternalInput").ap()
    A["moe_w1"] = nc.dram_tensor("moe_w1", [E * I, 2048], F32, kind="ExternalInput").ap()
    A["moe_w3"] = nc.dram_tensor("moe_w3", [E * I, 2048], F32, kind="ExternalInput").ap()
    A["moe_w2"] = nc.dram_tensor("moe_w2", [E * 2048, I], F32, kind="ExternalInput").ap()
    A["moe_ws1"] = nc.dram_tensor("moe_ws1", [I, 2048], F32, kind="ExternalInput").ap()
    A["moe_ws3"] = nc.dram_tensor("moe_ws3", [I, 2048], F32, kind="ExternalInput").ap()
    A["moe_ws2"] = nc.dram_tensor("moe_ws2", [2048, I], F32, kind="ExternalInput").ap()
    A["moe_dwd"] = nc.dram_tensor("moe_dwd", [8, 256], BF16, kind="Internal").ap()
    for nm, rows, cols in (("w1", E * I, 2048), ("w3", E * I, 2048), ("w2", E * 2048, I),
                           ("ws1", I, 2048), ("ws3", I, 2048), ("ws2", 2048, I)):
        A[f"moe_{nm}_bf"] = nc.dram_tensor(f"moe_{nm}_bf", [rows, cols], BF16,
                                           kind="Internal").ap()


DECLARES["l1moe"] = declare_l1moe
EMITTERS["l1moe"] = emit_l1moe



def fm_softmax_clip(K_, A, name, src_dram, dst_sb, dst_r, h, n_tok, dep=(),
                    sfx=""):
    plan = K_.plan
    cst, cst_r = K_.cst
    raw, raw_r0 = K_.sbuf(f"db_raw{sfx}", [128, 2048], BF16)
    raw_r = raw_r0
    expb, expb_r = K_.sbuf(f"db_exp{sfx}", [128, TT], BF16)
    srow, srow_r = K_.sbuf(f"db_srow{sfx}", [1, TT + 144], F32)
    brow, brow_r = K_.sbuf(f"db_brow{sfx}", [1, TT], BF16)
    colm, colm_r = K_.sbuf(f"db_colm{sfx}", [128, 4], F32)
    nb, nb_r = K_.sbuf(f"db_nb{sfx}", [128, 1], F32)
    rB, rB_r = K_.sbuf(f"db_rB{sfx}", [128, TT], F32)
    tld = e_dma(K_, "sp", raw[:, 0:n_tok], src_dram[h * 128:(h + 1) * 128, :],
                writes=[raw_r], deps=list(dep), tag=f"db_ld{name}{h}")
    cold = A["db_cold" + sfx]
    t1 = plan.op("dve", lambda e: e.reduce_max(colm[:, 0:1], raw[:, 0:n_tok], axis=AX),
                 reads=[raw_r], writes=[colm_r], deps=[tld], tag=f"db_m1{name}")
    t2 = e_dma(K_, "sp", cold[:, :], colm[:, 0:1], reads=[colm_r], tag=f"db_mo{name}")
    t3 = e_dma(K_, "sp", srow[0:1, 0:128], cold.rearrange("p o -> o p"),
               deps=[t2], writes=[srow_r], tag=f"db_mi{name}")
    t4 = plan.op("dve", lambda e: e.reduce_max(srow[0:1, 128:129], srow[0:1, 0:128],
                                               axis=AX),
                 reads=[srow_r], writes=[srow_r], deps=[t3], tag=f"db_mm{name}")
    t5 = plan.op("dve", lambda e: e.tensor_scalar_mul(srow[0:1, 129:130],
                                                      srow[0:1, 128:129], -1.0),
                 reads=[srow_r], writes=[srow_r], deps=[t4], tag=f"db_ng{name}")
    nbrow, nbrow_r = K_.sbuf(f"db_nbrow{sfx}", [1, 2], BF16)
    t5b = plan.op("dve", lambda e: e.tensor_copy(nbrow[0:1, 0:1], srow[0:1, 129:130]),
                  reads=[srow_r], writes=[nbrow_r], deps=[t5], tag=f"db_nb1{name}")
    ps5, ps5_r = K_.psum_f[5], K_.psum_fr[5]
    t6 = plan.op("pe", lambda e: e.matmul(ps5[:, 0:1], cst[0:1, 128:256], nbrow[0:1, 0:1],
                                          start=True, stop=True),
                 reads=[nbrow_r, cst_r], writes=[ps5_r], deps=[t5b], tag=f"db_nB{name}")
    t7 = plan.op("dve", lambda e: e.tensor_copy(nb[:, :], ps5[:, 0:1]),
                 reads=[ps5_r], writes=[nb_r], deps=[t6], tag=f"db_ne{name}")
    toks = []
    for s in range(n_tok // TT):
        te = plan.op("act", lambda e, s=s: e.activation(
            expb[:, 0:TT], raw[:, s * TT:(s + 1) * TT], AF.Exp,
            bias=nb[:, 0:1], scale=1.0),
            reads=[raw_r, nb_r], writes=[expb_r], deps=[t7], tag=f"db_ex{name}{s}")
        tm = plan.op("pe", lambda e, s=s: e.matmul(ps5[0:1, 0:TT], cst[:, 128:129],
                                                   expb[:, 0:TT],
                                                   start=True, stop=True),
                     reads=[expb_r, cst_r], writes=[ps5_r], deps=[te], tag=f"db_sm{name}{s}")
        tr = plan.op("dve", lambda e, s=s: e.reciprocal(srow[0:1, 144:144 + TT],
                                                        ps5[0:1, 0:TT]),
                     reads=[ps5_r], writes=[srow_r], deps=[tm], tag=f"db_rc{name}{s}")
        trb = plan.op("dve", lambda e, s=s: e.tensor_copy(brow[0:1, 0:TT],
                                                          srow[0:1, 144:144 + TT]),
                      reads=[srow_r], writes=[brow_r], deps=[tr], tag=f"db_rb{name}{s}")
        tB = plan.op("pe", lambda e, s=s: e.matmul(ps5[:, 0:TT], cst[0:1, 128:256],
                                                   brow[0:1, 0:TT],
                                                   start=True, stop=True),
                     reads=[brow_r, cst_r], writes=[ps5_r], deps=[trb], tag=f"db_rB{name}{s}")
        tE = plan.op("dve", lambda e: e.tensor_copy(rB[:, :], ps5[:, 0:TT]),
                     reads=[ps5_r], writes=[rB_r], deps=[tB], tag=f"db_rE{name}{s}")
        tmu = plan.op("pool", lambda e, s=s: e.tensor_mul(
            dst_sb[:, s * TT:(s + 1) * TT], expb[:, 0:TT], rB[:, :]),
            reads=[expb_r, rB_r], writes=[dst_r], deps=[tE, te], tag=f"db_mu{name}{s}")
        tcl = plan.op("dve", lambda e, s=s: e.tensor_scalar(
            dst_sb[:, s * TT:(s + 1) * TT], dst_sb[:, s * TT:(s + 1) * TT],
            1e-6, 1.0 - 1e-6, ALU.max, ALU.min),
            reads=[], writes=[dst_r], deps=[tmu], tag=f"db_cl{name}{s}")
        toks.append(tcl)
    return raw, raw_r, tld, toks


def emit_l1ab(K_, A, cfg):
    plan = K_.plan
    nf = cfg["n_full"]
    assert cfg["n_own"] == nf
    cst, cst_r = K_.cst
    Hh, DH, C = 16, 128, 64
    NCh = nf // C
    y, y_r, _ = ln_and_y(K_, A, cfg, "l1ab", A["x"], nf)

    def y_at(k, t):
        return y[:, k * nf + t * TT:(k * nf) + (t + 1) * TT], [y_r]

    stg = Stager(K_, "s", 512, 3)
    ntf_ = nf // TT
    tq = tk = []
    for tl in [list(range(i, min(i + 2, ntf_))) for i in range(0, ntf_, 2)]:
        tq = tq + proj_to_dram(K_, name="ab_q", wload=wload_nat(K_, A["l1ab_wq"]),
                               n_k=NKC, m_list=list(range(16)), t_list=tl,
                               rhs_at=y_at, dst=A["l1ab_qraw"], stg=stg)
        tk = tk + proj_to_dram(K_, name="ab_k", wload=wload_nat(K_, A["l1ab_wk"]),
                               n_k=NKC, m_list=list(range(16)), t_list=tl,
                               rhs_at=y_at, dst=A["l1ab_kraw"], stg=stg)
    tv = vproj_tm(K_, name="l1ab", wT_dram_bf=A["l1ab_wv"], wT_deps=[],
                  y=y, y_r=y_r, n_full=nf, n_vc=2048, dst=A["l1ab_vtm"], stg=stg,
                  nat_f32=True)
    nqt = nf // 256

    def kc_list(qt):
        return list(range(0, 2 * qt + 2))

    def mask_idx_of(qt, kc):
        if kc in (2 * qt, 2 * qt + 1):
            return qt * 2 + (kc - 2 * qt)
        return None

    tbase = softmax_attn(K_, name="l1ab_b", n_h=16, kv_of=lambda h: h,
                         n_q=nf, n_kv=nf, scale=1.0 / np.sqrt(128.0), Tq=256,
                         qf=A["l1ab_qraw"], kf=A["l1ab_kraw"], vt=A["l1ab_vtm"],
                         n_kvh=16, kc_list=kc_list,
                         bias_tab=A["abg_btab"],
                         bias_col_of=lambda qt, kc: qt * (nf // 128) + kc,
                         n_bias_cols=nqt * (nf // 128),
                         mask_tab=A["abg_mtab"][:, :], mask_idx_of=mask_idx_of,
                         n_masks=nqt * 2,
                         dst_dram=A["l1ab_obase"], stg=stg, dep_toks=tq + tk + tv)
    qs, qs_r = K_.sbuf("db_qs", [128, 2048], BF16)
    ks, ks_r = K_.sbuf("db_ks", [128, 2048], BF16)
    beta, beta_r = K_.sbuf("db_beta", [128, 2048], BF16)
    kb, kb_r = K_.sbuf("db_kb", [128, 2048], BF16)
    lgt, lgt_r = K_.sbuf("db_lg", [128, 2 * TT], BF16)
    dcst, dcst_r = K_.sbuf("db_csts", [64, 128], F32)
    tdc = e_dma(K_, "sp", dcst[:, :], A["db_cst"][:, :], writes=[dcst_r], tag="db_cst")
    St, St_r = K_.sbuf("db_S", [128, 128], F32)
    Sbp, Sbp_r = K_.sbuf("db_Sbp", [128, 128], BF16)
    Sbs, Sbs_r = K_.sbuf("db_Sbs", [128, 128], BF16)
    Arm, Arm_r = K_.sbuf("db_Arm", [64, 64 * 12], BF16)
    Rb, Rb_r = K_.sbuf("db_R", [64, 128], BF16)
    Gt, Gt_r = K_.sbuf("db_G", [64, 128], BF16)
    atn, atn_r = K_.sbuf("db_atn", [64, 64], BF16)
    vbc, vbc_r = K_.sbuf("db_vbc", [64, 128], BF16)
    ktc, ktc_r = K_.sbuf("db_ktc", [64, 128], BF16)
    scale_q = float(128.0 ** -0.5)
    olin_toks = {}
    for h in range(Hh):
        _, _, _, tqs = fm_softmax_clip(K_, A, "q", A["l1ab_qraw"], qs, qs_r, h, nf)
        kraw, kraw_r, tldk, tks = fm_softmax_clip(K_, A, "k", A["l1ab_kraw"], ks, ks_r, h, nf)
        tbet = []
        for s in range(nf // TT):
            sl_ = slice(s * TT, (s + 1) * TT)
            a1 = plan.op("act", lambda e, sl_=sl_: e.activation(
                lgt[:, 0:TT], kraw[:, sl_], AF.Abs),
                reads=[kraw_r], writes=[lgt_r], deps=[tldk], tag=f"db_ab{h}{s}")
            a2 = plan.op("act", lambda e: e.activation(lgt[:, TT:2 * TT], lgt[:, 0:TT],
                                                       AF.Exp, scale=-1.0),
                         reads=[lgt_r], writes=[lgt_r], deps=[a1], tag=f"db_e{h}{s}")
            a3 = plan.op("dve", lambda e: e.tensor_scalar_add(lgt[:, TT:2 * TT],
                                                              lgt[:, TT:2 * TT], 1.0),
                         reads=[lgt_r], writes=[lgt_r], deps=[a2], tag=f"db_p1{h}{s}")
            a4 = plan.op("act", lambda e: e.activation(lgt[:, TT:2 * TT], lgt[:, TT:2 * TT],
                                                       AF.Ln),
                         reads=[lgt_r], writes=[lgt_r], deps=[a3], tag=f"db_ln{h}{s}")
            a5 = plan.op("dve", lambda e, sl_=sl_: e.tensor_scalar_min(
                lgt[:, 0:TT], kraw[:, sl_], 0.0),
                reads=[kraw_r], writes=[lgt_r], deps=[a4], tag=f"db_mn{h}{s}")
            a6 = plan.op("dve", lambda e: e.tensor_sub(lgt[:, 0:TT], lgt[:, 0:TT],
                                                       lgt[:, TT:2 * TT]),
                         reads=[lgt_r], writes=[lgt_r], deps=[a5], tag=f"db_sb{h}{s}")
            a7 = plan.op("dve", lambda e, sl_=sl_: e.tensor_scalar(
                beta[:, sl_], lgt[:, 0:TT], 1.0 / 16.0, -16.0, ALU.mult, ALU.max),
                reads=[lgt_r], writes=[beta_r], deps=[a6], tag=f"db_bt{h}{s}")
            tbet.append(a7)
        tkb = []
        for s in range(nf // TT):
            sl_ = slice(s * TT, (s + 1) * TT)
            tkb.append(plan.op("dve", lambda e, sl_=sl_: e.tensor_mul(
                kb[:, sl_], ks[:, sl_], beta[:, sl_]),
                reads=[ks_r, beta_r], writes=[kb_r], deps=tks + tbet, tag=f"db_kb{h}{s}"))
        tS0 = plan.op("dve", lambda e: e.memset(St[:, :], 0.0), writes=[St_r], tag=f"db_S0{h}")
        tSp = plan.op("dve", lambda e: e.tensor_copy(Sbp[:, :], St[:, :]),
                      reads=[St_r], writes=[Sbp_r], deps=[tS0], tag=f"db_Sp{h}")
        tSs = plan.op("dve", lambda e: e.tensor_scalar_mul(Sbs[:, :], St[:, :], scale_q),
                      reads=[St_r], writes=[Sbs_r], deps=[tS0], tag=f"db_Ss{h}")
        for ci in range(NCh):
            cs = slice(ci * C, (ci + 1) * C)
            pb0, pb0_r = K_.psum_b[0], K_.psum_br[0]
            tt1 = plan.op("pe", lambda e, cs=cs: e.transpose(
                pb0[0:64, 0:128], beta[:, cs], cst[:, 0:128]),
                reads=[beta_r, cst_r], writes=[pb0_r], deps=tbet, tag=f"db_tb{h}_{ci}")
            tvb = e_dma(K_, "sp", vbc.rearrange("p d -> p d"),
                        A["l1ab_vtm"][ci * C:(ci + 1) * C, h * 128:(h + 1) * 128],
                        writes=[vbc_r], deps=tv, tag=f"db_lv{h}_{ci}")
            tvb2 = plan.op("dve", lambda e: e.tensor_mul(vbc[:, :], vbc[:, :], pb0[0:64, :].rearrange("p d -> p d") if False else pb0[0:64, 0:128]),
                           reads=[pb0_r], writes=[vbc_r], deps=[tvb, tt1], tag=f"db_vb{h}_{ci}")
            pb1, pb1_r = K_.psum_b[1], K_.psum_br[1]
            tt2 = plan.op("pe", lambda e, cs=cs: e.transpose(
                pb1[0:64, 0:128], ks[:, cs], cst[:, 0:128]),
                reads=[ks_r, cst_r], writes=[pb1_r], deps=tks, tag=f"db_tk{h}_{ci}")
            tkt = plan.op("dve", lambda e: e.tensor_copy(ktc[:, :], pb1[0:64, 0:128]),
                          reads=[pb1_r], writes=[ktc_r], deps=[tt2], tag=f"db_ke{h}_{ci}")
            ps0, ps0_r = K_.psum_f[0], K_.psum_fr[0]
            tL = plan.op("pe", lambda e, cs=cs: e.matmul(
                ps0[0:64, 0:64], kb[:, cs], ks[:, cs], start=True, stop=True),
                reads=[kb_r, ks_r], writes=[ps0_r], deps=tkb, tag=f"db_L{h}_{ci}")
            tA = plan.op("dve", lambda e: e.tensor_mul(Arm[:, 0:64], ps0[0:64, 0:64],
                                                       dcst[:, 0:64]),
                         reads=[ps0_r, dcst_r], writes=[Arm_r], deps=[tL, tdc],
                         tag=f"db_A{h}_{ci}")
            tAc = plan.op("pe", lambda e: e.transpose(pb0[0:64, 0:64], Arm[:, 0:64],
                                                      cst[0:64, 0:64]),
                          reads=[Arm_r, cst_r], writes=[pb0_r], deps=[tA],
                          tag=f"db_Ac{h}_{ci}")
            tAce = plan.op("dve", lambda e: e.tensor_copy(Arm[:, 64:128], pb0[0:64, 0:64]),
                           reads=[pb0_r], writes=[Arm_r], deps=[tAc], tag=f"db_Ae{h}_{ci}")
            prev_rm, prev_cm = 0, 1
            tpow = [tA, tAce]
            for lv in range(1, 6):
                rm_s, cm_s = 2 * lv, 2 * lv + 1
                t1_ = plan.op("pe", lambda e, a=prev_cm, b=prev_rm: e.matmul(
                    ps0[0:64, 0:64], Arm[:, a * 64:(a + 1) * 64],
                    Arm[:, b * 64:(b + 1) * 64], start=True, stop=True),
                    reads=[Arm_r], writes=[ps0_r], deps=tpow, tag=f"db_p{h}_{ci}_{lv}a")
                t2_ = plan.op("dve", lambda e, s=rm_s: e.tensor_copy(
                    Arm[:, s * 64:(s + 1) * 64], ps0[0:64, 0:64]),
                    reads=[ps0_r], writes=[Arm_r], deps=[t1_], tag=f"db_p{h}_{ci}_{lv}b")
                t3_ = plan.op("pe", lambda e, a=prev_rm, b=prev_cm: e.matmul(
                    ps0[0:64, 0:64], Arm[:, a * 64:(a + 1) * 64],
                    Arm[:, b * 64:(b + 1) * 64], start=True, stop=True),
                    reads=[Arm_r], writes=[ps0_r], deps=[t2_], tag=f"db_p{h}_{ci}_{lv}c")
                t4_ = plan.op("dve", lambda e, s=cm_s: e.tensor_copy(
                    Arm[:, s * 64:(s + 1) * 64], ps0[0:64, 0:64]),
                    reads=[ps0_r], writes=[Arm_r], deps=[t3_], tag=f"db_p{h}_{ci}_{lv}d")
                tpow = [t2_, t4_]
                prev_rm, prev_cm = rm_s, cm_s
            ps1, ps1_r = K_.psum_f[1], K_.psum_fr[1]
            tG1 = plan.op("pe", lambda e, cs=cs: e.matmul(
                ps1[0:64, 0:128], kb[:, cs], Sbp[:, :], start=True, stop=True),
                reads=[kb_r, Sbp_r], writes=[ps1_r], deps=tkb + [tSp],
                tag=f"db_G{h}_{ci}")
            tG2 = plan.op("dve", lambda e: e.tensor_sub(Gt[:, :], vbc[:, :],
                                                        ps1[0:64, 0:128]),
                          reads=[ps1_r, vbc_r], writes=[Gt_r], deps=[tG1, tvb2],
                          tag=f"db_Ge{h}_{ci}")
            tR = tG2
            Rsrc, Rsrc_r = Gt, Gt_r
            for lv in range(6):
                a_s = 2 * lv if lv > 0 else 0
                aslot = 1 if lv == 0 else 2 * lv + 1
                t1_ = plan.op("pe", lambda e, aslot=aslot, Rsrc=Rsrc:
                              e.matmul(ps1[0:64, 0:128],
                                       Arm[:, aslot * 64:(aslot + 1) * 64],
                                       Rsrc[:, :], start=True, stop=False),
                              reads=[Arm_r, Rsrc_r], writes=[ps1_r], deps=[tR],
                              tag=f"db_ap{h}_{ci}_{lv}a")
                t2_ = plan.op("pe", lambda e, Rsrc=Rsrc: e.matmul(
                    ps1[0:64, 0:128], cst[0:64, 0:64],
                    Rsrc[:, :], start=False, stop=True),
                    reads=[cst_r, Rsrc_r], writes=[ps1_r], deps=[t1_],
                    tag=f"db_ap{h}_{ci}_{lv}b")
                tR = plan.op("dve", lambda e: e.tensor_copy(Rb[:, :], ps1[0:64, 0:128]),
                             reads=[ps1_r], writes=[Rb_r], deps=[t2_],
                             tag=f"db_ap{h}_{ci}_{lv}c")
                Rsrc, Rsrc_r = Rb, Rb_r
            ps2, ps2_r = K_.psum_f[2], K_.psum_fr[2]
            ta1 = plan.op("pe", lambda e, cs=cs: e.matmul(
                ps2[0:64, 0:64], ks[:, cs], qs[:, cs], start=True, stop=True),
                reads=[ks_r, qs_r], writes=[ps2_r], deps=tks + tqs, tag=f"db_at{h}_{ci}")
            ta2 = plan.op("dve", lambda e: e.tensor_mul(atn[:, :], ps2[0:64, 0:64],
                                                        dcst[:, 64:128]),
                          reads=[ps2_r, dcst_r], writes=[atn_r], deps=[ta1, tdc],
                          tag=f"db_am{h}_{ci}")
            ps3, ps3_r = K_.psum_f[3], K_.psum_fr[3]
            to1 = plan.op("pe", lambda e, cs=cs: e.matmul(
                ps3[:, 0:64], Sbs[:, :], qs[:, cs], start=True, stop=False),
                reads=[Sbs_r, qs_r], writes=[ps3_r], deps=[tSs] + tqs,
                tag=f"db_o1{h}_{ci}")
            to2 = plan.op("pe", lambda e: e.matmul(ps3[:, 0:64], Rb[:, :], atn[:, :],
                                                   start=False, stop=True),
                          reads=[Rb_r, atn_r], writes=[ps3_r], deps=[tR, ta2],
                          tag=f"db_o2{h}_{ci}")
            tst = stg.store(ps3[:, 0:64], ps3_r,
                            A["l1ab_olin"][h * 128:(h + 1) * 128, ci * C:(ci + 1) * C],
                            tok=to2, w=64)
            olin_toks.setdefault(h, []).append(tst)
            ps4, ps4_r = K_.psum_f[4], K_.psum_fr[4]
            tS1 = plan.op("pe", lambda e: e.matmul(ps4[:, 0:128], ktc[:, :], Rb[:, :],
                                                   start=True, stop=True),
                          reads=[ktc_r, Rb_r], writes=[ps4_r], deps=[tkt, tR],
                          tag=f"db_S1{h}_{ci}")
            tS2 = plan.op("dve", lambda e: e.tensor_add(St[:, :], St[:, :], ps4[:, 0:128]),
                          reads=[ps4_r], writes=[St_r], deps=[tS1], tag=f"db_S2{h}_{ci}")
            tSp = plan.op("dve", lambda e: e.tensor_copy(Sbp[:, :], St[:, :]),
                          reads=[St_r], writes=[Sbp_r], deps=[tS2], tag=f"db_S3{h}_{ci}")
            tSs = plan.op("dve", lambda e: e.tensor_scalar_mul(Sbs[:, :], St[:, :], scale_q),
                          reads=[St_r], writes=[Sbs_r], deps=[tS2], tag=f"db_S4{h}_{ci}")
    mixb, mixb_r = K_.sbuf("db_mix", [128, 8 * TT], BF16)
    mixb_rs = [Res(f"db_mix{k}") for k in range(8)]
    mo, mo_r = K_.sbuf("db_mo", [128, 2 * TT], BF16)
    mo_rs = [Res("db_mo0"), Res("db_mo1")]
    xres, xres_r = [], []
    for i in range(3):
        a, r = K_.sbuf(f"xr{i}", [128, TT], F32)
        xres.append(a); xres_r.append(r)
    xi = [0]
    for t in range(ntf_):
        def m_at(kk, tt, t=t):
            sl8 = kk % 8
            sl_ = kk % 2
            tl1 = e_dma(K_, "sp", mo[:, sl_ * TT:(sl_ + 1) * TT],
                        A["l1ab_obase"][kk * 128:(kk + 1) * 128, t * TT:(t + 1) * TT],
                        writes=[mo_rs[sl_]], deps=tbase, tag=f"db_lb{t}_{kk}")
            tl2 = e_dma(K_, "pool", mixb[:, sl8 * TT:(sl8 + 1) * TT],
                        A["l1ab_olin"][kk * 128:(kk + 1) * 128, t * TT:(t + 1) * TT],
                        writes=[mixb_rs[sl8]], deps=olin_toks[kk], tag=f"db_ll{t}_{kk}")
            plan.op("dve", lambda e, sl8=sl8, sl_=sl_: e.tensor_add(
                mixb[:, sl8 * TT:(sl8 + 1) * TT], mixb[:, sl8 * TT:(sl8 + 1) * TT],
                mo[:, sl_ * TT:(sl_ + 1) * TT]),
                reads=[mo_rs[sl_]], writes=[mixb_rs[sl8]], deps=[tl1, tl2],
                tag=f"db_mx{t}_{kk}")
            return mixb[:, sl8 * TT:(sl8 + 1) * TT], [mixb_rs[sl8]]

        def o_sink(m, tt, ps, ps_r, tok, t=t):
            i = xi[0] % 3
            xi[0] += 1
            tl = e_dma(K_, "sp", xres[i][:, :],
                       A["x"][m * 128:(m + 1) * 128, t * TT:(t + 1) * TT],
                       writes=[xres_r[i]], tag="db_xr")
            ta = plan.op("dve", lambda e, i=i, ps=ps: e.scalar_tensor_tensor(
                xres[i][:, :], ps[:, :], 0.5, xres[i][:, :], ALU.mult, ALU.add),
                reads=[ps_r], writes=[xres_r[i]], deps=[tl, tok], tag="db_add")
            ts = e_dma(K_, "sp", A["xo"][m * 128:(m + 1) * 128, t * TT:(t + 1) * TT],
                       xres[i][:, :], reads=[xres_r[i]], tag="db_st")
            K_.store_toks.append(ts)
            return ts

        proj_fm(K_, name="db_wo", wload=wload_nat(K_, A["l1ab_wo"]),
                n_k=NKC, m_list=list(range(16)), t_list=[t], rhs_at=m_at,
                sink=o_sink, mg=3)


def declare_l1ab(nc, A, cfg):
    nf = cfg["n_full"]
    for w in ("wq", "wk", "wv", "wo"):
        A[f"l1ab_{w}"] = nc.dram_tensor(f"l1ab_{w}", [2048, 2048], F32,
                                        kind="ExternalInput").ap()
    nqt = nf // 256
    A["abg_btab"] = nc.dram_tensor("abg_btab", [16 * 128, nqt * (nf // 128)], F32,
                                   kind="ExternalInput").ap()
    A["abg_mtab"] = nc.dram_tensor("abg_mtab", [128, nqt * 2 * 256], BF16,
                                   kind="ExternalInput").ap()
    A["db_cst"] = nc.dram_tensor("db_cst", [64, 128], F32, kind="ExternalInput").ap()
    A["db_cold"] = nc.dram_tensor("db_cold", [128, 1], F32, kind="Internal").ap()
    A["l1ab_qraw"] = nc.dram_tensor("l1ab_qraw", [2048, nf], BF16, kind="Internal").ap()
    A["l1ab_kraw"] = nc.dram_tensor("l1ab_kraw", [2048, nf], BF16, kind="Internal").ap()
    A["l1ab_vtm"] = nc.dram_tensor("l1ab_vtm", [nf, 2048], BF16, kind="Internal").ap()
    A["l1ab_obase"] = nc.dram_tensor("l1ab_obase", [2048, nf], BF16, kind="Internal").ap()
    A["l1ab_olin"] = nc.dram_tensor("l1ab_olin", [2048, nf], BF16, kind="Internal").ap()


DECLARES["l1ab"] = declare_l1ab
EMITTERS["l1ab"] = emit_l1ab


def host_l1ab_tables(n_own=2048):
    pos = np.arange(n_own).astype(np.float64)
    nqt = n_own // 256
    nkc = n_own // 128
    btab = np.zeros((16 * 128, nqt * nkc), np.float32)
    mtab = np.zeros((128, nqt * 2 * 256), np.float32)
    for qt in range(nqt):
        qpos = pos[qt * 256:(qt + 1) * 256]
        for kc in range(nkc):
            kpos = pos[kc * 128:(kc + 1) * 128]
            if kpos.min() > qpos.max():
                btab[:, qt * nkc + kc] = -1e30
        for ci, kc in enumerate((2 * qt, 2 * qt + 1)):
            kpos = pos[kc * 128:(kc + 1) * 128]
            m = np.where(kpos[:, None] <= qpos[None, :], 0.0, -1e30).astype(np.float32)
            mtab[:, (qt * 2 + ci) * 256:(qt * 2 + ci + 1) * 256] = m
    dcst = np.zeros((64, 128), np.float32)
    ii = np.arange(64)
    dcst[:, 0:64] = np.where(ii[:, None] > ii[None, :], -1.0, 0.0)
    dcst[:, 64:128] = np.where(ii[:, None] <= ii[None, :], 128.0 ** -0.5, 0.0)
    return btab, mtab.astype(BF), dcst


def e_dma_sw(K_, q, pairs_of, reads=(), writes=(), deps=(), tag=""):
    n0 = len(pairs_of(0))
    assert len(pairs_of(1)) == n0

    def fn(eng, attach):
        for arm in K_.nc.switch_core_id(engines=[eng], n=128):
            p = int(arm) % 2
            for (o_ap, i_ap) in pairs_of(p):
                attach(eng.dma_start(out=o_ap, in_=i_ap))
    return K_.plan.op_sw(q, fn, reads=reads, writes=writes, deps=deps,
                         n_dma=n0, tag=tag)



NOWN = 1024


def host_striped_tables(p, n_heads=16, alibi=True, Tq=256, scale=None):
    if scale is None:
        scale = 1.0 / np.sqrt(128.0)
    own = np.arange(p, 16, 2)
    pos = np.arange(2048).astype(np.float64)
    spt = Tq // 128
    nqt = NOWN // Tq
    nkc = 16
    slopes = np.array([(2.0 ** (-8.0 / n_heads)) ** (i + 1) for i in range(n_heads)])
    if not alibi:
        slopes = slopes * 0.0
    btab = np.zeros((n_heads * 128, nqt * nkc), np.float32)
    mtab = np.zeros((128, nqt * 4 * Tq), np.float32)
    qrow = np.zeros((n_heads, NOWN), np.float32)
    for qt in range(nqt):
        gs = own[qt * spt:(qt + 1) * spt]
        qpos = np.concatenate([pos[g * 128:(g + 1) * 128] for g in gs])
        for kc in range(nkc):
            kpos = pos[kc * 128:(kc + 1) * 128]
            for h in range(n_heads):
                if kpos.min() > qpos.max():
                    btab[h * 128:(h + 1) * 128, qt * nkc + kc] = -1e30
                else:
                    btab[h * 128:(h + 1) * 128, qt * nkc + kc] = (
                        slopes[h] * kpos).astype(np.float32)
        cands = striped_cands(qt, Tq)
        for ci, kc in enumerate(cands):
            kpos = pos[kc * 128:(kc + 1) * 128]
            m = np.where(kpos[:, None] <= qpos[None, :], 0.0, -1e30)
            mtab[:, (qt * 4 + ci) * Tq:(qt * 4 + ci + 1) * Tq] = m.astype(np.float32)
    for h in range(n_heads):
        lpos = np.concatenate([pos[g * 128:(g + 1) * 128] for g in own])
        qrow[h] = (-slopes[h] * lpos / scale).astype(np.float32)
    return btab, mtab.astype(BF), qrow.astype(BF)


def striped_cands(qt, Tq=256):
    spt = Tq // 128
    base = 2 * spt * qt
    return [base + j for j in range(4) if base + j < 16]


def striped_kc_list(qt, Tq=256):
    spt = Tq // 128
    return list(range(0, min(16, 2 * spt * qt + spt + 2)))


def xres_at(K_, k, t, TT_=TT):
    x, xrs = K_.x_sb, K_.x_rs
    return x[:, k * NOWN + t * TT_:(k * NOWN) + t * TT_ + TT_], [xrs[k]]


def ln_resident(K_, uniq, ysh=None):
    y, y_r = K_.sbuf(f"y_{uniq}", [128, NKC * NOWN], BF16)

    def x_at(k, t):
        return xres_at(K_, k, t)

    def y_sink(k, t, emit):
        return emit(y[:, k * NOWN + t * TT: k * NOWN + (t + 1) * TT], [y_r])

    ln_fm(K_, name=uniq, n_tok=NOWN, x_at=x_at, y_sink=y_sink)
    tst = None
    if ysh is not None:
        def pairs(p):
            return [(ysh[k * 128:(k + 1) * 128, p * NOWN:(p + 1) * NOWN],
                     y[:, k * NOWN:(k + 1) * NOWN]) for k in range(NKC)]
        tst = e_dma_sw(K_, "sp", pairs, reads=[y_r], tag=f"ysh_{uniq}")
    return y, y_r, tst


def proj_to_shared(K_, *, name, wload, m_list, rhs_at, sh, stg, n_k=NKC, mg=3,
                   evac_of=None, extra_reads=(), evac_eng="dve", row_of=None):
    toks = []

    def sink(m, t, ps, ps_r, tok):
        a_, r_ = stg.slots[stg.i % len(stg.slots)]
        stg.i += 1
        ev = evac_of(m, t, ps) if evac_of else None
        if ev is None:
            t1 = K_.plan.op(evac_eng, lambda e, ps=ps: e.tensor_copy(a_[:, 0:TT], ps[:, 0:TT]),
                            reads=[ps_r] + list(extra_reads), writes=[r_], deps=[tok],
                            tag="pshev")
        else:
            t1 = K_.plan.op(evac_eng, lambda e, ev=ev: ev(e, a_[:, 0:TT]),
                            reads=[ps_r] + list(extra_reads), writes=[r_], deps=[tok],
                            tag="pshevc")
        row = row_of(m) if row_of else m * 128

        def pairs(p, row=row, t=t, a_=a_):
            return [(sh[row:row + 128, p * NOWN + t * TT: p * NOWN + (t + 1) * TT],
                     a_[:, 0:TT])]
        toks.append(e_dma_sw(K_, "sp", pairs, reads=[r_], tag="pshst"))
        return toks[-1]

    proj_fm(K_, name=name, wload=wload, n_k=n_k, m_list=m_list,
            t_list=[0, 1], rhs_at=rhs_at, sink=sink, mg=mg)
    return toks


def dram_rhs_loader(K_, name, src, n_cols_per_chunk, nslots=4, sw_own=False,
                    deps=()):
    buf, _ = K_.sbuf(f"rl_{name}", [128, nslots * TT], BF16)
    rs = [Res(f"rl_{name}{i}") for i in range(nslots)]
    ctr = [0]

    def rhs_at(k, t):
        i = ctr[0] % nslots
        ctr[0] += 1
        sl = buf[:, i * TT:(i + 1) * TT]
        if sw_own:
            def pairs(p, k=k, t=t, sl=sl):
                return [(sl, src[k * 128:(k + 1) * 128,
                                 p * NOWN + t * TT: p * NOWN + (t + 1) * TT])]
            e_dma_sw(K_, "sp", pairs, writes=[rs[i]], deps=list(deps), tag=f"rl{name}")
        else:
            e_dma(K_, "sp", sl, src[k * 128:(k + 1) * 128, t * TT:(t + 1) * TT],
                  writes=[rs[i]], deps=list(deps), tag=f"rl{name}")
        return sl, [rs[i]]
    return rhs_at


def resid_sink(K_, scale=None, bias_col=None, bias_res=None):
    def sink(m, t, ps, ps_r, tok):
        x, xrs = K_.x_sb, K_.x_rs
        sl = x[:, m * NOWN + t * TT: m * NOWN + (t + 1) * TT]
        extra = []
        if bias_res is not None:
            extra.append(bias_res)
        if scale is not None:
            t1 = K_.plan.op("dve", lambda e, ps=ps, sl=sl: e.scalar_tensor_tensor(
                sl, ps[:, 0:TT], scale, sl, ALU.mult, ALU.add),
                reads=[ps_r] + extra, writes=[xrs[m]], deps=[tok], tag="rs_add")
        elif bias_col is not None:
            t1 = K_.plan.op("dve", lambda e, ps=ps, sl=sl, m=m: e.scalar_tensor_tensor(
                sl, ps[:, 0:TT], bias_col(m), sl, ALU.add, ALU.add),
                reads=[ps_r] + extra, writes=[xrs[m]], deps=[tok], tag="rs_addb")
        else:
            t1 = K_.plan.op("dve", lambda e, ps=ps, sl=sl: e.tensor_add(
                sl, sl, ps[:, 0:TT]),
                reads=[ps_r] + extra, writes=[xrs[m]], deps=[tok], tag="rs_add0")
        return t1
    return sink


def emit_l0aa_F(K_, A):
    plan = K_.plan
    cq = cast_pass(K_, A["l0aa_wq_bf"], A["l0aa_wq"], 2048, 2048, tag="cwq")
    ck = cast_pass(K_, A["l0aa_wk_bf"], A["l0aa_wk"], 1024, 2048, tag="cwk")
    cv = cast_pass(K_, A["l0aa_wv_bf"], A["l0aa_wv"], 1024, 2048, tag="cwv")
    co = cast_pass(K_, A["l0aa_wo_bf"], A["l0aa_wo"], 2048, 2048, tag="cwo")
    with K_.phase():
        stg = Stager(K_, "s", 512, 3)
        y, y_r, tys = ln_resident(K_, "l0aa", ysh=A["sh1_y"])

        def y_at(k, t):
            return y[:, k * NOWN + t * TT: k * NOWN + (t + 1) * TT], [y_r]

        tq = proj_to_shared(K_, name="f0q", wload=wload_xbarT(K_, A["l0aa_wq_bf"], cq),
                            m_list=list(range(16)), rhs_at=y_at, sh=A["sh1_q"], stg=stg)
        tk = proj_to_shared(K_, name="f0k", wload=wload_xbarT(K_, A["l0aa_wk_bf"], ck),
                            m_list=list(range(8)), rhs_at=y_at, sh=A["sh1_k"], stg=stg)
        tv = []
        wt, wt_r = K_.sbuf("fwvT", [128, NKC * TT], BF16)
        wt_rs = [Res(f"fwvT{k}") for k in range(NKC)]
        for vt_i in range(2):
            wl = []
            for k in range(NKC):
                wl.append(e_xbar(K_, wt[:, k * TT:(k + 1) * TT],
                                 A["l0aa_wv_bf"][vt_i * TT:(vt_i + 1) * TT,
                                                 k * 128:(k + 1) * 128],
                                 writes=[wt_rs[k]], deps=cv, tag="fwvT"))
            for tb in range(8):
                ps, ps_r = K_.psum_f[tb % 2], K_.psum_fr[tb % 2]
                tok = None
                for k in range(NKC):
                    tok = plan.op("pe", lambda e, k=k, s=(k == 0), p=(k == NKC - 1),
                                  tb=tb, ps=ps: e.matmul(
                        ps[:, 0:TT], y[:, k * NOWN + tb * 128:k * NOWN + (tb + 1) * 128],
                        wt[:, k * TT:(k + 1) * TT], start=s, stop=p),
                        reads=[y_r, wt_rs[k]], writes=[ps_r], deps=[wl[k]],
                        tag=f"fvm{vt_i}_{tb}_{k}")
                a_, r_ = stg.slots[stg.i % len(stg.slots)]
                stg.i += 1
                t1 = plan.op("dve", lambda e, ps=ps, a_=a_: e.tensor_copy(
                    a_[:, 0:TT], ps[:, 0:TT]),
                    reads=[ps_r], writes=[r_], deps=[tok], tag="fvev")

                def pairs(p, tb=tb, vt_i=vt_i, a_=a_):
                    return [(A["sh1_vt"][p * NOWN + tb * 128: p * NOWN + (tb + 1) * 128,
                                         vt_i * TT:(vt_i + 1) * TT], a_[:, 0:TT])]
                tv.append(e_dma_sw(K_, "sp", pairs, reads=[r_], tag="fvst"))
        plan.barrier(deps=tq + tk + tv + [tys])
    with K_.phase():
        stg = Stager(K_, "s", 512, 3)
        softmax_attn(K_, name="f0a", n_h=16, kv_of=lambda h: h // 2,
                     n_q=NOWN, n_kv=2048, scale=1.0 / np.sqrt(128.0), Tq=512,
                     qf=A["sh1_q"], kf=A["sh1_k"], vt=A["sh1_vt"], n_kvh=8,
                     kc_list=lambda qt: list(range(16)),
                     dst_dram=A["sh1_a"], stg=stg,
                     q_stripes=lambda p: list(range(p, 16, 2)),
                     out_stripes=True)
        plan.barrier()
    with K_.phase():
        rhs = dram_rhs_loader(K_, "f0wo", A["sh1_a"], NOWN, sw_own=True)
        proj_fm(K_, name="f0wo", wload=wload_xbarT(K_, A["l0aa_wo_bf"], co),
                n_k=NKC, m_list=list(range(16)), t_list=[0, 1], rhs_at=rhs,
                sink=resid_sink(K_), mg=3)


def emit_l0ffn_F(K_, A, first):
    plan = K_.plan
    with K_.phase():
        sgn_toks = []
        tb1 = None
        if first:
            wrow, wrow_r = [], []
            for i in range(2):
                a, r = K_.sbuf(f"f_wr{i}", [128, 2048], F32)
                wrow.append(a); wrow_r.append(r)
            wsg, wsg_r = [], []
            for i in range(2):
                a, r = K_.sbuf(f"f_ws{i}", [128, 2048], BF16)
                wsg.append(a); wsg_r.append(r)
            amx, amx_r = K_.sbuf("f_amx", [128, 2], F32)
            for rci in range(16):
                b = rci % 2
                tl = e_dma(K_, "sp", wrow[b][:, :], A["l0f_w"][rci * 128:(rci + 1) * 128, :],
                           writes=[wrow_r[b]], tag=f"fw{rci}")
                ts = plan.op("act", lambda e, b=b: e.activation(wsg[b][:, :], wrow[b][:, :],
                                                                AF.Sign),
                             reads=[wrow_r[b]], writes=[wsg_r[b]], deps=[tl], tag=f"fsg{rci}")
                sgn_toks.append(e_dma(K_, "sp", A["l0f_ws_bf"][rci * 128:(rci + 1) * 128, :],
                                      wsg[b][:, :], reads=[wsg_r[b]], deps=[ts], tag=f"fso{rci}"))
                if rci == 0:
                    plan.op("dve", lambda e, b=b: e.reduce_max(
                        amx[:, 0:1], wrow[b][:, :], axis=AX, apply_absolute_value=True),
                        reads=[wrow_r[b]], writes=[amx_r], deps=[tl], tag="fmx0")
                else:
                    plan.op("dve", lambda e, b=b: e.reduce_max(
                        amx[:, 1:2], wrow[b][:, :], axis=AX, apply_absolute_value=True),
                        reads=[wrow_r[b]], writes=[amx_r], deps=[tl], tag=f"fmx{rci}")
                    plan.op("dve", lambda e: e.tensor_max(amx[:, 0:1], amx[:, 0:1], amx[:, 1:2]),
                            reads=[amx_r], writes=[amx_r], tag=f"fmm{rci}")
            tb1 = e_dma(K_, "sp", A["l0f_col"][:, :], amx[:, 0:1], reads=[amx_r], tag="fmxo")
        srow, srow_r = K_.sbuf("f_srow", [1, 130], F32)
        tb2 = e_dma(K_, "sp", srow[0:1, 0:128], A["l0f_col"].rearrange("p o -> o p"),
                    deps=([tb1] if tb1 else []), writes=[srow_r], tag="fmxi")
        tmx = plan.op("dve", lambda e: e.reduce_max(srow[0:1, 128:129], srow[0:1, 0:128],
                                                    axis=AX),
                      reads=[srow_r], writes=[srow_r], deps=[tb2], tag="fmxr")
        pcol, pcol_r = K_.sbuf("f_pcol", [128, 64], F32)
        pcolb, pcolb_r = K_.sbuf("f_pcolb", [128, 64], BF16)
        tpc = e_dma(K_, "sp", pcol[:, :], A["l0f_pcol"][:, :], writes=[pcol_r], tag="fpc")
        tpcb = plan.op("dve", lambda e: e.tensor_copy(pcolb[:, :], pcol[:, :]),
                       reads=[pcol_r], writes=[pcolb_r], deps=[tpc], tag="fpcb")
        srow_b, srow_b_r = K_.sbuf("f_srb", [1, 2], BF16)
        tsb = plan.op("dve", lambda e: e.tensor_copy(srow_b[0:1, 0:1], srow[0:1, 128:129]),
                      reads=[srow_r], writes=[srow_b_r], deps=[tmx], tag="fsb")
        cst, cst_r = K_.cst
        ps5, ps5_r = K_.psum_f[5], K_.psum_fr[5]
        tbc = plan.op("pe", lambda e: e.matmul(ps5[:, 0:1], cst[0:1, 128:256],
                                               srow_b[0:1, 0:1], start=True, stop=True),
                      reads=[srow_b_r, cst_r], writes=[ps5_r], deps=[tsb], tag="fsbb")
        scol, scol_r = K_.sbuf("f_scol", [128, 18], F32)
        tsc = plan.op("dve", lambda e: e.tensor_copy(scol[:, 16:17], ps5[:, 0:1]),
                      reads=[ps5_r], writes=[scol_r], deps=[tbc], tag="fsc")
        tsg = plan.op("dve", lambda e: e.tensor_scalar(
            scol[:, 0:16], pcol[:, 0:16], scol[:, 16:17], None, ALU.mult),
            reads=[pcol_r, scol_r], writes=[scol_r], deps=[tsc, tpc], tag="fsg_col")
        y, y_r, _ = ln_resident(K_, "l0f")
        y2, y2_r = K_.sbuf("y2_l0f", [128, NKC * NOWN], BF16)

        def x2_at(k, t):
            return y[:, k * NOWN + t * TT: k * NOWN + (t + 1) * TT], [y_r]

        def y2_sink(k, t, emit):
            return emit(y2[:, k * NOWN + t * TT: k * NOWN + (t + 1) * TT], [y2_r])

        ln_fm(K_, name="l0f2", n_tok=NOWN, x_at=x2_at, y_sink=y2_sink, in_bf16=True,
              gain_col=(lambda k: pcol[:, 32 + k:33 + k]), gain_res=pcol_r,
              gain_dep=tpc)
        ccol, ccol_r = K_.sbuf("f_ccol", [128, 16], F32)
        tmpo, tmpo_r = K_.sbuf("f_tmpo", [128, TT], F32)

        def rhs_at(k, t):
            if t == 99:
                return pcolb[:, 48 + k:49 + k], [pcolb_r]
            return y2[:, k * NOWN + t * TT: k * NOWN + (t + 1) * TT], [y2_r]

        def sink2(m, t, ps, ps_r, tok):
            if t == 99:
                return plan.op("dve", lambda e, m=m, ps=ps: e.tensor_copy(
                    ccol[:, m:m + 1], ps[:, 0:1]),
                    reads=[ps_r], writes=[ccol_r], deps=[tok], tag=f"fc{m}")
            x, xrs = K_.x_sb, K_.x_rs
            sl = x[:, m * NOWN + t * TT: m * NOWN + (t + 1) * TT]
            t1 = plan.op("dve", lambda e, m=m, ps=ps: e.tensor_scalar(
                tmpo[:, :], ps[:, 0:TT], ccol[:, m:m + 1], scol[:, m:m + 1],
                ALU.add, ALU.mult),
                reads=[ps_r, ccol_r, scol_r], writes=[tmpo_r], deps=[tok, tsg],
                tag="f_e1")
            t2 = plan.op("dve", lambda e, m=m, sl=sl: e.scalar_tensor_tensor(
                sl, tmpo[:, :], pcol[:, 16 + m:17 + m], sl, ALU.add, ALU.add),
                reads=[tmpo_r, pcol_r], writes=[K_.x_rs[m]], deps=[t1], tag="f_e2")
            return t2

        proj_fm(K_, name="f_mm", wload=wload_xbarT(K_, A["l0f_ws_bf"], sgn_toks),
                n_k=NKC, m_list=list(range(16)), t_list=[99, 0, 1],
                rhs_at=rhs_at, sink=sink2, mg=2)


def emit_l0ab_F(K_, A):
    plan = K_.plan
    cst, cst_r = K_.cst
    with K_.phase():
        stg = Stager(K_, "s", 512, 3)
        y, y_r, tys = ln_resident(K_, "l0ab", ysh=A["sh2_y"])

        def y_at(k, t):
            return y[:, k * NOWN + t * TT: k * NOWN + (t + 1) * TT], [y_r]

        bcols, bcols_r = K_.sbuf("ab_bcols", [128, 48], F32)
        tbc = e_dma(K_, "sp", bcols[:, :], A["ab_bcol"][:, :], writes=[bcols_r], tag="ab_bc")
        bvrow, bvrow_r = K_.sbuf("ab_bvr_s", [1, TT], F32)
        bvrow_b, bvrow_b_r = K_.sbuf("ab_bvrb_s", [1, TT], BF16)
        bvB, bvB_r = K_.sbuf("ab_bvB", [128, 2048], BF16)
        ps5, ps5_r = K_.psum_f[5], K_.psum_fr[5]
        for vt_i in range(4):
            tbv = e_dma(K_, "sp", bvrow[0:1, :], A["ab_bvrow"][0:1, vt_i * TT:(vt_i + 1) * TT],
                        writes=[bvrow_r], tag="ab_bv")
            tbvb = plan.op("dve", lambda e: e.tensor_copy(bvrow_b[0:1, :], bvrow[0:1, :]),
                           reads=[bvrow_r], writes=[bvrow_b_r], deps=[tbv], tag="ab_bvb")
            tm = plan.op("pe", lambda e: e.matmul(ps5[:, :], cst[0:1, 128:256],
                                                  bvrow_b[0:1, :], start=True, stop=True),
                         reads=[bvrow_b_r, cst_r], writes=[ps5_r], tag=f"ab_bvm{vt_i}")
            plan.op("dve", lambda e, v=vt_i: e.tensor_copy(bvB[:, v * TT:(v + 1) * TT],
                                                           ps5[:, :]),
                    reads=[ps5_r], writes=[bvB_r], deps=[tm], tag=f"ab_bve{vt_i}")

        def bias_evac(col):
            def eo(m, t, ps):
                def ev(e, out_ap, m=m, ps=ps):
                    return e.tensor_scalar(out_ap, ps[:, 0:TT],
                                           bcols[:, col + m:col + m + 1], None, ALU.add)
                return ev
            return eo

        tq = proj_to_shared(K_, name="f2q", wload=wload_nat(K_, A["l0ab_wq"]),
                            m_list=list(range(16)), rhs_at=y_at, sh=A["sh2_q"], stg=stg,
                            evac_of=bias_evac(0), extra_reads=[bcols_r])
        tk = proj_to_shared(K_, name="f2k", wload=wload_nat(K_, A["l0ab_wk"]),
                            m_list=list(range(16)), rhs_at=y_at, sh=A["sh2_k"], stg=stg,
                            evac_of=bias_evac(16), extra_reads=[bcols_r])
        tv = []
        wt, wt_r = K_.sbuf("fwvT", [128, NKC * TT], BF16)
        wt_rs = [Res(f"fwvT{k}") for k in range(NKC)]
        for vt_i in range(4):
            wl = []
            for k in range(NKC):
                wl.append(e_dma(K_, "pool", wt[:, k * TT:(k + 1) * TT],
                                A["l0ab_wv"][k * 128:(k + 1) * 128,
                                             vt_i * TT:(vt_i + 1) * TT],
                                writes=[wt_rs[k]], tag="fwvT2"))
            for tb in range(8):
                ps, ps_r = K_.psum_f[tb % 2], K_.psum_fr[tb % 2]
                tok = None
                for k in range(NKC):
                    tok = plan.op("pe", lambda e, k=k, s=(k == 0), p=(k == NKC - 1),
                                  tb=tb, ps=ps: e.matmul(
                        ps[:, 0:TT], y[:, k * NOWN + tb * 128:k * NOWN + (tb + 1) * 128],
                        wt[:, k * TT:(k + 1) * TT], start=s, stop=p),
                        reads=[y_r, wt_rs[k]], writes=[ps_r], deps=[wl[k]],
                        tag=f"f2vm{vt_i}_{tb}_{k}")
                a_, r_ = stg.slots[stg.i % len(stg.slots)]
                stg.i += 1
                t1 = plan.op("dve", lambda e, ps=ps, a_=a_, v=vt_i: e.tensor_add(
                    a_[:, 0:TT], ps[:, 0:TT], bvB[:, v * TT:(v + 1) * TT]),
                    reads=[ps_r, bvB_r], writes=[r_], deps=[tok], tag="f2vev")

                def pairs(p, tb=tb, vt_i=vt_i, a_=a_):
                    return [(A["sh2_vt"][p * NOWN + tb * 128: p * NOWN + (tb + 1) * 128,
                                         vt_i * TT:(vt_i + 1) * TT], a_[:, 0:TT])]
                tv.append(e_dma_sw(K_, "sp", pairs, reads=[r_], tag="f2vst"))
        plan.barrier(deps=tq + tk + tv + [tys])
        stg2 = Stager(K_, "s2", 512, 3)
        softmax_attn(K_, name="f2a", n_h=16, kv_of=lambda h: h,
                     n_q=NOWN, n_kv=2048, scale=1.0 / np.sqrt(128.0), Tq=256,
                     qf=A["sh2_q"], kf=A["sh2_k"], vt=A["sh2_vt"], n_kvh=16,
                     kc_list=lambda qt: striped_kc_list(qt, 256),
                     bias_tab=A["ab_btab"],
                     bias_col_of=lambda qt, kc: qt * 16 + kc, n_bias_cols=64,
                     mask_tab=A["ab_mtab"][:, :],
                     mask_idx_of=lambda qt, kc: (qt * 4 + striped_cands(qt, 256).index(kc))
                     if kc in striped_cands(qt, 256) else None,
                     n_masks=16, qrow_tab=A["ab_qrow"],
                     dst_dram=A["sh2_a"], stg=stg2,
                     q_stripes=lambda p: list(range(p, 16, 2)),
                     out_stripes=True)
        plan.barrier()
        rhs = dram_rhs_loader(K_, "f2wo", A["sh2_a"], NOWN, sw_own=True)
        proj_fm(K_, name="f2wo", wload=wload_nat(K_, A["l0ab_wo"]),
                n_k=NKC, m_list=list(range(16)), t_list=[0, 1], rhs_at=rhs,
                sink=resid_sink(K_, bias_col=lambda m: bcols[:, 32 + m:33 + m],
                                bias_res=bcols_r), mg=3)


def emit_l1aa_F(K_, A):
    plan = K_.plan
    cst, cst_r = K_.cst
    H, Dh, BS, NB = 16, 64, 256, 8
    qkv = A["l1aa_qkv"]
    qk_src = qkv.rearrange("(h r) c -> h r c", h=H)[:, 0:128, :]
    v_src = qkv.rearrange("(h r) c -> h r c", h=H)[:, 128:192, :]
    cqk = [e_dma(K_, "pool", A["l1aa_qk_bf"].rearrange("(h r) c -> h r c", h=H),
                 qk_src, tag="c_qk")]
    cv = [e_dma(K_, "pool", A["l1aa_v_bf"].rearrange("(h r) c -> h r c", h=H),
                v_src, tag="c_v")]
    cg = cast_pass(K_, A["l1aa_gw_bf"], A["l1aa_gw"], 2048, 2048, tag="c_gw")
    cow = cast_pass(K_, A["l1aa_ow_bf"], A["l1aa_ow"], 2048, 2048, tag="c_ow")
    with K_.phase():
        stg = Stager(K_, "s", 512, 3)
        y, y_r, tys = ln_resident(K_, "l1aa", ysh=A["sh3_y"])
        plan.barrier(deps=[tys])
        yld = dram_rhs_loader(K_, "f3y", A["sh3_y"], 2048)
        tqk = []
        for tl in ([0, 1], [2, 3]):
            tqk += proj_to_dram(K_, name="f3qk", wload=wload_xbarT(K_, A["l1aa_qk_bf"], cqk),
                                n_k=NKC, m_list=list(range(H)), t_list=tl,
                                rhs_at=yld, dst=A["l1aa_qkf"], stg=stg,
                                evac_of=lambda m, t, ps: (lambda e, out_ap, ps=ps:
                                    e.activation(out_ap, ps[:, 0:out_ap.shape[-1]], AF.Silu)),
                                evac_eng="act")
        tv = []
        wt, wt_r = K_.sbuf("f3wvT", [128, NKC * TT], BF16)
        wt_rs = [Res(f"f3wvT{k}") for k in range(NKC)]
        for vt_i in range(2):
            wl = []
            for k in range(NKC):
                wl.append(e_xbar(K_, wt[:, k * TT:(k + 1) * TT],
                                 A["l1aa_v_bf"][vt_i * TT:(vt_i + 1) * TT,
                                                k * 128:(k + 1) * 128],
                                 writes=[wt_rs[k]], deps=cv, tag="f3wvT"))
            for tb in range(16):
                ps, ps_r = K_.psum_f[tb % 2], K_.psum_fr[tb % 2]
                tok = None
                for k in range(NKC):
                    r_ap, r_reads = yld(k, tb // 4)
                    sl = r_ap[:, (tb % 4) * 128:(tb % 4 + 1) * 128]
                    tok = plan.op("pe", lambda e, k=k, s=(k == 0), p=(k == NKC - 1),
                                  sl=sl, ps=ps: e.matmul(
                        ps[:, 0:TT], sl, wt[:, k * TT:(k + 1) * TT], start=s, stop=p),
                        reads=r_reads + [wt_rs[k]], writes=[ps_r], deps=[wl[k]],
                        tag=f"f3vm{vt_i}_{tb}_{k}")
                tv.append(stg.store(ps[:, 0:TT], ps_r,
                                    A["l1aa_vtm"][tb * 128:(tb + 1) * 128,
                                                  vt_i * TT:(vt_i + 1) * TT],
                                    tok=tok, evac=lambda e, out_ap, ps=ps: e.activation(
                                        out_ap, ps[:, 0:TT], AF.Silu),
                                    evac_eng="act"))
        qkb, qkb_r = K_.sbuf("aa_qkb", [64, 2 * 512], BF16)
        qkb_rs = [Res("aa_qkb0"), Res("aa_qkb1")]
        vb, vb_r = K_.sbuf("aa_vb", [128, 4 * Dh], BF16)
        vb_rs = [Res("aa_vb0"), Res("aa_vb1")]
        ddt, ddt_r = K_.sbuf("aa_ddt", [128, 2 * BS], BF16)
        qdB, qdB_r = K_.sbuf("aa_qdB", [128, BS], F32)
        qdrow, qdrow_r = K_.sbuf("aa_qdrow", [1, BS], BF16)
        decay, decay_r = K_.sbuf("aa_decay", [128, 128], F32)
        tdec = e_dma(K_, "sp", decay[:, :], A["l1aa_decay"][:, :], writes=[decay_r],
                     tag="aa_dec")
        awd, awd_r = K_.sbuf("aa_awd", [128, 2 * BS], BF16)
        vkd, vkd_r = K_.sbuf("aa_vkd", [128, 2 * Dh], BF16)
        ktm, ktm_r = K_.sbuf("aa_ktm", [128, 2 * Dh], BF16)
        St, St_r = K_.sbuf("aa_S", [64, Dh], F32)
        Sb, Sb_r = K_.sbuf("aa_Sb", [64, Dh], BF16)
        ob, ob_r = K_.sbuf("aa_ob", [64, BS], F32)
        ash_toks = []
        for h in range(H):
            tdd = e_dma(K_, "sp", ddt.rearrange("p (c s) -> p c s", c=2),
                        A["l1aa_ddT"][h * 256:(h + 1) * 256, :]
                        .rearrange("(c p) s -> p c s", p=128), writes=[ddt_r],
                        tag=f"aa_dd{h}")
            tqd = e_dma(K_, "sp", qdrow[0:1, :], A["l1aa_qd"][h:h + 1, :],
                        writes=[qdrow_r], tag=f"aa_qd{h}")
            ps5, ps5_r = K_.psum_f[5], K_.psum_fr[5]
            tqb = plan.op("pe", lambda e: e.matmul(ps5[:, 0:BS], cst[0:1, 128:256],
                                                   qdrow[0:1, :], start=True, stop=True),
                          reads=[qdrow_r, cst_r], writes=[ps5_r], deps=[tqd],
                          tag=f"aa_qb{h}")
            tqe = plan.op("dve", lambda e: e.tensor_copy(qdB[:, :], ps5[:, 0:BS]),
                          reads=[ps5_r], writes=[qdB_r], deps=[tqb], tag=f"aa_qe{h}")
            tS0 = plan.op("dve", lambda e: e.memset(St[:, :], 0.0), writes=[St_r],
                          tag=f"aa_S0{h}")
            tSb = plan.op("dve", lambda e: e.tensor_copy(Sb[:, :], St[:, :]),
                          reads=[St_r], writes=[Sb_r], deps=[tS0], tag=f"aa_Sb{h}")
            for blk in range(NB):
                sl = blk % 2
                qk = qkb[:, sl * 512:(sl + 1) * 512]
                tq = e_dma(K_, "sp", qk.rearrange("p (b s) -> p b s", b=2),
                           A["l1aa_qkf"][h * 128:(h + 1) * 128, blk * BS:(blk + 1) * BS]
                           .rearrange("(b p) s -> p b s", b=2),
                           writes=[qkb_rs[sl]], deps=tqk, tag=f"aa_lq{h}_{blk}")
                vsl = vb[:, sl * 2 * Dh:(sl + 1) * 2 * Dh]
                tvl = e_dma(K_, "sp", vsl.rearrange("p (c d) -> p c d", c=2),
                            A["l1aa_vtm"][blk * BS:(blk + 1) * BS, h * Dh:(h + 1) * Dh]
                            .rearrange("(c p) d -> p c d", p=128),
                            writes=[vb_rs[sl]], deps=tv, tag=f"aa_lv{h}_{blk}")
                taw = []
                for c in range(2):
                    psn, psn_r = K_.psum_f[c], K_.psum_fr[c]
                    t1 = plan.op("pe", lambda e, c=c, qk=qk, psn=psn: e.matmul(
                        psn[:, 0:BS], qk[:, 256 + c * 128:256 + (c + 1) * 128],
                        qk[:, 0:BS], start=True, stop=True),
                        reads=[qkb_rs[sl]], writes=[psn_r], deps=[tq],
                        tag=f"aa_aw{h}_{blk}_{c}")
                    taw.append(plan.op("dve", lambda e, c=c, psn=psn: e.tensor_mul(
                        awd[:, c * BS:(c + 1) * BS], psn[:, 0:BS],
                        ddt[:, c * BS:(c + 1) * BS]),
                        reads=[psn_r, ddt_r], writes=[awd_r], deps=[t1, tdd],
                        tag=f"aa_awd{h}_{blk}_{c}"))
                ps2, ps2_r = K_.psum_f[2], K_.psum_fr[2]
                ps3, ps3_r = K_.psum_f[3], K_.psum_fr[3]
                tintra = None
                for c in range(2):
                    tintra = plan.op("pe", lambda e, c=c, vsl=vsl: e.matmul(
                        ps2[0:64, 0:BS], vsl[:, c * Dh:(c + 1) * Dh],
                        awd[:, c * BS:(c + 1) * BS], start=(c == 0), stop=(c == 1)),
                        reads=[vb_rs[sl], awd_r], writes=[ps2_r],
                        deps=[tvl] + taw, tag=f"aa_in{h}_{blk}_{c}")
                tinter = plan.op("pe", lambda e, qk=qk: e.matmul(
                    ps3[0:64, 0:BS], Sb[:, :], qk[:, 0:BS], start=True, stop=True),
                    reads=[Sb_r, qkb_rs[sl]], writes=[ps3_r], deps=[tq],
                    tag=f"aa_it{h}_{blk}")
                tob = plan.op("dve", lambda e: e.tensor_mul(ob[:, :], ps3[0:64, 0:BS],
                                                            qdB[0:64, :]),
                              reads=[ps3_r, qdB_r], writes=[ob_r], deps=[tinter, tqe],
                              tag=f"aa_ob{h}_{blk}")
                a_, r_ = stg.slots[stg.i % len(stg.slots)]
                stg.i += 1
                tev = plan.op("dve", lambda e, a_=a_: e.tensor_add(
                    a_[0:64, 0:BS], ob[:, :], ps2[0:64, 0:BS]),
                    reads=[ps2_r, ob_r], writes=[r_], deps=[tintra, tob], tag="aa_oev")

                def pairs(p, h=h, blk=blk, a_=a_):
                    return [(A["sh3_a"][p * 1024 + h * Dh: p * 1024 + (h + 1) * Dh,
                                        blk * BS:(blk + 1) * BS], a_[0:64, 0:BS])]
                ash_toks.append(e_dma_sw(K_, "sp", pairs, reads=[r_], tag="aa_ost"))
                tS = []
                for c in range(2):
                    pb, pb_r = K_.psum_b[c], K_.psum_br[c]
                    tt = plan.op("pe", lambda e, c=c, qk=qk, pb=pb: e.transpose(
                        pb[:, 0:64], qk[:, 256 + c * 128:256 + (c + 1) * 128],
                        cst[0:64, 0:64]),
                        reads=[qkb_rs[sl], cst_r], writes=[pb_r], deps=[tq],
                        tag=f"aa_tr{h}_{blk}_{c}")
                    te = plan.op("dve", lambda e, c=c, pb=pb: e.tensor_copy(
                        ktm[:, c * Dh:(c + 1) * Dh], pb[:, 0:64]),
                        reads=[pb_r], writes=[ktm_r], deps=[tt], tag=f"aa_ke{h}_{blk}_{c}")
                    tk2 = plan.op("dve", lambda e, c=c, vsl=vsl, h=h: e.tensor_scalar(
                        vkd[:, c * Dh:(c + 1) * Dh], vsl[:, c * Dh:(c + 1) * Dh],
                        decay[:, 2 * h + c:2 * h + c + 1], None, ALU.mult),
                        reads=[vb_rs[sl], decay_r], writes=[vkd_r], deps=[tvl, tdec],
                        tag=f"aa_vk{h}_{blk}_{c}")
                    tS.append(plan.op("pe", lambda e, c=c: e.matmul(
                        ps5[0:64, 0:Dh], ktm[:, c * Dh:(c + 1) * Dh],
                        vkd[:, c * Dh:(c + 1) * Dh], start=(c == 0), stop=(c == 1)),
                        reads=[ktm_r, vkd_r], writes=[ps5_r], deps=[te, tk2],
                        tag=f"aa_su{h}_{blk}_{c}"))
                tupd = plan.op("dve", lambda e, h=h: e.scalar_tensor_tensor(
                    St[:, :], St[:, :], decay[0:64, 64 + h:65 + h], ps5[0:64, 0:Dh],
                    ALU.mult, ALU.add),
                    reads=[ps5_r, decay_r], writes=[St_r], deps=tS, tag=f"aa_up{h}_{blk}")
                tSb = plan.op("dve", lambda e: e.tensor_copy(Sb[:, :], St[:, :]),
                              reads=[St_r], writes=[Sb_r], deps=[tupd],
                              tag=f"aa_sb{h}_{blk}")
        plan.barrier(deps=ash_toks)
        ach, ach_r0 = K_.sbuf("aa_ach", [128, 2 * TT], BF16)
        ach_rs = [Res("aa_ach0"), Res("aa_ach1")]

        def ach_load(k, t, sl):
            def pairs(p, k=k, t=t, sl=sl):
                return [(ach[:, sl * TT:(sl + 1) * TT],
                         A["sh3_a"][k * 128:(k + 1) * 128,
                                    p * NOWN + t * TT: p * NOWN + (t + 1) * TT])]
            return e_dma_sw(K_, "sp", pairs, writes=[ach_rs[sl]],
                            tag=f"aa_la{t}_{k}")
        nwcols, nwcols_r = K_.sbuf("aa_nw", [128, 16], F32)
        tnw = e_dma(K_, "sp", nwcols[:, :], A["l1aa_nw"][:, :], writes=[nwcols_r],
                    tag="aa_nw")
        an, an_r = K_.sbuf("aa_an", [128, 2 * TT], BF16)
        gsb, gsb_r = K_.sbuf("aa_gsb", [128, TT], BF16)
        gated, gated_r = K_.sbuf("aa_gated", [128, NKC * TT], BF16)
        gated_rs = [Res(f"aa_gated{k}") for k in range(NKC)]
        rstB, rstB_r = K_.sbuf("aa_rstB", [128, TT], BF16)
        rrow, rrow_r = K_.sbuf("aa_rrow", [1, 2 * TT], F32)
        rrbf, rrbf_r = K_.sbuf("aa_rrbf", [1, TT], BF16)
        sqs2, sqs2_r = K_.sbuf("aa_sq", [128, 2 * TT], BF16)
        wbufg, wbufg_r = K_.sbuf("aa_wg", [128, 128], BF16)
        gw_wl = wload_xbarT(K_, A["l1aa_gw_bf"], cg)
        for t in range(2):
            ps4, ps4_r = K_.psum_f[4], K_.psum_fr[4]
            for k in range(NKC):
                sl = k % 2
                tach = ach_load(k, t, sl)
                tsq = plan.op("act", lambda e, sl=sl: e.activation(
                    sqs2[:, sl * TT:(sl + 1) * TT],
                    ach[:, sl * TT:(sl + 1) * TT], AF.Square),
                    reads=[ach_rs[sl]], writes=[sqs2_r], deps=[tach],
                    tag=f"aa_sq{t}_{k}")
                plan.op("pe", lambda e, sl=sl, s=(k == 0), p=(k == NKC - 1): e.matmul(
                    ps4[0:1, 0:TT], cst[:, 128:129], sqs2[:, sl * TT:(sl + 1) * TT],
                    start=s, stop=p),
                    reads=[sqs2_r, cst_r], writes=[ps4_r], deps=[tsq],
                    tag=f"aa_ss{t}_{k}")
            t1 = plan.op("dve", lambda e: e.tensor_scalar(
                rrow[0:1, 0:TT], ps4[0:1, 0:TT], 1.0 / D, 1e-5, ALU.mult, ALU.add),
                reads=[ps4_r], writes=[rrow_r], tag=f"aa_ms{t}")
            t2 = plan.op("act", lambda e: e.activation(rrow[0:1, TT:2 * TT],
                                                       rrow[0:1, 0:TT], AF.Sqrt),
                         reads=[rrow_r], writes=[rrow_r], deps=[t1], tag=f"aa_sd{t}")
            t3 = plan.op("dve", lambda e: e.reciprocal(rrow[0:1, 0:TT],
                                                       rrow[0:1, TT:2 * TT]),
                         reads=[rrow_r], writes=[rrow_r], deps=[t2], tag=f"aa_rc{t}")
            t4 = plan.op("dve", lambda e: e.tensor_copy(rrbf[0:1, :], rrow[0:1, 0:TT]),
                         reads=[rrow_r], writes=[rrbf_r], deps=[t3], tag=f"aa_rb{t}")
            ps5b, ps5b_r = K_.psum_f[5], K_.psum_fr[5]
            t5 = plan.op("pe", lambda e: e.matmul(ps5b[:, 0:TT], cst[0:1, 128:256],
                                                  rrbf[0:1, :], start=True, stop=True),
                         reads=[rrbf_r, cst_r], writes=[ps5b_r], deps=[t4],
                         tag=f"aa_rB{t}")
            t6 = plan.op("dve", lambda e: e.tensor_copy(rstB[:, :], ps5b[:, 0:TT]),
                         reads=[ps5b_r], writes=[rstB_r], deps=[t5], tag=f"aa_rE{t}")
            y_, y_r_ = K_.sbuf("y_l1aa", [128, NKC * NOWN], BF16)
            for m in range(NKC):
                tokg = None
                for ki in range(NKC):
                    twg = gw_wl(ki, 0, [m], wbufg[:, 0:128], wbufg_r)
                    tokg = plan.op("pe", lambda e, ki=ki, s=(ki == 0),
                                   p=(ki == NKC - 1), t=t: e.matmul(
                        K_.psum_f[3][:, 0:TT], wbufg[:, 0:128],
                        y_[:, ki * NOWN + t * TT: ki * NOWN + (t + 1) * TT],
                        start=s, stop=p),
                        reads=[wbufg_r, y_r_], writes=[K_.psum_fr[3]], deps=[twg],
                        tag=f"aa_gm{t}_{m}_{ki}")
                tsig = plan.op("act", lambda e: e.activation(
                    gsb[:, :], K_.psum_f[3][:, 0:TT], AF.Sigmoid),
                    reads=[K_.psum_fr[3]], writes=[gsb_r], deps=[tokg],
                    tag=f"aa_sg{t}_{m}")
                sl = m % 2
                tach = ach_load(m, t, sl)
                tn1 = plan.op("pool", lambda e, sl=sl: e.tensor_mul(
                    an[:, sl * TT:(sl + 1) * TT],
                    ach[:, sl * TT:(sl + 1) * TT], rstB[:, :]),
                    reads=[ach_rs[sl], rstB_r], writes=[an_r], deps=[tach, t6],
                    tag=f"aa_n1{t}_{m}")
                tn2 = plan.op("dve", lambda e, sl=sl, m=m: e.tensor_scalar(
                    an[:, sl * TT:(sl + 1) * TT], an[:, sl * TT:(sl + 1) * TT],
                    nwcols[:, m:m + 1], None, ALU.mult),
                    reads=[nwcols_r], writes=[an_r], deps=[tn1, tnw], tag=f"aa_n2{t}_{m}")
                plan.op("pool", lambda e, sl=sl, m=m: e.tensor_mul(
                    gated[:, m * TT:(m + 1) * TT], an[:, sl * TT:(sl + 1) * TT],
                    gsb[:, :]),
                    reads=[an_r, gsb_r], writes=[gated_rs[m]], deps=[tn2, tsig],
                    tag=f"aa_g{t}_{m}")

            def g_at(kk, tt):
                return gated[:, kk * TT:(kk + 1) * TT], [gated_rs[kk]]

            proj_fm(K_, name="f3ow", wload=wload_xbarT(K_, A["l1aa_ow_bf"], cow),
                    n_k=NKC, m_list=list(range(16)), t_list=[t], rhs_at=g_at,
                    sink=resid_sink(K_), mg=3)


def emit_moe_casts(K_, A):
    def fn(eng, attach):
        for arm in K_.nc.switch_core_id(engines=[eng], n=128):
            q = int(arm) % 2
            attach(eng.dma_start(out=A["moe_w1t_bf"][q * 1024:(q + 1) * 1024, :],
                                 in_=A["moe_w1t"][q * 1024:(q + 1) * 1024, :]))
            attach(eng.dma_start(out=A["moe_w3t_bf"][q * 1024:(q + 1) * 1024, :],
                                 in_=A["moe_w3t"][q * 1024:(q + 1) * 1024, :]))
            attach(eng.dma_start(out=A["moe_w2t_bf"][q * 512:(q + 1) * 512, :],
                                 in_=A["moe_w2t"][q * 512:(q + 1) * 512, :]))
    return K_.plan.op_sw("pool", fn, n_dma=3, tag="moecast")


def emit_moe_F(K_, A, first):
    plan = K_.plan
    cst, cst_r = K_.cst
    E, I = 8, 1024
    NT = NOWN // 512
    if first:
        K_.moe_bar = plan.barrier(deps=[K_.moe_cast_tok])
    bar = K_.moe_bar
    with K_.phase():
        gws, gws_r = K_.sbuf("mg_gw", [8, 2048], BF16)
        tgw = e_dma(K_, "pool", gws[:, :], A["moe_gw"][:, :], writes=[gws_r], tag="mg_gw")
        gwT, gwT_r = K_.sbuf("mg_gwT", [128, 16 * 8], BF16)
        for k in range(NKC):
            pb, pb_r = K_.psum_b[0], K_.psum_br[0]
            tt = plan.op("pe", lambda e, k=k: e.transpose(
                pb[:, 0:8], gws[:, k * 128:(k + 1) * 128], cst[0:8, 0:8]),
                reads=[gws_r, cst_r], writes=[pb_r], deps=[tgw], tag=f"mg_t{k}")
            plan.op("dve", lambda e, k=k: e.tensor_copy(gwT[:, k * 8:(k + 1) * 8],
                                                        pb[:, 0:8]),
                    reads=[pb_r], writes=[gwT_r], deps=[tt], tag=f"mg_e{k}")
        gbrow, gbrow_r = K_.sbuf("mg_gbr", [1, 8], F32)
        gbrowb, gbrowb_r = K_.sbuf("mg_gbrb", [1, 8], BF16)
        tgb = e_dma(K_, "sp", gbrow[0:1, :], A["moe_gb"][:, :], writes=[gbrow_r], tag="mg_gb")
        tgbb = plan.op("dve", lambda e: e.tensor_copy(gbrowb[0:1, :], gbrow[0:1, :]),
                       reads=[gbrow_r], writes=[gbrowb_r], deps=[tgb], tag="mg_gbb")
        gbB, gbB_r = K_.sbuf("mg_gbB", [128, 8], F32)
        ps5, ps5_r = K_.psum_f[5], K_.psum_fr[5]
        tb1 = plan.op("pe", lambda e: e.matmul(ps5[:, 0:8], cst[0:1, 128:256],
                                               gbrowb[0:1, :], start=True, stop=True),
                      reads=[gbrowb_r, cst_r], writes=[ps5_r], deps=[tgbb], tag="mg_gbm")
        tb2 = plan.op("dve", lambda e: e.tensor_copy(gbB[:, :], ps5[:, 0:8]),
                      reads=[ps5_r], writes=[gbB_r], deps=[tb1], tag="mg_gbe")
        y, y_r, _ = ln_resident(K_, "moe")
        rt, rt_r = K_.sbuf("mg_rt", [128, 64], F32)
        dwT, dwT_r = K_.sbuf("mg_dwT", [8, NOWN], BF16)
        dwbf, dwbf_r = K_.sbuf("mg_dwbf", [128, 8], BF16)
        for ts in range(NOWN // 128):
            psl, psl_r = K_.psum_f[4], K_.psum_fr[4]
            tokl = None
            for k in range(NKC):
                tokl = plan.op("pe", lambda e, k=k, ts=ts, s=(k == 0),
                               p=(k == NKC - 1): e.matmul(
                    psl[:, 0:8],
                    y[:, k * NOWN + ts * 128: k * NOWN + (ts + 1) * 128],
                    gwT[:, k * 8:(k + 1) * 8], start=s, stop=p),
                    reads=[y_r, gwT_r], writes=[psl_r], tag=f"mg_l{ts}_{k}")
            sc_ = rt[:, 0:8]; mx_ = rt[:, 8:9]; nmx = rt[:, 9:10]
            sm_ = rt[:, 10:11]; rc_ = rt[:, 11:12]
            bi_ = rt[:, 16:24]; e1_ = rt[:, 24:32]; b2_ = rt[:, 32:40]
            m2_ = rt[:, 40:41]; e2_ = rt[:, 41:49]; msk_ = rt[:, 49:57]
            t1 = plan.op("dve", lambda e: e.reduce_max(mx_, psl[:, 0:8], axis=AX),
                         reads=[psl_r], writes=[rt_r], deps=[tokl], tag="mg_mx")
            t2 = plan.op("dve", lambda e: e.tensor_scalar_mul(nmx, mx_, -1.0),
                         reads=[rt_r], writes=[rt_r], deps=[t1], tag="mg_nm")
            t3 = plan.op("act", lambda e: e.activation(sc_, psl[:, 0:8], AF.Exp,
                                                       bias=nmx, scale=1.0),
                         reads=[psl_r, rt_r], writes=[rt_r], deps=[t2], tag="mg_ex")
            t4 = plan.op("dve", lambda e: e.reduce_sum(sm_, sc_, axis=AX),
                         reads=[rt_r], writes=[rt_r], deps=[t3], tag="mg_sm")
            t5 = plan.op("dve", lambda e: e.reciprocal(rc_, sm_),
                         reads=[rt_r], writes=[rt_r], deps=[t4], tag="mg_rc")
            t6 = plan.op("dve", lambda e: e.tensor_scalar(sc_, sc_, rc_, None,
                                                          ALU.mult),
                         reads=[rt_r], writes=[rt_r], deps=[t5], tag="mg_sc")
            t7 = plan.op("dve", lambda e: e.tensor_add(bi_, sc_, gbB[:, :]),
                         reads=[rt_r, gbB_r], writes=[rt_r], deps=[t6, tb2],
                         tag="mg_bi")
            t8 = plan.op("dve", lambda e: e.reduce_max(mx_, bi_, axis=AX),
                         reads=[rt_r], writes=[rt_r], deps=[t7], tag="mg_m1")
            t9 = plan.op("dve", lambda e: e.tensor_scalar(e1_, bi_, mx_, None,
                                                          ALU.is_equal),
                         reads=[rt_r], writes=[rt_r], deps=[t8], tag="mg_e1")
            t10 = plan.op("dve", lambda e: e.scalar_tensor_tensor(
                b2_, e1_, -1e30, bi_, ALU.mult, ALU.add),
                reads=[rt_r], writes=[rt_r], deps=[t9], tag="mg_b2")
            t11 = plan.op("dve", lambda e: e.reduce_max(m2_, b2_, axis=AX),
                          reads=[rt_r], writes=[rt_r], deps=[t10], tag="mg_m2")
            t12 = plan.op("dve", lambda e: e.tensor_scalar(e2_, b2_, m2_, None,
                                                           ALU.is_equal),
                          reads=[rt_r], writes=[rt_r], deps=[t11], tag="mg_e2")
            t13 = plan.op("dve", lambda e: e.tensor_add(msk_, e1_, e2_),
                          reads=[rt_r], writes=[rt_r], deps=[t12], tag="mg_mk")
            t14 = plan.op("dve", lambda e: e.tensor_mul(sc_, sc_, msk_),
                          reads=[rt_r], writes=[rt_r], deps=[t13], tag="mg_dw")
            t14b = plan.op("dve", lambda e: e.tensor_copy(dwbf[:, :], sc_),
                           reads=[rt_r], writes=[dwbf_r], deps=[t14], tag="mg_dwb")
            pst, pst_r = K_.psum_b[1], K_.psum_br[1]
            t15 = plan.op("pe", lambda e: e.transpose(pst[0:8, 0:128], dwbf[:, :],
                                                      cst[:, 0:128]),
                          reads=[dwbf_r, cst_r], writes=[pst_r], deps=[t14b],
                          tag="mg_tr")
            plan.op("dve", lambda e, ts=ts: e.tensor_copy(
                dwT[:, ts * 128:(ts + 1) * 128], pst[0:8, 0:128]),
                reads=[pst_r], writes=[dwT_r], deps=[t15], tag="mg_te")
        tdo = e_dma(K_, "sp", A["moe_dwd"][:, :], dwT[:, :], reads=[dwT_r], tag="mg_dwo")
        dwd_flat = A["moe_dwd"].rearrange("e s -> (e s)").rearrange("(o f) -> o f", o=1)
        hall, _ = K_.sbuf("mg_h", [128, 8 * NOWN], BF16)
        hall_rs = [Res(f"mg_h{s}") for s in range(8 * NT)]
        w1s, _ = K_.sbuf("mg_w1s", [128, 2 * NKC * 128], BF16)
        w3s, _ = K_.sbuf("mg_w3s", [128, 2 * NKC * 128], BF16)
        w1rs = [Res(f"mg_w1s{i}") for i in range(2)]
        w3rs = [Res(f"mg_w3s{i}") for i in range(2)]
        w2s, _ = K_.sbuf("mg_w2s", [128, 2 * 8 * 256], BF16)
        w2rs = [Res(f"mg_w2s{i}") for i in range(2)]
        sa, _ = K_.sbuf("mg_sa", [128, 2 * 512], BF16)
        sa_rs = [Res(f"mg_sa{i}") for i in range(2)]
        hf, _ = K_.sbuf("mg_hf", [128, 2 * 512], F32)
        hf_rs = [Res(f"mg_hf{i}") for i in range(2)]
        dwrow, _ = K_.sbuf("mg_dwrow", [1, 2 * NOWN], BF16)
        dwr_rs = [Res(f"mg_dwrow{i}") for i in range(2)]
        dwB, _ = K_.sbuf("mg_dwB", [128, 2 * NT * 512], BF16)
        dwB_rs = [Res(f"mg_dwB{i}") for i in range(2)]
        mgc = [0]; w2c = [0]; abc = [0]; cc_ = [0]
        for ee in range(9):
            dsl = ee % 2
            if ee < E:
                tdi = e_dma(K_, "sp", dwrow[0:1, dsl * NOWN:(dsl + 1) * NOWN],
                            dwd_flat[0:1, ee * NOWN:(ee + 1) * NOWN],
                            writes=[dwr_rs[dsl]], deps=[tdo], tag=f"mg_dwi{ee}")
                for tq in range(NT):
                    pi = 4 + (cc_[0] % 2); cc_[0] += 1
                    pbb, pbb_r = K_.psum_f[pi], K_.psum_fr[pi]
                    tbr = plan.op("pe", lambda e, dsl=dsl, tq=tq, pbb=pbb: e.matmul(
                        pbb[:, 0:512], cst[0:1, 128:256],
                        dwrow[0:1, dsl * NOWN + tq * 512: dsl * NOWN + (tq + 1) * 512],
                        start=True, stop=True),
                        reads=[dwr_rs[dsl], cst_r], writes=[pbb_r], deps=[tdi],
                        tag=f"mg_db{ee}_{tq}")
                    plan.op("dve", lambda e, dsl=dsl, tq=tq, pbb=pbb: e.tensor_copy(
                        dwB[:, (dsl * NT + tq) * 512:(dsl * NT + tq + 1) * 512],
                        pbb[:, 0:512]),
                        reads=[pbb_r], writes=[dwB_rs[dsl]], deps=[tbr],
                        tag=f"mg_de{ee}_{tq}")
            for m in range(8):
                sl = mgc[0] % 2; mgc[0] += 1
                base = sl * NKC * 128
                d1 = w1s[:, base:base + NKC * 128].rearrange("p (k c) -> p k c", k=NKC)
                d3 = w3s[:, base:base + NKC * 128].rearrange("p (k c) -> p k c", k=NKC)
                s1 = A["moe_w1t_bf"][:, ee * I + m * 128: ee * I + (m + 1) * 128]\
                    .rearrange("(k p) c -> p k c", p=128)
                s3 = A["moe_w3t_bf"][:, ee * I + m * 128: ee * I + (m + 1) * 128]\
                    .rearrange("(k p) c -> p k c", p=128)
                tw1 = e_dma(K_, "sp", d1, s1, writes=[w1rs[sl]], deps=[bar],
                            tag=f"mgw1_{ee}_{m}")
                tw3 = e_dma(K_, "sp", d3, s3, writes=[w3rs[sl]], deps=[bar],
                            tag=f"mgw3_{ee}_{m}")
                for tq in range(NT):
                    pi = abc[0] % 2; abc[0] += 1
                    pa, pa_r = K_.psum_f[pi], K_.psum_fr[pi]
                    pb2, pb2_r = K_.psum_f[2 + pi], K_.psum_fr[2 + pi]
                    tokA = tokB = None
                    for k in range(NKC):
                        l1 = w1s[:, base + k * 128: base + (k + 1) * 128]
                        l3 = w3s[:, base + k * 128: base + (k + 1) * 128]
                        r_ = y[:, k * NOWN + tq * 512: k * NOWN + (tq + 1) * 512]
                        tokA = plan.op("pe", lambda e, l1=l1, r_=r_, pa=pa,
                                       s=(k == 0), p=(k == NKC - 1): e.matmul(
                            pa[:, 0:512], l1, r_, start=s, stop=p),
                            reads=[w1rs[sl], y_r], writes=[pa_r], deps=[tw1],
                            tag=f"mgA{ee}_{m}_{tq}_{k}")
                        tokB = plan.op("pe", lambda e, l3=l3, r_=r_, pb2=pb2,
                                       s=(k == 0), p=(k == NKC - 1): e.matmul(
                            pb2[:, 0:512], l3, r_, start=s, stop=p),
                            reads=[w3rs[sl], y_r], writes=[pb2_r], deps=[tw3],
                            tag=f"mgB{ee}_{m}_{tq}_{k}")
                    si = (m * NT + tq) % 2
                    sa_ = sa[:, si * 512:(si + 1) * 512]
                    hf_ = hf[:, si * 512:(si + 1) * 512]
                    ts_ = plan.op("act", lambda e, sa_=sa_, pa=pa: e.activation(
                        sa_, pa[:, 0:512], AF.Silu),
                        reads=[pa_r], writes=[sa_rs[si]], deps=[tokA],
                        tag=f"mg_si{ee}_{m}_{tq}")
                    tm_ = plan.op("dve", lambda e, hf_=hf_, sa_=sa_, pb2=pb2:
                                  e.tensor_mul(hf_, sa_, pb2[:, 0:512]),
                                  reads=[pb2_r, sa_rs[si]], writes=[hf_rs[si]],
                                  deps=[tokB, ts_], tag=f"mg_hm{ee}_{m}_{tq}")
                    slot = m * NT + tq
                    hsl = hall[:, slot * 512:(slot + 1) * 512]
                    if ee < E:
                        plan.op("pool", lambda e, hsl=hsl, hf_=hf_, dsl=dsl, tq=tq:
                                e.tensor_mul(hsl, hf_,
                                             dwB[:, (dsl * NT + tq) * 512:
                                                 (dsl * NT + tq + 1) * 512]),
                                reads=[hf_rs[si], dwB_rs[dsl]], writes=[hall_rs[slot]],
                                deps=[tm_], tag=f"mg_hd{ee}_{m}_{tq}")
                    else:
                        plan.op("pool", lambda e, hsl=hsl, hf_=hf_:
                                e.tensor_copy(hsl, hf_),
                                reads=[hf_rs[si]], writes=[hall_rs[slot]],
                                deps=[tm_], tag=f"mg_hc{ee}_{m}_{tq}")
            for og in range(8):
                sl2 = w2c[0] % 2; w2c[0] += 1
                b2s = sl2 * 8 * 256
                d2 = w2s[:, b2s:b2s + 8 * 256].rearrange("p (k c) -> p k c", k=8)
                s2 = A["moe_w2t_bf"][:, ee * 2048 + og * 256: ee * 2048 + (og + 1) * 256]\
                    .rearrange("(k p) c -> p k c", p=128)
                tw2 = e_dma(K_, "sp", d2, s2, writes=[w2rs[sl2]], deps=[bar],
                            tag=f"mgw2_{ee}_{og}")
                for m_in in range(2):
                    m_out = og * 2 + m_in
                    for tq in range(NT):
                        pi = 4 + (cc_[0] % 2); cc_[0] += 1
                        pc, pc_r = K_.psum_f[pi], K_.psum_fr[pi]
                        tokC = None
                        for ki in range(8):
                            l2 = w2s[:, b2s + ki * 256 + m_in * 128:
                                     b2s + ki * 256 + (m_in + 1) * 128]
                            r_ = hall[:, (ki * NT + tq) * 512:(ki * NT + tq + 1) * 512]
                            tokC = plan.op("pe", lambda e, l2=l2, r_=r_, pc=pc,
                                           s=(ki == 0), p=(ki == 7): e.matmul(
                                pc[:, 0:512], l2, r_, start=s, stop=p),
                                reads=[w2rs[sl2], hall_rs[ki * NT + tq]],
                                writes=[pc_r], deps=[tw2],
                                tag=f"mgC{ee}_{m_out}_{tq}_{ki}")
                        xsl = K_.x_sb[:, m_out * NOWN + tq * 512:
                                      m_out * NOWN + (tq + 1) * 512]
                        plan.op("dve", lambda e, xsl=xsl, pc=pc: e.tensor_add(
                            xsl, xsl, pc[:, 0:512]),
                            reads=[pc_r], writes=[K_.x_rs[m_out]], deps=[tokC],
                            tag=f"mg_add{ee}_{m_out}_{tq}")


def emit_l1ab_F(K_, A):
    plan = K_.plan
    cst, cst_r = K_.cst
    Hh, DH, C = 8, 128, 64
    NCh = 2048 // C
    with K_.phase():
        stg = Stager(K_, "s", 512, 3)
        y, y_r, tys = ln_resident(K_, "l1ab", ysh=A["sh4_y"])
        plan.barrier(deps=[tys])
        yld = dram_rhs_loader(K_, "f4y", A["sh4_y"], 2048)
        tq = tk = tv = []
        for tl in ([0, 1], [2, 3]):
            tq = tq + proj_to_dram(K_, name="f4q", wload=wload_nat(K_, A["l1ab_wq"]),
                                   n_k=NKC, m_list=list(range(8)), t_list=tl,
                                   rhs_at=yld, dst=A["l1ab_qraw"], stg=stg)
            tk = tk + proj_to_dram(K_, name="f4k", wload=wload_nat(K_, A["l1ab_wk"]),
                                   n_k=NKC, m_list=list(range(8)), t_list=tl,
                                   rhs_at=yld, dst=A["l1ab_kraw"], stg=stg)
        wt, wt_r = K_.sbuf("f4wvT", [128, NKC * TT], BF16)
        wt_rs = [Res(f"f4wvT{k}") for k in range(NKC)]
        tv = []
        for vt_i in range(2):
            wl = []
            for k in range(NKC):
                wl.append(e_dma(K_, "pool", wt[:, k * TT:(k + 1) * TT],
                                A["l1ab_wv"][k * 128:(k + 1) * 128,
                                             vt_i * TT:(vt_i + 1) * TT],
                                writes=[wt_rs[k]], tag="f4wvT"))
            for tb in range(16):
                ps, ps_r = K_.psum_f[tb % 2], K_.psum_fr[tb % 2]
                tok = None
                for k in range(NKC):
                    r_ap, r_reads = yld(k, tb // 4)
                    slc = r_ap[:, (tb % 4) * 128:(tb % 4 + 1) * 128]
                    tok = plan.op("pe", lambda e, k=k, s=(k == 0), p=(k == NKC - 1),
                                  slc=slc, ps=ps: e.matmul(
                        ps[:, 0:TT], slc, wt[:, k * TT:(k + 1) * TT], start=s, stop=p),
                        reads=r_reads + [wt_rs[k]], writes=[ps_r], deps=[wl[k]],
                        tag=f"f4vm{vt_i}_{tb}_{k}")
                tv.append(stg.store(ps[:, 0:TT], ps_r,
                                    A["l1ab_vtm"][tb * 128:(tb + 1) * 128,
                                                  vt_i * TT:(vt_i + 1) * TT], tok=tok))
    with K_.phase():
        stg = Stager(K_, "s", 512, 3)
        stg2 = Stager(K_, "s2", 512, 3)
        nqt = 2048 // 256

        def kc_list(qt):
            return list(range(0, 2 * qt + 2))

        def mask_idx_of(qt, kc):
            if kc in (2 * qt, 2 * qt + 1):
                return qt * 2 + (kc - 2 * qt)
            return None

        tbase = softmax_attn(K_, name="f4a", n_h=8, kv_of=lambda h: h,
                             n_q=2048, n_kv=2048, scale=1.0 / np.sqrt(128.0), Tq=256,
                             qf=A["l1ab_qraw"], kf=A["l1ab_kraw"], vt=A["l1ab_vtm"],
                             n_kvh=8, kc_list=kc_list,
                             bias_tab=A["abg_btab"],
                             bias_col_of=lambda qt, kc: qt * 16 + kc,
                             n_bias_cols=nqt * 16,
                             mask_tab=A["abg_mtab"][:, :], mask_idx_of=mask_idx_of,
                             n_masks=nqt * 2,
                             dst_dram=A["l1ab_obase"], stg=stg2, dep_toks=tq + tk + tv)
        dcst, dcst_r = K_.sbuf("db_csts", [64, 128], F32)
        tdc = e_dma(K_, "sp", dcst[:, :], A["db_cst"][:, :], writes=[dcst_r], tag="db_cst")
        scale_q = float(128.0 ** -0.5)
        olin_toks = {}
        bufs = []
        for sub in range(2):
            sfx = "" if sub == 0 else "_1"
            B = {"sfx": sfx}
            B["qs"], B["qs_r"] = K_.sbuf(f"db_qs{sfx}", [128, 2048], BF16)
            B["ks"], B["ks_r"] = K_.sbuf(f"db_ks{sfx}", [128, 2048], BF16)
            B["beta"], B["beta_r"] = K_.sbuf(f"db_beta{sfx}", [128, 2048], BF16)
            B["kb"], B["kb_r"] = K_.sbuf(f"db_kb{sfx}", [128, 2048], BF16)
            B["lgt"], B["lgt_r"] = K_.sbuf(f"db_lg{sfx}", [128, 2 * TT], BF16)
            B["St"], B["St_r"] = K_.sbuf(f"db_S{sfx}", [128, 128], F32)
            B["Sbp"], B["Sbp_r"] = K_.sbuf(f"db_Sbp{sfx}", [128, 128], BF16)
            B["Sbs"], B["Sbs_r"] = K_.sbuf(f"db_Sbs{sfx}", [128, 128], BF16)
            B["Arm"], B["Arm_r"] = K_.sbuf(f"db_Arm{sfx}", [64, 64 * 12], BF16)
            B["Rb"], B["Rb_r"] = K_.sbuf(f"db_R{sfx}", [64, 128], BF16)
            B["Gt"], B["Gt_r"] = K_.sbuf(f"db_G{sfx}", [64, 128], BF16)
            B["atn"], B["atn_r"] = K_.sbuf(f"db_atn{sfx}", [64, 64], BF16)
            B["vbc"], B["vbc_r"] = K_.sbuf(f"db_vbc{sfx}", [64, 128], BF16)
            B["ktc"], B["ktc_r"] = K_.sbuf(f"db_ktc{sfx}", [64, 128], BF16)
            B["pL"], B["pL_r"] = K_.psum_f[3 * sub + 0], K_.psum_fr[3 * sub + 0]
            B["pG"], B["pG_r"] = K_.psum_f[3 * sub + 1], K_.psum_fr[3 * sub + 1]
            B["pO"], B["pO_r"] = K_.psum_f[3 * sub + 2], K_.psum_fr[3 * sub + 2]
            B["ptr"], B["ptr_r"] = K_.psum_b[sub], K_.psum_br[sub]
            bufs.append(B)

        def db_prep(h, B):
            sfx = B["sfx"]
            qs, qs_r = B["qs"], B["qs_r"]
            ks, ks_r = B["ks"], B["ks_r"]
            beta, beta_r = B["beta"], B["beta_r"]
            kb, kb_r = B["kb"], B["kb_r"]
            lgt, lgt_r = B["lgt"], B["lgt_r"]
            _, _, _, tqs = fm_softmax_clip(K_, A, "q", A["l1ab_qraw"], qs, qs_r, h,
                                           2048, sfx=sfx)
            kraw, kraw_r, tldk, tks = fm_softmax_clip(K_, A, "k", A["l1ab_kraw"], ks,
                                                      ks_r, h, 2048, sfx=sfx)
            tbet = []
            for s in range(4):
                sl_ = slice(s * TT, (s + 1) * TT)
                a1 = plan.op("act", lambda e, sl_=sl_: e.activation(
                    lgt[:, 0:TT], kraw[:, sl_], AF.Abs),
                    reads=[kraw_r], writes=[lgt_r], deps=[tldk], tag=f"db_ab{h}{s}")
                a2 = plan.op("act", lambda e: e.activation(lgt[:, TT:2 * TT],
                                                           lgt[:, 0:TT], AF.Exp,
                                                           scale=-1.0),
                             reads=[lgt_r], writes=[lgt_r], deps=[a1], tag=f"db_e{h}{s}")
                a3 = plan.op("dve", lambda e: e.tensor_scalar_add(lgt[:, TT:2 * TT],
                                                                  lgt[:, TT:2 * TT], 1.0),
                             reads=[lgt_r], writes=[lgt_r], deps=[a2], tag=f"db_p1{h}{s}")
                a4 = plan.op("act", lambda e: e.activation(lgt[:, TT:2 * TT],
                                                           lgt[:, TT:2 * TT], AF.Ln),
                             reads=[lgt_r], writes=[lgt_r], deps=[a3], tag=f"db_ln{h}{s}")
                a5 = plan.op("dve", lambda e, sl_=sl_: e.tensor_scalar_min(
                    lgt[:, 0:TT], kraw[:, sl_], 0.0),
                    reads=[kraw_r], writes=[lgt_r], deps=[a4], tag=f"db_mn{h}{s}")
                a6 = plan.op("dve", lambda e: e.tensor_sub(lgt[:, 0:TT], lgt[:, 0:TT],
                                                           lgt[:, TT:2 * TT]),
                             reads=[lgt_r], writes=[lgt_r], deps=[a5], tag=f"db_sb{h}{s}")
                a7 = plan.op("dve", lambda e, sl_=sl_: e.tensor_scalar(
                    beta[:, sl_], lgt[:, 0:TT], 1.0 / 16.0, -16.0, ALU.mult, ALU.max),
                    reads=[lgt_r], writes=[beta_r], deps=[a6], tag=f"db_bt{h}{s}")
                tbet.append(a7)
            tkb = []
            for s in range(4):
                sl_ = slice(s * TT, (s + 1) * TT)
                tkb.append(plan.op("dve", lambda e, sl_=sl_: e.tensor_mul(
                    kb[:, sl_], ks[:, sl_], beta[:, sl_]),
                    reads=[ks_r, beta_r], writes=[kb_r], deps=tks + tbet,
                    tag=f"db_kb{h}{s}"))
            tS0 = plan.op("dve", lambda e: e.memset(B["St"][:, :], 0.0),
                          writes=[B["St_r"]], tag=f"db_S0{h}")
            B["tSp"] = plan.op("dve", lambda e: e.tensor_copy(B["Sbp"][:, :], B["St"][:, :]),
                               reads=[B["St_r"]], writes=[B["Sbp_r"]], deps=[tS0],
                               tag=f"db_Sp{h}")
            B["tSs"] = plan.op("dve", lambda e: e.tensor_scalar_mul(
                B["Sbs"][:, :], B["St"][:, :], scale_q),
                reads=[B["St_r"]], writes=[B["Sbs_r"]], deps=[tS0], tag=f"db_Ss{h}")
            B["tqs"] = tqs; B["tks"] = tks; B["tbet"] = tbet; B["tkb"] = tkb

        def db_step(h, ci, B):
            qs, qs_r = B["qs"], B["qs_r"]
            yield
            ks, ks_r = B["ks"], B["ks_r"]
            yield
            beta, beta_r = B["beta"], B["beta_r"]
            yield
            kb, kb_r = B["kb"], B["kb_r"]
            yield
            St, St_r = B["St"], B["St_r"]
            yield
            Sbp, Sbp_r = B["Sbp"], B["Sbp_r"]
            yield
            Sbs, Sbs_r = B["Sbs"], B["Sbs_r"]
            yield
            Arm, Arm_r = B["Arm"], B["Arm_r"]
            yield
            Rb, Rb_r = B["Rb"], B["Rb_r"]
            yield
            Gt, Gt_r = B["Gt"], B["Gt_r"]
            yield
            atn, atn_r = B["atn"], B["atn_r"]
            yield
            vbc, vbc_r = B["vbc"], B["vbc_r"]
            yield
            ktc, ktc_r = B["ktc"], B["ktc_r"]
            yield
            pb0, pb0_r = B["ptr"], B["ptr_r"]
            yield
            ps0, ps0_r = B["pL"], B["pL_r"]
            yield
            ps1, ps1_r = B["pG"], B["pG_r"]
            yield
            psO, psO_r = B["pO"], B["pO_r"]
            yield
            tbet, tkb, tks, tqs = B["tbet"], B["tkb"], B["tks"], B["tqs"]
            yield
            tSp, tSs = B["tSp"], B["tSs"]
            yield
            cs = slice(ci * C, (ci + 1) * C)
            yield
            tt1 = plan.op("pe", lambda e, cs=cs: e.transpose(
                pb0[0:64, 0:128], beta[:, cs], cst[:, 0:128]),
                reads=[beta_r, cst_r], writes=[pb0_r], deps=tbet,
                tag=f"db_tb{h}_{ci}")
            yield
            tvb = e_dma(K_, "sp", vbc[:, :],
                        A["l1ab_vtm"][ci * C:(ci + 1) * C, h * 128:(h + 1) * 128],
                        writes=[vbc_r], deps=tv, tag=f"db_lv{h}_{ci}")
            yield
            tvb2 = plan.op("dve", lambda e: e.tensor_mul(vbc[:, :], vbc[:, :],
                                                         pb0[0:64, 0:128]),
                           reads=[pb0_r], writes=[vbc_r], deps=[tvb, tt1],
                           tag=f"db_vb{h}_{ci}")
            yield
            tt2 = plan.op("pe", lambda e, cs=cs: e.transpose(
                pb0[0:64, 0:128], ks[:, cs], cst[:, 0:128]),
                reads=[ks_r, cst_r], writes=[pb0_r], deps=tks,
                tag=f"db_tk{h}_{ci}")
            yield
            tkt = plan.op("dve", lambda e: e.tensor_copy(ktc[:, :], pb0[0:64, 0:128]),
                          reads=[pb0_r], writes=[ktc_r], deps=[tt2],
                          tag=f"db_ke{h}_{ci}")
            yield
            tL = plan.op("pe", lambda e, cs=cs: e.matmul(
                ps0[0:64, 0:64], kb[:, cs], ks[:, cs], start=True, stop=True),
                reads=[kb_r, ks_r], writes=[ps0_r], deps=tkb, tag=f"db_L{h}_{ci}")
            yield
            tA = plan.op("dve", lambda e: e.tensor_mul(Arm[:, 0:64], ps0[0:64, 0:64],
                                                       dcst[:, 0:64]),
                         reads=[ps0_r, dcst_r], writes=[Arm_r], deps=[tL, tdc],
                         tag=f"db_A{h}_{ci}")
            yield
            tAc = plan.op("pe", lambda e: e.transpose(pb0[0:64, 0:64], Arm[:, 0:64],
                                                      cst[0:64, 0:64]),
                          reads=[Arm_r, cst_r], writes=[pb0_r], deps=[tA],
                          tag=f"db_Ac{h}_{ci}")
            yield
            tAce = plan.op("dve", lambda e: e.tensor_copy(Arm[:, 64:128],
                                                          pb0[0:64, 0:64]),
                           reads=[pb0_r], writes=[Arm_r], deps=[tAc],
                           tag=f"db_Ae{h}_{ci}")
            yield
            prev_rm, prev_cm = 0, 1
            yield
            tpow = [tA, tAce]
            yield
            for lv in range(1, 6):
                rm_s, cm_s = 2 * lv, 2 * lv + 1
                yield
                t1_ = plan.op("pe", lambda e, a=prev_cm, b=prev_rm: e.matmul(
                    ps0[0:64, 0:64], Arm[:, a * 64:(a + 1) * 64],
                    Arm[:, b * 64:(b + 1) * 64], start=True, stop=True),
                    reads=[Arm_r], writes=[ps0_r], deps=tpow,
                    tag=f"db_p{h}_{ci}_{lv}a")
                yield
                t2_ = plan.op("dve", lambda e, s=rm_s: e.tensor_copy(
                    Arm[:, s * 64:(s + 1) * 64], ps0[0:64, 0:64]),
                    reads=[ps0_r], writes=[Arm_r], deps=[t1_],
                    tag=f"db_p{h}_{ci}_{lv}b")
                yield
                t3_ = plan.op("pe", lambda e, a=prev_rm, b=prev_cm: e.matmul(
                    ps0[0:64, 0:64], Arm[:, a * 64:(a + 1) * 64],
                    Arm[:, b * 64:(b + 1) * 64], start=True, stop=True),
                    reads=[Arm_r], writes=[ps0_r], deps=[t2_],
                    tag=f"db_p{h}_{ci}_{lv}c")
                yield
                t4_ = plan.op("dve", lambda e, s=cm_s: e.tensor_copy(
                    Arm[:, s * 64:(s + 1) * 64], ps0[0:64, 0:64]),
                    reads=[ps0_r], writes=[Arm_r], deps=[t3_],
                    tag=f"db_p{h}_{ci}_{lv}d")
                yield
                tpow = [t2_, t4_]
                yield
                prev_rm, prev_cm = rm_s, cm_s
                yield
            tG1 = plan.op("pe", lambda e, cs=cs: e.matmul(
                ps1[0:64, 0:128], kb[:, cs], Sbp[:, :], start=True, stop=True),
                reads=[kb_r, Sbp_r], writes=[ps1_r], deps=tkb + [tSp],
                tag=f"db_G{h}_{ci}")
            yield
            tG2 = plan.op("dve", lambda e: e.tensor_sub(Gt[:, :], vbc[:, :],
                                                        ps1[0:64, 0:128]),
                          reads=[ps1_r, vbc_r], writes=[Gt_r], deps=[tG1, tvb2],
                          tag=f"db_Ge{h}_{ci}")
            yield
            tR = tG2
            yield
            Rsrc, Rsrc_r = Gt, Gt_r
            yield
            for lv in range(6):
                aslot = 1 if lv == 0 else 2 * lv + 1
                yield
                t1_ = plan.op("pe", lambda e, aslot=aslot, Rsrc=Rsrc: e.matmul(
                    ps1[0:64, 0:128], Arm[:, aslot * 64:(aslot + 1) * 64],
                    Rsrc[:, :], start=True, stop=False),
                    reads=[Arm_r, Rsrc_r], writes=[ps1_r], deps=[tR],
                    tag=f"db_ap{h}_{ci}_{lv}a")
                yield
                t2_ = plan.op("pe", lambda e, Rsrc=Rsrc: e.matmul(
                    ps1[0:64, 0:128], cst[0:64, 0:64], Rsrc[:, :],
                    start=False, stop=True),
                    reads=[cst_r, Rsrc_r], writes=[ps1_r], deps=[t1_],
                    tag=f"db_ap{h}_{ci}_{lv}b")
                yield
                tR = plan.op("dve", lambda e: e.tensor_copy(Rb[:, :],
                                                            ps1[0:64, 0:128]),
                             reads=[ps1_r], writes=[Rb_r], deps=[t2_],
                             tag=f"db_ap{h}_{ci}_{lv}c")
                yield
                Rsrc, Rsrc_r = Rb, Rb_r
                yield
            ta1 = plan.op("pe", lambda e, cs=cs: e.matmul(
                ps0[0:64, 0:64], ks[:, cs], qs[:, cs], start=True, stop=True),
                reads=[ks_r, qs_r], writes=[ps0_r], deps=tks + tqs,
                tag=f"db_at{h}_{ci}")
            yield
            ta2 = plan.op("dve", lambda e: e.tensor_mul(atn[:, :], ps0[0:64, 0:64],
                                                        dcst[:, 64:128]),
                          reads=[ps0_r, dcst_r], writes=[atn_r], deps=[ta1, tdc],
                          tag=f"db_am{h}_{ci}")
            yield
            to1 = plan.op("pe", lambda e, cs=cs: e.matmul(
                psO[:, 0:64], Sbs[:, :], qs[:, cs], start=True, stop=False),
                reads=[Sbs_r, qs_r], writes=[psO_r], deps=[tSs] + tqs,
                tag=f"db_o1{h}_{ci}")
            yield
            to2 = plan.op("pe", lambda e: e.matmul(psO[:, 0:64], Rb[:, :], atn[:, :],
                                                   start=False, stop=True),
                          reads=[Rb_r, atn_r], writes=[psO_r], deps=[tR, ta2],
                          tag=f"db_o2{h}_{ci}")
            yield
            tst = stg.store(psO[:, 0:64], psO_r,
                            A["l1ab_olin"][h * 128:(h + 1) * 128, ci * C:(ci + 1) * C],
                            tok=to2, w=64)
            yield
            olin_toks.setdefault(h, []).append(tst)
            yield
            tS1 = plan.op("pe", lambda e: e.matmul(ps1[:, 0:128], ktc[:, :], Rb[:, :],
                                                   start=True, stop=True),
                          reads=[ktc_r, Rb_r], writes=[ps1_r], deps=[tkt, tR],
                          tag=f"db_S1{h}_{ci}")
            yield
            tS2 = plan.op("dve", lambda e: e.tensor_add(St[:, :], St[:, :],
                                                        ps1[:, 0:128]),
                          reads=[ps1_r], writes=[St_r], deps=[tS1],
                          tag=f"db_S2{h}_{ci}")
            yield
            B["tSp"] = plan.op("dve", lambda e: e.tensor_copy(Sbp[:, :], St[:, :]),
                               reads=[St_r], writes=[Sbp_r], deps=[tS2],
                               tag=f"db_S3{h}_{ci}")
            yield
            B["tSs"] = plan.op("dve", lambda e: e.tensor_scalar_mul(
                Sbs[:, :], St[:, :], scale_q),
                reads=[St_r], writes=[Sbs_r], deps=[tS2],
                tag=f"db_S4{h}_{ci}")
            yield

        for hp in range(Hh // 2):
            for sub in range(2):
                db_prep(2 * hp + sub, bufs[sub])
            for ci in range(NCh):
                gens = [db_step(2 * hp, ci, bufs[0]),
                        db_step(2 * hp + 1, ci, bufs[1])]
                while gens:
                    keep = []
                    for g in gens:
                        try:
                            next(g)
                            keep.append(g)
                        except StopIteration:
                            pass
                    gens = keep
        mo, mo_r = K_.sbuf("db_mo", [128, 2 * TT], BF16)
        mo_rs = [Res("db_mo0"), Res("db_mo1")]
        mx, mx_r0 = K_.sbuf("db_mixx", [128, TT], BF16)
        mix_toks = []
        for h in range(Hh):
            for t in range(4):
                sl_ = h % 2
                tl1 = e_dma(K_, "sp", mo[:, sl_ * TT:(sl_ + 1) * TT],
                            A["l1ab_obase"][h * 128:(h + 1) * 128, t * TT:(t + 1) * TT],
                            writes=[mo_rs[sl_]], deps=tbase, tag=f"db_lb{h}_{t}")
                tl2 = e_dma(K_, "pool", mx[:, :],
                            A["l1ab_olin"][h * 128:(h + 1) * 128, t * TT:(t + 1) * TT],
                            writes=[mx_r0], deps=olin_toks[h],
                            tag=f"db_ll{h}_{t}")
                tmx = plan.op("dve", lambda e, sl_=sl_: e.tensor_add(
                    mx[:, :], mx[:, :], mo[:, sl_ * TT:(sl_ + 1) * TT]),
                    reads=[mo_rs[sl_]], writes=[mx_r0], deps=[tl1, tl2],
                    tag=f"db_mx{h}_{t}")

                def pairs(p, h=h, t=t):
                    return [(A["sh4_o"][p * 1024 + h * 128: p * 1024 + (h + 1) * 128,
                                        t * TT:(t + 1) * TT], mx[:, :])]
                mix_toks.append(e_dma_sw(K_, "sp", pairs, reads=[mx_r0],
                                         tag=f"db_om{h}_{t}"))
        plan.barrier(deps=mix_toks)
        rhs = dram_rhs_loader(K_, "f4wo", A["sh4_o"], NOWN, sw_own=True)
        proj_fm(K_, name="f4wo", wload=wload_nat(K_, A["l1ab_wo"]),
                n_k=NKC, m_list=list(range(16)), t_list=[0, 1], rhs_at=rhs,
                sink=resid_sink(K_, scale=0.5), mg=3)


def host_l1aa_tables_F(p):
    ddT, qd, decay = host_l1aa_tables()
    hs = np.arange(p * 16, (p + 1) * 16)
    ddT_p = np.ascontiguousarray(ddT.reshape(32, 256, 256)[hs].reshape(16 * 256, 256))
    qd_p = np.ascontiguousarray(qd[hs])
    dec = np.zeros((128, 128), np.float32)
    for hl, g in enumerate(hs):
        dec[:, 2 * hl] = decay[:, 2 * g]
        dec[:, 2 * hl + 1] = decay[:, 2 * g + 1]
        dec[:, 64 + hl] = decay[:, 64 + g]
    return ddT_p, qd_p, dec


def declare_fused(nc):
    A = {}
    A["x0"] = nc.dram_tensor("x0", [2048, NOWN], F32, kind="ExternalInput").ap()
    A["xo"] = nc.dram_tensor("xo", [2048, NOWN], F32, kind="ExternalOutput").ap()
    A["cst_d"] = nc.dram_tensor("cst", [128, 512], BF16, kind="ExternalInput").ap()

    def inp(name, shape, dt=F32):
        A[name] = nc.dram_tensor(name, shape, dt, kind="ExternalInput").ap()

    def loc(name, shape, dt=BF16):
        A[name] = nc.dram_tensor(name, shape, dt, kind="Internal").ap()

    def sh(name, shape):
        A[name] = nc.dram_tensor(name, shape, BF16, kind="Internal",
                                 addr_space="Shared").ap()

    for w, rows in (("wq", 2048), ("wk", 1024), ("wv", 1024), ("wo", 2048)):
        inp(f"l0aa_{w}", [rows, 2048])
        loc(f"l0aa_{w}_bf", [rows, 2048])
    sh("sh1_y", [2048, 2048]); sh("sh1_q", [2048, 2048]); sh("sh1_k", [1024, 2048])
    sh("sh1_vt", [2048, 1024]); sh("sh1_a", [2048, 2048])
    inp("l0f_w", [2048, 2048]); inp("l0f_pcol", [128, 64])
    loc("l0f_ws_bf", [2048, 2048]); loc("l0f_col", [128, 1], F32)
    for w in ("wq", "wk", "wv", "wo"):
        inp(f"l0ab_{w}", [2048, 2048])
    inp("ab_bcol", [128, 48]); inp("ab_bvrow", [1, 2048])
    inp("ab_btab", [16 * 128, 64]); inp("ab_mtab", [128, 16 * 256], BF16)
    inp("ab_qrow", [16, NOWN], BF16)
    sh("sh2_y", [2048, 2048]); sh("sh2_q", [2048, 2048]); sh("sh2_k", [2048, 2048])
    sh("sh2_vt", [2048, 2048]); sh("sh2_a", [2048, 2048])
    inp("l1aa_qkv", [3072, 2048]); inp("l1aa_gw", [2048, 2048])
    inp("l1aa_ow", [2048, 2048]); inp("l1aa_nw", [128, 16])
    inp("l1aa_ddT", [16 * 256, 256], BF16); inp("l1aa_qd", [16, 256], BF16)
    inp("l1aa_decay", [128, 128])
    loc("l1aa_qk_bf", [2048, 2048]); loc("l1aa_v_bf", [1024, 2048])
    loc("l1aa_gw_bf", [2048, 2048]); loc("l1aa_ow_bf", [2048, 2048])
    loc("l1aa_qkf", [2048, 2048]); loc("l1aa_vtm", [2048, 1024])
    sh("sh3_y", [2048, 2048]); sh("sh3_a", [2048, 2048])
    inp("moe_gw", [8, 2048]); inp("moe_gb", [1, 8])
    inp("moe_w1t", [2048, 9 * 1024]); inp("moe_w3t", [2048, 9 * 1024])
    inp("moe_w2t", [1024, 9 * 2048])
    loc("moe_dwd", [8, NOWN])
    sh("moe_w1t_bf", [2048, 9 * 1024]); sh("moe_w3t_bf", [2048, 9 * 1024])
    sh("moe_w2t_bf", [1024, 9 * 2048])
    for w in ("wq", "wk", "wv"):
        inp(f"l1ab_{w}", [2048, 1024])
    inp("l1ab_wo", [2048, 2048])
    inp("abg_btab", [8 * 128, 128]); inp("abg_mtab", [128, 16 * 256], BF16)
    inp("db_cst", [64, 128])
    loc("db_cold", [128, 1], F32); loc("db_cold_1", [128, 1], F32)
    loc("l1ab_qraw", [1024, 2048]); loc("l1ab_kraw", [1024, 2048])
    loc("l1ab_vtm", [2048, 1024])
    loc("l1ab_obase", [1024, 2048]); loc("l1ab_olin", [1024, 2048])
    sh("sh4_y", [2048, 2048]); sh("sh4_o", [2048, 2048])
    return A


def build_fused():
    nc = bass.Bass("TRN2", target_bir_lowering=False, num_devices=8)
    A = declare_fused(nc)
    with contextlib.ExitStack() as stack:
        plan = Plan(nc)
        K_ = K(nc, plan, stack)
        cstsb, cst_r = K_.sbuf("cstsb", [128, 512], BF16)
        e_dma(K_, "sp", cstsb[:, :], A["cst_d"][:, :], writes=[cst_r], tag="ldcst")
        K_.cst = (cstsb, cst_r)
        x_sb, _ = K_.sbuf("x_res", [128, NKC * NOWN], F32)
        K_.x_sb = x_sb
        K_.x_rs = [Res(f"x_res{k}") for k in range(NKC)]
        for k in range(NKC):
            e_dma(K_, "sp", x_sb[:, k * NOWN:(k + 1) * NOWN],
                  A["x0"][k * 128:(k + 1) * 128, :], writes=[K_.x_rs[k]],
                  tag=f"ldx{k}")
        K_.moe_cast_tok = emit_moe_casts(K_, A)
        emit_l0aa_F(K_, A)
        emit_l0ffn_F(K_, A, True)
        emit_l0ab_F(K_, A)
        emit_l0ffn_F(K_, A, False)
        emit_l1aa_F(K_, A)
        emit_moe_F(K_, A, True)
        emit_l1ab_F(K_, A)
        emit_moe_F(K_, A, False)
        for k in range(NKC):
            K_.store_toks.append(e_dma(K_, "sp", A["xo"][k * 128:(k + 1) * 128, :],
                                       x_sb[:, k * NOWN:(k + 1) * NOWN],
                                       reads=[K_.x_rs[k]], tag=f"stx{k}"))
        plan.finalize(stack, K_.store_toks)
    return nc



_FUSED_NC = [None]


def _pcol(gamma, beta, lnw, lnb):
    p = np.zeros((128, 64), np.float32)
    for m in range(16):
        sl = slice(m * 128, (m + 1) * 128)
        p[:, m] = gamma[sl]; p[:, 16 + m] = beta[sl]
        p[:, 32 + m] = lnw[sl]; p[:, 48 + m] = lnb[sl]
    return p


def _cols16(v):
    p = np.zeros((128, 16), np.float32)
    for m in range(16):
        p[:, m] = v[m * 128:(m + 1) * 128]
    return p


def kernel(**inputs):
    I = {k: np.ascontiguousarray(np.asarray(v)) for k, v in inputs.items()}
    x = I["L0_attn_a_hidden_states"].astype(np.float32)
    CST = host_cst()
    pc = _pcol(I["L0_ffn_gamma"], I["L0_ffn_beta"], I["L0_ffn_ln_weight"],
               I["L0_ffn_ln_bias"])
    bcol = np.zeros((128, 48), np.float32)
    for m in range(16):
        sl = slice(m * 128, (m + 1) * 128)
        bcol[:, m] = I["L0_attn_b_bq"][sl]
        bcol[:, 16 + m] = I["L0_attn_b_bk"][sl]
        bcol[:, 32 + m] = I["L0_attn_b_bo"][sl]
    abt = [host_striped_tables(p, 16, True, 256) for p in range(2)]
    aat = [host_l1aa_tables_F(p) for p in range(2)]
    W1T = np.ascontiguousarray(np.concatenate(
        [I["L1_ffn_W1"].transpose(2, 0, 1).reshape(2048, 8192),
         I["L1_ffn_Ws1"].T], axis=1))
    W3T = np.ascontiguousarray(np.concatenate(
        [I["L1_ffn_W3"].transpose(2, 0, 1).reshape(2048, 8192),
         I["L1_ffn_Ws3"].T], axis=1))
    W2T = np.ascontiguousarray(np.concatenate(
        [I["L1_ffn_W2"].transpose(2, 0, 1).reshape(1024, 16384),
         I["L1_ffn_Ws2"].T], axis=1))
    nw = _cols16(I["L1_attn_a_norm_w"])
    btab, mtab, dcst = host_l1ab_tables(2048)
    ims = []
    for c in range(8):
        b, p = c // 2, c % 2
        im = {
            "x0": np.ascontiguousarray(x[b, p * 1024:(p + 1) * 1024].T),
            "cst": CST,
            "l0aa_wq": I["L0_attn_a_wq"], "l0aa_wk": I["L0_attn_a_wk"],
            "l0aa_wv": I["L0_attn_a_wv"], "l0aa_wo": I["L0_attn_a_wo"],
            "l0f_w": I["L0_ffn_weight"], "l0f_pcol": pc,
            "l0ab_wq": I["L0_attn_b_wq"], "l0ab_wk": I["L0_attn_b_wk"],
            "l0ab_wv": I["L0_attn_b_wv"], "l0ab_wo": I["L0_attn_b_wo"],
            "ab_bcol": bcol, "ab_bvrow": I["L0_attn_b_bv"].reshape(1, 2048),
            "ab_btab": abt[p][0], "ab_mtab": abt[p][1], "ab_qrow": abt[p][2],
            "l1aa_qkv": np.ascontiguousarray(
                I["L1_attn_a_qkv_w"][p * 3072:(p + 1) * 3072]),
            "l1aa_gw": I["L1_attn_a_gate_w"], "l1aa_ow": I["L1_attn_a_out_w"],
            "l1aa_nw": nw, "l1aa_ddT": aat[p][0], "l1aa_qd": aat[p][1],
            "l1aa_decay": aat[p][2],
            "moe_gw": I["L1_ffn_gate_w"], "moe_gb": I["L1_ffn_gate_b"].reshape(1, 8),
            "moe_w1t": W1T, "moe_w3t": W3T, "moe_w2t": W2T,
            "l1ab_wq": np.ascontiguousarray(
                I["L1_attn_b_Wq"][:, p * 1024:(p + 1) * 1024]),
            "l1ab_wk": np.ascontiguousarray(
                I["L1_attn_b_Wk"][:, p * 1024:(p + 1) * 1024]),
            "l1ab_wv": np.ascontiguousarray(
                I["L1_attn_b_Wv"][:, p * 1024:(p + 1) * 1024]),
            "l1ab_wo": I["L1_attn_b_Wo"],
            "abg_btab": np.ascontiguousarray(btab[0:1024, :]),
            "abg_mtab": mtab, "db_cst": dcst,
        }
        ims.append(im)
    if _FUSED_NC[0] is None:
        _FUSED_NC[0] = build_fused()
    res = run_bass_kernel_spmd(_FUSED_NC[0], ims, core_ids=list(range(8)))
    out = np.empty((4, 2048, 2048), np.float32)
    for c in range(8):
        b, p = c // 2, c % 2
        out[b, p * 1024:(p + 1) * 1024] = res.results[c]["xo"].T
    return out
